# version 08
# baseline (speedup 1.0000x reference)
import numpy as np
import ml_dtypes
from contextlib import ExitStack

import concourse.bass as bass
import concourse.mybir as mybir
from concourse.bass_utils import run_bass_kernel_spmd

F32 = mybir.dt.float32
BF16 = mybir.dt.bfloat16
I32 = mybir.dt.int32
AF = mybir.ActivationFunctionType
ALU = mybir.AluOpType
AX = mybir.AxisListType

D = 1024
NC = 8
DFF = 4096
NFF = 32
NTOK = 2048
NSEQ = 8192
HEADS = 16
HDIM = 64
NBF = 256
LN_EPS = 1e-5
RMS_EPS = 1e-8
PHI_EPS = 1e-6
DATA_NORM = float(np.float32(64.0 ** -0.25))
NITER_BISECT = 40
ROUTER_K = 1024.0


class Prog:
    SECTIONS = {
        "tensor": ("pe", 1),
        "vector": ("ve", 1),
        "scalar": ("sc", 1),
        "gpsimd": ("gp", 1),
    }

    def __init__(self, nc):
        self.nc = nc
        self.es = ExitStack()
        self.items = {s: [] for s in
                      ("tensor", "vector", "scalar", "gpsimd", "sync")}
        self.counts = {}
        self.n_cc = 0
        self.final_deps = []

    def _add(self, section, sem, inc, fn, deps):
        n = self.counts.get(sem, 0) + inc
        self.counts[sem] = n
        self.items[section].append((fn, [d for d in deps if d is not None],
                                    sem, inc))
        return (sem, n)

    def t(self, fn, deps=()):
        return self._add("tensor", "pe", 1, fn, deps)

    def v(self, fn, deps=()):
        return self._add("vector", "ve", 1, fn, deps)

    def s(self, fn, deps=()):
        return self._add("scalar", "sc", 1, fn, deps)

    def g(self, fn, deps=()):
        return self._add("gpsimd", "gp", 1, fn, deps)

    def sd(self, fn, deps=(), ch="ld"):
        return self._add("sync", f"syd_{ch}", 16, fn, deps)

    def ad(self, fn, deps=(), ch="ld"):
        return self._add("scalar", f"scd_{ch}", 16, fn, deps)

    def gd(self, fn, deps=(), ch="ld"):
        return self._add("gpsimd", f"gpd_{ch}", 16, fn, deps)

    def total(self, queue, ch):
        name = f"{queue}_{ch}"
        return (name, self.counts.get(name, 0))

    def col(self, fn, deps=()):
        self.n_cc += 1
        sem = f"cc{self.n_cc}"
        return self._add("gpsimd", sem, 1, fn, deps)

    def finish(self, deps):
        self.final_deps = list(deps)

    def build(self):
        nc = self.nc
        sems = {}
        for name in self.counts:
            sems[name] = self.es.enter_context(nc.semaphore(name))
        block = self.es.enter_context(nc.Block())
        hw = {}

        def coalesce(deps):
            best = {}
            for dsem, dval in deps:
                if best.get(dsem, 0) < dval:
                    best[dsem] = dval
            return best.items()

        def emit(section, eng):
            for fn, deps, sem, inc in self.items[section]:
                for dsem, dval in coalesce(deps):
                    key = (section, dsem)
                    if hw.get(key, 0) < dval:
                        hw[key] = dval
                        eng.wait_ge(sems[dsem], dval)
                ins = fn(eng)
                ins.then_inc(sems[sem], inc)
            if section == "sync":
                for dsem, dval in coalesce(self.final_deps):
                    key = (section, dsem)
                    if hw.get(key, 0) < dval:
                        hw[key] = dval
                        eng.wait_ge(sems[dsem], dval)

        if self.items["tensor"]:
            block.tensor(lambda t: emit("tensor", t))
        if self.items["vector"]:
            block.vector(lambda v: emit("vector", v))
        if self.items["scalar"]:
            block.scalar(lambda s: emit("scalar", s))
        if self.items["gpsimd"]:
            block.gpsimd(lambda g: emit("gpsimd", g))
        block.sync(lambda s: emit("sync", s))
        return self


def sb(nc, es, name, shape, dt=F32):
    return es.enter_context(nc.sbuf_tensor(name, shape, dt))


def ps(nc, es, name, shape, dt=F32):
    return es.enter_context(nc.psum_tensor(name, shape, dt))


def emit_bisect(P, s_rwf, s_rwo, s_ones2, s_oner, p_cnt, p_bc, st,
                dep_rwf, dep_rwo, dep_ident, dep_oner, dep_ones2,
                k=ROUTER_K, niter=NITER_BISECT):
    rmax, rmin = st.get("rmax"), st.get("rmin")
    t_lo, t_hi, t_mid = st["t_lo"], st["t_hi"], st["t_mid"]
    fs, fsn, rowcnt = st["fs"], st["fsn"], st["rowcnt"]
    mask64, maskrow = st["mask64"], st["maskrow"]

    if st.get("fixed_bounds"):
        c1 = P.v(lambda v: v.memset(t_hi[:], 1e4), [dep_rwf])
        c2 = P.v(lambda v: v.memset(t_lo[:], -1e4), [c1])
        last = P.v(lambda v: v.memset(t_mid[:], 0.0), [c2])
    else:
        ident = st["ident"]
        a = P.v(lambda v: v.reduce_max(rmax[:], s_rwf[:], axis=AX.X),
                [dep_rwf])
        b = P.v(lambda v: v.tensor_reduce(rmin[:], s_rwf[:], axis=AX.X,
                                          op=ALU.min), [dep_rwf, a])
        tr1 = P.t(lambda t: t.transpose(p_bc[0:1, 0:128], rmax[:],
                                        ident[:]), [a, dep_ident])
        tr2 = P.t(lambda t: t.transpose(p_bc[0:1, 128:256], rmin[:],
                                        ident[:]), [b])
        gh = P.v(lambda v: v.tensor_reduce(st["g_hi"][:], p_bc[0:1, 0:128],
                                           axis=AX.X, op=ALU.max),
                 [tr1, tr2])
        gl = P.v(lambda v: v.tensor_reduce(st["g_lo"][:],
                                           p_bc[0:1, 128:256],
                                           axis=AX.X, op=ALU.min),
                 [tr2, gh])
        bh = P.t(lambda t: t.matmul(p_bc[:, 256:257], lhsT=s_oner[:],
                                    rhs=st["g_hi"][:], start=True,
                                    stop=True), [gh, dep_oner])
        bl = P.t(lambda t: t.matmul(p_bc[:, 257:258], lhsT=s_oner[:],
                                    rhs=st["g_lo"][:], start=True,
                                    stop=True), [gl])
        c1 = P.v(lambda v: v.tensor_copy(t_hi[:], p_bc[:, 256:257]),
                 [bh, bl])
        c2 = P.v(lambda v: v.tensor_copy(t_lo[:], p_bc[:, 257:258]),
                 [bl, c1])
        c3 = P.v(lambda v: v.tensor_tensor(t_mid[:], t_lo[:], t_hi[:],
                                           ALU.add), [c2])
        last = P.v(lambda v: v.tensor_scalar(t_mid[:], t_mid[:], 0.5, None,
                                             op0=ALU.mult), [c3])
    prev_f1 = None
    for i in range(niter):
        slot = i % 2
        mk = P.v(lambda v: v.tensor_scalar(
            mask64[:], s_rwf[:], t_mid[:], 0.0, op0=ALU.is_ge, op1=ALU.add,
            accum_out=rowcnt[:]), [last, prev_f1])
        cm = P.t(lambda t, _s=slot: t.matmul(
            p_cnt[:, _s:_s + 1], lhsT=s_ones2[:], rhs=rowcnt[:],
            start=True, stop=True), [mk, dep_ones2, prev_f1])
        f1 = P.v(lambda v, _s=slot: v.tensor_scalar(
            fs[:], p_cnt[:, _s:_s + 1], k, None, op0=ALU.is_ge), [cm])
        f2 = P.v(lambda v: v.tensor_scalar(fsn[:], fs[:], -1, 1,
                                           op0=ALU.mult, op1=ALU.add), [f1])
        s1 = P.v(lambda v: v.copy_predicated(t_lo[:], fs[:], t_mid[:]), [f2])
        s2 = P.v(lambda v: v.copy_predicated(t_hi[:], fsn[:], t_mid[:]), [s1])
        s3 = P.v(lambda v: v.tensor_tensor(t_mid[:], t_lo[:], t_hi[:],
                                           ALU.add), [s2])
        last = P.v(lambda v: v.tensor_scalar(t_mid[:], t_mid[:], 0.5, None,
                                             op0=ALU.mult), [s3])
        prev_f1 = f1
    if maskrow is None:
        return last
    mrow = P.v(lambda v: v.tensor_scalar(
        maskrow[:], s_rwo[:], t_lo[0:1, 0:1], None, op0=ALU.is_ge),
        [last, dep_rwo])
    return mrow


def emit_ln(P, n, i, w, s_y, s_out, s_m, s_r, s_tmp, s_trow, s_onec, s_oner,
            p_s1, p_s2, p_mbc, p_rbc, ytoks, war, s_eps=None,
            extra_out_dep=None):
    s1tok = s2tok = None
    sqprev = war.get("sq")
    for j in range(NC):
        off = j * n + i * 512
        s1tok = P.t(lambda t, _o=off, _w=w, _j=j: t.matmul(
            p_s1[:, 0:_w], lhsT=s_onec[:], rhs=s_y[:, _o:_o + _w],
            start=(_j == 0), stop=(_j == NC - 1)),
            [ytoks[j], war.get("s1")])
        sq = P.s(lambda s, _o=off, _w=w: s.activation(
            s_tmp[:, 0:_w], s_y[:, _o:_o + _w], AF.Square),
            [ytoks[j], sqprev])
        s2tok = P.t(lambda t, _w=w, _j=j: t.matmul(
            p_s2[:, 0:_w], lhsT=s_onec[:], rhs=s_tmp[:, 0:_w],
            start=(_j == 0), stop=(_j == NC - 1)), [sq, war.get("s2")])
        sqprev = s2tok
    war["sq"] = sqprev
    o = i * 512 if s_m.shape[1] > 512 else 0
    f1 = P.v(lambda v, _w=w, _o=o: v.tensor_scalar(
        s_m[:, _o:_o + _w], p_s1[:, 0:_w], 1.0 / D, None, op0=ALU.mult),
        [s1tok])
    f2 = P.v(lambda v, _w=w, _o=o: v.tensor_scalar(
        s_r[:, _o:_o + _w], p_s2[:, 0:_w], 1.0 / D, None, op0=ALU.mult),
        [s2tok])
    war["s1"] = f1
    war["s2"] = f2
    f3 = P.v(lambda v, _w=w, _o=o: v.tensor_tensor(
        s_trow[:, 0:_w], s_m[:, _o:_o + _w], s_m[:, _o:_o + _w],
        ALU.mult), [f1, f2, war.get("trow")])
    f4 = P.v(lambda v, _w=w, _o=o: v.tensor_tensor(
        s_r[:, _o:_o + _w], s_r[:, _o:_o + _w], s_trow[:, 0:_w],
        ALU.subtract), [f3])
    war["trow"] = f4
    f5 = P.s(lambda s, _w=w, _o=o: s.activation(
        s_r[:, _o:_o + _w], s_r[:, _o:_o + _w], AF.Sqrt,
        bias=s_eps[:]), [f4, war.get("eps_dep")])
    war["eps_dep"] = None
    f6 = P.v(lambda v, _w=w, _o=o: v.reciprocal(
        s_r[:, _o:_o + _w], s_r[:, _o:_o + _w]), [f5])
    b1 = P.t(lambda t, _w=w, _o=o: t.matmul(
        p_mbc[:, 0:_w], lhsT=s_oner[:], rhs=s_m[:, _o:_o + _w],
        start=True, stop=True), [f1, war.get("bc")])
    b2 = P.t(lambda t, _w=w, _o=o: t.matmul(
        p_rbc[:, 0:_w], lhsT=s_oner[:], rhs=s_r[:, _o:_o + _w],
        start=True, stop=True), [f6])
    outtoks = []
    for j in range(NC):
        off = j * n + i * 512
        n1 = P.v(lambda v, _o=off, _w=w: v.tensor_tensor(
            s_out[:, _o:_o + _w], s_y[:, _o:_o + _w], p_mbc[:, 0:_w],
            ALU.subtract), [b1, b2, ytoks[j], extra_out_dep])
        n2 = P.v(lambda v, _o=off, _w=w: v.tensor_tensor(
            s_out[:, _o:_o + _w], s_out[:, _o:_o + _w], p_rbc[:, 0:_w],
            ALU.mult), [n1])
        outtoks.append(n2)
    war["bc"] = outtoks[-1]
    return outtoks


def fmview(ap, n):
    return ap.rearrange("(a p) n -> p a n", p=128)


def build_a1(n=NTOK):
    nc = bass.Bass("TRN2", target_bir_lowering=False, debug=False)
    XT = nc.dram_tensor("xt", [D, n], F32, kind="ExternalInput").ap()
    RWC = nc.dram_tensor("rwc", [128, NC], F32, kind="ExternalInput").ap()
    RB = nc.dram_tensor("rb", [1, 1], F32, kind="ExternalInput").ap()
    RWS = nc.dram_tensor("rws", [1, n], F32, kind="ExternalOutput").ap()

    P = Prog(nc)
    es = P.es
    nsub = (n + 511) // 512
    s_x = sb(nc, es, "s_x", [128, NC * n])
    s_rwc = sb(nc, es, "s_rwc", [128, NC])
    s_rb = sb(nc, es, "s_rb", [1, 1])
    s_rw = sb(nc, es, "s_rw", [1, n])
    p_rw = [ps(nc, es, f"p_rw{i}", [1, 512]) for i in range(2)]

    P.sd(lambda s: s.dma_start(
        out=s_x[:].rearrange("p (a n) -> p a n", a=NC), in_=fmview(XT, n)))
    P.sd(lambda s: s.dma_start(out=s_rwc[:], in_=RWC))
    P.sd(lambda s: s.dma_start(out=s_rb[:], in_=RB))
    lall = P.total("syd", "ld")
    lx = lw = lb = lall

    evs = []
    ev_prev = [None, None]
    for i in range(nsub):
        w = min(512, n - i * 512)
        pb = p_rw[i % 2]
        mmtok = None
        for j in range(NC):
            mmtok = P.t(lambda t, _i=i, _j=j, _w=w, _pb=pb: t.matmul(
                _pb[:, 0:_w], lhsT=s_rwc[:, _j:_j + 1],
                rhs=s_x[:, _j * n + _i * 512:_j * n + _i * 512 + _w],
                start=(_j == 0), stop=(_j == NC - 1)),
                [lx, lw, ev_prev[i % 2]])
        ev = P.v(lambda v, _i=i, _w=w, _pb=pb: v.tensor_scalar(
            s_rw[:, _i * 512:_i * 512 + _w], _pb[:, 0:_w], s_rb[:], None,
            op0=ALU.add), [mmtok, lb])
        ev_prev[i % 2] = ev
        evs.append(ev)
    stok = P.sd(lambda s: s.dma_start(out=RWS, in_=s_rw[:]), evs, ch="st")
    P.finish([stok])
    return P.build().nc


def build_a2(n=NTOK, niter=NITER_BISECT):
    nc = bass.Bass("TRN2", target_bir_lowering=False, debug=False)
    XT = nc.dram_tensor("xt", [D, n], F32, kind="ExternalInput").ap()
    XRT = nc.dram_tensor("xrevt", [D, n], F32, kind="ExternalInput").ap()
    RWF = nc.dram_tensor("rwf", [128, 64], F32, kind="ExternalInput").ap()
    RWO = nc.dram_tensor("rwo", [1, n], F32, kind="ExternalInput").ap()
    IDENT = nc.dram_tensor("ident", [128, 128], F32, kind="ExternalInput").ap()
    X1T = nc.dram_tensor("x1t", [D, n], F32, kind="ExternalOutput").ap()

    P = Prog(nc)
    es = P.es
    s_x = sb(nc, es, "s_x", [128, NC * n])
    s_xr = sb(nc, es, "s_xr", [128, NC * n])
    s_y = s_xr
    s_rwf = sb(nc, es, "s_rwf", [128, 64])
    s_rwo = sb(nc, es, "s_rwo", [1, n])
    s_ones2 = sb(nc, es, "s_ones2", [128, 128])
    s_oner = sb(nc, es, "s_oner", [1, 128])
    s_onec = sb(nc, es, "s_onec", [128, 1])
    s_ident = sb(nc, es, "s_ident", [128, 128])
    st = {kk: sb(nc, es, kk, [128, 1]) for kk in
          ("rmax", "rmin", "t_lo", "t_hi", "t_mid", "rowcnt")}
    st["fs"] = sb(nc, es, "fs", [128, 1], I32)
    st["fsn"] = sb(nc, es, "fsn", [128, 1], I32)
    st["g_hi"] = sb(nc, es, "g_hi", [1, 1])
    st["g_lo"] = sb(nc, es, "g_lo", [1, 1])
    st["mask64"] = sb(nc, es, "mask64", [128, 64])
    st["maskrow"] = sb(nc, es, "maskrow", [1, n])
    st["ident"] = s_ident
    s_m = sb(nc, es, "s_m", [1, n])
    s_r = sb(nc, es, "s_r", [1, n])
    s_tmp = sb(nc, es, "s_tmp", [128, 512])
    s_trow = sb(nc, es, "s_trow", [1, 512])
    s_eps = sb(nc, es, "s_eps", [1, 1])
    p_cnt = ps(nc, es, "p_cnt", [128, 8])
    p_bc = ps(nc, es, "p_bc", [128, 512])
    p_s1 = ps(nc, es, "p_s1", [1, 512])
    p_s2 = ps(nc, es, "p_s2", [1, 512])
    p_mbc = ps(nc, es, "p_mbc", [128, 512])
    p_rbc = ps(nc, es, "p_rbc", [128, 512])
    p_msk = ps(nc, es, "p_msk", [128, 512])

    P.sd(lambda s: s.dma_start(
        out=s_x[:].rearrange("p (a n) -> p a n", a=NC), in_=fmview(XT, n)))
    P.sd(lambda s: s.dma_start(
        out=s_xr[:].rearrange("p (a n) -> p a n", a=NC), in_=fmview(XRT, n)))
    P.sd(lambda s: s.dma_start(out=s_rwf[:], in_=RWF))
    P.sd(lambda s: s.dma_start(out=s_rwo[:], in_=RWO))
    P.sd(lambda s: s.dma_start(out=s_ident[:], in_=IDENT))
    lall = P.total("syd", "ld")
    lx = lxr = lrwf = lrwo = lid = lall
    m1 = P.v(lambda v: v.memset(s_ones2[:], 1.0))
    m2 = P.v(lambda v: v.memset(s_oner[:], 1.0))
    m3 = P.v(lambda v: v.memset(s_onec[:], 1.0))
    m4 = P.v(lambda v: v.memset(s_eps[:], LN_EPS))

    mrow = emit_bisect(P, s_rwf, s_rwo, s_ones2, s_oner, p_cnt, p_bc, st,
                       lrwf, lrwo, lid, m2, m1, niter=niter)

    nsub = (n + 511) // 512
    stoks = []
    war = {"eps_dep": m4}
    prev_msk = None
    for i in range(nsub):
        w = min(512, n - i * 512)
        mb = P.t(lambda t, _i=i, _w=w: t.matmul(
            p_msk[:, 0:_w], lhsT=s_oner[:],
            rhs=st["maskrow"][:, _i * 512:_i * 512 + _w],
            start=True, stop=True), [mrow, m2, prev_msk])
        ytoks = []
        for j in range(NC):
            off = j * n + i * 512
            ym = P.v(lambda v, _o=off, _w=w: v.tensor_tensor(
                s_y[:, _o:_o + _w], s_xr[:, _o:_o + _w], p_msk[:, 0:_w],
                ALU.mult), [mb, lxr])
            ya = P.v(lambda v, _o=off, _w=w: v.tensor_tensor(
                s_y[:, _o:_o + _w], s_y[:, _o:_o + _w], s_x[:, _o:_o + _w],
                ALU.add), [ym, lx])
            ytoks.append(ya)
        prev_msk = ytoks[-1]
        outtoks = emit_ln(P, n, i, w, s_y, s_x, s_m, s_r, s_tmp, s_trow,
                          s_onec, s_oner, p_s1, p_s2, p_mbc, p_rbc, ytoks, war,
                          s_eps=s_eps)
        stok = P.sd(lambda s, _i=i, _w=w: s.dma_start(
            out=fmview(X1T, n)[:, :, _i * 512:_i * 512 + _w],
            in_=s_x[:].rearrange("p (a n) -> p a n", a=NC)[
                :, :, _i * 512:_i * 512 + _w]), outtoks, ch="st")
        stoks.append(stok)
    P.finish(stoks)
    return P.build().nc


def build_b1():
    nc = bass.Bass("TRN2", target_bir_lowering=False, debug=False)
    WU = nc.dram_tensor("wupt_s", [D, 512], F32, kind="ExternalInput").ap()
    WD = nc.dram_tensor("wdnt_s", [DFF, 128], F32, kind="ExternalInput").ap()
    AP_ = nc.dram_tensor("aparts", [1, 2], F32, kind="ExternalOutput").ap()

    P = Prog(nc)
    es = P.es
    s_wu = sb(nc, es, "s_wu", [128, NC * 512])
    s_wd = sb(nc, es, "s_wd", [128, NFF * 128])
    s_acc = sb(nc, es, "s_acc", [128, NC + NFF])
    s_red = sb(nc, es, "s_red", [128, 2])
    s_onec = sb(nc, es, "s_onec", [128, 1])
    s_out = sb(nc, es, "s_out", [1, 2])
    p_o = ps(nc, es, "p_o", [1, 8])

    P.sd(lambda s: s.dma_start(
        out=s_wu[:].rearrange("p (a n) -> p a n", a=NC),
        in_=WU.rearrange("(a p) n -> p a n", p=128)))
    P.sd(lambda s: s.dma_start(
        out=s_wd[:].rearrange("p (a n) -> p a n", a=NFF),
        in_=WD.rearrange("(a p) n -> p a n", p=128)))
    lall = P.total("syd", "ld")
    m1 = P.v(lambda v: v.memset(s_onec[:], 1.0))

    rtoks = []
    for j in range(NC):
        rtoks.append(P.v(lambda v, _j=j: v.tensor_reduce(
            s_acc[:, _j:_j + 1], s_wu[:, _j * 512:(_j + 1) * 512],
            axis=AX.X, op=ALU.add, apply_absolute_value=True), [lall]))
    for j in range(NFF):
        rtoks.append(P.v(lambda v, _j=j: v.tensor_reduce(
            s_acc[:, NC + _j:NC + _j + 1], s_wd[:, _j * 128:(_j + 1) * 128],
            axis=AX.X, op=ALU.add, apply_absolute_value=True), [lall]))
    r1 = P.v(lambda v: v.tensor_reduce(
        s_red[:, 0:1], s_acc[:, 0:NC], axis=AX.X, op=ALU.add), rtoks)
    r2 = P.v(lambda v: v.tensor_reduce(
        s_red[:, 1:2], s_acc[:, NC:NC + NFF], axis=AX.X, op=ALU.add), rtoks)
    mm = P.t(lambda t: t.matmul(p_o[0:1, 0:2], lhsT=s_onec[:],
                                rhs=s_red[:, 0:2], start=True, stop=True),
             [r1, r2, m1])
    ev = P.v(lambda v: v.tensor_copy(s_out[:], p_o[0:1, 0:2]), [mm])
    stok = P.sd(lambda s: s.dma_start(out=AP_, in_=s_out[:]), [ev], ch="st")
    P.finish([stok])
    return P.build().nc


def build_b1b():
    nc = bass.Bass("TRN2", target_bir_lowering=False, debug=False)
    WU = nc.dram_tensor("wupt_s", [D, 512], F32, kind="ExternalInput").ap()
    WD = nc.dram_tensor("wdnt_s", [DFF, 128], F32, kind="ExternalInput").ap()
    APS = nc.dram_tensor("aparts8", [1, 16], F32, kind="ExternalInput").ap()
    WTU = nc.dram_tensor("wtu_s", [D, 512], BF16, kind="ExternalOutput").ap()
    WTD = nc.dram_tensor("wtd_s", [DFF, 128], BF16, kind="ExternalOutput").ap()

    P = Prog(nc)
    es = P.es
    s_wu = sb(nc, es, "s_wu", [128, NC * 512])
    s_wd = sb(nc, es, "s_wd", [128, NFF * 128])
    s_aps = sb(nc, es, "s_aps", [1, 16])
    s_th = sb(nc, es, "s_th", [1, 2])
    s_thc = sb(nc, es, "s_thc", [128, 2])
    s_oner = sb(nc, es, "s_oner", [1, 128])
    s_msk = sb(nc, es, "s_msk", [128, 512])
    s_sgn = sb(nc, es, "s_sgn", [128, 512])
    s_wtu = sb(nc, es, "s_wtu", [128, NC * 512], BF16)
    s_wtd = sb(nc, es, "s_wtd", [128, NFF * 128], BF16)
    p_b = ps(nc, es, "p_b", [128, 8])

    P.sd(lambda s: s.dma_start(
        out=s_wu[:].rearrange("p (a n) -> p a n", a=NC),
        in_=WU.rearrange("(a p) n -> p a n", p=128)))
    P.sd(lambda s: s.dma_start(
        out=s_wd[:].rearrange("p (a n) -> p a n", a=NFF),
        in_=WD.rearrange("(a p) n -> p a n", p=128)))
    P.sd(lambda s: s.dma_start(out=s_aps[:], in_=APS))
    lall = P.total("syd", "ld")
    m1 = P.v(lambda v: v.memset(s_oner[:], 1.0))
    a1 = P.v(lambda v: v.tensor_reduce(
        s_th[:, 0:1], s_aps[0:1, 0:16:2], axis=AX.X, op=ALU.add), [lall])
    a2 = P.v(lambda v: v.tensor_reduce(
        s_th[:, 1:2], s_aps[0:1, 1:16:2], axis=AX.X, op=ALU.add), [lall, a1])
    a3 = P.v(lambda v: v.tensor_scalar(
        s_th[:, 0:2], s_th[:, 0:2], 0.5 / (D * DFF), None, op0=ALU.mult),
        [a1, a2])
    bb = P.t(lambda t: t.matmul(p_b[:, 0:2], lhsT=s_oner[:], rhs=s_th[:, 0:2],
                                start=True, stop=True), [a3, m1])
    cc = P.v(lambda v: v.tensor_copy(s_thc[:], p_b[:, 0:2]), [bb])

    stoks = []
    prevm = prevs = None
    for which, s_w, s_wt, nch, wid, OUT in (
            (0, s_wu, s_wtu, NC, 512, WTU),
            (1, s_wd, s_wtd, NFF, 128, WTD)):
        for j in range(nch):
            sl = slice(j * wid, (j + 1) * wid)
            ab = P.s(lambda s, _sl=sl, _wd=wid, _sw=s_w: s.activation(
                s_msk[:, 0:_wd], _sw[:, _sl], AF.Abs), [lall, prevm])
            mk = P.v(lambda v, _w=which, _wd=wid: v.tensor_scalar(
                s_msk[:, 0:_wd], s_msk[:, 0:_wd], s_thc[:, _w:_w + 1], None,
                op0=ALU.is_gt), [cc, ab, prevm])
            sg = P.s(lambda s, _sl=sl, _wd=wid, _sw=s_w: s.activation(
                s_sgn[:, 0:_wd], _sw[:, _sl], AF.Sign), [lall, prevs, ab])
            wt = P.v(lambda v, _sl=sl, _wd=wid, _swt=s_wt: v.tensor_tensor(
                _swt[:, _sl], s_msk[:, 0:_wd], s_sgn[:, 0:_wd], ALU.mult),
                [mk, sg])
            prevm = prevs = wt
        stoks.append(P.sd(lambda s, _swt=s_wt, _n=nch, _wd=wid, _O=OUT:
                          s.dma_start(
            out=_O.rearrange("(a p) n -> p a n", p=128),
            in_=_swt[:].rearrange("p (a n) -> p a n", a=_n)),
            [prevm], ch="st"))
    P.finish(stoks)
    return P.build().nc


def build_b2(n=NTOK):
    nc = bass.Bass("TRN2", target_bir_lowering=False, debug=False)
    X1T = nc.dram_tensor("x1t", [D, n], F32, kind="ExternalInput").ap()
    WTU = nc.dram_tensor("wtu", [D, DFF], BF16, kind="ExternalInput").ap()
    WTD = nc.dram_tensor("wtd", [DFF, D], BF16, kind="ExternalInput").ap()
    X2T = nc.dram_tensor("x2t", [D, n], F32, kind="ExternalOutput").ap()

    P = Prog(nc)
    es = P.es
    s_x1 = [sb(nc, es, f"s_x1{i}", [128, NC * 512]) for i in range(2)]
    s_xb = sb(nc, es, "s_xb", [128, NC * 512], BF16)
    s_h = sb(nc, es, "s_h", [128, NFF * 512])
    s_hb = sb(nc, es, "s_hb", [128, NFF * 512], BF16)
    s_z = sb(nc, es, "s_z", [128, NC * 512])
    s_wu = [sb(nc, es, f"s_wu{i}", [128, NC * 128], BF16) for i in range(2)]
    s_wd = [sb(nc, es, f"s_wd{i}", [128, NFF * 128], BF16) for i in range(2)]
    s_tmp = sb(nc, es, "s_tmp", [128, 512])
    s_row = sb(nc, es, "s_row", [1, 512])
    s_hrow = sb(nc, es, "s_hrow", [1, 512])
    s_m = sb(nc, es, "s_m", [1, 512])
    s_r = sb(nc, es, "s_r", [1, 512])
    s_trow = sb(nc, es, "s_trow", [1, 512])
    s_eps = sb(nc, es, "s_eps", [1, 1])
    s_onec = sb(nc, es, "s_onec", [128, 1])
    s_oner = sb(nc, es, "s_oner", [1, 128])
    p_up = [ps(nc, es, f"p_up{i}", [128, 512]) for i in range(2)]
    p_dn = [ps(nc, es, f"p_dn{i}", [128, 512]) for i in range(2)]
    p_rms = ps(nc, es, "p_rms", [1, 512])
    p_h2 = ps(nc, es, "p_h2", [1, 512])
    p_bc1 = ps(nc, es, "p_bc1", [128, 512])
    p_bc2 = ps(nc, es, "p_bc2", [128, 512])

    lall = P.total("syd", "ld")
    m1 = P.v(lambda v: v.memset(s_onec[:], 1.0))
    m2 = P.v(lambda v: v.memset(s_oner[:], 1.0))
    m4 = P.v(lambda v: v.memset(s_eps[:], LN_EPS))

    nsub = (n + 511) // 512
    war = {"eps_dep": m4}
    wtoks = {}
    wconsumed = {}
    stoks = []
    xls = [None, None]
    xls[0] = P.sd(lambda s: s.dma_start(
        out=s_x1[0][:].rearrange("p (a n) -> p a n", a=NC),
        in_=fmview(X1T, n)[:, :, 0:512]), ch="x0")
    for i in range(nsub):
        w = min(512, n - i * 512)
        xb = i % 2
        if i + 1 < nsub:
            w2_ = min(512, n - (i + 1) * 512)
            xls[1 - xb] = P.sd(lambda s, _i=i, _b=1 - xb, _w2=w2_:
                               s.dma_start(
                out=s_x1[_b][:].rearrange("p (a n) -> p a n", a=NC)[
                    :, :, 0:_w2],
                in_=fmview(X1T, n)[:, :, (_i + 1) * 512:(_i + 1) * 512 + _w2]),
                [war.get(("x1free", 1 - xb))], ch=f"x{1 - xb}")
        lall = xls[xb]
        sqprev = war.get("sq")
        s1tok = None
        for j in range(NC):
            off = j * n + i * 512
            sq = P.s(lambda s, _o=j * 512, _w=w, _b=xb: s.activation(
                s_tmp[:, 0:_w], s_x1[_b][:, _o:_o + _w], AF.Square),
                [lall, sqprev])
            s1tok = P.t(lambda t, _w=w, _j=j: t.matmul(
                p_rms[:, 0:_w], lhsT=s_onec[:], rhs=s_tmp[:, 0:_w],
                start=(_j == 0), stop=(_j == NC - 1)),
                [sq, m1, war.get("rms")])
            sqprev = s1tok
        war["sq"] = sqprev
        r1 = P.v(lambda v, _w=w: v.tensor_scalar(
            s_row[:, 0:_w], p_rms[:, 0:_w], 1.0 / D, None, op0=ALU.mult),
            [s1tok, war.get("row")])
        war["rms"] = r1
        r2 = P.s(lambda s, _w=w: s.activation(
            s_row[:, 0:_w], s_row[:, 0:_w], AF.Sqrt), [r1])
        r3 = P.v(lambda v, _w=w: v.tensor_scalar(
            s_row[:, 0:_w], s_row[:, 0:_w], RMS_EPS, None, op0=ALU.add), [r2])
        r4 = P.v(lambda v, _w=w: v.reciprocal(
            s_row[:, 0:_w], s_row[:, 0:_w]), [r3])
        b1 = P.t(lambda t, _w=w: t.matmul(
            p_bc1[:, 0:_w], lhsT=s_oner[:], rhs=s_row[:, 0:_w],
            start=True, stop=True), [r4, m2, war.get("bc1")])
        casts = []
        for j in range(NC):
            off = j * n + i * 512
            casts.append(P.v(lambda v, _j=j, _w=w, _b=xb: v.tensor_tensor(
                s_xb[:, _j * 512:_j * 512 + _w],
                s_x1[_b][:, _j * 512:_j * 512 + _w],
                p_bc1[:, 0:_w], ALU.mult), [b1, lall, war.get("xb")]))
        war["row"] = casts[-1]

        hsq_prev = war.get("sq")
        h2tok = None
        geltoks = []
        for f in range(NFF):
            buf = f % 2
            wtok = P.sd(lambda s, _f=f, _b=buf: s.dma_start(
                out=s_wu[_b][:].rearrange("p (a n) -> p a n", a=NC),
                in_=WTU.rearrange("(a p) f -> p a f", p=128)[
                    :, :, _f * 128:(_f + 1) * 128]),
                [wconsumed.get(("wu", buf))], ch=f"wu{buf}")
            mmt = None
            for j in range(NC):
                mmt = P.t(lambda t, _f=f, _j=j, _b=buf, _w=w: t.matmul(
                    p_up[_f % 2][:, 0:_w],
                    lhsT=s_wu[_b][:, _j * 128:(_j + 1) * 128],
                    rhs=s_xb[:, _j * 512:_j * 512 + _w],
                    start=(_j == 0), stop=(_j == NC - 1)),
                    [wtok, casts[j], war.get(("up", f % 2))])
            wconsumed[("wu", buf)] = mmt
            gel = P.s(lambda s, _f=f, _w=w: s.activation(
                s_h[:, _f * 512:_f * 512 + _w], p_up[_f % 2][:, 0:_w],
                AF.Gelu), [mmt, war.get(("hread", f))])
            war[("up", f % 2)] = gel
            geltoks.append(gel)
            hsq = P.s(lambda s, _f=f, _w=w: s.activation(
                s_tmp[:, 0:_w], s_h[:, _f * 512:_f * 512 + _w], AF.Square),
                [gel, hsq_prev])
            h2tok = P.t(lambda t, _f=f, _w=w: t.matmul(
                p_h2[:, 0:_w], lhsT=s_onec[:], rhs=s_tmp[:, 0:_w],
                start=(_f == 0), stop=(_f == NFF - 1)),
                [hsq, war.get("h2")])
            hsq_prev = h2tok
        war["sq"] = hsq_prev
        hr1 = P.v(lambda v, _w=w: v.tensor_scalar(
            s_hrow[:, 0:_w], p_h2[:, 0:_w], 1.0 / DFF, None, op0=ALU.mult),
            [h2tok, war.get("hrow")])
        war["h2"] = hr1
        hr2 = P.s(lambda s, _w=w: s.activation(
            s_hrow[:, 0:_w], s_hrow[:, 0:_w], AF.Sqrt), [hr1])
        hr3 = P.v(lambda v, _w=w: v.tensor_scalar(
            s_hrow[:, 0:_w], s_hrow[:, 0:_w], RMS_EPS, None, op0=ALU.add),
            [hr2])
        hr4 = P.v(lambda v, _w=w: v.reciprocal(
            s_hrow[:, 0:_w], s_hrow[:, 0:_w]), [hr3])
        b2t = P.t(lambda t, _w=w: t.matmul(
            p_bc2[:, 0:_w], lhsT=s_oner[:], rhs=s_hrow[:, 0:_w],
            start=True, stop=True), [hr4, war.get("bc2")])
        hcasts = []
        for f in range(NFF):
            hcasts.append(P.v(lambda v, _f=f, _w=w: v.tensor_tensor(
                s_hb[:, _f * 512:_f * 512 + _w],
                s_h[:, _f * 512:_f * 512 + _w], p_bc2[:, 0:_w], ALU.mult),
                [b2t, geltoks[f], war.get("hb")]))
        war["hrow"] = hcasts[-1]

        ztoks = []
        for cjj in range(NC):
            buf = cjj % 2
            wtok = P.sd(lambda s, _c=cjj, _b=buf: s.dma_start(
                out=s_wd[_b][:].rearrange("p (a n) -> p a n", a=NFF),
                in_=WTD.rearrange("(a p) c -> p a c", p=128)[
                    :, :, _c * 128:(_c + 1) * 128]),
                [wconsumed.get(("wd", buf))], ch=f"wd{buf}")
            mmt = None
            for f in range(NFF):
                mmt = P.t(lambda t, _c=cjj, _f=f, _b=buf, _w=w: t.matmul(
                    p_dn[_c % 2][:, 0:_w],
                    lhsT=s_wd[_b][:, _f * 128:(_f + 1) * 128],
                    rhs=s_hb[:, _f * 512:_f * 512 + _w],
                    start=(_f == 0), stop=(_f == NFF - 1)),
                    [wtok, hcasts[f], war.get(("dn", cjj % 2))])
            wconsumed[("wd", buf)] = mmt
            zt = P.v(lambda v, _c=cjj, _w=w, _b=xb: v.tensor_tensor(
                s_z[:, _c * 512:_c * 512 + _w], p_dn[_c % 2][:, 0:_w],
                s_x1[_b][:, _c * 512:_c * 512 + _w], ALU.add),
                [mmt, lall, war.get(("zread", cjj))])
            war[("dn", cjj % 2)] = zt
            ztoks.append(zt)
        war["hb"] = ztoks[-1]
        war["xb"] = ztoks[-1]

        outtoks = emit_ln(P, 512, 0, w, s_z, s_z, s_m, s_r, s_tmp, s_trow,
                          s_onec, s_oner, p_rms[0:1, :], p_h2[0:1, :],
                          p_bc1, p_bc2, ztoks, war, s_eps=s_eps)
        war["rms"] = war["s1"]
        war["h2"] = war["s2"]
        war["bc1"] = war["bc"]
        war["bc2"] = war["bc"]
        for f in range(NFF):
            war[("hread", f)] = hcasts[f]
        stoks.append(P.sd(lambda s, _i=i, _w=w: s.dma_start(
            out=fmview(X2T, n)[:, :, _i * 512:_i * 512 + _w],
            in_=s_z[:].rearrange("p (a n) -> p a n", a=NC)[:, :, 0:_w]),
            outtoks, ch="st"))
        for cjj in range(NC):
            war[("zread", cjj)] = stoks[-1]
        war[("x1free", xb)] = ztoks[-1]
    P.finish(stoks)
    return P.build().nc


def build_c1(ns=NSEQ):
    nc = bass.Bass("TRN2", target_bir_lowering=False, debug=False)
    X2 = nc.dram_tensor("x2tb", [D, ns], F32, kind="ExternalInput").ap()
    WQ = nc.dram_tensor("wq", [D, 256], F32, kind="ExternalInput").ap()
    WK = nc.dram_tensor("wk", [D, 256], F32, kind="ExternalInput").ap()
    WV = nc.dram_tensor("wv", [D, 256], F32, kind="ExternalInput").ap()
    BQ = nc.dram_tensor("bq", [128, 2], F32, kind="ExternalInput").ap()
    BK = nc.dram_tensor("bk", [128, 2], F32, kind="ExternalInput").ap()
    BV = nc.dram_tensor("bvrow", [1, 256], F32, kind="ExternalInput").ap()
    PJ = nc.dram_tensor("projt", [64, 4 * 256], F32, kind="ExternalInput").ap()
    IDB = nc.dram_tensor("identb", [128, 128], BF16, kind="ExternalInput").ap()
    ND = nc.dram_tensor("ndt", [256, ns], BF16, kind="ExternalOutput").ap()

    P = Prog(nc)
    es = P.es
    s_xs = [sb(nc, es, f"s_xs{i}", [128, NC * 512], BF16) for i in range(2)]
    s_q = [sb(nc, es, f"s_q{i}", [128, ns], BF16) for i in range(2)]
    s_k = [sb(nc, es, f"s_k{i}", [128, ns], BF16) for i in range(2)]
    s_wq = sb(nc, es, "s_wq", [128, NC * 256], BF16)
    s_wk = sb(nc, es, "s_wk", [128, NC * 256], BF16)
    s_wv = sb(nc, es, "s_wv", [128, NC * 256], BF16)
    s_bq = sb(nc, es, "s_bq", [128, 2])
    s_bk = sb(nc, es, "s_bk", [128, 2])
    s_bv = sb(nc, es, "s_bv", [1, 256])
    s_pjr = sb(nc, es, "s_pjr", [128, 4 * 256])
    s_pj = sb(nc, es, "s_pj", [128, 4 * 256], BF16)
    s_idb = sb(nc, es, "s_idb", [128, 128], BF16)
    s_oner = sb(nc, es, "s_oner", [1, 128])
    s_vx = sb(nc, es, "s_vx", [128, (ns // 128) * 260], BF16)
    s_phik = sb(nc, es, "s_phik", [128, 256], BF16)
    s_exp = sb(nc, es, "s_exp", [128, 256])
    s_nmx = sb(nc, es, "s_nmx", [128, 1])
    s_kv = sb(nc, es, "s_kv", [128, 2 * 260])
    s_kvb = sb(nc, es, "s_kvb", [128, 2 * 260], BF16)
    s_phiqT = [sb(nc, es, f"s_phiqT{i}", [128, 512], BF16) for i in range(2)]
    s_den = sb(nc, es, "s_den", [1, 512])
    s_dbc = sb(nc, es, "s_dbc", [64, 512])
    s_nd = [sb(nc, es, f"s_nd{i}", [128, ns], BF16) for i in range(2)]
    p_a = ps(nc, es, "p_a", [128, 512])
    p_b = ps(nc, es, "p_b", [128, 512])
    p_c = ps(nc, es, "p_c", [128, 512])
    p_x = [ps(nc, es, f"p_x{i}", [128, 512]) for i in range(2)]
    p_kv = [ps(nc, es, f"p_kv{i}", [128, 512]) for i in range(2)]
    p_t = ps(nc, es, "p_t", [128, 1024], BF16)

    lw1 = P.gd(lambda g: g.dma_start(
        out=s_wq[:].rearrange("p (a n) -> p a n", a=NC),
        in_=WQ.rearrange("(a p) n -> p a n", p=128)), ch="w")
    lw2 = P.gd(lambda g: g.dma_start(
        out=s_wk[:].rearrange("p (a n) -> p a n", a=NC),
        in_=WK.rearrange("(a p) n -> p a n", p=128)), ch="w")
    lw3 = P.gd(lambda g: g.dma_start(
        out=s_wv[:].rearrange("p (a n) -> p a n", a=NC),
        in_=WV.rearrange("(a p) n -> p a n", p=128)), ch="w")
    lg = P.total("gpd", "w")
    P.sd(lambda s: s.dma_start(out=s_bq[:], in_=BQ))
    P.sd(lambda s: s.dma_start(out=s_bk[:], in_=BK))
    P.sd(lambda s: s.dma_start(out=s_bv[:], in_=BV))
    P.sd(lambda s: s.dma_start(out=s_idb[:], in_=IDB))
    P.sd(lambda s: s.dma_start(out=s_pjr[0:64, :], in_=PJ))
    P.sd(lambda s: s.dma_start(out=s_pjr[64:128, :], in_=PJ))
    lall = P.total("syd", "ld")
    m1 = P.v(lambda v: v.memset(s_oner[:], 1.0))
    m2 = P.v(lambda v: v.memset(s_vx[:], 1.0))
    pj = P.v(lambda v: v.tensor_scalar(
        s_pj[:], s_pjr[:], DATA_NORM, None, op0=ALU.mult), [lall])

    nsub = ns // 512
    war = {}
    xload = [None, None]
    xload[0] = P.gd(lambda g: g.dma_start(
        out=s_xs[0][:].rearrange("p (a n) -> p a n", a=NC),
        in_=fmview(X2, ns)[:, :, 0:512]), ch="xs0")

    for i in range(nsub):
        buf = i % 2
        if i + 1 < nsub:
            xload[1 - buf] = P.gd(lambda g, _i=i, _b=1 - buf: g.dma_start(
                out=s_xs[_b][:].rearrange("p (a n) -> p a n", a=NC),
                in_=fmview(X2, ns)[:, :, (_i + 1) * 512:(_i + 2) * 512]),
                [war.get(("xread", 1 - buf))], ch=f"xs{1 - buf}")
        lx = xload[buf]
        pq = [p_a, p_b]
        for W_, Sdst, Bias in ((s_wq, s_q, s_bq), (s_wk, s_k, s_bk)):
            for hp in range(2):
                mmt = None
                for j in range(NC):
                    mmt = P.t(lambda t, _j=j, _hp=hp, _b=buf, _W=W_: t.matmul(
                        pq[_hp][:, 0:512],
                        lhsT=_W[:, _j * 256 + _hp * 128:
                                _j * 256 + _hp * 128 + 128],
                        rhs=s_xs[_b][:, _j * 512:(_j + 1) * 512],
                        start=(_j == 0), stop=(_j == NC - 1)),
                        [lx, lg, war.get(("pq", hp))])
                ev = P.s(lambda s, _i=i, _hp=hp, _S=Sdst, _B=Bias:
                         s.activation(
                    _S[_hp][:, _i * 512:_i * 512 + 512], pq[_hp][:, 0:512],
                    AF.Identity, bias=_B[:, _hp:_hp + 1]), [mmt, lall])
                war[("pq", hp)] = ev
                if W_ is s_wk:
                    war[("kwrite", hp)] = ev
        lastv = None
        for r in range(4):
            rr = r * 128
            tt = i * 4 + r
            mmt = None
            for j in range(NC):
                mmt = P.t(lambda t, _j=j, _rr=rr, _b=buf: t.matmul(
                    p_c[:, 0:256],
                    lhsT=s_xs[_b][:, _j * 512 + _rr:_j * 512 + _rr + 128],
                    rhs=s_wv[:, _j * 256:(_j + 1) * 256],
                    start=(_j == 0), stop=False),
                    [lx, war.get("vread")])
            mmt = P.t(lambda t: t.matmul(
                p_c[:, 0:256], lhsT=s_oner[:], rhs=s_bv[:],
                start=False, stop=True), [mmt, m1, lall])
            vx = P.v(lambda v, _tt=tt: v.tensor_copy(
                s_vx[:, _tt * 260:(_tt + 1) * 260].rearrange(
                    "p (h c) -> p h c", h=4)[:, :, 0:64],
                p_c[:, 0:256].rearrange("p (h c) -> p h c", h=4)),
                [mmt, m2])
            war["vread"] = vx
            lastv = vx
        war[("xread", buf)] = lastv

    kvevs = []
    for h in range(4):
        hp, hb = h // 2, (h % 2) * 64
        kv0 = kv1 = None
        for tt in range(ns // 128):
            rr = tt * 128
            xp = P.t(lambda t, _hp=hp, _hb=hb, _rr=rr, _tt=tt, _h=h:
                     t.matmul(
                p_x[_tt % 2][:, 0:256],
                lhsT=s_k[_hp][_hb:_hb + 64, _rr:_rr + 128],
                rhs=s_pj[_hb:_hb + 64, _h * 256:(_h + 1) * 256],
                start=True, stop=True),
                [war.get(("kwrite", hp)), pj, war.get(("xp", tt % 2)),
                 war.get("vread")])
            rm = P.v(lambda v, _tt=tt: v.reduce_max(
                s_nmx[:], p_x[_tt % 2][:, 0:256], axis=AX.X, negate=True),
                [xp, war.get("nmxread")])
            ex = P.s(lambda s, _tt=tt: s.activation(
                s_exp[:], p_x[_tt % 2][:, 0:256], AF.Exp, bias=s_nmx[:]),
                [rm, war.get("exread")])
            war[("xp", tt % 2)] = ex
            war["nmxread"] = ex
            ph = P.v(lambda v: v.tensor_scalar(
                s_phik[:], s_exp[:], 1.0 / 16.0, PHI_EPS / 16.0,
                op0=ALU.mult, op1=ALU.add), [ex, war.get("phikread")])
            war["exread"] = ph
            kv0 = P.t(lambda t, _h=h, _tt=tt: t.matmul(
                p_kv[0][:, 0:65], lhsT=s_phik[:, 0:128],
                rhs=s_vx[:, _tt * 260 + _h * 65:_tt * 260 + (_h + 1) * 65],
                start=(_tt == 0), stop=(_tt == ns // 128 - 1)),
                [ph, war.get("kv0ev")])
            kv1 = P.t(lambda t, _h=h, _tt=tt: t.matmul(
                p_kv[1][:, 0:65], lhsT=s_phik[:, 128:256],
                rhs=s_vx[:, _tt * 260 + _h * 65:_tt * 260 + (_h + 1) * 65],
                start=(_tt == 0), stop=(_tt == ns // 128 - 1)),
                [ph, war.get("kv1ev")])
            war["phikread"] = kv1
        e1 = P.v(lambda v, _h=h: v.tensor_copy(
            s_kv[:, _h * 65:(_h + 1) * 65], p_kv[0][:, 0:65]), [kv0, kv1])
        e2 = P.v(lambda v, _h=h: v.tensor_copy(
            s_kv[:, 260 + _h * 65:260 + (_h + 1) * 65], p_kv[1][:, 0:65]),
            [e1])
        war["kv0ev"] = e1
        war["kv1ev"] = e2
        kvevs.append(e2)
    cb = P.v(lambda v: v.tensor_copy(s_kvb[:], s_kv[:]), kvevs)

    ndlast = None
    for i in range(nsub):
        for h in range(4):
            hp, hb = h // 2, (h % 2) * 64
            trtoks = []
            for r in range(4):
                n0 = i * 512 + r * 128
                xp = P.t(lambda t, _hp=hp, _hb=hb, _n0=n0, _r=r, _h=h:
                         t.matmul(
                    p_x[_r % 2][:, 0:256],
                    lhsT=s_q[_hp][_hb:_hb + 64, _n0:_n0 + 128],
                    rhs=s_pj[_hb:_hb + 64, _h * 256:(_h + 1) * 256],
                    start=True, stop=True), [pj, war.get(("xp", r % 2))])
                rm = P.v(lambda v, _r=r: v.reduce_max(
                    s_nmx[:], p_x[_r % 2][:, 0:256], axis=AX.X, negate=True),
                    [xp, war.get("nmxread")])
                ex = P.s(lambda s, _r=r: s.activation(
                    s_exp[:], p_x[_r % 2][:, 0:256], AF.Exp, bias=s_nmx[:]),
                    [rm, war.get("exread")])
                war[("xp", r % 2)] = ex
                war["nmxread"] = ex
                ph = P.v(lambda v: v.tensor_scalar(
                    s_phik[:], s_exp[:], 1.0 / 16.0, PHI_EPS / 16.0,
                    op0=ALU.mult, op1=ALU.add), [ex, war.get("phikread")])
                war["exread"] = ph
                t1 = P.t(lambda t: t.transpose(
                    p_t[:, 0:128], s_phik[:, 0:128], s_idb[:]),
                    [ph, lall, war.get("tr0")])
                t2 = P.t(lambda t: t.transpose(
                    p_t[:, 128:256], s_phik[:, 128:256], s_idb[:]), [t1])
                c1t = P.v(lambda v, _r=r: v.tensor_copy(
                    s_phiqT[0][:, _r * 128:(_r + 1) * 128], p_t[:, 0:128]),
                    [t2, war.get("pTread")])
                c2t = P.v(lambda v, _r=r: v.tensor_copy(
                    s_phiqT[1][:, _r * 128:(_r + 1) * 128], p_t[:, 128:256]),
                    [c1t])
                war["phikread"] = c2t
                war["tr0"] = c2t
                trtoks.append(c2t)
            nm = P.t(lambda t, _h=h: t.matmul(
                p_a[0:65, 0:512], lhsT=s_kvb[:, _h * 65:(_h + 1) * 65],
                rhs=s_phiqT[0][:], start=True, stop=False),
                trtoks + [cb, war.get("numread")])
            nm = P.t(lambda t, _h=h: t.matmul(
                p_a[0:65, 0:512],
                lhsT=s_kvb[:, 260 + _h * 65:260 + (_h + 1) * 65],
                rhs=s_phiqT[1][:], start=False, stop=True), [nm])
            war["pTread"] = nm
            d1 = P.v(lambda v: v.tensor_scalar(
                s_den[:], p_a[64:65, 0:512], PHI_EPS, None, op0=ALU.add),
                [nm, war.get("denread")])
            d2 = P.v(lambda v: v.reciprocal(s_den[:], s_den[:]), [d1])
            db = P.t(lambda t: t.matmul(
                p_b[0:64, 0:512], lhsT=s_oner[:, 0:64], rhs=s_den[:],
                start=True, stop=True), [d2, m1, war.get("dbcread")])
            dc = P.v(lambda v: v.tensor_copy(s_dbc[:], p_b[0:64, 0:512]),
                     [db])
            war["denread"] = dc
            nd = P.v(lambda v, _hp=hp, _hb=hb, _i=i: v.tensor_tensor(
                s_nd[_hp][_hb:_hb + 64, _i * 512:_i * 512 + 512],
                p_a[0:64, 0:512], s_dbc[:], ALU.mult), [dc, nm])
            war["numread"] = nd
            war["dbcread"] = nd
            ndlast = nd
    stoks = []
    for hp in range(2):
        stoks.append(P.sd(lambda s, _hp=hp: s.dma_start(
            out=ND[_hp * 128:(_hp + 1) * 128, :], in_=s_nd[_hp][:]),
            [ndlast], ch="st"))
    P.finish(stoks)
    return P.build().nc


def build_c2(n=NTOK):
    nc = bass.Bass("TRN2", target_bir_lowering=False, debug=False)
    NDS = nc.dram_tensor("ndts", [D, n], BF16, kind="ExternalInput").ap()
    WO = nc.dram_tensor("wo", [D, D], F32, kind="ExternalInput").ap()
    BO = nc.dram_tensor("bo", [128, NC], F32, kind="ExternalInput").ap()
    X2T = nc.dram_tensor("x2t", [D, n], F32, kind="ExternalInput").ap()
    X3T = nc.dram_tensor("x3t", [D, n], F32, kind="ExternalOutput").ap()

    P = Prog(nc)
    es = P.es
    s_ndv = sb(nc, es, "s_ndv", [128, NC * n], BF16)
    s_x2 = sb(nc, es, "s_x2", [128, NC * n])
    s_wo = sb(nc, es, "s_wo", [128, NC * D], BF16)
    s_bo = sb(nc, es, "s_bo", [128, NC])
    s_z = sb(nc, es, "s_z", [128, NC * 512])
    s_tmp = sb(nc, es, "s_tmp", [128, 512])
    s_m = sb(nc, es, "s_m", [1, 512])
    s_r = sb(nc, es, "s_r", [1, 512])
    s_trow = sb(nc, es, "s_trow", [1, 512])
    s_eps = sb(nc, es, "s_eps", [1, 1])
    s_onec = sb(nc, es, "s_onec", [128, 1])
    s_oner = sb(nc, es, "s_oner", [1, 128])
    p_o = [ps(nc, es, f"p_o{i}", [128, 512]) for i in range(2)]
    p_s1 = ps(nc, es, "p_s1", [1, 512])
    p_s2 = ps(nc, es, "p_s2", [1, 512])
    p_mbc = ps(nc, es, "p_mbc", [128, 512])
    p_rbc = ps(nc, es, "p_rbc", [128, 512])

    P.sd(lambda s: s.dma_start(
        out=s_ndv[:].rearrange("p (a n) -> p a n", a=NC), in_=fmview(NDS, n)))
    P.sd(lambda s: s.dma_start(
        out=s_x2[:].rearrange("p (a n) -> p a n", a=NC), in_=fmview(X2T, n)))
    P.sd(lambda s: s.dma_start(out=s_bo[:], in_=BO))
    lall = P.total("syd", "ld")
    lwo = P.gd(lambda g: g.dma_start(
        out=s_wo[:].rearrange("p (a n) -> p a n", a=NC),
        in_=WO.rearrange("(a p) n -> p a n", p=128)), ch="w")
    m1 = P.v(lambda v: v.memset(s_onec[:], 1.0))
    m2 = P.v(lambda v: v.memset(s_oner[:], 1.0))
    m4 = P.v(lambda v: v.memset(s_eps[:], LN_EPS))

    nsub = (n + 511) // 512
    war = {"eps_dep": m4}
    stoks = []
    for i in range(nsub):
        w = min(512, n - i * 512)
        ztoks = []
        for cj in range(NC):
            mmt = None
            for j in range(NC):
                mmt = P.t(lambda t, _c=cj, _j=j, _i=i, _w=w: t.matmul(
                    p_o[_c % 2][:, 0:_w],
                    lhsT=s_wo[:, _j * D + _c * 128:_j * D + (_c + 1) * 128],
                    rhs=s_ndv[:, _j * n + _i * 512:_j * n + _i * 512 + _w],
                    start=(_j == 0), stop=(_j == NC - 1)),
                    [lall, lwo, war.get(("o", cj % 2))])
            ev = P.s(lambda s, _c=cj, _w=w: s.activation(
                s_tmp[:, 0:_w], p_o[_c % 2][:, 0:_w], AF.Identity,
                bias=s_bo[:, _c:_c + 1]), [mmt, war.get("tmpread")])
            zt = P.v(lambda v, _c=cj, _i=i, _w=w: v.tensor_tensor(
                s_z[:, _c * 512:_c * 512 + _w], s_tmp[:, 0:_w],
                s_x2[:, _c * n + _i * 512:_c * n + _i * 512 + _w], ALU.add),
                [ev, war.get(("zread", cj))])
            war[("o", cj % 2)] = zt
            war["tmpread"] = zt
            ztoks.append(zt)
        war["sq"] = ztoks[-1]
        outtoks = emit_ln(P, 512, 0, w, s_z, s_z, s_m, s_r, s_tmp, s_trow,
                          s_onec, s_oner, p_s1, p_s2, p_mbc, p_rbc, ztoks,
                          war, s_eps=s_eps)
        war["tmpread"] = war["sq"]
        stoks.append(P.sd(lambda s, _i=i, _w=w: s.dma_start(
            out=fmview(X3T, n)[:, :, _i * 512:_i * 512 + _w],
            in_=s_z[:].rearrange("p (a n) -> p a n", a=NC)[:, :, 0:_w]),
            outtoks, ch="st"))
        for cj in range(NC):
            war[("zread", cj)] = stoks[-1]
    P.finish(stoks)
    return P.build().nc


def build_d(n=NTOK):
    nc = bass.Bass("TRN2", target_bir_lowering=False, debug=False)
    X3T = nc.dram_tensor("x3t", [D, n], F32, kind="ExternalInput").ap()
    WIN = nc.dram_tensor("win", [D, 64], F32, kind="ExternalInput").ap()
    BIN = nc.dram_tensor("bin", [64, 1], F32, kind="ExternalInput").ap()
    W1 = nc.dram_tensor("w1", [64, 16], F32, kind="ExternalInput").ap()
    B1 = nc.dram_tensor("b1", [64, 16], F32, kind="ExternalInput").ap()
    W2 = nc.dram_tensor("w2", [64, 16], F32, kind="ExternalInput").ap()
    B2_ = nc.dram_tensor("b2", [64, 1], F32, kind="ExternalInput").ap()
    WOUT = nc.dram_tensor("wout", [64, D], F32, kind="ExternalInput").ap()
    BOUT = nc.dram_tensor("bout", [128, NC], F32, kind="ExternalInput").ap()
    YT = nc.dram_tensor("yt", [D, n], F32, kind="ExternalOutput").ap()
    if dump:
        XD1 = nc.dram_tensor("xd1", [D, n], F32, kind="ExternalOutput").ap()
        XD2 = nc.dram_tensor("xd2", [D, n], F32, kind="ExternalOutput").ap()
        XD3 = nc.dram_tensor("xd3", [D, n], F32, kind="ExternalOutput").ap()
        WTUD = nc.dram_tensor("wtud", [D, DFF], BF16,
                              kind="ExternalOutput").ap()
        WTDD = nc.dram_tensor("wtdd", [DFF, D], BF16,
                              kind="ExternalOutput").ap()

    P = Prog(nc)
    es = P.es
    s_x3 = sb(nc, es, "s_x3", [128, NC * n])
    s_x3b = sb(nc, es, "s_x3b", [128, NC * n], BF16)
    s_winb = sb(nc, es, "s_winb", [128, NC * 64], BF16)
    s_bin = sb(nc, es, "s_bin", [64, 1])
    s_w1 = sb(nc, es, "s_w1", [64, 16])
    s_b1 = sb(nc, es, "s_b1", [64, 16])
    s_w2 = sb(nc, es, "s_w2", [64, 16])
    s_b2 = sb(nc, es, "s_b2", [64, 1])
    s_woutb = sb(nc, es, "s_woutb", [64, D], BF16)
    s_bout = sb(nc, es, "s_bout", [128, NC])
    s_s = sb(nc, es, "s_s", [64, 512])
    s_hi = sb(nc, es, "s_hi", [64, 512])
    s_g = sb(nc, es, "s_g", [64, 512])
    s_gb = sb(nc, es, "s_gb", [64, 512], BF16)
    s_z = sb(nc, es, "s_z", [128, NC * 512])
    s_tmp = sb(nc, es, "s_tmp", [128, 512])
    s_m = sb(nc, es, "s_m", [1, 512])
    s_r = sb(nc, es, "s_r", [1, 512])
    s_trow = sb(nc, es, "s_trow", [1, 512])
    s_eps = sb(nc, es, "s_eps", [1, 1])
    s_onec = sb(nc, es, "s_onec", [128, 1])
    s_oner = sb(nc, es, "s_oner", [1, 128])
    p_s = ps(nc, es, "p_s", [128, 512])
    p_o = [ps(nc, es, f"p_o{i}", [128, 512]) for i in range(2)]
    p_s1 = ps(nc, es, "p_s1", [1, 512])
    p_s2 = ps(nc, es, "p_s2", [1, 512])
    p_mbc = ps(nc, es, "p_mbc", [128, 512])
    p_rbc = ps(nc, es, "p_rbc", [128, 512])

    P.sd(lambda s: s.dma_start(
        out=s_x3[:].rearrange("p (a n) -> p a n", a=NC), in_=fmview(X3T, n)))
    P.sd(lambda s: s.dma_start(out=s_bin[:], in_=BIN))
    P.sd(lambda s: s.dma_start(out=s_w1[:], in_=W1))
    P.sd(lambda s: s.dma_start(out=s_b1[:], in_=B1))
    P.sd(lambda s: s.dma_start(out=s_w2[:], in_=W2))
    P.sd(lambda s: s.dma_start(out=s_b2[:], in_=B2_))
    P.sd(lambda s: s.dma_start(out=s_bout[:], in_=BOUT))
    lall = P.total("syd", "ld")
    lx3b = P.gd(lambda g: g.dma_start(
        out=s_x3b[:].rearrange("p (a n) -> p a n", a=NC), in_=fmview(X3T, n)),
        ch="w")
    lwin = P.gd(lambda g: g.dma_start(
        out=s_winb[:].rearrange("p (a n) -> p a n", a=NC),
        in_=WIN.rearrange("(a p) n -> p a n", p=128)), ch="w")
    lwout = P.gd(lambda g: g.dma_start(out=s_woutb[:], in_=WOUT), ch="w")
    lgw = P.total("gpd", "w")
    m1 = P.v(lambda v: v.memset(s_onec[:], 1.0))
    m2 = P.v(lambda v: v.memset(s_oner[:], 1.0))
    m4 = P.v(lambda v: v.memset(s_eps[:], LN_EPS))

    nsub = (n + 511) // 512
    war = {"eps_dep": m4}
    stoks = []
    for i in range(nsub):
        w = min(512, n - i * 512)
        mmt = None
        for j in range(NC):
            mmt = P.t(lambda t, _j=j, _i=i, _w=w: t.matmul(
                p_s[0:64, 0:_w], lhsT=s_winb[:, _j * 64:(_j + 1) * 64],
                rhs=s_x3b[:, _j * n + _i * 512:_j * n + _i * 512 + _w],
                start=(_j == 0), stop=(_j == NC - 1)),
                [lgw, war.get("psread")])
        sv = P.s(lambda s, _w=w: s.activation(
            s_s[:, 0:_w], p_s[0:64, 0:_w], AF.Identity, bias=s_bin[:]),
            [mmt, lall, war.get("sread")])
        war["psread"] = sv
        gt = None
        for ii in range(16):
            hi = P.s(lambda s, _ii=ii, _w=w: s.activation(
                s_hi[:, 0:_w], s_s[:, 0:_w], AF.Relu,
                scale=s_w1[:, _ii:_ii + 1], bias=s_b1[:, _ii:_ii + 1]),
                [sv, lall, war.get("hiread")])
            if ii == 0:
                gt = P.v(lambda v, _ii=ii, _w=w: v.tensor_scalar(
                    s_g[:, 0:_w], s_hi[:, 0:_w], s_w2[:, _ii:_ii + 1], None,
                    op0=ALU.mult), [hi, war.get("gread")])
            else:
                gt = P.v(lambda v, _ii=ii, _w=w: v.scalar_tensor_tensor(
                    s_g[:, 0:_w], s_hi[:, 0:_w], s_w2[:, _ii:_ii + 1],
                    s_g[:, 0:_w], op0=ALU.mult, op1=ALU.add), [hi, gt])
            war["hiread"] = gt
        war["sread"] = gt
        g2 = P.v(lambda v, _w=w: v.tensor_scalar(
            s_g[:, 0:_w], s_g[:, 0:_w], s_b2[:], None, op0=ALU.add), [gt])
        gb = P.v(lambda v, _w=w: v.tensor_copy(
            s_gb[:, 0:_w], s_g[:, 0:_w]), [g2, war.get("gbread")])
        war["gread"] = gb
        ztoks = []
        for cj in range(NC):
            mmt = P.t(lambda t, _c=cj, _w=w: t.matmul(
                p_o[_c % 2][:, 0:_w],
                lhsT=s_woutb[:, _c * 128:(_c + 1) * 128],
                rhs=s_gb[:, 0:_w], start=True, stop=True),
                [gb, lgw, war.get(("o", cj % 2))])
            ev = P.s(lambda s, _c=cj, _w=w: s.activation(
                s_tmp[:, 0:_w], p_o[_c % 2][:, 0:_w], AF.Identity,
                bias=s_bout[:, _c:_c + 1]), [mmt, war.get("tmpread")])
            zt = P.v(lambda v, _c=cj, _i=i, _w=w: v.tensor_tensor(
                s_z[:, _c * 512:_c * 512 + _w], s_tmp[:, 0:_w],
                s_x3[:, _c * n + _i * 512:_c * n + _i * 512 + _w], ALU.add),
                [ev, lall, war.get(("zread", cj))])
            war[("o", cj % 2)] = zt
            war["tmpread"] = zt
            ztoks.append(zt)
        war["gbread"] = ztoks[-1]
        war["sq"] = ztoks[-1]
        outtoks = emit_ln(P, 512, 0, w, s_z, s_z, s_m, s_r, s_tmp, s_trow,
                          s_onec, s_oner, p_s1, p_s2, p_mbc, p_rbc, ztoks,
                          war, s_eps=s_eps)
        war["tmpread"] = war["sq"]
        stoks.append(P.sd(lambda s, _i=i, _w=w: s.dma_start(
            out=fmview(YT, n)[:, :, _i * 512:_i * 512 + _w],
            in_=s_z[:].rearrange("p (a n) -> p a n", a=NC)[:, :, 0:_w]),
            outtoks, ch="st"))
        for cj in range(NC):
            war[("zread", cj)] = stoks[-1]
    P.finish(stoks)
    return P.build().nc


def _run(nc, maps):
    res = run_bass_kernel_spmd(nc, maps, core_ids=list(range(8)))
    return res.results


def kernel_unfused(L0_attn_x, L0_attn_router_w, L0_attn_router_b, L0_attn_wq,
           L0_attn_wk, L0_attn_wv, L0_ffn_w_up, L0_ffn_w_down, L1_attn_w_qkv,
           L1_attn_b_qkv, L1_attn_w_out, L1_attn_b_out, L1_attn_proj_matrix,
           L1_ffn_Win, L1_ffn_bin_, L1_ffn_W1, L1_ffn_b1, L1_ffn_W2,
           L1_ffn_b2, L1_ffn_Wout, L1_ffn_bout):
    _ = (L0_attn_wq.shape, L0_attn_wk.shape, L0_attn_wv.shape)
    x = np.ascontiguousarray(L0_attn_x, dtype=np.float32)
    ident = np.eye(128, dtype=np.float32)
    identb = np.eye(128, dtype=np.float32).astype(ml_dtypes.bfloat16)

    def core_bo(core):
        return core // 4, (core % 4) * NTOK

    xt = {}
    for core in range(8):
        b, o = core_bo(core)
        xt[core] = np.ascontiguousarray(x[b, o:o + NTOK].T)

    nc = build_a1()
    rwc = np.ascontiguousarray(L0_attn_router_w.reshape(NC, 128).T)
    rb = np.array([[L0_attn_router_b[0]]], dtype=np.float32)
    maps = [{"xt": xt[c], "rwc": rwc, "rb": rb} for c in range(8)]
    r = _run(nc, maps)
    rw_full = np.zeros((2, NSEQ), np.float32)
    for core in range(8):
        b, o = core_bo(core)
        rw_full[b, o:o + NTOK] = r[core]["rws"][0]

    nc = build_a2()
    maps = []
    for core in range(8):
        b, o = core_bo(core)
        maps.append({
            "xt": xt[core],
            "xrevt": np.ascontiguousarray(x[b][::-1][o:o + NTOK].T),
            "rwf": np.ascontiguousarray(rw_full[b].reshape(128, 64)),
            "rwo": np.ascontiguousarray(rw_full[b][o:o + NTOK][None, :]),
            "ident": ident,
        })
    r = _run(nc, maps)
    x1t = {c: r[c]["x1t"] for c in range(8)}

    wupt = np.ascontiguousarray(L0_ffn_w_up.T)
    wdnt = np.ascontiguousarray(L0_ffn_w_down.T)
    nc = build_b1()
    maps = [{"wupt_s": np.ascontiguousarray(wupt[:, 512 * c:512 * (c + 1)]),
             "wdnt_s": np.ascontiguousarray(wdnt[:, 128 * c:128 * (c + 1)])}
            for c in range(8)]
    r = _run(nc, maps)
    aparts = np.concatenate([r[c]["aparts"] for c in range(8)], axis=1)
    nc = build_b1b()
    for c in range(8):
        maps[c]["aparts8"] = aparts
    r = _run(nc, maps)
    wtu = np.ascontiguousarray(
        np.concatenate([r[c]["wtu_s"] for c in range(8)], axis=1))
    wtd = np.ascontiguousarray(
        np.concatenate([r[c]["wtd_s"] for c in range(8)], axis=1))

    nc = build_b2()
    maps = [{"x1t": x1t[c], "wtu": wtu, "wtd": wtd} for c in range(8)]
    r = _run(nc, maps)
    x2t = {c: r[c]["x2t"] for c in range(8)}
    x2b_full = {b: np.concatenate([x2t[b * 4 + g] for g in range(4)], axis=1)
                for b in range(2)}

    nc = build_c1()
    wqkv = L1_attn_w_qkv
    maps = []
    for core in range(8):
        b, g = core // 4, core % 4
        cols = slice(g * 256, (g + 1) * 256)
        maps.append({
            "x2tb": x2b_full[b],
            "wq": np.ascontiguousarray(wqkv[:, 0:D][:, cols]),
            "wk": np.ascontiguousarray(wqkv[:, D:2 * D][:, cols]),
            "wv": np.ascontiguousarray(wqkv[:, 2 * D:3 * D][:, cols]),
            "bq": np.ascontiguousarray(
                L1_attn_b_qkv[0:D][cols].reshape(2, 128).T),
            "bk": np.ascontiguousarray(
                L1_attn_b_qkv[D:2 * D][cols].reshape(2, 128).T),
            "bvrow": np.ascontiguousarray(
                L1_attn_b_qkv[2 * D:3 * D][cols][None, :]),
            "projt": np.ascontiguousarray(np.concatenate(
                [L1_attn_proj_matrix[4 * g + h].T for h in range(4)], axis=1)),
            "identb": identb,
        })
    r = _run(nc, maps)
    nd_full = {b: np.concatenate([r[b * 4 + g]["ndt"] for g in range(4)],
                                 axis=0) for b in range(2)}

    nc = build_c2()
    maps = []
    for core in range(8):
        b, o = core_bo(core)
        maps.append({
            "ndts": np.ascontiguousarray(nd_full[b][:, o:o + NTOK]),
            "wo": L1_attn_w_out,
            "bo": np.ascontiguousarray(L1_attn_b_out.reshape(NC, 128).T),
            "x2t": x2t[core],
        })
    r = _run(nc, maps)
    x3t = {c: r[c]["x3t"] for c in range(8)}

    nc = build_d()
    maps = []
    for core in range(8):
        maps.append({
            "x3t": x3t[core],
            "win": L1_ffn_Win,
            "bin": np.ascontiguousarray(L1_ffn_bin_[:, None]),
            "w1": L1_ffn_W1, "b1": L1_ffn_b1, "w2": L1_ffn_W2,
            "b2": np.ascontiguousarray(L1_ffn_b2[:, None]),
            "wout": L1_ffn_Wout,
            "bout": np.ascontiguousarray(L1_ffn_bout.reshape(NC, 128).T),
        })
    r = _run(nc, maps)
    out = np.zeros((2, NSEQ, D), np.float32)
    for core in range(8):
        b, o = core_bo(core)
        out[b, o:o + NTOK] = r[core]["yt"].T
    return out


HEADS = 16


def build_fused(n=NTOK, dump=False):
    nc = bass.Bass("TRN2", target_bir_lowering=False, debug=False)
    nsub = n // 512
    ntile = n // 128
    KOFF = max(NC * n, NFF * 512)
    XT = nc.dram_tensor("xt", [D, n], F32, kind="ExternalInput").ap()
    XRT = nc.dram_tensor("xrevt", [D, n], F32, kind="ExternalInput").ap()
    RWC = nc.dram_tensor("rwc", [128, NC], F32, kind="ExternalInput").ap()
    RB = nc.dram_tensor("rb", [1, 1], F32, kind="ExternalInput").ap()
    IDENT = nc.dram_tensor("ident", [128, 128], F32, kind="ExternalInput").ap()
    IDB = nc.dram_tensor("identb", [128, 128], BF16, kind="ExternalInput").ap()
    WUPT = nc.dram_tensor("wupt", [D, DFF], F32, kind="ExternalInput").ap()
    WDNT = nc.dram_tensor("wdnt", [DFF, D], F32, kind="ExternalInput").ap()
    WQKV = nc.dram_tensor("wqkv", [D, 3 * D], F32, kind="ExternalInput").ap()
    BQ = nc.dram_tensor("bq", [128, 8], F32, kind="ExternalInput").ap()
    BK = nc.dram_tensor("bk", [128, 8], F32, kind="ExternalInput").ap()
    BV = nc.dram_tensor("bvrow", [1, D], F32, kind="ExternalInput").ap()
    PJ = nc.dram_tensor("projt", [64, HEADS * 256], F32,
                        kind="ExternalInput").ap()
    WO = nc.dram_tensor("wo", [D, D], F32, kind="ExternalInput").ap()
    BO = nc.dram_tensor("bo", [128, NC], F32, kind="ExternalInput").ap()
    WIN = nc.dram_tensor("win", [D, 64], F32, kind="ExternalInput").ap()
    BIN = nc.dram_tensor("bin", [64, 1], F32, kind="ExternalInput").ap()
    W1 = nc.dram_tensor("w1", [64, 16], F32, kind="ExternalInput").ap()
    B1 = nc.dram_tensor("b1", [64, 16], F32, kind="ExternalInput").ap()
    W2 = nc.dram_tensor("w2", [64, 16], F32, kind="ExternalInput").ap()
    B2_ = nc.dram_tensor("b2", [64, 1], F32, kind="ExternalInput").ap()
    WOUT = nc.dram_tensor("wout", [64, D], F32, kind="ExternalInput").ap()
    BOUT = nc.dram_tensor("bout", [128, NC], F32, kind="ExternalInput").ap()
    YT = nc.dram_tensor("yt", [D, n], F32, kind="ExternalOutput").ap()
    if dump:
        XD1 = nc.dram_tensor("xd1", [D, n], F32, kind="ExternalOutput").ap()
        XD2 = nc.dram_tensor("xd2", [D, n], F32, kind="ExternalOutput").ap()
        XD3 = nc.dram_tensor("xd3", [D, n], F32, kind="ExternalOutput").ap()
        WTUD = nc.dram_tensor("wtud", [D, DFF], BF16,
                              kind="ExternalOutput").ap()
        WTDD = nc.dram_tensor("wtdd", [DFF, D], BF16,
                              kind="ExternalOutput").ap()
    cc1_in = nc.dram_tensor("cc1_in", [1, n], F32, kind="Internal").ap()
    cc1_out = nc.dram_tensor("cc1_out", [4, n], F32, kind="Internal").ap()
    kv_in = nc.dram_tensor("kv_in", [128, 2080], F32, kind="Internal").ap()
    kv_out = nc.dram_tensor("kv_out", [128, 2080], F32, kind="Internal").ap()
    WTU = nc.dram_tensor("wtu_sc", [D, DFF], BF16, kind="Internal").ap()
    WTD = nc.dram_tensor("wtd_sc", [DFF, D], BF16, kind="Internal").ap()

    P = Prog(nc)
    es = P.es
    SA = sb(nc, es, "SA", [128, NC * n])
    SB = sb(nc, es, "SB", [128, max(NC * n, NFF * 512)])
    SBb = SB[:].bitcast(BF16)
    SC = sb(nc, es, "SC", [128, 8192])
    SCb = SC[:].bitcast(BF16)
    SD = sb(nc, es, "SD", [128, 2080])
    SE = sb(nc, es, "SE", [128, 2048])
    SEb = SE[:].bitcast(BF16)
    AW = sb(nc, es, "AW", [128, 2048])
    AWb = AW[:].bitcast(BF16)
    s_p56 = sb(nc, es, "s_p56", [128, 1024])
    s_rwc = sb(nc, es, "s_rwc", [128, NC])
    s_rb = sb(nc, es, "s_rb", [1, 1])
    s_rwf = sb(nc, es, "s_rwf", [128, 4 * n // 128])
    s_idb = sb(nc, es, "s_idb", [128, 128], BF16)
    s_oner = sb(nc, es, "s_oner", [1, 128])
    s_onec = sb(nc, es, "s_onec", [128, 1])
    s_onecb = sb(nc, es, "s_onecb", [128, 1], BF16)
    s_ones2 = sb(nc, es, "s_ones2", [128, 128])
    s_eps = sb(nc, es, "s_eps", [1, 1])
    s_tmp = sb(nc, es, "s_tmp", [128, 512])
    s_m = sb(nc, es, "s_m", [1, 512])
    s_r = sb(nc, es, "s_r", [1, 512])
    s_hrow = sb(nc, es, "s_hrow", [1, 512])
    s_row = s_hrow
    s_trow = s_hrow
    s_acc = sb(nc, es, "s_acc", [128, 32])
    s_red = sb(nc, es, "s_red", [128, 2])
    s_th = sb(nc, es, "s_th", [1, 2])
    s_thc = sb(nc, es, "s_thc", [128, 2])
    st = {kk: sb(nc, es, kk, [128, 1]) for kk in
          ("t_lo", "t_hi", "t_mid", "rowcnt")}
    st["fixed_bounds"] = True
    st["fs"] = sb(nc, es, "fs", [128, 1], I32)
    st["fsn"] = sb(nc, es, "fsn", [128, 1], I32)
    st["mask64"] = sb(nc, es, "mask64", [128, 4 * n // 128])
    st["maskrow"] = None
    s_bq = sb(nc, es, "s_bq", [128, 8])
    s_bk = sb(nc, es, "s_bk", [128, 8])
    s_pj = sb(nc, es, "s_pj", [128, HEADS * 256], BF16)
    s_phik = sb(nc, es, "s_phik", [128, 256], BF16)
    s_nmx = sb(nc, es, "s_nmx", [128, 1])
    s_phiqT = [SEb[:, 0:512], SEb[:, 512:1024]]
    s_kvs = sb(nc, es, "s_kvs", [128, 132], BF16)
    s_bo = sb(nc, es, "s_bo", [128, NC])
    s_bin = sb(nc, es, "s_bin", [64, 1])
    s_w1 = sb(nc, es, "s_w1", [64, 16])
    s_b1 = sb(nc, es, "s_b1", [64, 16])
    s_w2 = sb(nc, es, "s_w2", [64, 16])
    s_b2 = sb(nc, es, "s_b2", [64, 1])
    s_bout = sb(nc, es, "s_bout", [128, NC])
    s_bv = s_p56[0:1, 0:D]
    s_exp = s_p56[:, 512:768]
    s_mrow = s_r[0:1, 0:512]
    s_den = s_p56[0:1, 0:512]
    s_dbc = s_p56[:, 0:512]
    s_s = SC[0:64, 0:512]
    s_hi = SD[0:64, 0:512]
    s_g = SC[0:64, 512:1024]
    s_gb = SCb[0:64, 3584:4096]
    s_winb = SCb[:, 2048:2560]
    s_woutb = SCb[0:64, 2560:3584]
    B = [ps(nc, es, f"B{i}", [128, 512]) for i in range(7)]
    B7 = ps(nc, es, "B7", [128, 1024], BF16)

    war = {}

    P.sd(lambda s: s.dma_start(
        out=SA[:].rearrange("p (a n) -> p a n", a=NC), in_=fmview(XT, n)),
        ch="x")
    P.sd(lambda s: s.dma_start(
        out=SB[:, 0:NC * n].rearrange("p (a n) -> p a n", a=NC),
        in_=fmview(XRT, n)), ch="x")
    lxall = P.total("syd", "x")
    for ap_, tile in ((RWC, s_rwc), (RB, s_rb),
                      (IDB, s_idb), (BQ, s_bq), (BK, s_bk),
                      (BO, s_bo), (BIN, s_bin), (W1, s_w1), (B1, s_b1),
                      (W2, s_w2), (B2_, s_b2), (BOUT, s_bout)):
        P.sd(lambda s, _a=ap_, _t=tile: s.dma_start(out=_t[:], in_=_a))
    lall = P.total("syd", "ld")
    m1 = P.v(lambda v: v.memset(s_oner[:], 1.0))
    m2 = P.v(lambda v: v.memset(s_onec[:], 1.0))
    m2b = P.v(lambda v: v.memset(s_onecb[:], 1.0))
    m3 = P.v(lambda v: v.memset(s_ones2[:], 1.0))
    m4 = P.v(lambda v: v.memset(s_eps[:], LN_EPS))
    pjt = None
    for half in range(2):
        P.sd(lambda s, _h=half: s.dma_start(
            out=SD[0:64, 0:2048], in_=PJ[:, _h * 2048:(_h + 1) * 2048]),
            [pjt], ch="pj")
        P.sd(lambda s, _h=half: s.dma_start(
            out=SD[64:128, 0:2048], in_=PJ[:, _h * 2048:(_h + 1) * 2048]),
            ch="pj")
        pjt = P.v(lambda v, _h=half: v.tensor_scalar(
            s_pj[:, _h * 2048:(_h + 1) * 2048], SD[:, 0:2048], DATA_NORM,
            None, op0=ALU.mult), [P.total("syd", "pj")])
    pj = pjt
    lbv = P.sd(lambda s: s.dma_start(out=s_bv, in_=BV), [pj], ch="bv")
    lbv = P.total("syd", "bv")

    s_rwo = SD[0:1, 0:n]
    rwevs = []
    for i in range(nsub):
        pb = B[i % 2]
        mmtok = None
        for j in range(NC):
            mmtok = P.t(lambda t, _i=i, _j=j, _pb=pb: t.matmul(
                _pb[0:1, 0:512], lhsT=s_rwc[:, _j:_j + 1],
                rhs=SA[:, _j * n + _i * 512:_j * n + _i * 512 + 512],
                start=(_j == 0), stop=(_j == NC - 1)),
                [lxall, lall, war.get(("rwp", i % 2))])
        ev = P.v(lambda v, _i=i, _pb=pb: v.tensor_scalar(
            s_rwo[:, _i * 512:_i * 512 + 512], _pb[0:1, 0:512], s_rb[:],
            None, op0=ALU.add), [mmtok, pj])
        war[("rwp", i % 2)] = ev
        rwevs.append(ev)
    P.gd(lambda g: g.dma_start(out=cc1_in, in_=s_rwo[:]), rwevs, ch="cc")
    cc1 = P.col(lambda g: g.collective_compute(
        "AllGather", ALU.bypass,
        replica_groups=[[0, 1, 2, 3], [4, 5, 6, 7]],
        ins=[cc1_in.opt()], outs=[cc1_out.opt()]),
        [P.total("gpd", "cc")])
    P.gd(lambda g: g.dma_start(
        out=s_rwf[:], in_=cc1_out.rearrange("r (p f) -> (r p) f", p=32)),
        [cc1], ch="g1")
    lrwf = P.total("gpd", "g1")

    stat_toks = []
    bufs = [AW[:, 0:2048], SC[:, 0:2048]]
    acc_i = 0
    for wi, (WT, chw, a_) in enumerate(((WUPT, 256, NC), (WDNT, 64, NFF))):
        for ci in range(16):
            buf = bufs[ci % 2]
            ld = P.gd(lambda g, _W=WT, _ci=ci, _chw=chw, _a=a_, _buf=buf:
                      g.dma_start(
                out=_buf.rearrange("p (a m) -> p a m", a=_a),
                in_=_W.rearrange("(a p) m -> p a m", p=128)[
                    :, :, _ci * _chw:(_ci + 1) * _chw]),
                [war.get(("wst", ci % 2))], ch=f"wst{ci % 2}")
            rt = P.v(lambda v, _buf=buf, _i=acc_i: v.tensor_reduce(
                s_acc[:, _i:_i + 1], _buf, axis=AX.X, op=ALU.add,
                apply_absolute_value=True), [ld])
            war[("wst", ci % 2)] = rt
            stat_toks.append(rt)
            acc_i += 1
    r1 = P.v(lambda v: v.tensor_reduce(
        s_red[:, 0:1], s_acc[:, 0:16], axis=AX.X, op=ALU.add), stat_toks)
    r2 = P.v(lambda v: v.tensor_reduce(
        s_red[:, 1:2], s_acc[:, 16:32], axis=AX.X, op=ALU.add), [r1])
    apm = P.t(lambda t: t.matmul(B[2][0:1, 0:2], lhsT=s_onec[:],
                                 rhs=s_red[:, 0:2], start=True, stop=True),
              [r2, m2])
    a3 = P.v(lambda v: v.tensor_scalar(
        s_th[:, 0:2], B[2][0:1, 0:2], 0.5 / (D * DFF), None, op0=ALU.mult),
        [apm])
    bb = P.t(lambda t: t.matmul(B[2][:, 2:4], lhsT=s_oner[:],
                                rhs=s_th[:, 0:2], start=True, stop=True),
             [a3, m1])
    thc = P.v(lambda v: v.tensor_copy(s_thc[:], B[2][:, 2:4]), [bb])

    bis_last = emit_bisect(P, s_rwf, None, s_ones2, s_oner, B[3], B[4], st,
                           lrwf, None, lall, m1, m3, niter=50)

    prevab = None
    for wi, (WT, OUTT, chw, a_) in enumerate(
            ((WUPT, WTU, 256, NC), (WDNT, WTD, 64, NFF))):
        for ci in range(16):
            buf = bufs[ci % 2]
            ld = P.gd(lambda g, _W=WT, _ci=ci, _chw=chw, _a=a_, _buf=buf:
                      g.dma_start(
                out=_buf.rearrange("p (a m) -> p a m", a=_a),
                in_=_W.rearrange("(a p) m -> p a m", p=128)[
                    :, :, _ci * _chw:(_ci + 1) * _chw]),
                [war.get(("wtz", ci % 2)), stat_toks[-1]],
                ch=f"wtz{ci % 2}")
            wtoks = []
            for gi in range(4):
                sl = slice(gi * 512, (gi + 1) * 512)
                ab = P.s(lambda s, _buf=buf, _sl=sl: s.activation(
                    s_tmp[:], _buf[:, _sl], AF.Abs), [ld, prevab])
                mk = P.v(lambda v, _wi=wi: v.tensor_scalar(
                    s_tmp[:], s_tmp[:], s_thc[:, _wi:_wi + 1], None,
                    op0=ALU.is_gt), [thc, ab])
                wm = P.v(lambda v, _buf=buf, _sl=sl: v.tensor_tensor(
                    s_tmp[:], s_tmp[:], _buf[:, _sl], ALU.mult), [mk])
                wt = P.s(lambda s, _sl=sl: s.activation(
                    SEb[:, _sl], s_tmp[:], AF.Sign),
                    [wm, war.get("sebread")])
                prevab = wt
                wtoks.append(wt)
            war[("wtz", ci % 2)] = wtoks[-1]
            stok = P.gd(lambda g, _O=OUTT, _ci=ci, _chw=chw, _a=a_:
                        g.dma_start(
                out=_O.rearrange("(a p) m -> p a m", p=128)[
                    :, :, _ci * _chw:(_ci + 1) * _chw],
                in_=SEb[:, 0:2048].rearrange("p (a m) -> p a m", a=_a)),
                wtoks, ch="wts")
            war["sebread"] = stok
    wts_done = P.total("gpd", "wts")
    if dump:
        P.sd(lambda s: s.dma_start(out=WTUD, in_=WTU), [wts_done], ch="dbg")
        P.sd(lambda s: s.dma_start(out=WTDD, in_=WTD), [wts_done], ch="dbg")

    ln_war = {"eps_dep": m4, "sq": prevab}
    prev_msk = None
    prev_mrow = None
    x1toks = {}
    for i in range(nsub):
        mr = P.v(lambda v, _i=i: v.tensor_scalar(
            s_mrow[:], s_rwo[:, _i * 512:_i * 512 + 512],
            st["t_lo"][0:1, 0:1], None, op0=ALU.is_ge),
            [bis_last, prev_mrow, pj,
             (x1toks[i - 1][-1] if i > 0 else None)])
        mb = P.t(lambda t: t.matmul(
            B[5][:, 0:512], lhsT=s_oner[:], rhs=s_mrow[:],
            start=True, stop=True), [mr, m1, prev_msk])
        prev_mrow = mb
        ytoks = []
        for j in range(NC):
            off = j * n + i * 512
            ym = P.v(lambda v, _o=off: v.tensor_tensor(
                SB[:, _o:_o + 512], SB[:, _o:_o + 512], B[5][:, 0:512],
                ALU.mult), [mb, lxall])
            ya = P.v(lambda v, _o=off: v.tensor_tensor(
                SB[:, _o:_o + 512], SB[:, _o:_o + 512], SA[:, _o:_o + 512],
                ALU.add), [ym])
            ytoks.append(ya)
        prev_msk = ytoks[-1]
        x1toks[i] = emit_ln(P, n, i, 512, SB, SA, s_m, s_r, s_tmp, s_trow,
                            s_onec, s_oner, B[6][0:1, :], B[2][0:1, :],
                            B[3], B[4], ytoks, ln_war, s_eps=s_eps)

    p3done = x1toks[nsub - 1][-1]
    snap1 = None
    if dump:
        snap1 = P.sd(lambda s: s.dma_start(
            out=XD1.rearrange("(a p) m -> p a m", p=128),
            in_=SA[:].rearrange("p (a m) -> p a m", a=NC)),
            [t for i_ in range(nsub) for t in x1toks[i_]], ch="dbg")
    fw = {"eps_dep": m4, "sq": ln_war.get("sq"), "trow": ln_war.get("trow"),
          "rowread": ln_war.get("trow"), "rms": p3done, "h2": p3done,
          "bc1": p3done, "bc2": p3done}
    for _c in range(NC):
        fw[("zread", _c)] = snap1 if dump else prev_mrow
    x2toks = {}
    for i in range(nsub):
        sqprev = fw.get("sq")
        s1tok = None
        for j in range(NC):
            off = j * n + i * 512
            sq = P.s(lambda s, _o=off: s.activation(
                s_tmp[:], SA[:, _o:_o + 512], AF.Square),
                [x1toks[i][j], sqprev])
            s1tok = P.t(lambda t, _j=j: t.matmul(
                B[4][0:1, 0:512], lhsT=s_onec[:], rhs=s_tmp[:],
                start=(_j == 0), stop=(_j == NC - 1)),
                [sq, m2, fw.get("rms")])
            sqprev = s1tok
        fw["sq"] = sqprev
        r1_ = P.v(lambda v: v.tensor_scalar(
            s_row[:], B[4][0:1, 0:512], 1.0 / D, None, op0=ALU.mult),
            [s1tok, fw.get("rowread"), fw.get("trow")])
        fw["rms"] = r1_
        r2_ = P.s(lambda s: s.activation(s_row[:], s_row[:], AF.Sqrt), [r1_])
        r3_ = P.v(lambda v: v.tensor_scalar(
            s_row[:], s_row[:], RMS_EPS, None, op0=ALU.add), [r2_])
        r4_ = P.v(lambda v: v.reciprocal(s_row[:], s_row[:]), [r3_])
        b1_ = P.t(lambda t: t.matmul(
            B[6][:, 0:512], lhsT=s_oner[:], rhs=s_row[:],
            start=True, stop=True), [r4_, m1, fw.get("bc1")])
        casts = []
        for j in range(NC):
            off = j * n + i * 512
            casts.append(P.v(lambda v, _o=off, _j=j: v.tensor_tensor(
                SEb[:, _j * 512:(_j + 1) * 512], SA[:, _o:_o + 512],
                B[6][:, 0:512], ALU.mult),
                [b1_, wts_done, fw.get("xb")]))
        fw["rowread"] = casts[-1]
        hsq_prev = fw.get("sq")
        h2tok = None
        geltoks = []
        for f in range(NFF):
            wbuf = f % 2
            wtok = P.sd(lambda s, _f=f, _b=wbuf: s.dma_start(
                out=AWb[:, _b * 1024:(_b + 1) * 1024].rearrange(
                    "p (a m) -> p a m", a=NC),
                in_=WTU.rearrange("(a p) m -> p a m", p=128)[
                    :, :, _f * 128:(_f + 1) * 128]),
                [wts_done, fw.get(("wu", wbuf)), fw.get("wdread")],
                ch=f"wu{wbuf}")
            mmt = None
            for j in range(NC):
                mmt = P.t(lambda t, _f=f, _j=j, _b=wbuf: t.matmul(
                    B[_f % 2][:, 0:512],
                    lhsT=AWb[:, _b * 1024 + _j * 128:
                             _b * 1024 + (_j + 1) * 128],
                    rhs=SEb[:, _j * 512:(_j + 1) * 512],
                    start=(_j == 0), stop=(_j == NC - 1)),
                    [wtok, casts[j], fw.get(("up", f % 2))])
            fw[("wu", wbuf)] = mmt
            gel = P.s(lambda s, _f=f: s.activation(
                SB[:, _f * 512:(_f + 1) * 512], B[_f % 2][:, 0:512],
                AF.Gelu), [mmt, fw.get(("hread", f))])
            fw[("up", f % 2)] = gel
            geltoks.append(gel)
            hsq = P.s(lambda s, _f=f: s.activation(
                s_tmp[:], SB[:, _f * 512:(_f + 1) * 512], AF.Square),
                [gel, hsq_prev])
            h2tok = P.t(lambda t, _f=f: t.matmul(
                B[5][0:1, 0:512], lhsT=s_onec[:], rhs=s_tmp[:],
                start=(_f == 0), stop=(_f == NFF - 1)),
                [hsq, fw.get("h2")])
            hsq_prev = h2tok
        fw["sq"] = hsq_prev
        hr1 = P.v(lambda v: v.tensor_scalar(
            s_hrow[:], B[5][0:1, 0:512], 1.0 / DFF, None, op0=ALU.mult),
            [h2tok, fw.get("hrowread"), fw.get("trow")])
        fw["h2"] = hr1
        hr2 = P.s(lambda s: s.activation(s_hrow[:], s_hrow[:], AF.Sqrt),
                  [hr1])
        hr3 = P.v(lambda v: v.tensor_scalar(
            s_hrow[:], s_hrow[:], RMS_EPS, None, op0=ALU.add), [hr2])
        hr4 = P.v(lambda v: v.reciprocal(s_hrow[:], s_hrow[:]), [hr3])
        b2t = P.t(lambda t: t.matmul(
            B[6][:, 0:512], lhsT=s_oner[:], rhs=s_hrow[:],
            start=True, stop=True), [hr4, casts[-1], fw.get("bc2")])
        fw["bc1"] = b2t
        hcasts = []
        for f in range(NFF):
            hcasts.append(P.v(lambda v, _f=f: v.tensor_tensor(
                SCb[:, _f * 512:(_f + 1) * 512],
                SB[:, _f * 512:(_f + 1) * 512], B[6][:, 0:512], ALU.mult),
                [b2t, geltoks[f], wts_done, fw.get("hb")]))
        fw["hrowread"] = hcasts[-1]
        ztoks = []
        for cjj in range(NC):
            mmt = None
            for half in range(2):
                wtok = P.sd(lambda s, _c=cjj, _h=half: s.dma_start(
                    out=AWb[:, 0:2048].rearrange("p (a m) -> p a m", a=16),
                    in_=WTD.rearrange("(a p) m -> p a m", p=128)[
                        :, 16 * _h:16 * (_h + 1),
                        _c * 128:(_c + 1) * 128]),
                    [wts_done, fw.get("wdread"),
                     fw.get(("wu", 0)), fw.get(("wu", 1))], ch="wd")
                for f16 in range(16):
                    f = half * 16 + f16
                    mmt = P.t(lambda t, _c=cjj, _f=f, _f16=f16: t.matmul(
                        B[2 + _c % 2][:, 0:512],
                        lhsT=AWb[:, _f16 * 128:(_f16 + 1) * 128],
                        rhs=SCb[:, _f * 512:(_f + 1) * 512],
                        start=(_f == 0), stop=(_f == NFF - 1)),
                        [wtok, hcasts[f], fw.get(("dn", cjj % 2))])
                fw["wdread"] = mmt
            zt = P.v(lambda v, _c=cjj, _i=i: v.tensor_tensor(
                SA[:, _c * n + _i * 512:_c * n + _i * 512 + 512],
                B[2 + _c % 2][:, 0:512],
                SA[:, _c * n + _i * 512:_c * n + _i * 512 + 512], ALU.add),
                [mmt, fw.get(("zread", cjj))])
            fw[("dn", cjj % 2)] = zt
            ztoks.append(zt)
        fw["hb"] = ztoks[-1]
        fw["xb"] = ztoks[-1]
        fw["sq"] = ztoks[-1]
        outt = emit_ln(P, n, i, 512, SA, SA, s_m, s_r, s_tmp, s_trow,
                       s_onec, s_oner, B[4][0:1, :], B[5][0:1, :], B[6],
                       B[0], ztoks, fw, s_eps=s_eps)
        fw["rms"] = fw["s1"]
        fw["h2"] = fw["s2"]
        fw["bc2"] = fw["bc"]
        fw[("up", 0)] = fw["bc"]
        for cjj in range(NC):
            fw[("zread", cjj)] = outt[-1]
        x2toks[i] = outt
    p4done = x2toks[nsub - 1][-1]
    snap2 = None
    if dump:
        snap2 = P.sd(lambda s: s.dma_start(
            out=XD2.rearrange("(a p) m -> p a m", p=128),
            in_=SA[:].rearrange("p (a m) -> p a m", a=NC)),
            [t for i_ in range(nsub) for t in x2toks[i_]], ch="dbg")
        p4done = snap2

    for wi, (coff, dst, sbias) in enumerate(((0, 0, s_bq), (D, KOFF, s_bk))):
        for half in range(2):
            wln = P.gd(lambda g, _c=coff, _h=half: g.dma_start(
                out=AWb[:, 0:NC * 512].rearrange("p (a m) -> p a m", a=NC),
                in_=WQKV.rearrange("(a p) m -> p a m", p=128)[
                    :, :, _c + _h * 512:_c + (_h + 1) * 512]),
                [war.get("awfree"), p4done], ch="wqk")
            for i in range(nsub):
                casts = []
                for j in range(NC):
                    off = j * n + i * 512
                    casts.append(P.v(lambda v, _o=off, _j=j: v.tensor_copy(
                        SEb[:, _j * 512:(_j + 1) * 512], SA[:, _o:_o + 512]),
                        [x2toks[i][j], p4done, war.get("x2b")]))
                for hq in range(4):
                    hp = half * 4 + hq
                    mmt = None
                    for j in range(NC):
                        mmt = P.t(lambda t, _j=j, _hq=hq: t.matmul(
                            B[_hq % 2][:, 0:512],
                            lhsT=AWb[:, _j * 512 + _hq * 128:
                                     _j * 512 + _hq * 128 + 128],
                            rhs=SEb[:, _j * 512:(_j + 1) * 512],
                            start=(_j == 0), stop=(_j == NC - 1)),
                            [wln, casts[j], war.get(("pqk", hq % 2))])
                    ev = P.s(lambda s, _i=i, _hp=hp, _hq=hq, _d=dst,
                             _sb=sbias: s.activation(
                        SBb[:, _d + _hp * n + _i * 512:
                            _d + _hp * n + _i * 512 + 512],
                        B[_hq % 2][:, 0:512], AF.Identity,
                        bias=_sb[:, _hp:_hp + 1]), [mmt, lall, p4done])
                    war[("pqk", hq % 2)] = ev
                    war[("qk", wi, hp)] = ev
                war["x2b"] = war[("pqk", 1)]
            war["awfree"] = war[("pqk", 1)]
    for half in range(2):
        wlv = P.gd(lambda g, _h=half: g.dma_start(
            out=AWb[:, 0:NC * 512].rearrange("p (a m) -> p a m", a=NC),
            in_=WQKV.rearrange("(a p) m -> p a m", p=128)[
                :, :, 2 * D + _h * 512:2 * D + (_h + 1) * 512]),
            [war["awfree"]], ch="wv")
        for i in range(nsub):
            casts = []
            for j in range(NC):
                off = j * n + i * 512
                casts.append(P.v(lambda v, _o=off, _j=j: v.tensor_copy(
                    SEb[:, _j * 512:(_j + 1) * 512], SA[:, _o:_o + 512]),
                    [x2toks[i][j], war.get("x2b")]))
            for r in range(4):
                tt = i * 4 + r
                mmt = None
                for j in range(NC):
                    mmt = P.t(lambda t, _j=j, _r=r: t.matmul(
                        B[2][:, 0:512],
                        lhsT=SEb[:, _j * 512 + _r * 128:
                                 _j * 512 + _r * 128 + 128],
                        rhs=AWb[:, _j * 512:(_j + 1) * 512],
                        start=(_j == 0), stop=False),
                        [wlv, casts[j], war.get("pv")])
                mmt = P.t(lambda t, _h=half: t.matmul(
                    B[2][:, 0:512], lhsT=s_oner[:],
                    rhs=s_bv[0:1, _h * 512:(_h + 1) * 512],
                    start=False, stop=True), [mmt, m1, lbv])
                vx = P.v(lambda v, _tt=tt, _h=half: v.tensor_copy(
                    SCb[:, _tt * 1024 + _h * 512:
                        _tt * 1024 + (_h + 1) * 512],
                    B[2][:, 0:512]), [mmt, war.get("vxw")])
                war["pv"] = vx
                war["vxw"] = vx
                war[("vx", tt, half)] = vx
            war["x2b"] = war["pv"]
        war["awfree"] = war["pv"]
    for h in range(HEADS):
        hp, hb = h // 2, (h % 2) * 64
        vhalf, hv = h // 8, h % 8
        kv0 = kv1 = None
        for tt in range(ntile):
            rr = tt * 128
            xp = P.t(lambda t, _hp=hp, _hb=hb, _rr=rr, _tt=tt, _h=h:
                     t.matmul(
                B[4 + _tt % 2][:, 0:256],
                lhsT=SBb[_hb:_hb + 64,
                         KOFF + _hp * n + _rr:KOFF + _hp * n + _rr + 128],
                rhs=s_pj[_hb:_hb + 64, _h * 256:(_h + 1) * 256],
                start=True, stop=True),
                [war[("qk", 1, hp)], pj, war.get(("xpk", tt % 2))])
            rm = P.v(lambda v, _tt=tt: v.reduce_max(
                s_nmx[:], B[4 + _tt % 2][:, 0:256], axis=AX.X, negate=True),
                [xp, war.get("nmxread")])
            ex = P.s(lambda s, _tt=tt: s.activation(
                s_exp[:], B[4 + _tt % 2][:, 0:256], AF.Exp, bias=s_nmx[:]),
                [rm, war.get("exread")])
            war[("xpk", tt % 2)] = ex
            war["nmxread"] = ex
            ph = P.v(lambda v: v.tensor_scalar(
                s_phik[:], s_exp[:], 1.0 / 16.0, PHI_EPS / 16.0,
                op0=ALU.mult, op1=ALU.add), [ex, war.get("phikread")])
            war["exread"] = ph
            vsl = SCb[:, tt * 1024 + vhalf * 512 + hv * 64:
                      tt * 1024 + vhalf * 512 + hv * 64 + 64]
            kv0 = P.t(lambda t, _tt=tt, _vsl=vsl: t.matmul(
                B[6][:, 0:64], lhsT=s_phik[:, 0:128], rhs=_vsl,
                start=(_tt == 0), stop=False),
                [ph, war[("vx", tt, vhalf)], war.get("kv0ev")])
            kv0 = P.t(lambda t, _tt=tt: t.matmul(
                B[6][:, 64:65], lhsT=s_phik[:, 0:128], rhs=s_onecb[:],
                start=False, stop=(_tt == ntile - 1)), [kv0, m2b])
            kv1 = P.t(lambda t, _tt=tt, _vsl=vsl: t.matmul(
                B[3][:, 0:64], lhsT=s_phik[:, 128:256], rhs=_vsl,
                start=(_tt == 0), stop=False),
                [ph, war[("vx", tt, vhalf)], war.get("kv1ev")])
            kv1 = P.t(lambda t, _tt=tt: t.matmul(
                B[3][:, 64:65], lhsT=s_phik[:, 128:256], rhs=s_onecb[:],
                start=False, stop=(_tt == ntile - 1)), [kv1, m2b])
            war["phikread"] = kv1
        e1 = P.v(lambda v, _h=h: v.tensor_copy(
            SD[:, _h * 65:(_h + 1) * 65], B[6][:, 0:65]),
            [kv0, kv1, (p4done if h == 0 else None), war.get("sdfree")])
        e2 = P.v(lambda v, _h=h: v.tensor_copy(
            SD[:, 1040 + _h * 65:1040 + (_h + 1) * 65], B[3][:, 0:65]),
            [e1])
        war["kv0ev"] = e1
        war["kv1ev"] = e2
        war[("kvh", h)] = e2
    dkv = P.gd(lambda gg: gg.dma_start(out=kv_in, in_=SD[:, 0:2080]),
               [war[("kvh", hh)] for hh in range(HEADS)], ch="kvio")
    cc3 = P.col(lambda gg: gg.collective_compute(
        "AllReduce", ALU.add, replica_groups=[[0, 1, 2, 3], [4, 5, 6, 7]],
        ins=[kv_in.opt()], outs=[kv_out.opt()]), [P.total("gpd", "kvio")])
    P.gd(lambda gg: gg.dma_start(out=SD[:, 0:2080], in_=kv_out),
         [cc3], ch="kvb")
    lkvb = P.total("gpd", "kvb")

    pw = {"eps_dep": m4, "sq": fw.get("sq"), "trow": fw.get("trow")}
    war["pTread"] = war["x2b"]
    x3toks = {}
    for i in range(nsub):
        ndtoks = []
        for h in range(HEADS):
            hp, hb = h // 2, (h % 2) * 64
            kc = P.v(lambda v, _h=h: v.tensor_copy(
                s_kvs[:, 0:65], SD[:, _h * 65:(_h + 1) * 65]),
                [lkvb, war.get("kvsread")])
            kc2 = P.v(lambda v, _h=h: v.tensor_copy(
                s_kvs[:, 65:130],
                SD[:, 1040 + _h * 65:1040 + (_h + 1) * 65]), [kc])
            trtoks = []
            for r in range(4):
                n0 = i * 512 + r * 128
                xq = P.t(lambda t, _hp=hp, _hb=hb, _n0=n0, _r=r, _h=h:
                         t.matmul(
                    B[4 + _r % 2][:, 0:256],
                    lhsT=SBb[_hb:_hb + 64,
                             _hp * n + _n0:_hp * n + _n0 + 128],
                    rhs=s_pj[_hb:_hb + 64, _h * 256:(_h + 1) * 256],
                    start=True, stop=True),
                    [pj, war[("qk", 0, hp)], war.get(("xpk", r % 2))])
                rm = P.v(lambda v, _r=r: v.reduce_max(
                    s_nmx[:], B[4 + _r % 2][:, 0:256], axis=AX.X,
                    negate=True), [xq, war.get("nmxread")])
                ex = P.s(lambda s, _r=r: s.activation(
                    s_exp[:], B[4 + _r % 2][:, 0:256], AF.Exp,
                    bias=s_nmx[:]), [rm, war.get("exread")])
                war[("xpk", r % 2)] = ex
                war["nmxread"] = ex
                phq = P.v(lambda v: v.tensor_scalar(
                    s_phik[:], s_exp[:], 1.0 / 16.0, PHI_EPS / 16.0,
                    op0=ALU.mult, op1=ALU.add), [ex, war.get("phikread")])
                war["exread"] = phq
                t1 = P.t(lambda t: t.transpose(
                    B7[:, 0:128], s_phik[:, 0:128], s_idb[:]),
                    [phq, lall, war.get("trread")])
                t2 = P.t(lambda t: t.transpose(
                    B7[:, 128:256], s_phik[:, 128:256], s_idb[:]), [t1])
                c1t = P.v(lambda v, _r=r: v.tensor_copy(
                    s_phiqT[0][:, _r * 128:(_r + 1) * 128], B7[:, 0:128]),
                    [t2, war.get("pTread")])
                c2t = P.v(lambda v, _r=r: v.tensor_copy(
                    s_phiqT[1][:, _r * 128:(_r + 1) * 128], B7[:, 128:256]),
                    [c1t])
                war["phikread"] = c2t
                war["trread"] = c2t
                trtoks.append(c2t)
            nm = P.t(lambda t: t.matmul(
                B[0][0:65, 0:512], lhsT=s_kvs[:, 0:65],
                rhs=s_phiqT[0][:], start=True, stop=False),
                trtoks + [kc2, war.get("numread")])
            nm = P.t(lambda t: t.matmul(
                B[0][0:65, 0:512], lhsT=s_kvs[:, 65:130],
                rhs=s_phiqT[1][:], start=False, stop=True), [nm])
            war["pTread"] = nm
            war["kvsread"] = nm
            d1 = P.v(lambda v: v.tensor_scalar(
                s_den[:], B[0][64:65, 0:512], PHI_EPS, None, op0=ALU.add),
                [nm, war.get("denread")])
            d2 = P.v(lambda v: v.reciprocal(s_den[:], s_den[:]), [d1])
            db = P.t(lambda t: t.matmul(
                B[1][:, 0:512], lhsT=s_oner[:], rhs=s_den[:],
                start=True, stop=True), [d2, m1, war.get("dbcread")])
            dc = P.v(lambda v: v.tensor_copy(s_dbc[:], B[1][:, 0:512]),
                     [db])
            war["denread"] = dc
            nd = P.v(lambda v, _hp=hp, _hb=hb, _i=i: v.tensor_tensor(
                SCb[_hb:_hb + 64,
                    _hp * n + _i * 512:_hp * n + _i * 512 + 512],
                B[0][0:64, 0:512], s_dbc[_hb:_hb + 64, :], ALU.mult),
                [dc, nm, war.get("ndw")])
            war["numread"] = nd
            war["dbcread"] = nd
            ndtoks.append(nd)
        ztoks = []
        for whalf in range(2):
            wlo = P.gd(lambda g, _h=whalf: g.dma_start(
                out=AWb[:, 0:NC * 512].rearrange("p (a m) -> p a m", a=NC),
                in_=WO.rearrange("(a p) m -> p a m", p=128)[
                    :, :, _h * 512:(_h + 1) * 512]),
                [war.get("awfree"), pw.get("wofree")], ch="wo")
            for cj4 in range(4):
                cj = whalf * 4 + cj4
                mmt = None
                for j in range(NC):
                    mmt = P.t(lambda t, _c4=cj4, _j=j, _i=i: t.matmul(
                        B[2 + _c4 % 2][:, 0:512],
                        lhsT=AWb[:, _j * 512 + _c4 * 128:
                                 _j * 512 + (_c4 + 1) * 128],
                        rhs=SCb[:, _j * n + _i * 512:
                                _j * n + _i * 512 + 512],
                        start=(_j == 0), stop=(_j == NC - 1)),
                        ndtoks + [wlo, pw.get(("o", cj4 % 2))])
                pw["wofree"] = mmt
                ev = P.s(lambda s, _c=cj, _c4=cj4: s.activation(
                    s_tmp[:], B[2 + _c4 % 2][:, 0:512], AF.Identity,
                    bias=s_bo[:, _c:_c + 1]), [mmt, lall, pw.get("tread")])
                zt = P.v(lambda v, _c=cj, _i=i: v.tensor_tensor(
                    SA[:, _c * n + _i * 512:_c * n + _i * 512 + 512],
                    s_tmp[:],
                    SA[:, _c * n + _i * 512:_c * n + _i * 512 + 512],
                    ALU.add), [ev, pw.get(("zr", cj)), snap2])
                pw[("o", cj4 % 2)] = zt
                pw["tread"] = zt
                ztoks.append(zt)
        war["awfree"] = pw["wofree"]
        pw["sq"] = ztoks[-1]
        outt = emit_ln(P, n, i, 512, SA, SA, s_m, s_r, s_tmp, s_trow,
                       s_onec, s_oner, B[4][0:1, :], B[5][0:1, :], B[6],
                       B[1], ztoks, pw, s_eps=s_eps)
        for cj in range(NC):
            pw[("zr", cj)] = outt[-1]
        war[("xpk", 0)] = outt[-1]
        war[("xpk", 1)] = outt[-1]
        if i + 1 < nsub:
            war["ndw"] = outt[-1]
        x3toks[i] = outt

    x3done = x3toks[nsub - 1][-1]
    snap3 = None
    if dump:
        snap3 = P.sd(lambda s: s.dma_start(
            out=XD3.rearrange("(a p) m -> p a m", p=128),
            in_=SA[:].rearrange("p (a m) -> p a m", a=NC)),
            [t for i_ in range(nsub) for t in x3toks[i_]], ch="dbg")
        x3done = snap3
    P.gd(lambda g: g.dma_start(
        out=s_winb[:].rearrange("p (a m) -> p a m", a=NC),
        in_=WIN.rearrange("(a p) m -> p a m", p=128)), [x3done], ch="wsm")
    P.gd(lambda g: g.dma_start(out=s_woutb[:], in_=WOUT), [x3done],
         ch="wsm")
    lwsm = P.total("gpd", "wsm")
    dw = {"eps_dep": m4, "sq": pw.get("sq"), "trow": pw.get("trow")}
    stoks = []
    for i in range(nsub):
        casts = []
        for j in range(NC):
            off = j * n + i * 512
            casts.append(P.v(lambda v, _o=off, _j=j: v.tensor_copy(
                SEb[:, _j * 512:(_j + 1) * 512], SA[:, _o:_o + 512]),
                [x3toks[i][j], dw.get("x3b")]))
        mmt = None
        for j in range(NC):
            mmt = P.t(lambda t, _j=j: t.matmul(
                B[6][0:64, 0:512], lhsT=s_winb[:, _j * 64:(_j + 1) * 64],
                rhs=SEb[:, _j * 512:(_j + 1) * 512],
                start=(_j == 0), stop=(_j == NC - 1)),
                [lwsm, casts[j], dw.get("psread")])
        sv = P.s(lambda s: s.activation(
            s_s[:], B[6][0:64, 0:512], AF.Identity, bias=s_bin[:]),
            [mmt, lall, dw.get("sread")])
        dw["psread"] = sv
        gt = None
        for ii in range(16):
            hi = P.s(lambda s, _ii=ii: s.activation(
                s_hi[:], s_s[:], AF.Relu,
                scale=s_w1[:, _ii:_ii + 1], bias=s_b1[:, _ii:_ii + 1]),
                [sv, lall, dw.get("hiread"), dw.get("hi_war"), lkvb])
            if ii == 0:
                gt = P.v(lambda v: v.tensor_scalar(
                    s_g[:], s_hi[:], s_w2[:, 0:1], None, op0=ALU.mult),
                    [hi, dw.get("gread")])
            else:
                gt = P.v(lambda v, _ii=ii: v.scalar_tensor_tensor(
                    s_g[:], s_hi[:], s_w2[:, _ii:_ii + 1], s_g[:],
                    op0=ALU.mult, op1=ALU.add), [hi, gt])
            dw["hiread"] = gt
        dw["sread"] = gt
        g2 = P.v(lambda v: v.tensor_scalar(
            s_g[:], s_g[:], s_b2[:], None, op0=ALU.add), [gt])
        gb = P.v(lambda v: v.tensor_copy(s_gb[:], s_g[:]),
                 [g2, dw.get("gbread")])
        dw["gread"] = gb
        ztoks = []
        for cj in range(NC):
            mmt = P.t(lambda t, _c=cj: t.matmul(
                B[2 + _c % 2][:, 0:512],
                lhsT=s_woutb[:, _c * 128:(_c + 1) * 128],
                rhs=s_gb[:], start=True, stop=True),
                [gb, lwsm, dw.get(("o", cj % 2))])
            ev = P.s(lambda s, _c=cj: s.activation(
                s_tmp[:], B[2 + _c % 2][:, 0:512], AF.Identity,
                bias=s_bout[:, _c:_c + 1]), [mmt, dw.get("tread")])
            zt = P.v(lambda v, _c=cj, _i=i: v.tensor_tensor(
                SA[:, _c * n + _i * 512:_c * n + _i * 512 + 512], s_tmp[:],
                SA[:, _c * n + _i * 512:_c * n + _i * 512 + 512], ALU.add),
                [ev, dw.get(("zr", cj)), gb, snap3])
            dw[("o", cj % 2)] = zt
            dw["tread"] = zt
            ztoks.append(zt)
        dw["gbread"] = ztoks[-1]
        dw["x3b"] = ztoks[-1]
        dw["sq"] = ztoks[-1]
        outt = emit_ln(P, n, i, 512, SA, SA, s_m, s_r, s_tmp, s_trow,
                       s_onec, s_oner, B[4][0:1, :], B[5][0:1, :], B[0],
                       B[1], ztoks, dw, s_eps=s_eps)
        stok = P.sd(lambda s, _i=i: s.dma_start(
            out=fmview(YT, n)[:, :, _i * 512:_i * 512 + 512],
            in_=SA[:].rearrange("p (a m) -> p a m", a=NC)[
                :, :, _i * 512:_i * 512 + 512]), outt, ch="st")
        for cj in range(NC):
            dw[("zr", cj)] = stok
        dw["hi_war"] = stok
        stoks.append(stok)
    P.finish(stoks)
    P.build()
    return nc


_FUSED_NC = None


def kernel(L0_attn_x, L0_attn_router_w, L0_attn_router_b, L0_attn_wq,
           L0_attn_wk, L0_attn_wv, L0_ffn_w_up, L0_ffn_w_down, L1_attn_w_qkv,
           L1_attn_b_qkv, L1_attn_w_out, L1_attn_b_out, L1_attn_proj_matrix,
           L1_ffn_Win, L1_ffn_bin_, L1_ffn_W1, L1_ffn_b1, L1_ffn_W2,
           L1_ffn_b2, L1_ffn_Wout, L1_ffn_bout):
    _ = (L0_attn_wq.shape, L0_attn_wk.shape, L0_attn_wv.shape)
    global _FUSED_NC
    if _FUSED_NC is None:
        _FUSED_NC = build_fused(NTOK)
    nc = _FUSED_NC
    x = np.ascontiguousarray(L0_attn_x, dtype=np.float32)
    identb = np.eye(128, dtype=np.float32).astype(ml_dtypes.bfloat16)
    bqkv = L1_attn_b_qkv
    shared = {
        "rwc": np.ascontiguousarray(L0_attn_router_w.reshape(NC, 128).T),
        "rb": np.array([[L0_attn_router_b[0]]], dtype=np.float32),
        "identb": identb,
        "ident": np.eye(128, dtype=np.float32),
        "wupt": np.ascontiguousarray(L0_ffn_w_up.T),
        "wdnt": np.ascontiguousarray(L0_ffn_w_down.T),
        "wqkv": np.ascontiguousarray(L1_attn_w_qkv),
        "bq": np.ascontiguousarray(bqkv[0:D].reshape(NC, 128).T),
        "bk": np.ascontiguousarray(bqkv[D:2 * D].reshape(NC, 128).T),
        "bvrow": np.ascontiguousarray(bqkv[2 * D:3 * D][None, :]),
        "projt": np.ascontiguousarray(np.concatenate(
            [L1_attn_proj_matrix[h].T for h in range(HEADS)], axis=1)),
        "wo": np.ascontiguousarray(L1_attn_w_out),
        "bo": np.ascontiguousarray(L1_attn_b_out.reshape(NC, 128).T),
        "win": np.ascontiguousarray(L1_ffn_Win),
        "bin": np.ascontiguousarray(L1_ffn_bin_[:, None]),
        "w1": np.ascontiguousarray(L1_ffn_W1),
        "b1": np.ascontiguousarray(L1_ffn_b1),
        "w2": np.ascontiguousarray(L1_ffn_W2),
        "b2": np.ascontiguousarray(L1_ffn_b2[:, None]),
        "wout": np.ascontiguousarray(L1_ffn_Wout),
        "bout": np.ascontiguousarray(L1_ffn_bout.reshape(NC, 128).T),
    }
    maps = []
    for core in range(8):
        b, o = core // 4, (core % 4) * NTOK
        m = dict(shared)
        m["xt"] = np.ascontiguousarray(x[b, o:o + NTOK].T)
        m["xrevt"] = np.ascontiguousarray(x[b][::-1][o:o + NTOK].T)
        maps.append(m)
    res = run_bass_kernel_spmd(nc, maps, core_ids=list(range(8)))
    out = np.zeros((2, NSEQ, D), np.float32)
    for core in range(8):
        b, o = core // 4, (core % 4) * NTOK
        out[b, o:o + NTOK] = res.results[core]["yt"].T
    return out


# version 09
# speedup vs baseline: 1.1073x; 1.1073x over previous
import numpy as np
import ml_dtypes
from contextlib import ExitStack

import concourse.bass as bass
import concourse.mybir as mybir
from concourse.bass_utils import run_bass_kernel_spmd

F32 = mybir.dt.float32
BF16 = mybir.dt.bfloat16
I32 = mybir.dt.int32
AF = mybir.ActivationFunctionType
ALU = mybir.AluOpType
AX = mybir.AxisListType

D = 1024
NC = 8
DFF = 4096
NFF = 32
NTOK = 2048
NSEQ = 8192
HEADS = 16
HDIM = 64
NBF = 256
LN_EPS = 1e-5
RMS_EPS = 1e-8
PHI_EPS = 1e-6
DATA_NORM = float(np.float32(64.0 ** -0.25))
NITER_BISECT = 40
ROUTER_K = 1024.0


class Prog:
    SECTIONS = {
        "tensor": ("pe", 1),
        "vector": ("ve", 1),
        "scalar": ("sc", 1),
        "gpsimd": ("gp", 1),
    }

    def __init__(self, nc):
        self.nc = nc
        self.es = ExitStack()
        self.items = {s: [] for s in
                      ("tensor", "vector", "scalar", "gpsimd", "sync")}
        self.counts = {}
        self.n_cc = 0
        self.final_deps = []

    def _add(self, section, sem, inc, fn, deps):
        n = self.counts.get(sem, 0) + inc
        self.counts[sem] = n
        self.items[section].append((fn, [d for d in deps if d is not None],
                                    sem, inc))
        return (sem, n)

    def t(self, fn, deps=()):
        return self._add("tensor", "pe", 1, fn, deps)

    def v(self, fn, deps=()):
        return self._add("vector", "ve", 1, fn, deps)

    def s(self, fn, deps=()):
        return self._add("scalar", "sc", 1, fn, deps)

    def g(self, fn, deps=()):
        return self._add("gpsimd", "gp", 1, fn, deps)

    def sd(self, fn, deps=(), ch="ld"):
        return self._add("sync", f"syd_{ch}", 16, fn, deps)

    def ad(self, fn, deps=(), ch="ld"):
        return self._add("scalar", f"scd_{ch}", 16, fn, deps)

    def gd(self, fn, deps=(), ch="ld"):
        return self._add("gpsimd", f"gpd_{ch}", 16, fn, deps)

    def total(self, queue, ch):
        name = f"{queue}_{ch}"
        return (name, self.counts.get(name, 0))

    def col(self, fn, deps=()):
        self.n_cc += 1
        sem = f"cc{self.n_cc}"
        return self._add("gpsimd", sem, 1, fn, deps)

    def finish(self, deps):
        self.final_deps = list(deps)

    def build(self):
        nc = self.nc
        sems = {}
        for name in self.counts:
            sems[name] = self.es.enter_context(nc.semaphore(name))
        block = self.es.enter_context(nc.Block())
        hw = {}

        def coalesce(deps):
            best = {}
            for dsem, dval in deps:
                if best.get(dsem, 0) < dval:
                    best[dsem] = dval
            return best.items()

        def emit(section, eng):
            for fn, deps, sem, inc in self.items[section]:
                for dsem, dval in coalesce(deps):
                    key = (section, dsem)
                    if hw.get(key, 0) < dval:
                        hw[key] = dval
                        eng.wait_ge(sems[dsem], dval)
                ins = fn(eng)
                ins.then_inc(sems[sem], inc)
            if section == "sync":
                for dsem, dval in coalesce(self.final_deps):
                    key = (section, dsem)
                    if hw.get(key, 0) < dval:
                        hw[key] = dval
                        eng.wait_ge(sems[dsem], dval)

        if self.items["tensor"]:
            block.tensor(lambda t: emit("tensor", t))
        if self.items["vector"]:
            block.vector(lambda v: emit("vector", v))
        if self.items["scalar"]:
            block.scalar(lambda s: emit("scalar", s))
        if self.items["gpsimd"]:
            block.gpsimd(lambda g: emit("gpsimd", g))
        block.sync(lambda s: emit("sync", s))
        return self


def sb(nc, es, name, shape, dt=F32):
    return es.enter_context(nc.sbuf_tensor(name, shape, dt))


def ps(nc, es, name, shape, dt=F32):
    return es.enter_context(nc.psum_tensor(name, shape, dt))


def emit_bisect(P, s_rwf, s_rwo, s_ones2, s_oner, p_cnt, p_bc, st,
                dep_rwf, dep_rwo, dep_ident, dep_oner, dep_ones2,
                k=ROUTER_K, niter=NITER_BISECT):
    rmax, rmin = st.get("rmax"), st.get("rmin")
    t_lo, t_hi, t_mid = st["t_lo"], st["t_hi"], st["t_mid"]
    fs, fsn, rowcnt = st["fs"], st["fsn"], st["rowcnt"]
    mask64, maskrow = st["mask64"], st["maskrow"]

    if st.get("fixed_bounds"):
        c1 = P.v(lambda v: v.memset(t_hi[:], 1e4), [dep_rwf])
        c2 = P.v(lambda v: v.memset(t_lo[:], -1e4), [c1])
        last = P.v(lambda v: v.memset(t_mid[:], 0.0), [c2])
    else:
        ident = st["ident"]
        a = P.v(lambda v: v.reduce_max(rmax[:], s_rwf[:], axis=AX.X),
                [dep_rwf])
        b = P.v(lambda v: v.tensor_reduce(rmin[:], s_rwf[:], axis=AX.X,
                                          op=ALU.min), [dep_rwf, a])
        tr1 = P.t(lambda t: t.transpose(p_bc[0:1, 0:128], rmax[:],
                                        ident[:]), [a, dep_ident])
        tr2 = P.t(lambda t: t.transpose(p_bc[0:1, 128:256], rmin[:],
                                        ident[:]), [b])
        gh = P.v(lambda v: v.tensor_reduce(st["g_hi"][:], p_bc[0:1, 0:128],
                                           axis=AX.X, op=ALU.max),
                 [tr1, tr2])
        gl = P.v(lambda v: v.tensor_reduce(st["g_lo"][:],
                                           p_bc[0:1, 128:256],
                                           axis=AX.X, op=ALU.min),
                 [tr2, gh])
        bh = P.t(lambda t: t.matmul(p_bc[:, 256:257], lhsT=s_oner[:],
                                    rhs=st["g_hi"][:], start=True,
                                    stop=True), [gh, dep_oner])
        bl = P.t(lambda t: t.matmul(p_bc[:, 257:258], lhsT=s_oner[:],
                                    rhs=st["g_lo"][:], start=True,
                                    stop=True), [gl])
        c1 = P.v(lambda v: v.tensor_copy(t_hi[:], p_bc[:, 256:257]),
                 [bh, bl])
        c2 = P.v(lambda v: v.tensor_copy(t_lo[:], p_bc[:, 257:258]),
                 [bl, c1])
        c3 = P.v(lambda v: v.tensor_tensor(t_mid[:], t_lo[:], t_hi[:],
                                           ALU.add), [c2])
        last = P.v(lambda v: v.tensor_scalar(t_mid[:], t_mid[:], 0.5, None,
                                             op0=ALU.mult), [c3])
    prev_f1 = None
    for i in range(niter):
        slot = i % 2
        mk = P.v(lambda v: v.tensor_scalar(
            mask64[:], s_rwf[:], t_mid[:], 0.0, op0=ALU.is_ge, op1=ALU.add,
            accum_out=rowcnt[:]), [last, prev_f1])
        cm = P.t(lambda t, _s=slot: t.matmul(
            p_cnt[:, _s:_s + 1], lhsT=s_ones2[:], rhs=rowcnt[:],
            start=True, stop=True), [mk, dep_ones2, prev_f1])
        f1 = P.v(lambda v, _s=slot: v.tensor_scalar(
            fs[:], p_cnt[:, _s:_s + 1], k, None, op0=ALU.is_ge), [cm])
        f2 = P.v(lambda v: v.tensor_scalar(fsn[:], fs[:], -1, 1,
                                           op0=ALU.mult, op1=ALU.add), [f1])
        s1 = P.v(lambda v: v.copy_predicated(t_lo[:], fs[:], t_mid[:]), [f2])
        s2 = P.v(lambda v: v.copy_predicated(t_hi[:], fsn[:], t_mid[:]), [s1])
        s3 = P.v(lambda v: v.tensor_tensor(t_mid[:], t_lo[:], t_hi[:],
                                           ALU.add), [s2])
        last = P.v(lambda v: v.tensor_scalar(t_mid[:], t_mid[:], 0.5, None,
                                             op0=ALU.mult), [s3])
        prev_f1 = f1
    if maskrow is None:
        return last
    mrow = P.v(lambda v: v.tensor_scalar(
        maskrow[:], s_rwo[:], t_lo[0:1, 0:1], None, op0=ALU.is_ge),
        [last, dep_rwo])
    return mrow


def emit_ln(P, n, i, w, s_y, s_out, s_m, s_r, s_tmp, s_trow, s_onec, s_oner,
            p_s1, p_s2, p_mbc, p_rbc, ytoks, war, s_eps=None,
            extra_out_dep=None, s_onecb=None):
    s1tok = s2tok = None
    sqprev = war.get("sq")
    tmpb = s_tmp[:].bitcast(BF16) if s_onecb is not None else None
    for j in range(NC):
        off = j * n + i * 512
        s1tok = P.t(lambda t, _o=off, _w=w, _j=j: t.matmul(
            p_s1[:, 0:_w], lhsT=s_onec[:], rhs=s_y[:, _o:_o + _w],
            start=(_j == 0), stop=(_j == NC - 1)),
            [ytoks[j], war.get("s1")])
        if s_onecb is not None:
            sq = P.s(lambda s, _o=off, _w=w: s.activation(
                tmpb[:, 0:_w], s_y[:, _o:_o + _w], AF.Square),
                [ytoks[j], sqprev])
            s2tok = P.t(lambda t, _w=w, _j=j: t.matmul(
                p_s2[:, 0:_w], lhsT=s_onecb[:], rhs=tmpb[:, 0:_w],
                start=(_j == 0), stop=(_j == NC - 1)),
                [sq, war.get("s2")])
        else:
            sq = P.s(lambda s, _o=off, _w=w: s.activation(
                s_tmp[:, 0:_w], s_y[:, _o:_o + _w], AF.Square),
                [ytoks[j], sqprev])
            s2tok = P.t(lambda t, _w=w, _j=j: t.matmul(
                p_s2[:, 0:_w], lhsT=s_onec[:], rhs=s_tmp[:, 0:_w],
                start=(_j == 0), stop=(_j == NC - 1)), [sq, war.get("s2")])
        sqprev = s2tok
    war["sq"] = sqprev
    o = i * 512 if s_m.shape[1] > 512 else 0
    f1 = P.v(lambda v, _w=w, _o=o: v.tensor_scalar(
        s_m[:, _o:_o + _w], p_s1[:, 0:_w], 1.0 / D, None, op0=ALU.mult),
        [s1tok])
    f2 = P.v(lambda v, _w=w, _o=o: v.tensor_scalar(
        s_r[:, _o:_o + _w], p_s2[:, 0:_w], 1.0 / D, None, op0=ALU.mult),
        [s2tok])
    war["s1"] = f1
    war["s2"] = f2
    f3 = P.v(lambda v, _w=w, _o=o: v.tensor_tensor(
        s_trow[:, 0:_w], s_m[:, _o:_o + _w], s_m[:, _o:_o + _w],
        ALU.mult), [f1, f2, war.get("trow")])
    f4 = P.v(lambda v, _w=w, _o=o: v.tensor_tensor(
        s_r[:, _o:_o + _w], s_r[:, _o:_o + _w], s_trow[:, 0:_w],
        ALU.subtract), [f3])
    war["trow"] = f4
    f5 = P.s(lambda s, _w=w, _o=o: s.activation(
        s_r[:, _o:_o + _w], s_r[:, _o:_o + _w], AF.Ln,
        bias=s_eps[:]), [f4, war.get("eps_dep")])
    war["eps_dep"] = None
    f6 = P.s(lambda s, _w=w, _o=o: s.activation(
        s_r[:, _o:_o + _w], s_r[:, _o:_o + _w], AF.Exp, scale=-0.5), [f5])
    b1 = P.t(lambda t, _w=w, _o=o: t.matmul(
        p_mbc[:, 0:_w], lhsT=s_oner[:], rhs=s_m[:, _o:_o + _w],
        start=True, stop=True), [f1, war.get("bc")])
    b2 = P.t(lambda t, _w=w, _o=o: t.matmul(
        p_rbc[:, 0:_w], lhsT=s_oner[:], rhs=s_r[:, _o:_o + _w],
        start=True, stop=True), [f6])
    outtoks = []
    for j in range(NC):
        off = j * n + i * 512
        n1 = P.v(lambda v, _o=off, _w=w: v.tensor_tensor(
            s_out[:, _o:_o + _w], s_y[:, _o:_o + _w], p_mbc[:, 0:_w],
            ALU.subtract), [b1, b2, ytoks[j], extra_out_dep])
        n2 = P.v(lambda v, _o=off, _w=w: v.tensor_tensor(
            s_out[:, _o:_o + _w], s_out[:, _o:_o + _w], p_rbc[:, 0:_w],
            ALU.mult), [n1])
        outtoks.append(n2)
    war["bc"] = outtoks[-1]
    return outtoks


def fmview(ap, n):
    return ap.rearrange("(a p) n -> p a n", p=128)


def build_a1(n=NTOK):
    nc = bass.Bass("TRN2", target_bir_lowering=False, debug=False)
    XT = nc.dram_tensor("xt", [D, n], F32, kind="ExternalInput").ap()
    RWC = nc.dram_tensor("rwc", [128, NC], F32, kind="ExternalInput").ap()
    RB = nc.dram_tensor("rb", [1, 1], F32, kind="ExternalInput").ap()
    RWS = nc.dram_tensor("rws", [1, n], F32, kind="ExternalOutput").ap()

    P = Prog(nc)
    es = P.es
    nsub = (n + 511) // 512
    s_x = sb(nc, es, "s_x", [128, NC * n])
    s_rwc = sb(nc, es, "s_rwc", [128, NC])
    s_rb = sb(nc, es, "s_rb", [1, 1])
    s_rw = sb(nc, es, "s_rw", [1, n])
    p_rw = [ps(nc, es, f"p_rw{i}", [1, 512]) for i in range(2)]

    P.sd(lambda s: s.dma_start(
        out=s_x[:].rearrange("p (a n) -> p a n", a=NC), in_=fmview(XT, n)))
    P.sd(lambda s: s.dma_start(out=s_rwc[:], in_=RWC))
    P.sd(lambda s: s.dma_start(out=s_rb[:], in_=RB))
    lall = P.total("syd", "ld")
    lx = lw = lb = lall

    evs = []
    ev_prev = [None, None]
    for i in range(nsub):
        w = min(512, n - i * 512)
        pb = p_rw[i % 2]
        mmtok = None
        for j in range(NC):
            mmtok = P.t(lambda t, _i=i, _j=j, _w=w, _pb=pb: t.matmul(
                _pb[:, 0:_w], lhsT=s_rwc[:, _j:_j + 1],
                rhs=s_x[:, _j * n + _i * 512:_j * n + _i * 512 + _w],
                start=(_j == 0), stop=(_j == NC - 1)),
                [lx, lw, ev_prev[i % 2]])
        ev = P.v(lambda v, _i=i, _w=w, _pb=pb: v.tensor_scalar(
            s_rw[:, _i * 512:_i * 512 + _w], _pb[:, 0:_w], s_rb[:], None,
            op0=ALU.add), [mmtok, lb])
        ev_prev[i % 2] = ev
        evs.append(ev)
    stok = P.sd(lambda s: s.dma_start(out=RWS, in_=s_rw[:]), evs, ch="st")
    P.finish([stok])
    return P.build().nc


def build_a2(n=NTOK, niter=NITER_BISECT):
    nc = bass.Bass("TRN2", target_bir_lowering=False, debug=False)
    XT = nc.dram_tensor("xt", [D, n], F32, kind="ExternalInput").ap()
    XRT = nc.dram_tensor("xrevt", [D, n], F32, kind="ExternalInput").ap()
    RWF = nc.dram_tensor("rwf", [128, 64], F32, kind="ExternalInput").ap()
    RWO = nc.dram_tensor("rwo", [1, n], F32, kind="ExternalInput").ap()
    IDENT = nc.dram_tensor("ident", [128, 128], F32, kind="ExternalInput").ap()
    X1T = nc.dram_tensor("x1t", [D, n], F32, kind="ExternalOutput").ap()

    P = Prog(nc)
    es = P.es
    s_x = sb(nc, es, "s_x", [128, NC * n])
    s_xr = sb(nc, es, "s_xr", [128, NC * n])
    s_y = s_xr
    s_rwf = sb(nc, es, "s_rwf", [128, 64])
    s_rwo = sb(nc, es, "s_rwo", [1, n])
    s_ones2 = sb(nc, es, "s_ones2", [128, 128])
    s_oner = sb(nc, es, "s_oner", [1, 128])
    s_onec = sb(nc, es, "s_onec", [128, 1])
    s_ident = sb(nc, es, "s_ident", [128, 128])
    st = {kk: sb(nc, es, kk, [128, 1]) for kk in
          ("rmax", "rmin", "t_lo", "t_hi", "t_mid", "rowcnt")}
    st["fs"] = sb(nc, es, "fs", [128, 1], I32)
    st["fsn"] = sb(nc, es, "fsn", [128, 1], I32)
    st["g_hi"] = sb(nc, es, "g_hi", [1, 1])
    st["g_lo"] = sb(nc, es, "g_lo", [1, 1])
    st["mask64"] = sb(nc, es, "mask64", [128, 64])
    st["maskrow"] = sb(nc, es, "maskrow", [1, n])
    st["ident"] = s_ident
    s_m = sb(nc, es, "s_m", [1, n])
    s_r = sb(nc, es, "s_r", [1, n])
    s_tmp = sb(nc, es, "s_tmp", [128, 512])
    s_trow = sb(nc, es, "s_trow", [1, 512])
    s_eps = sb(nc, es, "s_eps", [1, 1])
    s_eps8 = sb(nc, es, "s_eps8", [1, 1])
    s_epsphi = sb(nc, es, "s_epsphi", [1, 1])
    p_cnt = ps(nc, es, "p_cnt", [128, 8])
    p_bc = ps(nc, es, "p_bc", [128, 512])
    p_s1 = ps(nc, es, "p_s1", [1, 512])
    p_s2 = ps(nc, es, "p_s2", [1, 512])
    p_mbc = ps(nc, es, "p_mbc", [128, 512])
    p_rbc = ps(nc, es, "p_rbc", [128, 512])
    p_msk = ps(nc, es, "p_msk", [128, 512])

    P.sd(lambda s: s.dma_start(
        out=s_x[:].rearrange("p (a n) -> p a n", a=NC), in_=fmview(XT, n)))
    P.sd(lambda s: s.dma_start(
        out=s_xr[:].rearrange("p (a n) -> p a n", a=NC), in_=fmview(XRT, n)))
    P.sd(lambda s: s.dma_start(out=s_rwf[:], in_=RWF))
    P.sd(lambda s: s.dma_start(out=s_rwo[:], in_=RWO))
    P.sd(lambda s: s.dma_start(out=s_ident[:], in_=IDENT))
    lall = P.total("syd", "ld")
    lx = lxr = lrwf = lrwo = lid = lall
    m1 = P.v(lambda v: v.memset(s_ones2[:], 1.0))
    m2 = P.v(lambda v: v.memset(s_oner[:], 1.0))
    m3 = P.v(lambda v: v.memset(s_onec[:], 1.0))
    m4 = P.v(lambda v: v.memset(s_eps[:], LN_EPS))

    mrow = emit_bisect(P, s_rwf, s_rwo, s_ones2, s_oner, p_cnt, p_bc, st,
                       lrwf, lrwo, lid, m2, m1, niter=niter)

    nsub = (n + 511) // 512
    stoks = []
    war = {"eps_dep": m4}
    prev_msk = None
    for i in range(nsub):
        w = min(512, n - i * 512)
        mb = P.t(lambda t, _i=i, _w=w: t.matmul(
            p_msk[:, 0:_w], lhsT=s_oner[:],
            rhs=st["maskrow"][:, _i * 512:_i * 512 + _w],
            start=True, stop=True), [mrow, m2, prev_msk])
        ytoks = []
        for j in range(NC):
            off = j * n + i * 512
            ym = P.v(lambda v, _o=off, _w=w: v.tensor_tensor(
                s_y[:, _o:_o + _w], s_xr[:, _o:_o + _w], p_msk[:, 0:_w],
                ALU.mult), [mb, lxr])
            ya = P.v(lambda v, _o=off, _w=w: v.tensor_tensor(
                s_y[:, _o:_o + _w], s_y[:, _o:_o + _w], s_x[:, _o:_o + _w],
                ALU.add), [ym, lx])
            ytoks.append(ya)
        prev_msk = ytoks[-1]
        outtoks = emit_ln(P, n, i, w, s_y, s_x, s_m, s_r, s_tmp, s_trow,
                          s_onec, s_oner, p_s1, p_s2, p_mbc, p_rbc, ytoks, war,
                          s_eps=s_eps)
        stok = P.sd(lambda s, _i=i, _w=w: s.dma_start(
            out=fmview(X1T, n)[:, :, _i * 512:_i * 512 + _w],
            in_=s_x[:].rearrange("p (a n) -> p a n", a=NC)[
                :, :, _i * 512:_i * 512 + _w]), outtoks, ch="st")
        stoks.append(stok)
    P.finish(stoks)
    return P.build().nc


def build_b1():
    nc = bass.Bass("TRN2", target_bir_lowering=False, debug=False)
    WU = nc.dram_tensor("wupt_s", [D, 512], F32, kind="ExternalInput").ap()
    WD = nc.dram_tensor("wdnt_s", [DFF, 128], F32, kind="ExternalInput").ap()
    AP_ = nc.dram_tensor("aparts", [1, 2], F32, kind="ExternalOutput").ap()

    P = Prog(nc)
    es = P.es
    s_wu = sb(nc, es, "s_wu", [128, NC * 512])
    s_wd = sb(nc, es, "s_wd", [128, NFF * 128])
    s_acc = sb(nc, es, "s_acc", [128, NC + NFF])
    s_red = sb(nc, es, "s_red", [128, 2])
    s_onec = sb(nc, es, "s_onec", [128, 1])
    s_out = sb(nc, es, "s_out", [1, 2])
    p_o = ps(nc, es, "p_o", [1, 8])

    P.sd(lambda s: s.dma_start(
        out=s_wu[:].rearrange("p (a n) -> p a n", a=NC),
        in_=WU.rearrange("(a p) n -> p a n", p=128)))
    P.sd(lambda s: s.dma_start(
        out=s_wd[:].rearrange("p (a n) -> p a n", a=NFF),
        in_=WD.rearrange("(a p) n -> p a n", p=128)))
    lall = P.total("syd", "ld")
    m1 = P.v(lambda v: v.memset(s_onec[:], 1.0))

    rtoks = []
    for j in range(NC):
        rtoks.append(P.v(lambda v, _j=j: v.tensor_reduce(
            s_acc[:, _j:_j + 1], s_wu[:, _j * 512:(_j + 1) * 512],
            axis=AX.X, op=ALU.add, apply_absolute_value=True), [lall]))
    for j in range(NFF):
        rtoks.append(P.v(lambda v, _j=j: v.tensor_reduce(
            s_acc[:, NC + _j:NC + _j + 1], s_wd[:, _j * 128:(_j + 1) * 128],
            axis=AX.X, op=ALU.add, apply_absolute_value=True), [lall]))
    r1 = P.v(lambda v: v.tensor_reduce(
        s_red[:, 0:1], s_acc[:, 0:NC], axis=AX.X, op=ALU.add), rtoks)
    r2 = P.v(lambda v: v.tensor_reduce(
        s_red[:, 1:2], s_acc[:, NC:NC + NFF], axis=AX.X, op=ALU.add), rtoks)
    mm = P.t(lambda t: t.matmul(p_o[0:1, 0:2], lhsT=s_onec[:],
                                rhs=s_red[:, 0:2], start=True, stop=True),
             [r1, r2, m1])
    ev = P.v(lambda v: v.tensor_copy(s_out[:], p_o[0:1, 0:2]), [mm])
    stok = P.sd(lambda s: s.dma_start(out=AP_, in_=s_out[:]), [ev], ch="st")
    P.finish([stok])
    return P.build().nc


def build_b1b():
    nc = bass.Bass("TRN2", target_bir_lowering=False, debug=False)
    WU = nc.dram_tensor("wupt_s", [D, 512], F32, kind="ExternalInput").ap()
    WD = nc.dram_tensor("wdnt_s", [DFF, 128], F32, kind="ExternalInput").ap()
    APS = nc.dram_tensor("aparts8", [1, 16], F32, kind="ExternalInput").ap()
    WTU = nc.dram_tensor("wtu_s", [D, 512], BF16, kind="ExternalOutput").ap()
    WTD = nc.dram_tensor("wtd_s", [DFF, 128], BF16, kind="ExternalOutput").ap()

    P = Prog(nc)
    es = P.es
    s_wu = sb(nc, es, "s_wu", [128, NC * 512])
    s_wd = sb(nc, es, "s_wd", [128, NFF * 128])
    s_aps = sb(nc, es, "s_aps", [1, 16])
    s_th = sb(nc, es, "s_th", [1, 2])
    s_thc = sb(nc, es, "s_thc", [128, 2])
    s_oner = sb(nc, es, "s_oner", [1, 128])
    s_msk = sb(nc, es, "s_msk", [128, 512])
    s_sgn = sb(nc, es, "s_sgn", [128, 512])
    s_wtu = sb(nc, es, "s_wtu", [128, NC * 512], BF16)
    s_wtd = sb(nc, es, "s_wtd", [128, NFF * 128], BF16)
    p_b = ps(nc, es, "p_b", [128, 8])

    P.sd(lambda s: s.dma_start(
        out=s_wu[:].rearrange("p (a n) -> p a n", a=NC),
        in_=WU.rearrange("(a p) n -> p a n", p=128)))
    P.sd(lambda s: s.dma_start(
        out=s_wd[:].rearrange("p (a n) -> p a n", a=NFF),
        in_=WD.rearrange("(a p) n -> p a n", p=128)))
    P.sd(lambda s: s.dma_start(out=s_aps[:], in_=APS))
    lall = P.total("syd", "ld")
    m1 = P.v(lambda v: v.memset(s_oner[:], 1.0))
    a1 = P.v(lambda v: v.tensor_reduce(
        s_th[:, 0:1], s_aps[0:1, 0:16:2], axis=AX.X, op=ALU.add), [lall])
    a2 = P.v(lambda v: v.tensor_reduce(
        s_th[:, 1:2], s_aps[0:1, 1:16:2], axis=AX.X, op=ALU.add), [lall, a1])
    a3 = P.v(lambda v: v.tensor_scalar(
        s_th[:, 0:2], s_th[:, 0:2], 0.5 / (D * DFF), None, op0=ALU.mult),
        [a1, a2])
    bb = P.t(lambda t: t.matmul(p_b[:, 0:2], lhsT=s_oner[:], rhs=s_th[:, 0:2],
                                start=True, stop=True), [a3, m1])
    cc = P.v(lambda v: v.tensor_copy(s_thc[:], p_b[:, 0:2]), [bb])

    stoks = []
    prevm = prevs = None
    for which, s_w, s_wt, nch, wid, OUT in (
            (0, s_wu, s_wtu, NC, 512, WTU),
            (1, s_wd, s_wtd, NFF, 128, WTD)):
        for j in range(nch):
            sl = slice(j * wid, (j + 1) * wid)
            ab = P.s(lambda s, _sl=sl, _wd=wid, _sw=s_w: s.activation(
                s_msk[:, 0:_wd], _sw[:, _sl], AF.Abs), [lall, prevm])
            mk = P.v(lambda v, _w=which, _wd=wid: v.tensor_scalar(
                s_msk[:, 0:_wd], s_msk[:, 0:_wd], s_thc[:, _w:_w + 1], None,
                op0=ALU.is_gt), [cc, ab, prevm])
            sg = P.s(lambda s, _sl=sl, _wd=wid, _sw=s_w: s.activation(
                s_sgn[:, 0:_wd], _sw[:, _sl], AF.Sign), [lall, prevs, ab])
            wt = P.v(lambda v, _sl=sl, _wd=wid, _swt=s_wt: v.tensor_tensor(
                _swt[:, _sl], s_msk[:, 0:_wd], s_sgn[:, 0:_wd], ALU.mult),
                [mk, sg])
            prevm = prevs = wt
        stoks.append(P.sd(lambda s, _swt=s_wt, _n=nch, _wd=wid, _O=OUT:
                          s.dma_start(
            out=_O.rearrange("(a p) n -> p a n", p=128),
            in_=_swt[:].rearrange("p (a n) -> p a n", a=_n)),
            [prevm], ch="st"))
    P.finish(stoks)
    return P.build().nc


def build_b2(n=NTOK):
    nc = bass.Bass("TRN2", target_bir_lowering=False, debug=False)
    X1T = nc.dram_tensor("x1t", [D, n], F32, kind="ExternalInput").ap()
    WTU = nc.dram_tensor("wtu", [D, DFF], BF16, kind="ExternalInput").ap()
    WTD = nc.dram_tensor("wtd", [DFF, D], BF16, kind="ExternalInput").ap()
    X2T = nc.dram_tensor("x2t", [D, n], F32, kind="ExternalOutput").ap()

    P = Prog(nc)
    es = P.es
    s_x1 = [sb(nc, es, f"s_x1{i}", [128, NC * 512]) for i in range(2)]
    s_xb = sb(nc, es, "s_xb", [128, NC * 512], BF16)
    s_h = sb(nc, es, "s_h", [128, NFF * 512])
    s_hb = sb(nc, es, "s_hb", [128, NFF * 512], BF16)
    s_z = sb(nc, es, "s_z", [128, NC * 512])
    s_wu = [sb(nc, es, f"s_wu{i}", [128, NC * 128], BF16) for i in range(2)]
    s_wd = [sb(nc, es, f"s_wd{i}", [128, NFF * 128], BF16) for i in range(2)]
    s_tmp = sb(nc, es, "s_tmp", [128, 512])
    s_row = sb(nc, es, "s_row", [1, 512])
    s_hrow = sb(nc, es, "s_hrow", [1, 512])
    s_m = sb(nc, es, "s_m", [1, 512])
    s_r = sb(nc, es, "s_r", [1, 512])
    s_trow = sb(nc, es, "s_trow", [1, 512])
    s_eps = sb(nc, es, "s_eps", [1, 1])
    s_eps8 = sb(nc, es, "s_eps8", [1, 1])
    s_epsphi = sb(nc, es, "s_epsphi", [1, 1])
    s_onec = sb(nc, es, "s_onec", [128, 1])
    s_oner = sb(nc, es, "s_oner", [1, 128])
    p_up = [ps(nc, es, f"p_up{i}", [128, 512]) for i in range(2)]
    p_dn = [ps(nc, es, f"p_dn{i}", [128, 512]) for i in range(2)]
    p_rms = ps(nc, es, "p_rms", [1, 512])
    p_h2 = ps(nc, es, "p_h2", [1, 512])
    p_bc1 = ps(nc, es, "p_bc1", [128, 512])
    p_bc2 = ps(nc, es, "p_bc2", [128, 512])

    lall = P.total("syd", "ld")
    m1 = P.v(lambda v: v.memset(s_onec[:], 1.0))
    m2 = P.v(lambda v: v.memset(s_oner[:], 1.0))
    m4 = P.v(lambda v: v.memset(s_eps[:], LN_EPS))

    nsub = (n + 511) // 512
    war = {"eps_dep": m4}
    wtoks = {}
    wconsumed = {}
    stoks = []
    xls = [None, None]
    xls[0] = P.sd(lambda s: s.dma_start(
        out=s_x1[0][:].rearrange("p (a n) -> p a n", a=NC),
        in_=fmview(X1T, n)[:, :, 0:512]), ch="x0")
    for i in range(nsub):
        w = min(512, n - i * 512)
        xb = i % 2
        if i + 1 < nsub:
            w2_ = min(512, n - (i + 1) * 512)
            xls[1 - xb] = P.sd(lambda s, _i=i, _b=1 - xb, _w2=w2_:
                               s.dma_start(
                out=s_x1[_b][:].rearrange("p (a n) -> p a n", a=NC)[
                    :, :, 0:_w2],
                in_=fmview(X1T, n)[:, :, (_i + 1) * 512:(_i + 1) * 512 + _w2]),
                [war.get(("x1free", 1 - xb))], ch=f"x{1 - xb}")
        lall = xls[xb]
        sqprev = war.get("sq")
        s1tok = None
        for j in range(NC):
            off = j * n + i * 512
            sq = P.s(lambda s, _o=j * 512, _w=w, _b=xb: s.activation(
                s_tmp[:, 0:_w], s_x1[_b][:, _o:_o + _w], AF.Square),
                [lall, sqprev])
            s1tok = P.t(lambda t, _w=w, _j=j: t.matmul(
                p_rms[:, 0:_w], lhsT=s_onec[:], rhs=s_tmp[:, 0:_w],
                start=(_j == 0), stop=(_j == NC - 1)),
                [sq, m1, war.get("rms")])
            sqprev = s1tok
        war["sq"] = sqprev
        r1 = P.v(lambda v, _w=w: v.tensor_scalar(
            s_row[:, 0:_w], p_rms[:, 0:_w], 1.0 / D, None, op0=ALU.mult),
            [s1tok, war.get("row")])
        war["rms"] = r1
        r2 = P.s(lambda s, _w=w: s.activation(
            s_row[:, 0:_w], s_row[:, 0:_w], AF.Sqrt), [r1])
        r3 = P.v(lambda v, _w=w: v.tensor_scalar(
            s_row[:, 0:_w], s_row[:, 0:_w], RMS_EPS, None, op0=ALU.add), [r2])
        r4 = P.v(lambda v, _w=w: v.reciprocal(
            s_row[:, 0:_w], s_row[:, 0:_w]), [r3])
        b1 = P.t(lambda t, _w=w: t.matmul(
            p_bc1[:, 0:_w], lhsT=s_oner[:], rhs=s_row[:, 0:_w],
            start=True, stop=True), [r4, m2, war.get("bc1")])
        casts = []
        for j in range(NC):
            off = j * n + i * 512
            casts.append(P.v(lambda v, _j=j, _w=w, _b=xb: v.tensor_tensor(
                s_xb[:, _j * 512:_j * 512 + _w],
                s_x1[_b][:, _j * 512:_j * 512 + _w],
                p_bc1[:, 0:_w], ALU.mult), [b1, lall, war.get("xb")]))
        war["row"] = casts[-1]

        hsq_prev = war.get("sq")
        h2tok = None
        geltoks = []
        for f in range(NFF):
            buf = f % 2
            wtok = P.sd(lambda s, _f=f, _b=buf: s.dma_start(
                out=s_wu[_b][:].rearrange("p (a n) -> p a n", a=NC),
                in_=WTU.rearrange("(a p) f -> p a f", p=128)[
                    :, :, _f * 128:(_f + 1) * 128]),
                [wconsumed.get(("wu", buf))], ch=f"wu{buf}")
            mmt = None
            for j in range(NC):
                mmt = P.t(lambda t, _f=f, _j=j, _b=buf, _w=w: t.matmul(
                    p_up[_f % 2][:, 0:_w],
                    lhsT=s_wu[_b][:, _j * 128:(_j + 1) * 128],
                    rhs=s_xb[:, _j * 512:_j * 512 + _w],
                    start=(_j == 0), stop=(_j == NC - 1)),
                    [wtok, casts[j], war.get(("up", f % 2))])
            wconsumed[("wu", buf)] = mmt
            gel = P.s(lambda s, _f=f, _w=w: s.activation(
                s_h[:, _f * 512:_f * 512 + _w], p_up[_f % 2][:, 0:_w],
                AF.Gelu), [mmt, war.get(("hread", f))])
            war[("up", f % 2)] = gel
            geltoks.append(gel)
            hsq = P.s(lambda s, _f=f, _w=w: s.activation(
                s_tmp[:, 0:_w], s_h[:, _f * 512:_f * 512 + _w], AF.Square),
                [gel, hsq_prev])
            h2tok = P.t(lambda t, _f=f, _w=w: t.matmul(
                p_h2[:, 0:_w], lhsT=s_onec[:], rhs=s_tmp[:, 0:_w],
                start=(_f == 0), stop=(_f == NFF - 1)),
                [hsq, war.get("h2")])
            hsq_prev = h2tok
        war["sq"] = hsq_prev
        hr1 = P.v(lambda v, _w=w: v.tensor_scalar(
            s_hrow[:, 0:_w], p_h2[:, 0:_w], 1.0 / DFF, None, op0=ALU.mult),
            [h2tok, war.get("hrow")])
        war["h2"] = hr1
        hr2 = P.s(lambda s, _w=w: s.activation(
            s_hrow[:, 0:_w], s_hrow[:, 0:_w], AF.Sqrt), [hr1])
        hr3 = P.v(lambda v, _w=w: v.tensor_scalar(
            s_hrow[:, 0:_w], s_hrow[:, 0:_w], RMS_EPS, None, op0=ALU.add),
            [hr2])
        hr4 = P.v(lambda v, _w=w: v.reciprocal(
            s_hrow[:, 0:_w], s_hrow[:, 0:_w]), [hr3])
        b2t = P.t(lambda t, _w=w: t.matmul(
            p_bc2[:, 0:_w], lhsT=s_oner[:], rhs=s_hrow[:, 0:_w],
            start=True, stop=True), [hr4, war.get("bc2")])
        hcasts = []
        for f in range(NFF):
            hcasts.append(P.v(lambda v, _f=f, _w=w: v.tensor_tensor(
                s_hb[:, _f * 512:_f * 512 + _w],
                s_h[:, _f * 512:_f * 512 + _w], p_bc2[:, 0:_w], ALU.mult),
                [b2t, geltoks[f], war.get("hb")]))
        war["hrow"] = hcasts[-1]

        ztoks = []
        for cjj in range(NC):
            buf = cjj % 2
            wtok = P.sd(lambda s, _c=cjj, _b=buf: s.dma_start(
                out=s_wd[_b][:].rearrange("p (a n) -> p a n", a=NFF),
                in_=WTD.rearrange("(a p) c -> p a c", p=128)[
                    :, :, _c * 128:(_c + 1) * 128]),
                [wconsumed.get(("wd", buf))], ch=f"wd{buf}")
            mmt = None
            for f in range(NFF):
                mmt = P.t(lambda t, _c=cjj, _f=f, _b=buf, _w=w: t.matmul(
                    p_dn[_c % 2][:, 0:_w],
                    lhsT=s_wd[_b][:, _f * 128:(_f + 1) * 128],
                    rhs=s_hb[:, _f * 512:_f * 512 + _w],
                    start=(_f == 0), stop=(_f == NFF - 1)),
                    [wtok, hcasts[f], war.get(("dn", cjj % 2))])
            wconsumed[("wd", buf)] = mmt
            zt = P.v(lambda v, _c=cjj, _w=w, _b=xb: v.tensor_tensor(
                s_z[:, _c * 512:_c * 512 + _w], p_dn[_c % 2][:, 0:_w],
                s_x1[_b][:, _c * 512:_c * 512 + _w], ALU.add),
                [mmt, lall, war.get(("zread", cjj))])
            war[("dn", cjj % 2)] = zt
            ztoks.append(zt)
        war["hb"] = ztoks[-1]
        war["xb"] = ztoks[-1]

        outtoks = emit_ln(P, 512, 0, w, s_z, s_z, s_m, s_r, s_tmp, s_trow,
                          s_onec, s_oner, p_rms[0:1, :], p_h2[0:1, :],
                          p_bc1, p_bc2, ztoks, war, s_eps=s_eps)
        war["rms"] = war["s1"]
        war["h2"] = war["s2"]
        war["bc1"] = war["bc"]
        war["bc2"] = war["bc"]
        for f in range(NFF):
            war[("hread", f)] = hcasts[f]
        stoks.append(P.sd(lambda s, _i=i, _w=w: s.dma_start(
            out=fmview(X2T, n)[:, :, _i * 512:_i * 512 + _w],
            in_=s_z[:].rearrange("p (a n) -> p a n", a=NC)[:, :, 0:_w]),
            outtoks, ch="st"))
        for cjj in range(NC):
            war[("zread", cjj)] = stoks[-1]
        war[("x1free", xb)] = ztoks[-1]
    P.finish(stoks)
    return P.build().nc


def build_c1(ns=NSEQ):
    nc = bass.Bass("TRN2", target_bir_lowering=False, debug=False)
    X2 = nc.dram_tensor("x2tb", [D, ns], F32, kind="ExternalInput").ap()
    WQ = nc.dram_tensor("wq", [D, 256], F32, kind="ExternalInput").ap()
    WK = nc.dram_tensor("wk", [D, 256], F32, kind="ExternalInput").ap()
    WV = nc.dram_tensor("wv", [D, 256], F32, kind="ExternalInput").ap()
    BQ = nc.dram_tensor("bq", [128, 2], F32, kind="ExternalInput").ap()
    BK = nc.dram_tensor("bk", [128, 2], F32, kind="ExternalInput").ap()
    BV = nc.dram_tensor("bvrow", [1, 256], F32, kind="ExternalInput").ap()
    PJ = nc.dram_tensor("projt", [64, 4 * 256], F32, kind="ExternalInput").ap()
    IDB = nc.dram_tensor("identb", [128, 128], BF16, kind="ExternalInput").ap()
    ND = nc.dram_tensor("ndt", [256, ns], BF16, kind="ExternalOutput").ap()

    P = Prog(nc)
    es = P.es
    s_xs = [sb(nc, es, f"s_xs{i}", [128, NC * 512], BF16) for i in range(2)]
    s_q = [sb(nc, es, f"s_q{i}", [128, ns], BF16) for i in range(2)]
    s_k = [sb(nc, es, f"s_k{i}", [128, ns], BF16) for i in range(2)]
    s_wq = sb(nc, es, "s_wq", [128, NC * 256], BF16)
    s_wk = sb(nc, es, "s_wk", [128, NC * 256], BF16)
    s_wv = sb(nc, es, "s_wv", [128, NC * 256], BF16)
    s_bq = sb(nc, es, "s_bq", [128, 2])
    s_bk = sb(nc, es, "s_bk", [128, 2])
    s_bv = sb(nc, es, "s_bv", [1, 256])
    s_pjr = sb(nc, es, "s_pjr", [128, 4 * 256])
    s_pj = sb(nc, es, "s_pj", [128, 4 * 256], BF16)
    s_idb = sb(nc, es, "s_idb", [128, 128], BF16)
    s_oner = sb(nc, es, "s_oner", [1, 128])
    s_vx = sb(nc, es, "s_vx", [128, (ns // 128) * 260], BF16)
    s_phik = sb(nc, es, "s_phik", [128, 256], BF16)
    s_exp = sb(nc, es, "s_exp", [128, 256])
    s_nmx = sb(nc, es, "s_nmx", [128, 1])
    s_kv = sb(nc, es, "s_kv", [128, 2 * 260])
    s_kvb = sb(nc, es, "s_kvb", [128, 2 * 260], BF16)
    s_phiqT = [sb(nc, es, f"s_phiqT{i}", [128, 512], BF16) for i in range(2)]
    s_den = sb(nc, es, "s_den", [1, 512])
    s_dbc = sb(nc, es, "s_dbc", [64, 512])
    s_nd = [sb(nc, es, f"s_nd{i}", [128, ns], BF16) for i in range(2)]
    p_a = ps(nc, es, "p_a", [128, 512])
    p_b = ps(nc, es, "p_b", [128, 512])
    p_c = ps(nc, es, "p_c", [128, 512])
    p_x = [ps(nc, es, f"p_x{i}", [128, 512]) for i in range(2)]
    p_kv = [ps(nc, es, f"p_kv{i}", [128, 512]) for i in range(2)]
    p_t = ps(nc, es, "p_t", [128, 1024], BF16)

    lw1 = P.gd(lambda g: g.dma_start(
        out=s_wq[:].rearrange("p (a n) -> p a n", a=NC),
        in_=WQ.rearrange("(a p) n -> p a n", p=128)), ch="w")
    lw2 = P.gd(lambda g: g.dma_start(
        out=s_wk[:].rearrange("p (a n) -> p a n", a=NC),
        in_=WK.rearrange("(a p) n -> p a n", p=128)), ch="w")
    lw3 = P.gd(lambda g: g.dma_start(
        out=s_wv[:].rearrange("p (a n) -> p a n", a=NC),
        in_=WV.rearrange("(a p) n -> p a n", p=128)), ch="w")
    lg = P.total("gpd", "w")
    P.sd(lambda s: s.dma_start(out=s_bq[:], in_=BQ))
    P.sd(lambda s: s.dma_start(out=s_bk[:], in_=BK))
    P.sd(lambda s: s.dma_start(out=s_bv[:], in_=BV))
    P.sd(lambda s: s.dma_start(out=s_idb[:], in_=IDB))
    P.sd(lambda s: s.dma_start(out=s_pjr[0:64, :], in_=PJ))
    P.sd(lambda s: s.dma_start(out=s_pjr[64:128, :], in_=PJ))
    lall = P.total("syd", "ld")
    m1 = P.v(lambda v: v.memset(s_oner[:], 1.0))
    m2 = P.v(lambda v: v.memset(s_vx[:], 1.0))
    pj = P.v(lambda v: v.tensor_scalar(
        s_pj[:], s_pjr[:], DATA_NORM, None, op0=ALU.mult), [lall])

    nsub = ns // 512
    war = {}
    xload = [None, None]
    xload[0] = P.gd(lambda g: g.dma_start(
        out=s_xs[0][:].rearrange("p (a n) -> p a n", a=NC),
        in_=fmview(X2, ns)[:, :, 0:512]), ch="xs0")

    for i in range(nsub):
        buf = i % 2
        if i + 1 < nsub:
            xload[1 - buf] = P.gd(lambda g, _i=i, _b=1 - buf: g.dma_start(
                out=s_xs[_b][:].rearrange("p (a n) -> p a n", a=NC),
                in_=fmview(X2, ns)[:, :, (_i + 1) * 512:(_i + 2) * 512]),
                [war.get(("xread", 1 - buf))], ch=f"xs{1 - buf}")
        lx = xload[buf]
        pq = [p_a, p_b]
        for W_, Sdst, Bias in ((s_wq, s_q, s_bq), (s_wk, s_k, s_bk)):
            for hp in range(2):
                mmt = None
                for j in range(NC):
                    mmt = P.t(lambda t, _j=j, _hp=hp, _b=buf, _W=W_: t.matmul(
                        pq[_hp][:, 0:512],
                        lhsT=_W[:, _j * 256 + _hp * 128:
                                _j * 256 + _hp * 128 + 128],
                        rhs=s_xs[_b][:, _j * 512:(_j + 1) * 512],
                        start=(_j == 0), stop=(_j == NC - 1)),
                        [lx, lg, war.get(("pq", hp))])
                ev = P.s(lambda s, _i=i, _hp=hp, _S=Sdst, _B=Bias:
                         s.activation(
                    _S[_hp][:, _i * 512:_i * 512 + 512], pq[_hp][:, 0:512],
                    AF.Identity, bias=_B[:, _hp:_hp + 1]), [mmt, lall])
                war[("pq", hp)] = ev
                if W_ is s_wk:
                    war[("kwrite", hp)] = ev
        lastv = None
        for r in range(4):
            rr = r * 128
            tt = i * 4 + r
            mmt = None
            for j in range(NC):
                mmt = P.t(lambda t, _j=j, _rr=rr, _b=buf: t.matmul(
                    p_c[:, 0:256],
                    lhsT=s_xs[_b][:, _j * 512 + _rr:_j * 512 + _rr + 128],
                    rhs=s_wv[:, _j * 256:(_j + 1) * 256],
                    start=(_j == 0), stop=False),
                    [lx, war.get("vread")])
            mmt = P.t(lambda t: t.matmul(
                p_c[:, 0:256], lhsT=s_oner[:], rhs=s_bv[:],
                start=False, stop=True), [mmt, m1, lall])
            vx = P.v(lambda v, _tt=tt: v.tensor_copy(
                s_vx[:, _tt * 260:(_tt + 1) * 260].rearrange(
                    "p (h c) -> p h c", h=4)[:, :, 0:64],
                p_c[:, 0:256].rearrange("p (h c) -> p h c", h=4)),
                [mmt, m2])
            war["vread"] = vx
            lastv = vx
        war[("xread", buf)] = lastv

    kvevs = []
    for h in range(4):
        hp, hb = h // 2, (h % 2) * 64
        kv0 = kv1 = None
        for tt in range(ns // 128):
            rr = tt * 128
            xp = P.t(lambda t, _hp=hp, _hb=hb, _rr=rr, _tt=tt, _h=h:
                     t.matmul(
                p_x[_tt % 2][:, 0:256],
                lhsT=s_k[_hp][_hb:_hb + 64, _rr:_rr + 128],
                rhs=s_pj[_hb:_hb + 64, _h * 256:(_h + 1) * 256],
                start=True, stop=True),
                [war.get(("kwrite", hp)), pj, war.get(("xp", tt % 2)),
                 war.get("vread")])
            rm = P.v(lambda v, _tt=tt: v.reduce_max(
                s_nmx[:], p_x[_tt % 2][:, 0:256], axis=AX.X, negate=True),
                [xp, war.get("nmxread")])
            ex = P.s(lambda s, _tt=tt: s.activation(
                s_exp[:], p_x[_tt % 2][:, 0:256], AF.Exp, bias=s_nmx[:]),
                [rm, war.get("exread")])
            war[("xp", tt % 2)] = ex
            war["nmxread"] = ex
            ph = P.v(lambda v: v.tensor_scalar(
                s_phik[:], s_exp[:], 1.0 / 16.0, PHI_EPS / 16.0,
                op0=ALU.mult, op1=ALU.add), [ex, war.get("phikread")])
            war["exread"] = ph
            kv0 = P.t(lambda t, _h=h, _tt=tt: t.matmul(
                p_kv[0][:, 0:65], lhsT=s_phik[:, 0:128],
                rhs=s_vx[:, _tt * 260 + _h * 65:_tt * 260 + (_h + 1) * 65],
                start=(_tt == 0), stop=(_tt == ns // 128 - 1)),
                [ph, war.get("kv0ev")])
            kv1 = P.t(lambda t, _h=h, _tt=tt: t.matmul(
                p_kv[1][:, 0:65], lhsT=s_phik[:, 128:256],
                rhs=s_vx[:, _tt * 260 + _h * 65:_tt * 260 + (_h + 1) * 65],
                start=(_tt == 0), stop=(_tt == ns // 128 - 1)),
                [ph, war.get("kv1ev")])
            war["phikread"] = kv1
        e1 = P.v(lambda v, _h=h: v.tensor_copy(
            s_kv[:, _h * 65:(_h + 1) * 65], p_kv[0][:, 0:65]), [kv0, kv1])
        e2 = P.v(lambda v, _h=h: v.tensor_copy(
            s_kv[:, 260 + _h * 65:260 + (_h + 1) * 65], p_kv[1][:, 0:65]),
            [e1])
        war["kv0ev"] = e1
        war["kv1ev"] = e2
        kvevs.append(e2)
    cb = P.v(lambda v: v.tensor_copy(s_kvb[:], s_kv[:]), kvevs)

    ndlast = None
    for i in range(nsub):
        for h in range(4):
            hp, hb = h // 2, (h % 2) * 64
            trtoks = []
            for r in range(4):
                n0 = i * 512 + r * 128
                xp = P.t(lambda t, _hp=hp, _hb=hb, _n0=n0, _r=r, _h=h:
                         t.matmul(
                    p_x[_r % 2][:, 0:256],
                    lhsT=s_q[_hp][_hb:_hb + 64, _n0:_n0 + 128],
                    rhs=s_pj[_hb:_hb + 64, _h * 256:(_h + 1) * 256],
                    start=True, stop=True), [pj, war.get(("xp", r % 2))])
                rm = P.v(lambda v, _r=r: v.reduce_max(
                    s_nmx[:], p_x[_r % 2][:, 0:256], axis=AX.X, negate=True),
                    [xp, war.get("nmxread")])
                ex = P.s(lambda s, _r=r: s.activation(
                    s_exp[:], p_x[_r % 2][:, 0:256], AF.Exp, bias=s_nmx[:]),
                    [rm, war.get("exread")])
                war[("xp", r % 2)] = ex
                war["nmxread"] = ex
                ph = P.v(lambda v: v.tensor_scalar(
                    s_phik[:], s_exp[:], 1.0 / 16.0, PHI_EPS / 16.0,
                    op0=ALU.mult, op1=ALU.add), [ex, war.get("phikread")])
                war["exread"] = ph
                t1 = P.t(lambda t: t.transpose(
                    p_t[:, 0:128], s_phik[:, 0:128], s_idb[:]),
                    [ph, lall, war.get("tr0")])
                t2 = P.t(lambda t: t.transpose(
                    p_t[:, 128:256], s_phik[:, 128:256], s_idb[:]), [t1])
                c1t = P.v(lambda v, _r=r: v.tensor_copy(
                    s_phiqT[0][:, _r * 128:(_r + 1) * 128], p_t[:, 0:128]),
                    [t2, war.get("pTread")])
                c2t = P.v(lambda v, _r=r: v.tensor_copy(
                    s_phiqT[1][:, _r * 128:(_r + 1) * 128], p_t[:, 128:256]),
                    [c1t])
                war["phikread"] = c2t
                war["tr0"] = c2t
                trtoks.append(c2t)
            nm = P.t(lambda t, _h=h: t.matmul(
                p_a[0:65, 0:512], lhsT=s_kvb[:, _h * 65:(_h + 1) * 65],
                rhs=s_phiqT[0][:], start=True, stop=False),
                trtoks + [cb, war.get("numread")])
            nm = P.t(lambda t, _h=h: t.matmul(
                p_a[0:65, 0:512],
                lhsT=s_kvb[:, 260 + _h * 65:260 + (_h + 1) * 65],
                rhs=s_phiqT[1][:], start=False, stop=True), [nm])
            war["pTread"] = nm
            d1 = P.v(lambda v: v.tensor_scalar(
                s_den[:], p_a[64:65, 0:512], PHI_EPS, None, op0=ALU.add),
                [nm, war.get("denread")])
            d2 = P.v(lambda v: v.reciprocal(s_den[:], s_den[:]), [d1])
            db = P.t(lambda t: t.matmul(
                p_b[0:64, 0:512], lhsT=s_oner[:, 0:64], rhs=s_den[:],
                start=True, stop=True), [d2, m1, war.get("dbcread")])
            dc = P.v(lambda v: v.tensor_copy(s_dbc[:], p_b[0:64, 0:512]),
                     [db])
            war["denread"] = dc
            nd = P.v(lambda v, _hp=hp, _hb=hb, _i=i: v.tensor_tensor(
                s_nd[_hp][_hb:_hb + 64, _i * 512:_i * 512 + 512],
                p_a[0:64, 0:512], s_dbc[:], ALU.mult), [dc, nm])
            war["numread"] = nd
            war["dbcread"] = nd
            ndlast = nd
    stoks = []
    for hp in range(2):
        stoks.append(P.sd(lambda s, _hp=hp: s.dma_start(
            out=ND[_hp * 128:(_hp + 1) * 128, :], in_=s_nd[_hp][:]),
            [ndlast], ch="st"))
    P.finish(stoks)
    return P.build().nc


def build_c2(n=NTOK):
    nc = bass.Bass("TRN2", target_bir_lowering=False, debug=False)
    NDS = nc.dram_tensor("ndts", [D, n], BF16, kind="ExternalInput").ap()
    WO = nc.dram_tensor("wo", [D, D], F32, kind="ExternalInput").ap()
    BO = nc.dram_tensor("bo", [128, NC], F32, kind="ExternalInput").ap()
    X2T = nc.dram_tensor("x2t", [D, n], F32, kind="ExternalInput").ap()
    X3T = nc.dram_tensor("x3t", [D, n], F32, kind="ExternalOutput").ap()

    P = Prog(nc)
    es = P.es
    s_ndv = sb(nc, es, "s_ndv", [128, NC * n], BF16)
    s_x2 = sb(nc, es, "s_x2", [128, NC * n])
    s_wo = sb(nc, es, "s_wo", [128, NC * D], BF16)
    s_bo = sb(nc, es, "s_bo", [128, NC])
    s_z = sb(nc, es, "s_z", [128, NC * 512])
    s_tmp = sb(nc, es, "s_tmp", [128, 512])
    s_tmpb = s_tmp[:].bitcast(BF16)
    s_m = sb(nc, es, "s_m", [1, 512])
    s_r = sb(nc, es, "s_r", [1, 512])
    s_trow = sb(nc, es, "s_trow", [1, 512])
    s_eps = sb(nc, es, "s_eps", [1, 1])
    s_eps8 = sb(nc, es, "s_eps8", [1, 1])
    s_epsphi = sb(nc, es, "s_epsphi", [1, 1])
    s_onec = sb(nc, es, "s_onec", [128, 1])
    s_oner = sb(nc, es, "s_oner", [1, 128])
    p_o = [ps(nc, es, f"p_o{i}", [128, 512]) for i in range(2)]
    p_s1 = ps(nc, es, "p_s1", [1, 512])
    p_s2 = ps(nc, es, "p_s2", [1, 512])
    p_mbc = ps(nc, es, "p_mbc", [128, 512])
    p_rbc = ps(nc, es, "p_rbc", [128, 512])

    P.sd(lambda s: s.dma_start(
        out=s_ndv[:].rearrange("p (a n) -> p a n", a=NC), in_=fmview(NDS, n)))
    P.sd(lambda s: s.dma_start(
        out=s_x2[:].rearrange("p (a n) -> p a n", a=NC), in_=fmview(X2T, n)))
    P.sd(lambda s: s.dma_start(out=s_bo[:], in_=BO))
    lall = P.total("syd", "ld")
    lwo = P.gd(lambda g: g.dma_start(
        out=s_wo[:].rearrange("p (a n) -> p a n", a=NC),
        in_=WO.rearrange("(a p) n -> p a n", p=128)), ch="w")
    m1 = P.v(lambda v: v.memset(s_onec[:], 1.0))
    m2 = P.v(lambda v: v.memset(s_oner[:], 1.0))
    m4 = P.v(lambda v: v.memset(s_eps[:], LN_EPS))

    nsub = (n + 511) // 512
    war = {"eps_dep": m4}
    stoks = []
    for i in range(nsub):
        w = min(512, n - i * 512)
        ztoks = []
        for cj in range(NC):
            mmt = None
            for j in range(NC):
                mmt = P.t(lambda t, _c=cj, _j=j, _i=i, _w=w: t.matmul(
                    p_o[_c % 2][:, 0:_w],
                    lhsT=s_wo[:, _j * D + _c * 128:_j * D + (_c + 1) * 128],
                    rhs=s_ndv[:, _j * n + _i * 512:_j * n + _i * 512 + _w],
                    start=(_j == 0), stop=(_j == NC - 1)),
                    [lall, lwo, war.get(("o", cj % 2))])
            ev = P.s(lambda s, _c=cj, _w=w: s.activation(
                s_tmp[:, 0:_w], p_o[_c % 2][:, 0:_w], AF.Identity,
                bias=s_bo[:, _c:_c + 1]), [mmt, war.get("tmpread")])
            zt = P.v(lambda v, _c=cj, _i=i, _w=w: v.tensor_tensor(
                s_z[:, _c * 512:_c * 512 + _w], s_tmp[:, 0:_w],
                s_x2[:, _c * n + _i * 512:_c * n + _i * 512 + _w], ALU.add),
                [ev, war.get(("zread", cj))])
            war[("o", cj % 2)] = zt
            war["tmpread"] = zt
            ztoks.append(zt)
        war["sq"] = ztoks[-1]
        outtoks = emit_ln(P, 512, 0, w, s_z, s_z, s_m, s_r, s_tmp, s_trow,
                          s_onec, s_oner, p_s1, p_s2, p_mbc, p_rbc, ztoks,
                          war, s_eps=s_eps)
        war["tmpread"] = war["sq"]
        stoks.append(P.sd(lambda s, _i=i, _w=w: s.dma_start(
            out=fmview(X3T, n)[:, :, _i * 512:_i * 512 + _w],
            in_=s_z[:].rearrange("p (a n) -> p a n", a=NC)[:, :, 0:_w]),
            outtoks, ch="st"))
        for cj in range(NC):
            war[("zread", cj)] = stoks[-1]
    P.finish(stoks)
    return P.build().nc


def build_d(n=NTOK):
    nc = bass.Bass("TRN2", target_bir_lowering=False, debug=False)
    X3T = nc.dram_tensor("x3t", [D, n], F32, kind="ExternalInput").ap()
    WIN = nc.dram_tensor("win", [D, 64], F32, kind="ExternalInput").ap()
    BIN = nc.dram_tensor("bin", [64, 1], F32, kind="ExternalInput").ap()
    W1 = nc.dram_tensor("w1", [64, 16], F32, kind="ExternalInput").ap()
    B1 = nc.dram_tensor("b1", [64, 16], F32, kind="ExternalInput").ap()
    W2 = nc.dram_tensor("w2", [64, 16], F32, kind="ExternalInput").ap()
    B2_ = nc.dram_tensor("b2", [64, 1], F32, kind="ExternalInput").ap()
    WOUT = nc.dram_tensor("wout", [64, D], F32, kind="ExternalInput").ap()
    BOUT = nc.dram_tensor("bout", [128, NC], F32, kind="ExternalInput").ap()
    YT = nc.dram_tensor("yt", [D, n], F32, kind="ExternalOutput").ap()
    if dump:
        XD1 = nc.dram_tensor("xd1", [D, n], F32, kind="ExternalOutput").ap()
        XD2 = nc.dram_tensor("xd2", [D, n], F32, kind="ExternalOutput").ap()
        XD3 = nc.dram_tensor("xd3", [D, n], F32, kind="ExternalOutput").ap()
        WTUD = nc.dram_tensor("wtud", [D, DFF], BF16,
                              kind="ExternalOutput").ap()
        WTDD = nc.dram_tensor("wtdd", [DFF, D], BF16,
                              kind="ExternalOutput").ap()

    P = Prog(nc)
    es = P.es
    s_x3 = sb(nc, es, "s_x3", [128, NC * n])
    s_x3b = sb(nc, es, "s_x3b", [128, NC * n], BF16)
    s_winb = sb(nc, es, "s_winb", [128, NC * 64], BF16)
    s_bin = sb(nc, es, "s_bin", [64, 1])
    s_w1 = sb(nc, es, "s_w1", [64, 16])
    s_b1 = sb(nc, es, "s_b1", [64, 16])
    s_w2 = sb(nc, es, "s_w2", [64, 16])
    s_b2 = sb(nc, es, "s_b2", [64, 1])
    s_woutb = sb(nc, es, "s_woutb", [64, D], BF16)
    s_bout = sb(nc, es, "s_bout", [128, NC])
    s_s = sb(nc, es, "s_s", [64, 512])
    s_hi = sb(nc, es, "s_hi", [64, 512])
    s_g = sb(nc, es, "s_g", [64, 512])
    s_gb = sb(nc, es, "s_gb", [64, 512], BF16)
    s_z = sb(nc, es, "s_z", [128, NC * 512])
    s_tmp = sb(nc, es, "s_tmp", [128, 512])
    s_tmpb = s_tmp[:].bitcast(BF16)
    s_m = sb(nc, es, "s_m", [1, 512])
    s_r = sb(nc, es, "s_r", [1, 512])
    s_trow = sb(nc, es, "s_trow", [1, 512])
    s_eps = sb(nc, es, "s_eps", [1, 1])
    s_eps8 = sb(nc, es, "s_eps8", [1, 1])
    s_epsphi = sb(nc, es, "s_epsphi", [1, 1])
    s_onec = sb(nc, es, "s_onec", [128, 1])
    s_oner = sb(nc, es, "s_oner", [1, 128])
    p_s = ps(nc, es, "p_s", [128, 512])
    p_o = [ps(nc, es, f"p_o{i}", [128, 512]) for i in range(2)]
    p_s1 = ps(nc, es, "p_s1", [1, 512])
    p_s2 = ps(nc, es, "p_s2", [1, 512])
    p_mbc = ps(nc, es, "p_mbc", [128, 512])
    p_rbc = ps(nc, es, "p_rbc", [128, 512])

    P.sd(lambda s: s.dma_start(
        out=s_x3[:].rearrange("p (a n) -> p a n", a=NC), in_=fmview(X3T, n)))
    P.sd(lambda s: s.dma_start(out=s_bin[:], in_=BIN))
    P.sd(lambda s: s.dma_start(out=s_w1[:], in_=W1))
    P.sd(lambda s: s.dma_start(out=s_b1[:], in_=B1))
    P.sd(lambda s: s.dma_start(out=s_w2[:], in_=W2))
    P.sd(lambda s: s.dma_start(out=s_b2[:], in_=B2_))
    P.sd(lambda s: s.dma_start(out=s_bout[:], in_=BOUT))
    lall = P.total("syd", "ld")
    lx3b = P.gd(lambda g: g.dma_start(
        out=s_x3b[:].rearrange("p (a n) -> p a n", a=NC), in_=fmview(X3T, n)),
        ch="w")
    lwin = P.gd(lambda g: g.dma_start(
        out=s_winb[:].rearrange("p (a n) -> p a n", a=NC),
        in_=WIN.rearrange("(a p) n -> p a n", p=128)), ch="w")
    lwout = P.gd(lambda g: g.dma_start(out=s_woutb[:], in_=WOUT), ch="w")
    lgw = P.total("gpd", "w")
    m1 = P.v(lambda v: v.memset(s_onec[:], 1.0))
    m2 = P.v(lambda v: v.memset(s_oner[:], 1.0))
    m4 = P.v(lambda v: v.memset(s_eps[:], LN_EPS))

    nsub = (n + 511) // 512
    war = {"eps_dep": m4}
    stoks = []
    for i in range(nsub):
        w = min(512, n - i * 512)
        mmt = None
        for j in range(NC):
            mmt = P.t(lambda t, _j=j, _i=i, _w=w: t.matmul(
                p_s[0:64, 0:_w], lhsT=s_winb[:, _j * 64:(_j + 1) * 64],
                rhs=s_x3b[:, _j * n + _i * 512:_j * n + _i * 512 + _w],
                start=(_j == 0), stop=(_j == NC - 1)),
                [lgw, war.get("psread")])
        sv = P.s(lambda s, _w=w: s.activation(
            s_s[:, 0:_w], p_s[0:64, 0:_w], AF.Identity, bias=s_bin[:]),
            [mmt, lall, war.get("sread")])
        war["psread"] = sv
        gt = None
        for ii in range(16):
            hi = P.s(lambda s, _ii=ii, _w=w: s.activation(
                s_hi[:, 0:_w], s_s[:, 0:_w], AF.Relu,
                scale=s_w1[:, _ii:_ii + 1], bias=s_b1[:, _ii:_ii + 1]),
                [sv, lall, war.get("hiread")])
            if ii == 0:
                gt = P.v(lambda v, _ii=ii, _w=w: v.tensor_scalar(
                    s_g[:, 0:_w], s_hi[:, 0:_w], s_w2[:, _ii:_ii + 1], None,
                    op0=ALU.mult), [hi, war.get("gread")])
            else:
                gt = P.v(lambda v, _ii=ii, _w=w: v.scalar_tensor_tensor(
                    s_g[:, 0:_w], s_hi[:, 0:_w], s_w2[:, _ii:_ii + 1],
                    s_g[:, 0:_w], op0=ALU.mult, op1=ALU.add), [hi, gt])
            war["hiread"] = gt
        war["sread"] = gt
        g2 = P.v(lambda v, _w=w: v.tensor_scalar(
            s_g[:, 0:_w], s_g[:, 0:_w], s_b2[:], None, op0=ALU.add), [gt])
        gb = P.v(lambda v, _w=w: v.tensor_copy(
            s_gb[:, 0:_w], s_g[:, 0:_w]), [g2, war.get("gbread")])
        war["gread"] = gb
        ztoks = []
        for cj in range(NC):
            mmt = P.t(lambda t, _c=cj, _w=w: t.matmul(
                p_o[_c % 2][:, 0:_w],
                lhsT=s_woutb[:, _c * 128:(_c + 1) * 128],
                rhs=s_gb[:, 0:_w], start=True, stop=True),
                [gb, lgw, war.get(("o", cj % 2))])
            ev = P.s(lambda s, _c=cj, _w=w: s.activation(
                s_tmp[:, 0:_w], p_o[_c % 2][:, 0:_w], AF.Identity,
                bias=s_bout[:, _c:_c + 1]), [mmt, war.get("tmpread")])
            zt = P.v(lambda v, _c=cj, _i=i, _w=w: v.tensor_tensor(
                s_z[:, _c * 512:_c * 512 + _w], s_tmp[:, 0:_w],
                s_x3[:, _c * n + _i * 512:_c * n + _i * 512 + _w], ALU.add),
                [ev, lall, war.get(("zread", cj))])
            war[("o", cj % 2)] = zt
            war["tmpread"] = zt
            ztoks.append(zt)
        war["gbread"] = ztoks[-1]
        war["sq"] = ztoks[-1]
        outtoks = emit_ln(P, 512, 0, w, s_z, s_z, s_m, s_r, s_tmp, s_trow,
                          s_onec, s_oner, p_s1, p_s2, p_mbc, p_rbc, ztoks,
                          war, s_eps=s_eps)
        war["tmpread"] = war["sq"]
        stoks.append(P.sd(lambda s, _i=i, _w=w: s.dma_start(
            out=fmview(YT, n)[:, :, _i * 512:_i * 512 + _w],
            in_=s_z[:].rearrange("p (a n) -> p a n", a=NC)[:, :, 0:_w]),
            outtoks, ch="st"))
        for cj in range(NC):
            war[("zread", cj)] = stoks[-1]
    P.finish(stoks)
    return P.build().nc


def _run(nc, maps):
    res = run_bass_kernel_spmd(nc, maps, core_ids=list(range(8)))
    return res.results


def kernel_unfused(L0_attn_x, L0_attn_router_w, L0_attn_router_b, L0_attn_wq,
           L0_attn_wk, L0_attn_wv, L0_ffn_w_up, L0_ffn_w_down, L1_attn_w_qkv,
           L1_attn_b_qkv, L1_attn_w_out, L1_attn_b_out, L1_attn_proj_matrix,
           L1_ffn_Win, L1_ffn_bin_, L1_ffn_W1, L1_ffn_b1, L1_ffn_W2,
           L1_ffn_b2, L1_ffn_Wout, L1_ffn_bout):
    _ = (L0_attn_wq.shape, L0_attn_wk.shape, L0_attn_wv.shape)
    x = np.ascontiguousarray(L0_attn_x, dtype=np.float32)
    ident = np.eye(128, dtype=np.float32)
    identb = np.eye(128, dtype=np.float32).astype(ml_dtypes.bfloat16)

    def core_bo(core):
        return core // 4, (core % 4) * NTOK

    xt = {}
    for core in range(8):
        b, o = core_bo(core)
        xt[core] = np.ascontiguousarray(x[b, o:o + NTOK].T)

    nc = build_a1()
    rwc = np.ascontiguousarray(L0_attn_router_w.reshape(NC, 128).T)
    rb = np.array([[L0_attn_router_b[0]]], dtype=np.float32)
    maps = [{"xt": xt[c], "rwc": rwc, "rb": rb} for c in range(8)]
    r = _run(nc, maps)
    rw_full = np.zeros((2, NSEQ), np.float32)
    for core in range(8):
        b, o = core_bo(core)
        rw_full[b, o:o + NTOK] = r[core]["rws"][0]

    nc = build_a2()
    maps = []
    for core in range(8):
        b, o = core_bo(core)
        maps.append({
            "xt": xt[core],
            "xrevt": np.ascontiguousarray(x[b][::-1][o:o + NTOK].T),
            "rwf": np.ascontiguousarray(rw_full[b].reshape(128, 64)),
            "rwo": np.ascontiguousarray(rw_full[b][o:o + NTOK][None, :]),
            "ident": ident,
        })
    r = _run(nc, maps)
    x1t = {c: r[c]["x1t"] for c in range(8)}

    wupt = np.ascontiguousarray(L0_ffn_w_up.T)
    wdnt = np.ascontiguousarray(L0_ffn_w_down.T)
    nc = build_b1()
    maps = [{"wupt_s": np.ascontiguousarray(wupt[:, 512 * c:512 * (c + 1)]),
             "wdnt_s": np.ascontiguousarray(wdnt[:, 128 * c:128 * (c + 1)])}
            for c in range(8)]
    r = _run(nc, maps)
    aparts = np.concatenate([r[c]["aparts"] for c in range(8)], axis=1)
    nc = build_b1b()
    for c in range(8):
        maps[c]["aparts8"] = aparts
    r = _run(nc, maps)
    wtu = np.ascontiguousarray(
        np.concatenate([r[c]["wtu_s"] for c in range(8)], axis=1))
    wtd = np.ascontiguousarray(
        np.concatenate([r[c]["wtd_s"] for c in range(8)], axis=1))

    nc = build_b2()
    maps = [{"x1t": x1t[c], "wtu": wtu, "wtd": wtd} for c in range(8)]
    r = _run(nc, maps)
    x2t = {c: r[c]["x2t"] for c in range(8)}
    x2b_full = {b: np.concatenate([x2t[b * 4 + g] for g in range(4)], axis=1)
                for b in range(2)}

    nc = build_c1()
    wqkv = L1_attn_w_qkv
    maps = []
    for core in range(8):
        b, g = core // 4, core % 4
        cols = slice(g * 256, (g + 1) * 256)
        maps.append({
            "x2tb": x2b_full[b],
            "wq": np.ascontiguousarray(wqkv[:, 0:D][:, cols]),
            "wk": np.ascontiguousarray(wqkv[:, D:2 * D][:, cols]),
            "wv": np.ascontiguousarray(wqkv[:, 2 * D:3 * D][:, cols]),
            "bq": np.ascontiguousarray(
                L1_attn_b_qkv[0:D][cols].reshape(2, 128).T),
            "bk": np.ascontiguousarray(
                L1_attn_b_qkv[D:2 * D][cols].reshape(2, 128).T),
            "bvrow": np.ascontiguousarray(
                L1_attn_b_qkv[2 * D:3 * D][cols][None, :]),
            "projt": np.ascontiguousarray(np.concatenate(
                [L1_attn_proj_matrix[4 * g + h].T for h in range(4)], axis=1)),
            "identb": identb,
        })
    r = _run(nc, maps)
    nd_full = {b: np.concatenate([r[b * 4 + g]["ndt"] for g in range(4)],
                                 axis=0) for b in range(2)}

    nc = build_c2()
    maps = []
    for core in range(8):
        b, o = core_bo(core)
        maps.append({
            "ndts": np.ascontiguousarray(nd_full[b][:, o:o + NTOK]),
            "wo": L1_attn_w_out,
            "bo": np.ascontiguousarray(L1_attn_b_out.reshape(NC, 128).T),
            "x2t": x2t[core],
        })
    r = _run(nc, maps)
    x3t = {c: r[c]["x3t"] for c in range(8)}

    nc = build_d()
    maps = []
    for core in range(8):
        maps.append({
            "x3t": x3t[core],
            "win": L1_ffn_Win,
            "bin": np.ascontiguousarray(L1_ffn_bin_[:, None]),
            "w1": L1_ffn_W1, "b1": L1_ffn_b1, "w2": L1_ffn_W2,
            "b2": np.ascontiguousarray(L1_ffn_b2[:, None]),
            "wout": L1_ffn_Wout,
            "bout": np.ascontiguousarray(L1_ffn_bout.reshape(NC, 128).T),
        })
    r = _run(nc, maps)
    out = np.zeros((2, NSEQ, D), np.float32)
    for core in range(8):
        b, o = core_bo(core)
        out[b, o:o + NTOK] = r[core]["yt"].T
    return out


HEADS = 16


def build_fused(n=NTOK, dump=False):
    nc = bass.Bass("TRN2", target_bir_lowering=False, debug=False)
    nsub = n // 512
    ntile = n // 128
    KOFF = max(NC * n, NFF * 512)
    XT = nc.dram_tensor("xt", [D, n], F32, kind="ExternalInput").ap()
    XRT = nc.dram_tensor("xrevt", [D, n], F32, kind="ExternalInput").ap()
    RWC = nc.dram_tensor("rwc", [128, NC], F32, kind="ExternalInput").ap()
    RB = nc.dram_tensor("rb", [1, 1], F32, kind="ExternalInput").ap()
    IDENT = nc.dram_tensor("ident", [128, 128], F32, kind="ExternalInput").ap()
    IDB = nc.dram_tensor("identb", [128, 128], BF16, kind="ExternalInput").ap()
    WUPT = nc.dram_tensor("wupt", [D, DFF], F32, kind="ExternalInput").ap()
    WDNT = nc.dram_tensor("wdnt", [DFF, D], F32, kind="ExternalInput").ap()
    WQKV = nc.dram_tensor("wqkv", [D, 3 * D], F32, kind="ExternalInput").ap()
    BQ = nc.dram_tensor("bq", [128, 8], F32, kind="ExternalInput").ap()
    BK = nc.dram_tensor("bk", [128, 8], F32, kind="ExternalInput").ap()
    BV = nc.dram_tensor("bvrow", [1, D], F32, kind="ExternalInput").ap()
    PJ = nc.dram_tensor("projt", [64, HEADS * 256], F32,
                        kind="ExternalInput").ap()
    WO = nc.dram_tensor("wo", [D, D], F32, kind="ExternalInput").ap()
    BO = nc.dram_tensor("bo", [128, NC], F32, kind="ExternalInput").ap()
    WIN = nc.dram_tensor("win", [D, 64], F32, kind="ExternalInput").ap()
    BIN = nc.dram_tensor("bin", [64, 1], F32, kind="ExternalInput").ap()
    W1 = nc.dram_tensor("w1", [64, 16], F32, kind="ExternalInput").ap()
    B1 = nc.dram_tensor("b1", [64, 16], F32, kind="ExternalInput").ap()
    W2 = nc.dram_tensor("w2", [64, 16], F32, kind="ExternalInput").ap()
    B2_ = nc.dram_tensor("b2", [64, 1], F32, kind="ExternalInput").ap()
    WOUT = nc.dram_tensor("wout", [64, D], F32, kind="ExternalInput").ap()
    BOUT = nc.dram_tensor("bout", [128, NC], F32, kind="ExternalInput").ap()
    YT = nc.dram_tensor("yt", [D, n], F32, kind="ExternalOutput").ap()
    if dump:
        XD1 = nc.dram_tensor("xd1", [D, n], F32, kind="ExternalOutput").ap()
        XD2 = nc.dram_tensor("xd2", [D, n], F32, kind="ExternalOutput").ap()
        XD3 = nc.dram_tensor("xd3", [D, n], F32, kind="ExternalOutput").ap()
        WTUD = nc.dram_tensor("wtud", [D, DFF], BF16,
                              kind="ExternalOutput").ap()
        WTDD = nc.dram_tensor("wtdd", [DFF, D], BF16,
                              kind="ExternalOutput").ap()
    cc1_in = nc.dram_tensor("cc1_in", [1, n], F32, kind="Internal").ap()
    cc1_out = nc.dram_tensor("cc1_out", [4, n], F32, kind="Internal").ap()
    kv_in = nc.dram_tensor("kv_in", [128, 2080], F32, kind="Internal").ap()
    kv_out = nc.dram_tensor("kv_out", [128, 2080], F32, kind="Internal").ap()
    WTU = nc.dram_tensor("wtu_sc", [D, DFF], BF16, kind="Internal").ap()
    WTD = nc.dram_tensor("wtd_sc", [DFF, D], BF16, kind="Internal").ap()

    P = Prog(nc)
    es = P.es
    SA = sb(nc, es, "SA", [128, NC * n])
    SB = sb(nc, es, "SB", [128, max(NC * n, NFF * 512)])
    SBb = SB[:].bitcast(BF16)
    SC = sb(nc, es, "SC", [128, 8192])
    SCb = SC[:].bitcast(BF16)
    SD = sb(nc, es, "SD", [128, 2080])
    SE = sb(nc, es, "SE", [128, 2048])
    SEb = SE[:].bitcast(BF16)
    AW = sb(nc, es, "AW", [128, 2048])
    AWb = AW[:].bitcast(BF16)
    s_p56 = sb(nc, es, "s_p56", [128, 1024])
    s_rwc = sb(nc, es, "s_rwc", [128, NC])
    s_rb = sb(nc, es, "s_rb", [1, 1])
    s_rwf = sb(nc, es, "s_rwf", [128, 4 * n // 128])
    s_idb = sb(nc, es, "s_idb", [128, 128], BF16)
    s_oner = sb(nc, es, "s_oner", [1, 128])
    s_onec = sb(nc, es, "s_onec", [128, 1])
    s_onecb = sb(nc, es, "s_onecb", [128, 1], BF16)
    s_ones2 = sb(nc, es, "s_ones2", [128, 128])
    s_eps = sb(nc, es, "s_eps", [1, 1])
    s_eps8 = sb(nc, es, "s_eps8", [1, 1])
    s_epsphi = sb(nc, es, "s_epsphi", [1, 1])
    s_tmp = sb(nc, es, "s_tmp", [128, 512])
    s_tmpb = s_tmp[:].bitcast(BF16)
    s_m = sb(nc, es, "s_m", [1, 512])
    s_r = sb(nc, es, "s_r", [1, 512])
    s_hrow = sb(nc, es, "s_hrow", [1, 512])
    s_row = s_hrow
    s_trow = s_hrow
    s_acc = sb(nc, es, "s_acc", [128, 32])
    s_red = sb(nc, es, "s_red", [128, 2])
    s_th = sb(nc, es, "s_th", [1, 2])
    s_thc = sb(nc, es, "s_thc", [128, 2])
    st = {kk: sb(nc, es, kk, [128, 1]) for kk in
          ("t_lo", "t_hi", "t_mid", "rowcnt")}
    st["fixed_bounds"] = True
    st["fs"] = sb(nc, es, "fs", [128, 1], I32)
    st["fsn"] = sb(nc, es, "fsn", [128, 1], I32)
    st["mask64"] = sb(nc, es, "mask64", [128, 4 * n // 128])
    st["maskrow"] = None
    s_bq = sb(nc, es, "s_bq", [128, 8])
    s_bk = sb(nc, es, "s_bk", [128, 8])
    s_pj = sb(nc, es, "s_pj", [128, HEADS * 256], BF16)
    s_phik = sb(nc, es, "s_phik", [128, 256], BF16)
    s_nmx = sb(nc, es, "s_nmx", [128, 1])
    s_phiqT = [SEb[:, 0:512], SEb[:, 512:1024]]
    s_kvs = sb(nc, es, "s_kvs", [128, 132], BF16)
    s_bo = sb(nc, es, "s_bo", [128, NC])
    s_bin = sb(nc, es, "s_bin", [64, 1])
    s_w1 = sb(nc, es, "s_w1", [64, 16])
    s_b1 = sb(nc, es, "s_b1", [64, 16])
    s_w2 = sb(nc, es, "s_w2", [64, 16])
    s_b2 = sb(nc, es, "s_b2", [64, 1])
    s_bout = sb(nc, es, "s_bout", [128, NC])
    s_bv = s_p56[0:1, 0:D]
    s_exp = s_p56[:, 512:768]
    s_mrow = s_r[0:1, 0:512]
    s_den = s_p56[0:1, 0:512]
    s_dbc = s_p56[:, 0:512]
    s_s = SC[0:64, 0:512]
    s_hi = SD[0:64, 0:512]
    s_g = SC[0:64, 512:1024]
    s_gb = SCb[0:64, 3584:4096]
    s_winb = SCb[:, 2048:2560]
    s_woutb = SCb[0:64, 2560:3584]
    B = [ps(nc, es, f"B{i}", [128, 512]) for i in range(7)]
    B7 = ps(nc, es, "B7", [128, 1024], BF16)

    war = {}

    P.sd(lambda s: s.dma_start(
        out=SA[:].rearrange("p (a n) -> p a n", a=NC), in_=fmview(XT, n)),
        ch="x")
    P.sd(lambda s: s.dma_start(
        out=SB[:, 0:NC * n].rearrange("p (a n) -> p a n", a=NC),
        in_=fmview(XRT, n)), ch="x")
    lxall = P.total("syd", "x")
    for ap_, tile in ((RWC, s_rwc), (RB, s_rb),
                      (IDB, s_idb), (BQ, s_bq), (BK, s_bk),
                      (BO, s_bo), (BIN, s_bin), (W1, s_w1), (B1, s_b1),
                      (W2, s_w2), (B2_, s_b2), (BOUT, s_bout)):
        P.sd(lambda s, _a=ap_, _t=tile: s.dma_start(out=_t[:], in_=_a))
    lall = P.total("syd", "ld")
    m1 = P.v(lambda v: v.memset(s_oner[:], 1.0))
    m2 = P.v(lambda v: v.memset(s_onec[:], 1.0))
    m2b = P.v(lambda v: v.memset(s_onecb[:], 1.0))
    m3 = P.v(lambda v: v.memset(s_ones2[:], 1.0))
    m4 = P.v(lambda v: v.memset(s_eps[:], LN_EPS))
    m4b = P.v(lambda v: v.memset(s_eps8[:], RMS_EPS))
    m4c = P.v(lambda v: v.memset(s_epsphi[:], PHI_EPS))
    pjt = None
    for half in range(2):
        P.sd(lambda s, _h=half: s.dma_start(
            out=SD[0:64, 0:2048], in_=PJ[:, _h * 2048:(_h + 1) * 2048]),
            [pjt], ch="pj")
        P.sd(lambda s, _h=half: s.dma_start(
            out=SD[64:128, 0:2048], in_=PJ[:, _h * 2048:(_h + 1) * 2048]),
            ch="pj")
        pjt = P.v(lambda v, _h=half: v.tensor_scalar(
            s_pj[:, _h * 2048:(_h + 1) * 2048], SD[:, 0:2048], DATA_NORM,
            None, op0=ALU.mult), [P.total("syd", "pj")])
    pj = pjt
    lbv = P.sd(lambda s: s.dma_start(out=s_bv, in_=BV), [pj], ch="bv")
    lbv = P.total("syd", "bv")

    s_rwo = SD[0:1, 0:n]
    rwevs = []
    for i in range(nsub):
        pb = B[i % 2]
        mmtok = None
        for j in range(NC):
            mmtok = P.t(lambda t, _i=i, _j=j, _pb=pb: t.matmul(
                _pb[0:1, 0:512], lhsT=s_rwc[:, _j:_j + 1],
                rhs=SA[:, _j * n + _i * 512:_j * n + _i * 512 + 512],
                start=(_j == 0), stop=(_j == NC - 1)),
                [lxall, lall, war.get(("rwp", i % 2))])
        ev = P.v(lambda v, _i=i, _pb=pb: v.tensor_scalar(
            s_rwo[:, _i * 512:_i * 512 + 512], _pb[0:1, 0:512], s_rb[:],
            None, op0=ALU.add), [mmtok, pj])
        war[("rwp", i % 2)] = ev
        rwevs.append(ev)
    P.gd(lambda g: g.dma_start(out=cc1_in, in_=s_rwo[:]), rwevs, ch="cc")
    cc1 = P.col(lambda g: g.collective_compute(
        "AllGather", ALU.bypass,
        replica_groups=[[0, 1, 2, 3], [4, 5, 6, 7]],
        ins=[cc1_in.opt()], outs=[cc1_out.opt()]),
        [P.total("gpd", "cc")])
    P.gd(lambda g: g.dma_start(
        out=s_rwf[:], in_=cc1_out.rearrange("r (p f) -> (r p) f", p=32)),
        [cc1], ch="g1")
    lrwf = P.total("gpd", "g1")

    stat_toks = []
    bufs = [AW[:, 0:2048], SC[:, 0:2048]]
    acc_i = 0
    for wi, (WT, chw, a_) in enumerate(((WUPT, 256, NC), (WDNT, 64, NFF))):
        for ci in range(16):
            buf = bufs[ci % 2]
            ld = P.gd(lambda g, _W=WT, _ci=ci, _chw=chw, _a=a_, _buf=buf:
                      g.dma_start(
                out=_buf.rearrange("p (a m) -> p a m", a=_a),
                in_=_W.rearrange("(a p) m -> p a m", p=128)[
                    :, :, _ci * _chw:(_ci + 1) * _chw]),
                [war.get(("wst", ci % 2))], ch=f"wst{ci % 2}")
            rt = P.v(lambda v, _buf=buf, _i=acc_i: v.tensor_reduce(
                s_acc[:, _i:_i + 1], _buf, axis=AX.X, op=ALU.add,
                apply_absolute_value=True), [ld])
            war[("wst", ci % 2)] = rt
            stat_toks.append(rt)
            acc_i += 1
    r1 = P.v(lambda v: v.tensor_reduce(
        s_red[:, 0:1], s_acc[:, 0:16], axis=AX.X, op=ALU.add), stat_toks)
    r2 = P.v(lambda v: v.tensor_reduce(
        s_red[:, 1:2], s_acc[:, 16:32], axis=AX.X, op=ALU.add), [r1])
    apm = P.t(lambda t: t.matmul(B[2][0:1, 0:2], lhsT=s_onec[:],
                                 rhs=s_red[:, 0:2], start=True, stop=True),
              [r2, m2])
    a3 = P.v(lambda v: v.tensor_scalar(
        s_th[:, 0:2], B[2][0:1, 0:2], 0.5 / (D * DFF), None, op0=ALU.mult),
        [apm])
    bb = P.t(lambda t: t.matmul(B[2][:, 2:4], lhsT=s_oner[:],
                                rhs=s_th[:, 0:2], start=True, stop=True),
             [a3, m1])
    thc = P.v(lambda v: v.tensor_copy(s_thc[:], B[2][:, 2:4]), [bb])

    bis_last = emit_bisect(P, s_rwf, None, s_ones2, s_oner, B[3], B[4], st,
                           lrwf, None, lall, m1, m3, niter=50)

    prevab = None
    for wi, (WT, OUTT, chw, a_) in enumerate(
            ((WUPT, WTU, 256, NC), (WDNT, WTD, 64, NFF))):
        for ci in range(16):
            buf = bufs[ci % 2]
            ld = P.gd(lambda g, _W=WT, _ci=ci, _chw=chw, _a=a_, _buf=buf:
                      g.dma_start(
                out=_buf.rearrange("p (a m) -> p a m", a=_a),
                in_=_W.rearrange("(a p) m -> p a m", p=128)[
                    :, :, _ci * _chw:(_ci + 1) * _chw]),
                [war.get(("wtz", ci % 2)), stat_toks[-1]],
                ch=f"wtz{ci % 2}")
            wtoks = []
            for gi in range(4):
                sl = slice(gi * 512, (gi + 1) * 512)
                ab = P.s(lambda s, _buf=buf, _sl=sl: s.activation(
                    s_tmp[:], _buf[:, _sl], AF.Abs), [ld, prevab])
                mk = P.v(lambda v, _wi=wi: v.tensor_scalar(
                    s_tmp[:], s_tmp[:], s_thc[:, _wi:_wi + 1], None,
                    op0=ALU.is_gt), [thc, ab])
                wm = P.v(lambda v, _buf=buf, _sl=sl: v.tensor_tensor(
                    s_tmp[:], s_tmp[:], _buf[:, _sl], ALU.mult), [mk])
                wt = P.s(lambda s, _sl=sl: s.activation(
                    SEb[:, _sl], s_tmp[:], AF.Sign),
                    [wm, war.get("sebread")])
                prevab = wt
                wtoks.append(wt)
            war[("wtz", ci % 2)] = wtoks[-1]
            stok = P.gd(lambda g, _O=OUTT, _ci=ci, _chw=chw, _a=a_:
                        g.dma_start(
                out=_O.rearrange("(a p) m -> p a m", p=128)[
                    :, :, _ci * _chw:(_ci + 1) * _chw],
                in_=SEb[:, 0:2048].rearrange("p (a m) -> p a m", a=_a)),
                wtoks, ch="wts")
            war["sebread"] = stok
    wts_done = P.total("gpd", "wts")
    if dump:
        P.sd(lambda s: s.dma_start(out=WTUD, in_=WTU), [wts_done], ch="dbg")
        P.sd(lambda s: s.dma_start(out=WTDD, in_=WTD), [wts_done], ch="dbg")

    ln_war = {"eps_dep": m4, "sq": prevab}
    prev_msk = None
    prev_mrow = None
    x1toks = {}
    for i in range(nsub):
        mr = P.v(lambda v, _i=i: v.tensor_scalar(
            s_mrow[:], s_rwo[:, _i * 512:_i * 512 + 512],
            st["t_lo"][0:1, 0:1], None, op0=ALU.is_ge),
            [bis_last, prev_mrow, pj,
             (x1toks[i - 1][-1] if i > 0 else None)])
        mb = P.t(lambda t: t.matmul(
            B[5][:, 0:512], lhsT=s_oner[:], rhs=s_mrow[:],
            start=True, stop=True), [mr, m1, prev_msk])
        prev_mrow = mb
        ytoks = []
        for j in range(NC):
            off = j * n + i * 512
            ym = P.v(lambda v, _o=off: v.tensor_tensor(
                SB[:, _o:_o + 512], SB[:, _o:_o + 512], B[5][:, 0:512],
                ALU.mult), [mb, lxall])
            ya = P.v(lambda v, _o=off: v.tensor_tensor(
                SB[:, _o:_o + 512], SB[:, _o:_o + 512], SA[:, _o:_o + 512],
                ALU.add), [ym])
            ytoks.append(ya)
        prev_msk = ytoks[-1]
        x1toks[i] = emit_ln(P, n, i, 512, SB, SA, s_m, s_r, s_tmp, s_trow,
                            s_onec, s_oner, B[6][0:1, :], B[2][0:1, :],
                            B[3], B[4], ytoks, ln_war, s_eps=s_eps, s_onecb=s_onecb)

    p3done = x1toks[nsub - 1][-1]
    snap1 = None
    if dump:
        snap1 = P.sd(lambda s: s.dma_start(
            out=XD1.rearrange("(a p) m -> p a m", p=128),
            in_=SA[:].rearrange("p (a m) -> p a m", a=NC)),
            [t for i_ in range(nsub) for t in x1toks[i_]], ch="dbg")
    fw = {"eps_dep": m4, "sq": ln_war.get("sq"), "trow": ln_war.get("trow"),
          "rowread": ln_war.get("trow"), "rms": p3done, "h2": p3done,
          "bc1": p3done, "bc2": p3done}
    for _c in range(NC):
        fw[("zread", _c)] = snap1 if dump else prev_mrow
    x2toks = {}
    for i in range(nsub):
        sqprev = fw.get("sq")
        s1tok = None
        for j in range(NC):
            off = j * n + i * 512
            sq = P.s(lambda s, _o=off: s.activation(
                s_tmp[:], SA[:, _o:_o + 512], AF.Square),
                [x1toks[i][j], sqprev])
            s1tok = P.t(lambda t, _j=j: t.matmul(
                B[4][0:1, 0:512], lhsT=s_onec[:], rhs=s_tmp[:],
                start=(_j == 0), stop=(_j == NC - 1)),
                [sq, m2, fw.get("rms")])
            sqprev = s1tok
        fw["sq"] = sqprev
        r1_ = P.v(lambda v: v.tensor_scalar(
            s_row[:], B[4][0:1, 0:512], 1.0 / D, None, op0=ALU.mult),
            [s1tok, fw.get("rowread"), fw.get("trow")])
        fw["rms"] = r1_
        r2_ = P.s(lambda s: s.activation(s_row[:], s_row[:], AF.Sqrt), [r1_])
        r3_ = P.s(lambda s: s.activation(s_row[:], s_row[:], AF.Identity,
                                         bias=s_eps8[:]), [r2_, m4b])
        r4a = P.s(lambda s: s.activation(s_row[:], s_row[:], AF.Ln), [r3_])
        r4_ = P.s(lambda s: s.activation(s_row[:], s_row[:], AF.Exp,
                                         scale=-1.0), [r4a])
        b1_ = P.t(lambda t: t.matmul(
            B[6][:, 0:512], lhsT=s_oner[:], rhs=s_row[:],
            start=True, stop=True), [r4_, m1, fw.get("bc1")])
        casts = []
        for j in range(NC):
            off = j * n + i * 512
            casts.append(P.v(lambda v, _o=off, _j=j: v.tensor_tensor(
                SEb[:, _j * 512:(_j + 1) * 512], SA[:, _o:_o + 512],
                B[6][:, 0:512], ALU.mult),
                [b1_, wts_done, fw.get("xb")]))
        fw["rowread"] = casts[-1]
        hsq_prev = fw.get("sq")
        h2tok = None
        geltoks = []
        for f in range(NFF):
            wbuf = f % 2
            wtok = P.sd(lambda s, _f=f, _b=wbuf: s.dma_start(
                out=AWb[:, _b * 1024:(_b + 1) * 1024].rearrange(
                    "p (a m) -> p a m", a=NC),
                in_=WTU.rearrange("(a p) m -> p a m", p=128)[
                    :, :, _f * 128:(_f + 1) * 128]),
                [wts_done, fw.get(("wu", wbuf)),
                 fw.get(("wdread", wbuf))], ch=f"wu{wbuf}")
            mmt = None
            for j in range(NC):
                mmt = P.t(lambda t, _f=f, _j=j, _b=wbuf: t.matmul(
                    B[_f % 2][:, 0:512],
                    lhsT=AWb[:, _b * 1024 + _j * 128:
                             _b * 1024 + (_j + 1) * 128],
                    rhs=SEb[:, _j * 512:(_j + 1) * 512],
                    start=(_j == 0), stop=(_j == NC - 1)),
                    [wtok, casts[j], fw.get(("up", f % 2))])
            fw[("wu", wbuf)] = mmt
            gel = P.s(lambda s, _f=f: s.activation(
                SB[:, _f * 512:(_f + 1) * 512], B[_f % 2][:, 0:512],
                AF.Gelu), [mmt, fw.get(("hread", f))])
            fw[("up", f % 2)] = gel
            geltoks.append(gel)
            hsq = P.s(lambda s, _f=f: s.activation(
                s_tmpb[:, 0:512], SB[:, _f * 512:(_f + 1) * 512],
                AF.Square), [gel, hsq_prev])
            h2tok = P.t(lambda t, _f=f: t.matmul(
                B[5][0:1, 0:512], lhsT=s_onecb[:], rhs=s_tmpb[:, 0:512],
                start=(_f == 0), stop=(_f == NFF - 1)),
                [hsq, fw.get("h2")])
            hsq_prev = h2tok
        fw["sq"] = hsq_prev
        hr1 = P.v(lambda v: v.tensor_scalar(
            s_hrow[:], B[5][0:1, 0:512], 1.0 / DFF, None, op0=ALU.mult),
            [h2tok, fw.get("hrowread"), fw.get("trow")])
        fw["h2"] = hr1
        hr2 = P.s(lambda s: s.activation(s_hrow[:], s_hrow[:], AF.Sqrt),
                  [hr1])
        hr3 = P.s(lambda s: s.activation(s_hrow[:], s_hrow[:], AF.Identity,
                                         bias=s_eps8[:]), [hr2, m4b])
        hr4a = P.s(lambda s: s.activation(s_hrow[:], s_hrow[:], AF.Ln),
                   [hr3])
        hr4 = P.s(lambda s: s.activation(s_hrow[:], s_hrow[:], AF.Exp,
                                         scale=-1.0), [hr4a])
        b2t = P.t(lambda t: t.matmul(
            B[6][:, 0:512], lhsT=s_oner[:], rhs=s_hrow[:],
            start=True, stop=True), [hr4, casts[-1], fw.get("bc2")])
        fw["bc1"] = b2t
        hcasts = []
        for f in range(NFF):
            hcasts.append(P.v(lambda v, _f=f: v.tensor_tensor(
                SCb[:, _f * 512:(_f + 1) * 512],
                SB[:, _f * 512:(_f + 1) * 512], B[6][:, 0:512], ALU.mult),
                [b2t, geltoks[f], wts_done, fw.get("hb")]))
        fw["hrowread"] = hcasts[-1]
        ztoks = []
        for cjj in range(NC):
            mmt = None
            for quar in range(4):
                qb = quar % 2
                wtok = P.sd(lambda s, _c=cjj, _q=quar, _qb=qb: s.dma_start(
                    out=AWb[:, _qb * 1024:(_qb + 1) * 1024].rearrange(
                        "p (a m) -> p a m", a=8),
                    in_=WTD.rearrange("(a p) m -> p a m", p=128)[
                        :, 8 * _q:8 * (_q + 1),
                        _c * 128:(_c + 1) * 128]),
                    [wts_done, fw.get(("wdread", qb)),
                     fw.get(("wu", 0)), fw.get(("wu", 1))], ch=f"wd{qb}")
                for f8 in range(8):
                    f = quar * 8 + f8
                    mmt = P.t(lambda t, _c=cjj, _f=f, _f8=f8, _qb=qb:
                              t.matmul(
                        B[2 + _c % 2][:, 0:512],
                        lhsT=AWb[:, _qb * 1024 + _f8 * 128:
                                 _qb * 1024 + (_f8 + 1) * 128],
                        rhs=SCb[:, _f * 512:(_f + 1) * 512],
                        start=(_f == 0), stop=(_f == NFF - 1)),
                        [wtok, hcasts[f], fw.get(("dn", cjj % 2))])
                fw[("wdread", qb)] = mmt
            zt = P.v(lambda v, _c=cjj, _i=i: v.tensor_tensor(
                SA[:, _c * n + _i * 512:_c * n + _i * 512 + 512],
                B[2 + _c % 2][:, 0:512],
                SA[:, _c * n + _i * 512:_c * n + _i * 512 + 512], ALU.add),
                [mmt, fw.get(("zread", cjj))])
            fw[("dn", cjj % 2)] = zt
            ztoks.append(zt)
        fw["hb"] = ztoks[-1]
        fw["xb"] = ztoks[-1]
        fw["sq"] = ztoks[-1]
        outt = emit_ln(P, n, i, 512, SA, SA, s_m, s_r, s_tmp, s_trow,
                       s_onec, s_oner, B[4][0:1, :], B[5][0:1, :], B[6],
                       B[0], ztoks, fw, s_eps=s_eps, s_onecb=s_onecb)
        fw["rms"] = fw["s1"]
        fw["h2"] = fw["s2"]
        fw["bc2"] = fw["bc"]
        fw[("up", 0)] = fw["bc"]
        for cjj in range(NC):
            fw[("zread", cjj)] = outt[-1]
        x2toks[i] = outt
    p4done = x2toks[nsub - 1][-1]
    snap2 = None
    if dump:
        snap2 = P.sd(lambda s: s.dma_start(
            out=XD2.rearrange("(a p) m -> p a m", p=128),
            in_=SA[:].rearrange("p (a m) -> p a m", a=NC)),
            [t for i_ in range(nsub) for t in x2toks[i_]], ch="dbg")
        p4done = snap2

    for wi, (coff, dst, sbias) in enumerate(((0, 0, s_bq), (D, KOFF, s_bk))):
        for half in range(2):
            wln = P.gd(lambda g, _c=coff, _h=half: g.dma_start(
                out=AWb[:, 0:NC * 512].rearrange("p (a m) -> p a m", a=NC),
                in_=WQKV.rearrange("(a p) m -> p a m", p=128)[
                    :, :, _c + _h * 512:_c + (_h + 1) * 512]),
                [war.get("awfree"), p4done], ch="wqk")
            for i in range(nsub):
                casts = []
                for j in range(NC):
                    off = j * n + i * 512
                    casts.append(P.v(lambda v, _o=off, _j=j: v.tensor_copy(
                        SEb[:, _j * 512:(_j + 1) * 512], SA[:, _o:_o + 512]),
                        [x2toks[i][j], p4done, war.get("x2b")]))
                for hq in range(4):
                    hp = half * 4 + hq
                    mmt = None
                    for j in range(NC):
                        mmt = P.t(lambda t, _j=j, _hq=hq: t.matmul(
                            B[_hq % 2][:, 0:512],
                            lhsT=AWb[:, _j * 512 + _hq * 128:
                                     _j * 512 + _hq * 128 + 128],
                            rhs=SEb[:, _j * 512:(_j + 1) * 512],
                            start=(_j == 0), stop=(_j == NC - 1)),
                            [wln, casts[j], war.get(("pqk", hq % 2))])
                    ev = P.s(lambda s, _i=i, _hp=hp, _hq=hq, _d=dst,
                             _sb=sbias: s.activation(
                        SBb[:, _d + _hp * n + _i * 512:
                            _d + _hp * n + _i * 512 + 512],
                        B[_hq % 2][:, 0:512], AF.Identity,
                        bias=_sb[:, _hp:_hp + 1]), [mmt, lall, p4done])
                    war[("pqk", hq % 2)] = ev
                    war[("qk", wi, hp)] = ev
                war["x2b"] = war[("pqk", 1)]
            war["awfree"] = war[("pqk", 1)]
    for half in range(2):
        wlv = P.gd(lambda g, _h=half: g.dma_start(
            out=AWb[:, 0:NC * 512].rearrange("p (a m) -> p a m", a=NC),
            in_=WQKV.rearrange("(a p) m -> p a m", p=128)[
                :, :, 2 * D + _h * 512:2 * D + (_h + 1) * 512]),
            [war["awfree"]], ch="wv")
        for i in range(nsub):
            casts = []
            for j in range(NC):
                off = j * n + i * 512
                casts.append(P.v(lambda v, _o=off, _j=j: v.tensor_copy(
                    SEb[:, _j * 512:(_j + 1) * 512], SA[:, _o:_o + 512]),
                    [x2toks[i][j], war.get("x2b")]))
            for r in range(4):
                tt = i * 4 + r
                mmt = None
                for j in range(NC):
                    mmt = P.t(lambda t, _j=j, _r=r: t.matmul(
                        B[2][:, 0:512],
                        lhsT=SEb[:, _j * 512 + _r * 128:
                                 _j * 512 + _r * 128 + 128],
                        rhs=AWb[:, _j * 512:(_j + 1) * 512],
                        start=(_j == 0), stop=False),
                        [wlv, casts[j], war.get("pv")])
                mmt = P.t(lambda t, _h=half: t.matmul(
                    B[2][:, 0:512], lhsT=s_oner[:],
                    rhs=s_bv[0:1, _h * 512:(_h + 1) * 512],
                    start=False, stop=True), [mmt, m1, lbv])
                vx = P.v(lambda v, _tt=tt, _h=half: v.tensor_copy(
                    SCb[:, _tt * 1024 + _h * 512:
                        _tt * 1024 + (_h + 1) * 512],
                    B[2][:, 0:512]), [mmt, war.get("vxw")])
                war["pv"] = vx
                war["vxw"] = vx
                war[("vx", tt, half)] = vx
            war["x2b"] = war["pv"]
        war["awfree"] = war["pv"]
    for h in range(HEADS):
        hp, hb = h // 2, (h % 2) * 64
        vhalf, hv = h // 8, h % 8
        kv0 = kv1 = None
        for tt in range(ntile):
            rr = tt * 128
            xp = P.t(lambda t, _hp=hp, _hb=hb, _rr=rr, _tt=tt, _h=h:
                     t.matmul(
                B[4 + _tt % 2][:, 0:256],
                lhsT=SBb[_hb:_hb + 64,
                         KOFF + _hp * n + _rr:KOFF + _hp * n + _rr + 128],
                rhs=s_pj[_hb:_hb + 64, _h * 256:(_h + 1) * 256],
                start=True, stop=True),
                [war[("qk", 1, hp)], pj, war.get(("xpk", tt % 2))])
            rm = P.v(lambda v, _tt=tt: v.reduce_max(
                s_nmx[:], B[4 + _tt % 2][:, 0:256], axis=AX.X, negate=True),
                [xp, war.get("nmxread")])
            ex = P.s(lambda s, _tt=tt: s.activation(
                s_exp[:], B[4 + _tt % 2][:, 0:256], AF.Exp, bias=s_nmx[:]),
                [rm, war.get("exread")])
            war[("xpk", tt % 2)] = ex
            war["nmxread"] = ex
            ph = P.v(lambda v: v.tensor_scalar(
                s_phik[:], s_exp[:], 1.0 / 16.0, PHI_EPS / 16.0,
                op0=ALU.mult, op1=ALU.add), [ex, war.get("phikread")])
            war["exread"] = ph
            vsl = SCb[:, tt * 1024 + vhalf * 512 + hv * 64:
                      tt * 1024 + vhalf * 512 + hv * 64 + 64]
            kv0 = P.t(lambda t, _tt=tt, _vsl=vsl: t.matmul(
                B[6][:, 0:64], lhsT=s_phik[:, 0:128], rhs=_vsl,
                start=(_tt == 0), stop=False),
                [ph, war[("vx", tt, vhalf)], war.get("kv0ev")])
            kv0 = P.t(lambda t, _tt=tt: t.matmul(
                B[6][:, 64:65], lhsT=s_phik[:, 0:128], rhs=s_onecb[:],
                start=False, stop=(_tt == ntile - 1)), [kv0, m2b])
            kv1 = P.t(lambda t, _tt=tt, _vsl=vsl: t.matmul(
                B[3][:, 0:64], lhsT=s_phik[:, 128:256], rhs=_vsl,
                start=(_tt == 0), stop=False),
                [ph, war[("vx", tt, vhalf)], war.get("kv1ev")])
            kv1 = P.t(lambda t, _tt=tt: t.matmul(
                B[3][:, 64:65], lhsT=s_phik[:, 128:256], rhs=s_onecb[:],
                start=False, stop=(_tt == ntile - 1)), [kv1, m2b])
            war["phikread"] = kv1
        e1 = P.v(lambda v, _h=h: v.tensor_copy(
            SD[:, _h * 65:(_h + 1) * 65], B[6][:, 0:65]),
            [kv0, kv1, (p4done if h == 0 else None), war.get("sdfree")])
        e2 = P.v(lambda v, _h=h: v.tensor_copy(
            SD[:, 1040 + _h * 65:1040 + (_h + 1) * 65], B[3][:, 0:65]),
            [e1])
        war["kv0ev"] = e1
        war["kv1ev"] = e2
        war[("kvh", h)] = e2
    dkv = P.gd(lambda gg: gg.dma_start(out=kv_in, in_=SD[:, 0:2080]),
               [war[("kvh", hh)] for hh in range(HEADS)], ch="kvio")
    cc3 = P.col(lambda gg: gg.collective_compute(
        "AllReduce", ALU.add, replica_groups=[[0, 1, 2, 3], [4, 5, 6, 7]],
        ins=[kv_in.opt()], outs=[kv_out.opt()]), [P.total("gpd", "kvio")])
    P.gd(lambda gg: gg.dma_start(out=SD[:, 0:2080], in_=kv_out),
         [cc3], ch="kvb")
    lkvb = P.total("gpd", "kvb")

    pw = {"eps_dep": m4, "sq": fw.get("sq"), "trow": fw.get("trow")}
    war["pTread"] = war["x2b"]
    x3toks = {}
    for i in range(nsub):
        ndtoks = []
        for h in range(HEADS):
            hp, hb = h // 2, (h % 2) * 64
            kc = P.v(lambda v, _h=h: v.tensor_copy(
                s_kvs[:, 0:65], SD[:, _h * 65:(_h + 1) * 65]),
                [lkvb, war.get("kvsread")])
            kc2 = P.v(lambda v, _h=h: v.tensor_copy(
                s_kvs[:, 65:130],
                SD[:, 1040 + _h * 65:1040 + (_h + 1) * 65]), [kc])
            trtoks = []
            for r in range(4):
                n0 = i * 512 + r * 128
                xq = P.t(lambda t, _hp=hp, _hb=hb, _n0=n0, _r=r, _h=h:
                         t.matmul(
                    B[4 + _r % 2][:, 0:256],
                    lhsT=SBb[_hb:_hb + 64,
                             _hp * n + _n0:_hp * n + _n0 + 128],
                    rhs=s_pj[_hb:_hb + 64, _h * 256:(_h + 1) * 256],
                    start=True, stop=True),
                    [pj, war[("qk", 0, hp)], war.get(("xpk", r % 2))])
                rm = P.v(lambda v, _r=r: v.reduce_max(
                    s_nmx[:], B[4 + _r % 2][:, 0:256], axis=AX.X,
                    negate=True), [xq, war.get("nmxread")])
                ex = P.s(lambda s, _r=r: s.activation(
                    s_exp[:], B[4 + _r % 2][:, 0:256], AF.Exp,
                    bias=s_nmx[:]), [rm, war.get("exread")])
                war[("xpk", r % 2)] = ex
                war["nmxread"] = ex
                phq = P.v(lambda v: v.tensor_scalar(
                    s_phik[:], s_exp[:], 1.0 / 16.0, PHI_EPS / 16.0,
                    op0=ALU.mult, op1=ALU.add), [ex, war.get("phikread")])
                war["exread"] = phq
                t1 = P.t(lambda t: t.transpose(
                    B7[:, 0:128], s_phik[:, 0:128], s_idb[:]),
                    [phq, lall, war.get("trread")])
                t2 = P.t(lambda t: t.transpose(
                    B7[:, 128:256], s_phik[:, 128:256], s_idb[:]), [t1])
                c1t = P.v(lambda v, _r=r: v.tensor_copy(
                    s_phiqT[0][:, _r * 128:(_r + 1) * 128], B7[:, 0:128]),
                    [t2, war.get("pTread")])
                c2t = P.v(lambda v, _r=r: v.tensor_copy(
                    s_phiqT[1][:, _r * 128:(_r + 1) * 128], B7[:, 128:256]),
                    [c1t])
                war["phikread"] = c2t
                war["trread"] = c2t
                trtoks.append(c2t)
            nm = P.t(lambda t: t.matmul(
                B[0][0:65, 0:512], lhsT=s_kvs[:, 0:65],
                rhs=s_phiqT[0][:], start=True, stop=False),
                trtoks + [kc2, war.get("numread")])
            nm = P.t(lambda t: t.matmul(
                B[0][0:65, 0:512], lhsT=s_kvs[:, 65:130],
                rhs=s_phiqT[1][:], start=False, stop=True), [nm])
            war["pTread"] = nm
            war["kvsread"] = nm
            d1 = P.s(lambda s: s.activation(
                s_den[:], B[0][64:65, 0:512], AF.Ln, bias=s_epsphi[:]),
                [nm, m4c, war.get("denread")])
            d2 = P.s(lambda s: s.activation(
                s_den[:], s_den[:], AF.Exp, scale=-1.0), [d1])
            db = P.t(lambda t: t.matmul(
                B[1][:, 0:512], lhsT=s_oner[:], rhs=s_den[:],
                start=True, stop=True), [d2, m1, war.get("dbcread")])
            dc = P.v(lambda v: v.tensor_copy(s_dbc[:], B[1][:, 0:512]),
                     [db])
            war["denread"] = dc
            nd = P.v(lambda v, _hp=hp, _hb=hb, _i=i: v.tensor_tensor(
                SCb[_hb:_hb + 64,
                    _hp * n + _i * 512:_hp * n + _i * 512 + 512],
                B[0][0:64, 0:512], s_dbc[_hb:_hb + 64, :], ALU.mult),
                [dc, nm, war.get("ndw")])
            war["numread"] = nd
            war["dbcread"] = nd
            ndtoks.append(nd)
        ztoks = []
        for whalf in range(2):
            wlo = P.gd(lambda g, _h=whalf: g.dma_start(
                out=AWb[:, 0:NC * 512].rearrange("p (a m) -> p a m", a=NC),
                in_=WO.rearrange("(a p) m -> p a m", p=128)[
                    :, :, _h * 512:(_h + 1) * 512]),
                [war.get("awfree"), pw.get("wofree")], ch="wo")
            for cj4 in range(4):
                cj = whalf * 4 + cj4
                mmt = None
                for j in range(NC):
                    mmt = P.t(lambda t, _c4=cj4, _j=j, _i=i: t.matmul(
                        B[2 + _c4 % 2][:, 0:512],
                        lhsT=AWb[:, _j * 512 + _c4 * 128:
                                 _j * 512 + (_c4 + 1) * 128],
                        rhs=SCb[:, _j * n + _i * 512:
                                _j * n + _i * 512 + 512],
                        start=(_j == 0), stop=(_j == NC - 1)),
                        ndtoks + [wlo, pw.get(("o", cj4 % 2))])
                pw["wofree"] = mmt
                ev = P.s(lambda s, _c=cj, _c4=cj4: s.activation(
                    s_tmp[:], B[2 + _c4 % 2][:, 0:512], AF.Identity,
                    bias=s_bo[:, _c:_c + 1]), [mmt, lall, pw.get("tread")])
                zt = P.v(lambda v, _c=cj, _i=i: v.tensor_tensor(
                    SA[:, _c * n + _i * 512:_c * n + _i * 512 + 512],
                    s_tmp[:],
                    SA[:, _c * n + _i * 512:_c * n + _i * 512 + 512],
                    ALU.add), [ev, pw.get(("zr", cj)), snap2])
                pw[("o", cj4 % 2)] = zt
                pw["tread"] = zt
                ztoks.append(zt)
        war["awfree"] = pw["wofree"]
        pw["sq"] = ztoks[-1]
        outt = emit_ln(P, n, i, 512, SA, SA, s_m, s_r, s_tmp, s_trow,
                       s_onec, s_oner, B[4][0:1, :], B[5][0:1, :], B[6],
                       B[1], ztoks, pw, s_eps=s_eps, s_onecb=s_onecb)
        for cj in range(NC):
            pw[("zr", cj)] = outt[-1]
        war[("xpk", 0)] = outt[-1]
        war[("xpk", 1)] = outt[-1]
        if i + 1 < nsub:
            war["ndw"] = outt[-1]
        x3toks[i] = outt

    x3done = x3toks[nsub - 1][-1]
    snap3 = None
    if dump:
        snap3 = P.sd(lambda s: s.dma_start(
            out=XD3.rearrange("(a p) m -> p a m", p=128),
            in_=SA[:].rearrange("p (a m) -> p a m", a=NC)),
            [t for i_ in range(nsub) for t in x3toks[i_]], ch="dbg")
        x3done = snap3
    P.gd(lambda g: g.dma_start(
        out=s_winb[:].rearrange("p (a m) -> p a m", a=NC),
        in_=WIN.rearrange("(a p) m -> p a m", p=128)), [x3done], ch="wsm")
    P.gd(lambda g: g.dma_start(out=s_woutb[:], in_=WOUT), [x3done],
         ch="wsm")
    lwsm = P.total("gpd", "wsm")
    dw = {"eps_dep": m4, "sq": pw.get("sq"), "trow": pw.get("trow")}
    stoks = []
    for i in range(nsub):
        casts = []
        for j in range(NC):
            off = j * n + i * 512
            casts.append(P.v(lambda v, _o=off, _j=j: v.tensor_copy(
                SEb[:, _j * 512:(_j + 1) * 512], SA[:, _o:_o + 512]),
                [x3toks[i][j], dw.get("x3b")]))
        mmt = None
        for j in range(NC):
            mmt = P.t(lambda t, _j=j: t.matmul(
                B[6][0:64, 0:512], lhsT=s_winb[:, _j * 64:(_j + 1) * 64],
                rhs=SEb[:, _j * 512:(_j + 1) * 512],
                start=(_j == 0), stop=(_j == NC - 1)),
                [lwsm, casts[j], dw.get("psread")])
        sv = P.s(lambda s: s.activation(
            s_s[:], B[6][0:64, 0:512], AF.Identity, bias=s_bin[:]),
            [mmt, lall, dw.get("sread")])
        dw["psread"] = sv
        gt = None
        for ii in range(16):
            hi = P.s(lambda s, _ii=ii: s.activation(
                s_hi[:], s_s[:], AF.Relu,
                scale=s_w1[:, _ii:_ii + 1], bias=s_b1[:, _ii:_ii + 1]),
                [sv, lall, dw.get("hiread"), dw.get("hi_war"), lkvb])
            if ii == 0:
                gt = P.v(lambda v: v.tensor_scalar(
                    s_g[:], s_hi[:], s_w2[:, 0:1], None, op0=ALU.mult),
                    [hi, dw.get("gread")])
            else:
                gt = P.v(lambda v, _ii=ii: v.scalar_tensor_tensor(
                    s_g[:], s_hi[:], s_w2[:, _ii:_ii + 1], s_g[:],
                    op0=ALU.mult, op1=ALU.add), [hi, gt])
            dw["hiread"] = gt
        dw["sread"] = gt
        g2 = P.v(lambda v: v.tensor_scalar(
            s_g[:], s_g[:], s_b2[:], None, op0=ALU.add), [gt])
        gb = P.v(lambda v: v.tensor_copy(s_gb[:], s_g[:]),
                 [g2, dw.get("gbread")])
        dw["gread"] = gb
        ztoks = []
        for cj in range(NC):
            mmt = P.t(lambda t, _c=cj: t.matmul(
                B[2 + _c % 2][:, 0:512],
                lhsT=s_woutb[:, _c * 128:(_c + 1) * 128],
                rhs=s_gb[:], start=True, stop=True),
                [gb, lwsm, dw.get(("o", cj % 2))])
            ev = P.s(lambda s, _c=cj: s.activation(
                s_tmp[:], B[2 + _c % 2][:, 0:512], AF.Identity,
                bias=s_bout[:, _c:_c + 1]), [mmt, dw.get("tread")])
            zt = P.v(lambda v, _c=cj, _i=i: v.tensor_tensor(
                SA[:, _c * n + _i * 512:_c * n + _i * 512 + 512], s_tmp[:],
                SA[:, _c * n + _i * 512:_c * n + _i * 512 + 512], ALU.add),
                [ev, dw.get(("zr", cj)), gb, snap3])
            dw[("o", cj % 2)] = zt
            dw["tread"] = zt
            ztoks.append(zt)
        dw["gbread"] = ztoks[-1]
        dw["x3b"] = ztoks[-1]
        dw["sq"] = ztoks[-1]
        outt = emit_ln(P, n, i, 512, SA, SA, s_m, s_r, s_tmp, s_trow,
                       s_onec, s_oner, B[4][0:1, :], B[5][0:1, :], B[0],
                       B[1], ztoks, dw, s_eps=s_eps, s_onecb=s_onecb)
        stok = P.sd(lambda s, _i=i: s.dma_start(
            out=fmview(YT, n)[:, :, _i * 512:_i * 512 + 512],
            in_=SA[:].rearrange("p (a m) -> p a m", a=NC)[
                :, :, _i * 512:_i * 512 + 512]), outt, ch="st")
        for cj in range(NC):
            dw[("zr", cj)] = stok
        dw["hi_war"] = stok
        stoks.append(stok)
    P.finish(stoks)
    P.build()
    return nc


_FUSED_NC = None


def kernel(L0_attn_x, L0_attn_router_w, L0_attn_router_b, L0_attn_wq,
           L0_attn_wk, L0_attn_wv, L0_ffn_w_up, L0_ffn_w_down, L1_attn_w_qkv,
           L1_attn_b_qkv, L1_attn_w_out, L1_attn_b_out, L1_attn_proj_matrix,
           L1_ffn_Win, L1_ffn_bin_, L1_ffn_W1, L1_ffn_b1, L1_ffn_W2,
           L1_ffn_b2, L1_ffn_Wout, L1_ffn_bout):
    _ = (L0_attn_wq.shape, L0_attn_wk.shape, L0_attn_wv.shape)
    global _FUSED_NC
    if _FUSED_NC is None:
        _FUSED_NC = build_fused(NTOK)
    nc = _FUSED_NC
    x = np.ascontiguousarray(L0_attn_x, dtype=np.float32)
    identb = np.eye(128, dtype=np.float32).astype(ml_dtypes.bfloat16)
    bqkv = L1_attn_b_qkv
    shared = {
        "rwc": np.ascontiguousarray(L0_attn_router_w.reshape(NC, 128).T),
        "rb": np.array([[L0_attn_router_b[0]]], dtype=np.float32),
        "identb": identb,
        "ident": np.eye(128, dtype=np.float32),
        "wupt": np.ascontiguousarray(L0_ffn_w_up.T),
        "wdnt": np.ascontiguousarray(L0_ffn_w_down.T),
        "wqkv": np.ascontiguousarray(L1_attn_w_qkv),
        "bq": np.ascontiguousarray(bqkv[0:D].reshape(NC, 128).T),
        "bk": np.ascontiguousarray(bqkv[D:2 * D].reshape(NC, 128).T),
        "bvrow": np.ascontiguousarray(bqkv[2 * D:3 * D][None, :]),
        "projt": np.ascontiguousarray(np.concatenate(
            [L1_attn_proj_matrix[h].T for h in range(HEADS)], axis=1)),
        "wo": np.ascontiguousarray(L1_attn_w_out),
        "bo": np.ascontiguousarray(L1_attn_b_out.reshape(NC, 128).T),
        "win": np.ascontiguousarray(L1_ffn_Win),
        "bin": np.ascontiguousarray(L1_ffn_bin_[:, None]),
        "w1": np.ascontiguousarray(L1_ffn_W1),
        "b1": np.ascontiguousarray(L1_ffn_b1),
        "w2": np.ascontiguousarray(L1_ffn_W2),
        "b2": np.ascontiguousarray(L1_ffn_b2[:, None]),
        "wout": np.ascontiguousarray(L1_ffn_Wout),
        "bout": np.ascontiguousarray(L1_ffn_bout.reshape(NC, 128).T),
    }
    maps = []
    for core in range(8):
        b, o = core // 4, (core % 4) * NTOK
        m = dict(shared)
        m["xt"] = np.ascontiguousarray(x[b, o:o + NTOK].T)
        m["xrevt"] = np.ascontiguousarray(x[b][::-1][o:o + NTOK].T)
        maps.append(m)
    res = run_bass_kernel_spmd(nc, maps, core_ids=list(range(8)))
    out = np.zeros((2, NSEQ, D), np.float32)
    for core in range(8):
        b, o = core // 4, (core % 4) * NTOK
        out[b, o:o + NTOK] = res.results[core]["yt"].T
    return out


# version 10
# speedup vs baseline: 1.1518x; 1.0402x over previous
import numpy as np
import ml_dtypes
from contextlib import ExitStack

import concourse.bass as bass
import concourse.mybir as mybir
from concourse.bass_utils import run_bass_kernel_spmd

F32 = mybir.dt.float32
BF16 = mybir.dt.bfloat16
I32 = mybir.dt.int32
AF = mybir.ActivationFunctionType
ALU = mybir.AluOpType
AX = mybir.AxisListType

D = 1024
NC = 8
DFF = 4096
NFF = 32
NTOK = 2048
NSEQ = 8192
HEADS = 16
HDIM = 64
NBF = 256
LN_EPS = 1e-5
RMS_EPS = 1e-8
PHI_EPS = 1e-6
DATA_NORM = float(np.float32(64.0 ** -0.25))
NITER_BISECT = 40
ROUTER_K = 1024.0


class Prog:
    SECTIONS = {
        "tensor": ("pe", 1),
        "vector": ("ve", 1),
        "scalar": ("sc", 1),
        "gpsimd": ("gp", 1),
    }

    def __init__(self, nc):
        self.nc = nc
        self.es = ExitStack()
        self.items = {s: [] for s in
                      ("tensor", "vector", "scalar", "gpsimd", "sync")}
        self.counts = {}
        self.n_cc = 0
        self.final_deps = []

    def _add(self, section, sem, inc, fn, deps):
        n = self.counts.get(sem, 0) + inc
        self.counts[sem] = n
        self.items[section].append((fn, [d for d in deps if d is not None],
                                    sem, inc))
        return (sem, n)

    def t(self, fn, deps=()):
        return self._add("tensor", "pe", 1, fn, deps)

    def v(self, fn, deps=()):
        return self._add("vector", "ve", 1, fn, deps)

    def s(self, fn, deps=()):
        return self._add("scalar", "sc", 1, fn, deps)

    def g(self, fn, deps=()):
        return self._add("gpsimd", "gp", 1, fn, deps)

    def sd(self, fn, deps=(), ch="ld"):
        return self._add("sync", f"syd_{ch}", 16, fn, deps)

    def ad(self, fn, deps=(), ch="ld"):
        return self._add("scalar", f"scd_{ch}", 16, fn, deps)

    def gd(self, fn, deps=(), ch="ld"):
        return self._add("gpsimd", f"gpd_{ch}", 16, fn, deps)

    def total(self, queue, ch):
        name = f"{queue}_{ch}"
        return (name, self.counts.get(name, 0))

    def col(self, fn, deps=()):
        self.n_cc += 1
        sem = f"cc{self.n_cc}"
        return self._add("gpsimd", sem, 1, fn, deps)

    def finish(self, deps):
        self.final_deps = list(deps)

    def build(self):
        nc = self.nc
        sems = {}
        for name in self.counts:
            sems[name] = self.es.enter_context(nc.semaphore(name))
        block = self.es.enter_context(nc.Block())
        hw = {}

        def coalesce(deps):
            best = {}
            for dsem, dval in deps:
                if best.get(dsem, 0) < dval:
                    best[dsem] = dval
            return best.items()

        def emit(section, eng):
            for fn, deps, sem, inc in self.items[section]:
                for dsem, dval in coalesce(deps):
                    key = (section, dsem)
                    if hw.get(key, 0) < dval:
                        hw[key] = dval
                        eng.wait_ge(sems[dsem], dval)
                ins = fn(eng)
                ins.then_inc(sems[sem], inc)
            if section == "sync":
                for dsem, dval in coalesce(self.final_deps):
                    key = (section, dsem)
                    if hw.get(key, 0) < dval:
                        hw[key] = dval
                        eng.wait_ge(sems[dsem], dval)

        if self.items["tensor"]:
            block.tensor(lambda t: emit("tensor", t))
        if self.items["vector"]:
            block.vector(lambda v: emit("vector", v))
        if self.items["scalar"]:
            block.scalar(lambda s: emit("scalar", s))
        if self.items["gpsimd"]:
            block.gpsimd(lambda g: emit("gpsimd", g))
        block.sync(lambda s: emit("sync", s))
        return self


def sb(nc, es, name, shape, dt=F32):
    return es.enter_context(nc.sbuf_tensor(name, shape, dt))


def ps(nc, es, name, shape, dt=F32):
    return es.enter_context(nc.psum_tensor(name, shape, dt))


def emit_bisect(P, s_rwf, s_rwo, s_ones2, s_oner, p_cnt, p_bc, st,
                dep_rwf, dep_rwo, dep_ident, dep_oner, dep_ones2,
                k=ROUTER_K, niter=NITER_BISECT):
    rmax, rmin = st.get("rmax"), st.get("rmin")
    t_lo, t_hi, t_mid = st["t_lo"], st["t_hi"], st["t_mid"]
    fs, fsn, rowcnt = st["fs"], st["fsn"], st["rowcnt"]
    mask64, maskrow = st["mask64"], st["maskrow"]

    if st.get("fixed_bounds"):
        c1 = P.v(lambda v: v.memset(t_hi[:], 1e4), [dep_rwf])
        c2 = P.v(lambda v: v.memset(t_lo[:], -1e4), [c1])
        last = P.v(lambda v: v.memset(t_mid[:], 0.0), [c2])
    else:
        ident = st["ident"]
        a = P.v(lambda v: v.reduce_max(rmax[:], s_rwf[:], axis=AX.X),
                [dep_rwf])
        b = P.v(lambda v: v.tensor_reduce(rmin[:], s_rwf[:], axis=AX.X,
                                          op=ALU.min), [dep_rwf, a])
        tr1 = P.t(lambda t: t.transpose(p_bc[0:1, 0:128], rmax[:],
                                        ident[:]), [a, dep_ident])
        tr2 = P.t(lambda t: t.transpose(p_bc[0:1, 128:256], rmin[:],
                                        ident[:]), [b])
        gh = P.v(lambda v: v.tensor_reduce(st["g_hi"][:], p_bc[0:1, 0:128],
                                           axis=AX.X, op=ALU.max),
                 [tr1, tr2])
        gl = P.v(lambda v: v.tensor_reduce(st["g_lo"][:],
                                           p_bc[0:1, 128:256],
                                           axis=AX.X, op=ALU.min),
                 [tr2, gh])
        bh = P.t(lambda t: t.matmul(p_bc[:, 256:257], lhsT=s_oner[:],
                                    rhs=st["g_hi"][:], start=True,
                                    stop=True), [gh, dep_oner])
        bl = P.t(lambda t: t.matmul(p_bc[:, 257:258], lhsT=s_oner[:],
                                    rhs=st["g_lo"][:], start=True,
                                    stop=True), [gl])
        c1 = P.v(lambda v: v.tensor_copy(t_hi[:], p_bc[:, 256:257]),
                 [bh, bl])
        c2 = P.v(lambda v: v.tensor_copy(t_lo[:], p_bc[:, 257:258]),
                 [bl, c1])
        c3 = P.v(lambda v: v.tensor_tensor(t_mid[:], t_lo[:], t_hi[:],
                                           ALU.add), [c2])
        last = P.v(lambda v: v.tensor_scalar(t_mid[:], t_mid[:], 0.5, None,
                                             op0=ALU.mult), [c3])
    prev_f1 = None
    for i in range(niter):
        slot = i % 2
        mk = P.v(lambda v: v.tensor_scalar(
            mask64[:], s_rwf[:], t_mid[:], 0.0, op0=ALU.is_ge, op1=ALU.add,
            accum_out=rowcnt[:]), [last, prev_f1])
        cm = P.t(lambda t, _s=slot: t.matmul(
            p_cnt[:, _s:_s + 1], lhsT=s_ones2[:], rhs=rowcnt[:],
            start=True, stop=True), [mk, dep_ones2, prev_f1])
        f1 = P.v(lambda v, _s=slot: v.tensor_scalar(
            fs[:], p_cnt[:, _s:_s + 1], k, None, op0=ALU.is_ge), [cm])
        f2 = P.v(lambda v: v.tensor_scalar(fsn[:], fs[:], -1, 1,
                                           op0=ALU.mult, op1=ALU.add), [f1])
        s1 = P.v(lambda v: v.copy_predicated(t_lo[:], fs[:], t_mid[:]), [f2])
        s2 = P.v(lambda v: v.copy_predicated(t_hi[:], fsn[:], t_mid[:]), [s1])
        s3 = P.v(lambda v: v.tensor_tensor(t_mid[:], t_lo[:], t_hi[:],
                                           ALU.add), [s2])
        last = P.v(lambda v: v.tensor_scalar(t_mid[:], t_mid[:], 0.5, None,
                                             op0=ALU.mult), [s3])
        prev_f1 = f1
    if maskrow is None:
        return last
    mrow = P.v(lambda v: v.tensor_scalar(
        maskrow[:], s_rwo[:], t_lo[0:1, 0:1], None, op0=ALU.is_ge),
        [last, dep_rwo])
    return mrow


def emit_ln(P, n, i, w, s_y, s_out, s_m, s_r, s_tmp, s_trow, s_onec, s_oner,
            p_s1, p_s2, p_mbc, p_rbc, ytoks, war, s_eps=None,
            extra_out_dep=None, s_onecb=None):
    s1tok = s2tok = None
    sqprev = war.get("sq")
    tmpb = s_tmp[:].bitcast(BF16) if s_onecb is not None else None
    for j in range(NC):
        off = j * n + i * 512
        s1tok = P.t(lambda t, _o=off, _w=w, _j=j: t.matmul(
            p_s1[:, 0:_w], lhsT=s_onec[:], rhs=s_y[:, _o:_o + _w],
            start=(_j == 0), stop=(_j == NC - 1)),
            [ytoks[j], war.get("s1")])
        if s_onecb is not None:
            sq = P.s(lambda s, _o=off, _w=w: s.activation(
                tmpb[:, 0:_w], s_y[:, _o:_o + _w], AF.Square),
                [ytoks[j], sqprev])
            s2tok = P.t(lambda t, _w=w, _j=j: t.matmul(
                p_s2[:, 0:_w], lhsT=s_onecb[:], rhs=tmpb[:, 0:_w],
                start=(_j == 0), stop=(_j == NC - 1)),
                [sq, war.get("s2")])
        else:
            sq = P.s(lambda s, _o=off, _w=w: s.activation(
                s_tmp[:, 0:_w], s_y[:, _o:_o + _w], AF.Square),
                [ytoks[j], sqprev])
            s2tok = P.t(lambda t, _w=w, _j=j: t.matmul(
                p_s2[:, 0:_w], lhsT=s_onec[:], rhs=s_tmp[:, 0:_w],
                start=(_j == 0), stop=(_j == NC - 1)), [sq, war.get("s2")])
        sqprev = s2tok
    war["sq"] = sqprev
    o = i * 512 if s_m.shape[1] > 512 else 0
    f1 = P.v(lambda v, _w=w, _o=o: v.tensor_scalar(
        s_m[:, _o:_o + _w], p_s1[:, 0:_w], 1.0 / D, None, op0=ALU.mult),
        [s1tok])
    f2 = P.v(lambda v, _w=w, _o=o: v.tensor_scalar(
        s_r[:, _o:_o + _w], p_s2[:, 0:_w], 1.0 / D, None, op0=ALU.mult),
        [s2tok])
    war["s1"] = f1
    war["s2"] = f2
    f3 = P.v(lambda v, _w=w, _o=o: v.tensor_tensor(
        s_trow[:, 0:_w], s_m[:, _o:_o + _w], s_m[:, _o:_o + _w],
        ALU.mult), [f1, f2, war.get("trow")])
    f4 = P.v(lambda v, _w=w, _o=o: v.tensor_tensor(
        s_r[:, _o:_o + _w], s_r[:, _o:_o + _w], s_trow[:, 0:_w],
        ALU.subtract), [f3])
    war["trow"] = f4
    f5 = P.s(lambda s, _w=w, _o=o: s.activation(
        s_r[:, _o:_o + _w], s_r[:, _o:_o + _w], AF.Ln,
        bias=s_eps[:]), [f4, war.get("eps_dep")])
    war["eps_dep"] = None
    f6 = P.s(lambda s, _w=w, _o=o: s.activation(
        s_r[:, _o:_o + _w], s_r[:, _o:_o + _w], AF.Exp, scale=-0.5), [f5])
    b1 = P.t(lambda t, _w=w, _o=o: t.matmul(
        p_mbc[:, 0:_w], lhsT=s_oner[:], rhs=s_m[:, _o:_o + _w],
        start=True, stop=True), [f1, war.get("bc")])
    b2 = P.t(lambda t, _w=w, _o=o: t.matmul(
        p_rbc[:, 0:_w], lhsT=s_oner[:], rhs=s_r[:, _o:_o + _w],
        start=True, stop=True), [f6])
    outtoks = []
    for j in range(NC):
        off = j * n + i * 512
        n1 = P.v(lambda v, _o=off, _w=w: v.tensor_tensor(
            s_out[:, _o:_o + _w], s_y[:, _o:_o + _w], p_mbc[:, 0:_w],
            ALU.subtract), [b1, b2, ytoks[j], extra_out_dep])
        n2 = P.v(lambda v, _o=off, _w=w: v.tensor_tensor(
            s_out[:, _o:_o + _w], s_out[:, _o:_o + _w], p_rbc[:, 0:_w],
            ALU.mult), [n1])
        outtoks.append(n2)
    war["bc"] = outtoks[-1]
    return outtoks


def fmview(ap, n):
    return ap.rearrange("(a p) n -> p a n", p=128)


def build_a1(n=NTOK):
    nc = bass.Bass("TRN2", target_bir_lowering=False, debug=False)
    XT = nc.dram_tensor("xt", [D, n], F32, kind="ExternalInput").ap()
    RWC = nc.dram_tensor("rwc", [128, NC], F32, kind="ExternalInput").ap()
    RB = nc.dram_tensor("rb", [1, 1], F32, kind="ExternalInput").ap()
    RWS = nc.dram_tensor("rws", [1, n], F32, kind="ExternalOutput").ap()

    P = Prog(nc)
    es = P.es
    nsub = (n + 511) // 512
    s_x = sb(nc, es, "s_x", [128, NC * n])
    s_rwc = sb(nc, es, "s_rwc", [128, NC])
    s_rb = sb(nc, es, "s_rb", [1, 1])
    s_rw = sb(nc, es, "s_rw", [1, n])
    p_rw = [ps(nc, es, f"p_rw{i}", [1, 512]) for i in range(2)]

    P.sd(lambda s: s.dma_start(
        out=s_x[:].rearrange("p (a n) -> p a n", a=NC), in_=fmview(XT, n)))
    P.sd(lambda s: s.dma_start(out=s_rwc[:], in_=RWC))
    P.sd(lambda s: s.dma_start(out=s_rb[:], in_=RB))
    lall = P.total("syd", "ld")
    lx = lw = lb = lall

    evs = []
    ev_prev = [None, None]
    for i in range(nsub):
        w = min(512, n - i * 512)
        pb = p_rw[i % 2]
        mmtok = None
        for j in range(NC):
            mmtok = P.t(lambda t, _i=i, _j=j, _w=w, _pb=pb: t.matmul(
                _pb[:, 0:_w], lhsT=s_rwc[:, _j:_j + 1],
                rhs=s_x[:, _j * n + _i * 512:_j * n + _i * 512 + _w],
                start=(_j == 0), stop=(_j == NC - 1)),
                [lx, lw, ev_prev[i % 2]])
        ev = P.v(lambda v, _i=i, _w=w, _pb=pb: v.tensor_scalar(
            s_rw[:, _i * 512:_i * 512 + _w], _pb[:, 0:_w], s_rb[:], None,
            op0=ALU.add), [mmtok, lb])
        ev_prev[i % 2] = ev
        evs.append(ev)
    stok = P.sd(lambda s: s.dma_start(out=RWS, in_=s_rw[:]), evs, ch="st")
    P.finish([stok])
    return P.build().nc


def build_a2(n=NTOK, niter=NITER_BISECT):
    nc = bass.Bass("TRN2", target_bir_lowering=False, debug=False)
    XT = nc.dram_tensor("xt", [D, n], F32, kind="ExternalInput").ap()
    XRT = nc.dram_tensor("xrevt", [D, n], F32, kind="ExternalInput").ap()
    RWF = nc.dram_tensor("rwf", [128, 64], F32, kind="ExternalInput").ap()
    RWO = nc.dram_tensor("rwo", [1, n], F32, kind="ExternalInput").ap()
    IDENT = nc.dram_tensor("ident", [128, 128], F32, kind="ExternalInput").ap()
    X1T = nc.dram_tensor("x1t", [D, n], F32, kind="ExternalOutput").ap()

    P = Prog(nc)
    es = P.es
    s_x = sb(nc, es, "s_x", [128, NC * n])
    s_xr = sb(nc, es, "s_xr", [128, NC * n])
    s_y = s_xr
    s_rwf = sb(nc, es, "s_rwf", [128, 64])
    s_rwo = sb(nc, es, "s_rwo", [1, n])
    s_ones2 = sb(nc, es, "s_ones2", [128, 128])
    s_oner = sb(nc, es, "s_oner", [1, 128])
    s_onec = sb(nc, es, "s_onec", [128, 1])
    s_ident = sb(nc, es, "s_ident", [128, 128])
    st = {kk: sb(nc, es, kk, [128, 1]) for kk in
          ("rmax", "rmin", "t_lo", "t_hi", "t_mid", "rowcnt")}
    st["fs"] = sb(nc, es, "fs", [128, 1], I32)
    st["fsn"] = sb(nc, es, "fsn", [128, 1], I32)
    st["g_hi"] = sb(nc, es, "g_hi", [1, 1])
    st["g_lo"] = sb(nc, es, "g_lo", [1, 1])
    st["mask64"] = sb(nc, es, "mask64", [128, 64])
    st["maskrow"] = sb(nc, es, "maskrow", [1, n])
    st["ident"] = s_ident
    s_m = sb(nc, es, "s_m", [1, n])
    s_r = sb(nc, es, "s_r", [1, n])
    s_tmp = sb(nc, es, "s_tmp", [128, 512])
    s_trow = sb(nc, es, "s_trow", [1, 512])
    s_eps = sb(nc, es, "s_eps", [1, 1])
    s_eps8 = sb(nc, es, "s_eps8", [1, 1])
    s_epsphi = sb(nc, es, "s_epsphi", [1, 1])
    p_cnt = ps(nc, es, "p_cnt", [128, 8])
    p_bc = ps(nc, es, "p_bc", [128, 512])
    p_s1 = ps(nc, es, "p_s1", [1, 512])
    p_s2 = ps(nc, es, "p_s2", [1, 512])
    p_mbc = ps(nc, es, "p_mbc", [128, 512])
    p_rbc = ps(nc, es, "p_rbc", [128, 512])
    p_msk = ps(nc, es, "p_msk", [128, 512])

    P.sd(lambda s: s.dma_start(
        out=s_x[:].rearrange("p (a n) -> p a n", a=NC), in_=fmview(XT, n)))
    P.sd(lambda s: s.dma_start(
        out=s_xr[:].rearrange("p (a n) -> p a n", a=NC), in_=fmview(XRT, n)))
    P.sd(lambda s: s.dma_start(out=s_rwf[:], in_=RWF))
    P.sd(lambda s: s.dma_start(out=s_rwo[:], in_=RWO))
    P.sd(lambda s: s.dma_start(out=s_ident[:], in_=IDENT))
    lall = P.total("syd", "ld")
    lx = lxr = lrwf = lrwo = lid = lall
    m1 = P.v(lambda v: v.memset(s_ones2[:], 1.0))
    m2 = P.v(lambda v: v.memset(s_oner[:], 1.0))
    m3 = P.v(lambda v: v.memset(s_onec[:], 1.0))
    m4 = P.v(lambda v: v.memset(s_eps[:], LN_EPS))

    mrow = emit_bisect(P, s_rwf, s_rwo, s_ones2, s_oner, p_cnt, p_bc, st,
                       lrwf, lrwo, lid, m2, m1, niter=niter)

    nsub = (n + 511) // 512
    stoks = []
    war = {"eps_dep": m4}
    prev_msk = None
    for i in range(nsub):
        w = min(512, n - i * 512)
        mb = P.t(lambda t, _i=i, _w=w: t.matmul(
            p_msk[:, 0:_w], lhsT=s_oner[:],
            rhs=st["maskrow"][:, _i * 512:_i * 512 + _w],
            start=True, stop=True), [mrow, m2, prev_msk])
        ytoks = []
        for j in range(NC):
            off = j * n + i * 512
            ym = P.v(lambda v, _o=off, _w=w: v.tensor_tensor(
                s_y[:, _o:_o + _w], s_xr[:, _o:_o + _w], p_msk[:, 0:_w],
                ALU.mult), [mb, lxr])
            ya = P.v(lambda v, _o=off, _w=w: v.tensor_tensor(
                s_y[:, _o:_o + _w], s_y[:, _o:_o + _w], s_x[:, _o:_o + _w],
                ALU.add), [ym, lx])
            ytoks.append(ya)
        prev_msk = ytoks[-1]
        outtoks = emit_ln(P, n, i, w, s_y, s_x, s_m, s_r, s_tmp, s_trow,
                          s_onec, s_oner, p_s1, p_s2, p_mbc, p_rbc, ytoks, war,
                          s_eps=s_eps)
        stok = P.sd(lambda s, _i=i, _w=w: s.dma_start(
            out=fmview(X1T, n)[:, :, _i * 512:_i * 512 + _w],
            in_=s_x[:].rearrange("p (a n) -> p a n", a=NC)[
                :, :, _i * 512:_i * 512 + _w]), outtoks, ch="st")
        stoks.append(stok)
    P.finish(stoks)
    return P.build().nc


def build_b1():
    nc = bass.Bass("TRN2", target_bir_lowering=False, debug=False)
    WU = nc.dram_tensor("wupt_s", [D, 512], F32, kind="ExternalInput").ap()
    WD = nc.dram_tensor("wdnt_s", [DFF, 128], F32, kind="ExternalInput").ap()
    AP_ = nc.dram_tensor("aparts", [1, 2], F32, kind="ExternalOutput").ap()

    P = Prog(nc)
    es = P.es
    s_wu = sb(nc, es, "s_wu", [128, NC * 512])
    s_wd = sb(nc, es, "s_wd", [128, NFF * 128])
    s_acc = sb(nc, es, "s_acc", [128, NC + NFF])
    s_red = sb(nc, es, "s_red", [128, 2])
    s_onec = sb(nc, es, "s_onec", [128, 1])
    s_out = sb(nc, es, "s_out", [1, 2])
    p_o = ps(nc, es, "p_o", [1, 8])

    P.sd(lambda s: s.dma_start(
        out=s_wu[:].rearrange("p (a n) -> p a n", a=NC),
        in_=WU.rearrange("(a p) n -> p a n", p=128)))
    P.sd(lambda s: s.dma_start(
        out=s_wd[:].rearrange("p (a n) -> p a n", a=NFF),
        in_=WD.rearrange("(a p) n -> p a n", p=128)))
    lall = P.total("syd", "ld")
    m1 = P.v(lambda v: v.memset(s_onec[:], 1.0))

    rtoks = []
    for j in range(NC):
        rtoks.append(P.v(lambda v, _j=j: v.tensor_reduce(
            s_acc[:, _j:_j + 1], s_wu[:, _j * 512:(_j + 1) * 512],
            axis=AX.X, op=ALU.add, apply_absolute_value=True), [lall]))
    for j in range(NFF):
        rtoks.append(P.v(lambda v, _j=j: v.tensor_reduce(
            s_acc[:, NC + _j:NC + _j + 1], s_wd[:, _j * 128:(_j + 1) * 128],
            axis=AX.X, op=ALU.add, apply_absolute_value=True), [lall]))
    r1 = P.v(lambda v: v.tensor_reduce(
        s_red[:, 0:1], s_acc[:, 0:NC], axis=AX.X, op=ALU.add), rtoks)
    r2 = P.v(lambda v: v.tensor_reduce(
        s_red[:, 1:2], s_acc[:, NC:NC + NFF], axis=AX.X, op=ALU.add), rtoks)
    mm = P.t(lambda t: t.matmul(p_o[0:1, 0:2], lhsT=s_onec[:],
                                rhs=s_red[:, 0:2], start=True, stop=True),
             [r1, r2, m1])
    ev = P.v(lambda v: v.tensor_copy(s_out[:], p_o[0:1, 0:2]), [mm])
    stok = P.sd(lambda s: s.dma_start(out=AP_, in_=s_out[:]), [ev], ch="st")
    P.finish([stok])
    return P.build().nc


def build_b1b():
    nc = bass.Bass("TRN2", target_bir_lowering=False, debug=False)
    WU = nc.dram_tensor("wupt_s", [D, 512], F32, kind="ExternalInput").ap()
    WD = nc.dram_tensor("wdnt_s", [DFF, 128], F32, kind="ExternalInput").ap()
    APS = nc.dram_tensor("aparts8", [1, 16], F32, kind="ExternalInput").ap()
    WTU = nc.dram_tensor("wtu_s", [D, 512], BF16, kind="ExternalOutput").ap()
    WTD = nc.dram_tensor("wtd_s", [DFF, 128], BF16, kind="ExternalOutput").ap()

    P = Prog(nc)
    es = P.es
    s_wu = sb(nc, es, "s_wu", [128, NC * 512])
    s_wd = sb(nc, es, "s_wd", [128, NFF * 128])
    s_aps = sb(nc, es, "s_aps", [1, 16])
    s_th = sb(nc, es, "s_th", [1, 2])
    s_thc = sb(nc, es, "s_thc", [128, 2])
    s_oner = sb(nc, es, "s_oner", [1, 128])
    s_msk = sb(nc, es, "s_msk", [128, 512])
    s_sgn = sb(nc, es, "s_sgn", [128, 512])
    s_wtu = sb(nc, es, "s_wtu", [128, NC * 512], BF16)
    s_wtd = sb(nc, es, "s_wtd", [128, NFF * 128], BF16)
    p_b = ps(nc, es, "p_b", [128, 8])

    P.sd(lambda s: s.dma_start(
        out=s_wu[:].rearrange("p (a n) -> p a n", a=NC),
        in_=WU.rearrange("(a p) n -> p a n", p=128)))
    P.sd(lambda s: s.dma_start(
        out=s_wd[:].rearrange("p (a n) -> p a n", a=NFF),
        in_=WD.rearrange("(a p) n -> p a n", p=128)))
    P.sd(lambda s: s.dma_start(out=s_aps[:], in_=APS))
    lall = P.total("syd", "ld")
    m1 = P.v(lambda v: v.memset(s_oner[:], 1.0))
    a1 = P.v(lambda v: v.tensor_reduce(
        s_th[:, 0:1], s_aps[0:1, 0:16:2], axis=AX.X, op=ALU.add), [lall])
    a2 = P.v(lambda v: v.tensor_reduce(
        s_th[:, 1:2], s_aps[0:1, 1:16:2], axis=AX.X, op=ALU.add), [lall, a1])
    a3 = P.v(lambda v: v.tensor_scalar(
        s_th[:, 0:2], s_th[:, 0:2], 0.5 / (D * DFF), None, op0=ALU.mult),
        [a1, a2])
    bb = P.t(lambda t: t.matmul(p_b[:, 0:2], lhsT=s_oner[:], rhs=s_th[:, 0:2],
                                start=True, stop=True), [a3, m1])
    cc = P.v(lambda v: v.tensor_copy(s_thc[:], p_b[:, 0:2]), [bb])

    stoks = []
    prevm = prevs = None
    for which, s_w, s_wt, nch, wid, OUT in (
            (0, s_wu, s_wtu, NC, 512, WTU),
            (1, s_wd, s_wtd, NFF, 128, WTD)):
        for j in range(nch):
            sl = slice(j * wid, (j + 1) * wid)
            ab = P.s(lambda s, _sl=sl, _wd=wid, _sw=s_w: s.activation(
                s_msk[:, 0:_wd], _sw[:, _sl], AF.Abs), [lall, prevm])
            mk = P.v(lambda v, _w=which, _wd=wid: v.tensor_scalar(
                s_msk[:, 0:_wd], s_msk[:, 0:_wd], s_thc[:, _w:_w + 1], None,
                op0=ALU.is_gt), [cc, ab, prevm])
            sg = P.s(lambda s, _sl=sl, _wd=wid, _sw=s_w: s.activation(
                s_sgn[:, 0:_wd], _sw[:, _sl], AF.Sign), [lall, prevs, ab])
            wt = P.v(lambda v, _sl=sl, _wd=wid, _swt=s_wt: v.tensor_tensor(
                _swt[:, _sl], s_msk[:, 0:_wd], s_sgn[:, 0:_wd], ALU.mult),
                [mk, sg])
            prevm = prevs = wt
        stoks.append(P.sd(lambda s, _swt=s_wt, _n=nch, _wd=wid, _O=OUT:
                          s.dma_start(
            out=_O.rearrange("(a p) n -> p a n", p=128),
            in_=_swt[:].rearrange("p (a n) -> p a n", a=_n)),
            [prevm], ch="st"))
    P.finish(stoks)
    return P.build().nc


def build_b2(n=NTOK):
    nc = bass.Bass("TRN2", target_bir_lowering=False, debug=False)
    X1T = nc.dram_tensor("x1t", [D, n], F32, kind="ExternalInput").ap()
    WTU = nc.dram_tensor("wtu", [D, DFF], BF16, kind="ExternalInput").ap()
    WTD = nc.dram_tensor("wtd", [DFF, D], BF16, kind="ExternalInput").ap()
    X2T = nc.dram_tensor("x2t", [D, n], F32, kind="ExternalOutput").ap()

    P = Prog(nc)
    es = P.es
    s_x1 = [sb(nc, es, f"s_x1{i}", [128, NC * 512]) for i in range(2)]
    s_xb = sb(nc, es, "s_xb", [128, NC * 512], BF16)
    s_h = sb(nc, es, "s_h", [128, NFF * 512])
    s_hb = sb(nc, es, "s_hb", [128, NFF * 512], BF16)
    s_z = sb(nc, es, "s_z", [128, NC * 512])
    s_wu = [sb(nc, es, f"s_wu{i}", [128, NC * 128], BF16) for i in range(2)]
    s_wd = [sb(nc, es, f"s_wd{i}", [128, NFF * 128], BF16) for i in range(2)]
    s_tmp = sb(nc, es, "s_tmp", [128, 512])
    s_row = sb(nc, es, "s_row", [1, 512])
    s_hrow = sb(nc, es, "s_hrow", [1, 512])
    s_m = sb(nc, es, "s_m", [1, 512])
    s_r = sb(nc, es, "s_r", [1, 512])
    s_trow = sb(nc, es, "s_trow", [1, 512])
    s_eps = sb(nc, es, "s_eps", [1, 1])
    s_eps8 = sb(nc, es, "s_eps8", [1, 1])
    s_epsphi = sb(nc, es, "s_epsphi", [1, 1])
    s_onec = sb(nc, es, "s_onec", [128, 1])
    s_oner = sb(nc, es, "s_oner", [1, 128])
    p_up = [ps(nc, es, f"p_up{i}", [128, 512]) for i in range(2)]
    p_dn = [ps(nc, es, f"p_dn{i}", [128, 512]) for i in range(2)]
    p_rms = ps(nc, es, "p_rms", [1, 512])
    p_h2 = ps(nc, es, "p_h2", [1, 512])
    p_bc1 = ps(nc, es, "p_bc1", [128, 512])
    p_bc2 = ps(nc, es, "p_bc2", [128, 512])

    lall = P.total("syd", "ld")
    m1 = P.v(lambda v: v.memset(s_onec[:], 1.0))
    m2 = P.v(lambda v: v.memset(s_oner[:], 1.0))
    m4 = P.v(lambda v: v.memset(s_eps[:], LN_EPS))

    nsub = (n + 511) // 512
    war = {"eps_dep": m4}
    wtoks = {}
    wconsumed = {}
    stoks = []
    xls = [None, None]
    xls[0] = P.sd(lambda s: s.dma_start(
        out=s_x1[0][:].rearrange("p (a n) -> p a n", a=NC),
        in_=fmview(X1T, n)[:, :, 0:512]), ch="x0")
    for i in range(nsub):
        w = min(512, n - i * 512)
        xb = i % 2
        if i + 1 < nsub:
            w2_ = min(512, n - (i + 1) * 512)
            xls[1 - xb] = P.sd(lambda s, _i=i, _b=1 - xb, _w2=w2_:
                               s.dma_start(
                out=s_x1[_b][:].rearrange("p (a n) -> p a n", a=NC)[
                    :, :, 0:_w2],
                in_=fmview(X1T, n)[:, :, (_i + 1) * 512:(_i + 1) * 512 + _w2]),
                [war.get(("x1free", 1 - xb))], ch=f"x{1 - xb}")
        lall = xls[xb]
        sqprev = war.get("sq")
        s1tok = None
        for j in range(NC):
            off = j * n + i * 512
            sq = P.s(lambda s, _o=j * 512, _w=w, _b=xb: s.activation(
                s_tmp[:, 0:_w], s_x1[_b][:, _o:_o + _w], AF.Square),
                [lall, sqprev])
            s1tok = P.t(lambda t, _w=w, _j=j: t.matmul(
                p_rms[:, 0:_w], lhsT=s_onec[:], rhs=s_tmp[:, 0:_w],
                start=(_j == 0), stop=(_j == NC - 1)),
                [sq, m1, war.get("rms")])
            sqprev = s1tok
        war["sq"] = sqprev
        r1 = P.v(lambda v, _w=w: v.tensor_scalar(
            s_row[:, 0:_w], p_rms[:, 0:_w], 1.0 / D, None, op0=ALU.mult),
            [s1tok, war.get("row")])
        war["rms"] = r1
        r2 = P.s(lambda s, _w=w: s.activation(
            s_row[:, 0:_w], s_row[:, 0:_w], AF.Sqrt), [r1])
        r3 = P.v(lambda v, _w=w: v.tensor_scalar(
            s_row[:, 0:_w], s_row[:, 0:_w], RMS_EPS, None, op0=ALU.add), [r2])
        r4 = P.v(lambda v, _w=w: v.reciprocal(
            s_row[:, 0:_w], s_row[:, 0:_w]), [r3])
        b1 = P.t(lambda t, _w=w: t.matmul(
            p_bc1[:, 0:_w], lhsT=s_oner[:], rhs=s_row[:, 0:_w],
            start=True, stop=True), [r4, m2, war.get("bc1")])
        casts = []
        for j in range(NC):
            off = j * n + i * 512
            casts.append(P.v(lambda v, _j=j, _w=w, _b=xb: v.tensor_tensor(
                s_xb[:, _j * 512:_j * 512 + _w],
                s_x1[_b][:, _j * 512:_j * 512 + _w],
                p_bc1[:, 0:_w], ALU.mult), [b1, lall, war.get("xb")]))
        war["row"] = casts[-1]

        hsq_prev = war.get("sq")
        h2tok = None
        geltoks = []
        for f in range(NFF):
            buf = f % 2
            wtok = P.sd(lambda s, _f=f, _b=buf: s.dma_start(
                out=s_wu[_b][:].rearrange("p (a n) -> p a n", a=NC),
                in_=WTU.rearrange("(a p) f -> p a f", p=128)[
                    :, :, _f * 128:(_f + 1) * 128]),
                [wconsumed.get(("wu", buf))], ch=f"wu{buf}")
            mmt = None
            for j in range(NC):
                mmt = P.t(lambda t, _f=f, _j=j, _b=buf, _w=w: t.matmul(
                    p_up[_f % 2][:, 0:_w],
                    lhsT=s_wu[_b][:, _j * 128:(_j + 1) * 128],
                    rhs=s_xb[:, _j * 512:_j * 512 + _w],
                    start=(_j == 0), stop=(_j == NC - 1)),
                    [wtok, casts[j], war.get(("up", f % 2))])
            wconsumed[("wu", buf)] = mmt
            gel = P.s(lambda s, _f=f, _w=w: s.activation(
                s_h[:, _f * 512:_f * 512 + _w], p_up[_f % 2][:, 0:_w],
                AF.Gelu), [mmt, war.get(("hread", f))])
            war[("up", f % 2)] = gel
            geltoks.append(gel)
            hsq = P.s(lambda s, _f=f, _w=w: s.activation(
                s_tmp[:, 0:_w], s_h[:, _f * 512:_f * 512 + _w], AF.Square),
                [gel, hsq_prev])
            h2tok = P.t(lambda t, _f=f, _w=w: t.matmul(
                p_h2[:, 0:_w], lhsT=s_onec[:], rhs=s_tmp[:, 0:_w],
                start=(_f == 0), stop=(_f == NFF - 1)),
                [hsq, war.get("h2")])
            hsq_prev = h2tok
        war["sq"] = hsq_prev
        hr1 = P.v(lambda v, _w=w: v.tensor_scalar(
            s_hrow[:, 0:_w], p_h2[:, 0:_w], 1.0 / DFF, None, op0=ALU.mult),
            [h2tok, war.get("hrow")])
        war["h2"] = hr1
        hr2 = P.s(lambda s, _w=w: s.activation(
            s_hrow[:, 0:_w], s_hrow[:, 0:_w], AF.Sqrt), [hr1])
        hr3 = P.v(lambda v, _w=w: v.tensor_scalar(
            s_hrow[:, 0:_w], s_hrow[:, 0:_w], RMS_EPS, None, op0=ALU.add),
            [hr2])
        hr4 = P.v(lambda v, _w=w: v.reciprocal(
            s_hrow[:, 0:_w], s_hrow[:, 0:_w]), [hr3])
        b2t = P.t(lambda t, _w=w: t.matmul(
            p_bc2[:, 0:_w], lhsT=s_oner[:], rhs=s_hrow[:, 0:_w],
            start=True, stop=True), [hr4, war.get("bc2")])
        hcasts = []
        for f in range(NFF):
            hcasts.append(P.v(lambda v, _f=f, _w=w: v.tensor_tensor(
                s_hb[:, _f * 512:_f * 512 + _w],
                s_h[:, _f * 512:_f * 512 + _w], p_bc2[:, 0:_w], ALU.mult),
                [b2t, geltoks[f], war.get("hb")]))
        war["hrow"] = hcasts[-1]

        ztoks = []
        for cjj in range(NC):
            buf = cjj % 2
            wtok = P.sd(lambda s, _c=cjj, _b=buf: s.dma_start(
                out=s_wd[_b][:].rearrange("p (a n) -> p a n", a=NFF),
                in_=WTD.rearrange("(a p) c -> p a c", p=128)[
                    :, :, _c * 128:(_c + 1) * 128]),
                [wconsumed.get(("wd", buf))], ch=f"wd{buf}")
            mmt = None
            for f in range(NFF):
                mmt = P.t(lambda t, _c=cjj, _f=f, _b=buf, _w=w: t.matmul(
                    p_dn[_c % 2][:, 0:_w],
                    lhsT=s_wd[_b][:, _f * 128:(_f + 1) * 128],
                    rhs=s_hb[:, _f * 512:_f * 512 + _w],
                    start=(_f == 0), stop=(_f == NFF - 1)),
                    [wtok, hcasts[f], war.get(("dn", cjj % 2))])
            wconsumed[("wd", buf)] = mmt
            zt = P.v(lambda v, _c=cjj, _w=w, _b=xb: v.tensor_tensor(
                s_z[:, _c * 512:_c * 512 + _w], p_dn[_c % 2][:, 0:_w],
                s_x1[_b][:, _c * 512:_c * 512 + _w], ALU.add),
                [mmt, lall, war.get(("zread", cjj))])
            war[("dn", cjj % 2)] = zt
            ztoks.append(zt)
        war["hb"] = ztoks[-1]
        war["xb"] = ztoks[-1]

        outtoks = emit_ln(P, 512, 0, w, s_z, s_z, s_m, s_r, s_tmp, s_trow,
                          s_onec, s_oner, p_rms[0:1, :], p_h2[0:1, :],
                          p_bc1, p_bc2, ztoks, war, s_eps=s_eps)
        war["rms"] = war["s1"]
        war["h2"] = war["s2"]
        war["bc1"] = war["bc"]
        war["bc2"] = war["bc"]
        for f in range(NFF):
            war[("hread", f)] = hcasts[f]
        stoks.append(P.sd(lambda s, _i=i, _w=w: s.dma_start(
            out=fmview(X2T, n)[:, :, _i * 512:_i * 512 + _w],
            in_=s_z[:].rearrange("p (a n) -> p a n", a=NC)[:, :, 0:_w]),
            outtoks, ch="st"))
        for cjj in range(NC):
            war[("zread", cjj)] = stoks[-1]
        war[("x1free", xb)] = ztoks[-1]
    P.finish(stoks)
    return P.build().nc


def build_c1(ns=NSEQ):
    nc = bass.Bass("TRN2", target_bir_lowering=False, debug=False)
    X2 = nc.dram_tensor("x2tb", [D, ns], F32, kind="ExternalInput").ap()
    WQ = nc.dram_tensor("wq", [D, 256], F32, kind="ExternalInput").ap()
    WK = nc.dram_tensor("wk", [D, 256], F32, kind="ExternalInput").ap()
    WV = nc.dram_tensor("wv", [D, 256], F32, kind="ExternalInput").ap()
    BQ = nc.dram_tensor("bq", [128, 2], F32, kind="ExternalInput").ap()
    BK = nc.dram_tensor("bk", [128, 2], F32, kind="ExternalInput").ap()
    BV = nc.dram_tensor("bvrow", [1, 256], F32, kind="ExternalInput").ap()
    PJ = nc.dram_tensor("projt", [64, 4 * 256], F32, kind="ExternalInput").ap()
    IDB = nc.dram_tensor("identb", [128, 128], BF16, kind="ExternalInput").ap()
    ND = nc.dram_tensor("ndt", [256, ns], BF16, kind="ExternalOutput").ap()

    P = Prog(nc)
    es = P.es
    s_xs = [sb(nc, es, f"s_xs{i}", [128, NC * 512], BF16) for i in range(2)]
    s_q = [sb(nc, es, f"s_q{i}", [128, ns], BF16) for i in range(2)]
    s_k = [sb(nc, es, f"s_k{i}", [128, ns], BF16) for i in range(2)]
    s_wq = sb(nc, es, "s_wq", [128, NC * 256], BF16)
    s_wk = sb(nc, es, "s_wk", [128, NC * 256], BF16)
    s_wv = sb(nc, es, "s_wv", [128, NC * 256], BF16)
    s_bq = sb(nc, es, "s_bq", [128, 2])
    s_bk = sb(nc, es, "s_bk", [128, 2])
    s_bv = sb(nc, es, "s_bv", [1, 256])
    s_pjr = sb(nc, es, "s_pjr", [128, 4 * 256])
    s_pj = sb(nc, es, "s_pj", [128, 4 * 256], BF16)
    s_idb = sb(nc, es, "s_idb", [128, 128], BF16)
    s_oner = sb(nc, es, "s_oner", [1, 128])
    s_vx = sb(nc, es, "s_vx", [128, (ns // 128) * 260], BF16)
    s_phik = sb(nc, es, "s_phik", [128, 256], BF16)
    s_exp = sb(nc, es, "s_exp", [128, 256])
    s_nmx = sb(nc, es, "s_nmx", [128, 1])
    s_kv = sb(nc, es, "s_kv", [128, 2 * 260])
    s_kvb = sb(nc, es, "s_kvb", [128, 2 * 260], BF16)
    s_phiqT = [sb(nc, es, f"s_phiqT{i}", [128, 512], BF16) for i in range(2)]
    s_den = sb(nc, es, "s_den", [1, 512])
    s_dbc = sb(nc, es, "s_dbc", [64, 512])
    s_nd = [sb(nc, es, f"s_nd{i}", [128, ns], BF16) for i in range(2)]
    p_a = ps(nc, es, "p_a", [128, 512])
    p_b = ps(nc, es, "p_b", [128, 512])
    p_c = ps(nc, es, "p_c", [128, 512])
    p_x = [ps(nc, es, f"p_x{i}", [128, 512]) for i in range(2)]
    p_kv = [ps(nc, es, f"p_kv{i}", [128, 512]) for i in range(2)]
    p_t = ps(nc, es, "p_t", [128, 1024], BF16)

    lw1 = P.gd(lambda g: g.dma_start(
        out=s_wq[:].rearrange("p (a n) -> p a n", a=NC),
        in_=WQ.rearrange("(a p) n -> p a n", p=128)), ch="w")
    lw2 = P.gd(lambda g: g.dma_start(
        out=s_wk[:].rearrange("p (a n) -> p a n", a=NC),
        in_=WK.rearrange("(a p) n -> p a n", p=128)), ch="w")
    lw3 = P.gd(lambda g: g.dma_start(
        out=s_wv[:].rearrange("p (a n) -> p a n", a=NC),
        in_=WV.rearrange("(a p) n -> p a n", p=128)), ch="w")
    lg = P.total("gpd", "w")
    P.sd(lambda s: s.dma_start(out=s_bq[:], in_=BQ))
    P.sd(lambda s: s.dma_start(out=s_bk[:], in_=BK))
    P.sd(lambda s: s.dma_start(out=s_bv[:], in_=BV))
    P.sd(lambda s: s.dma_start(out=s_idb[:], in_=IDB))
    P.sd(lambda s: s.dma_start(out=s_pjr[0:64, :], in_=PJ))
    P.sd(lambda s: s.dma_start(out=s_pjr[64:128, :], in_=PJ))
    lall = P.total("syd", "ld")
    m2 = P.v(lambda v: v.memset(s_vx[:], 1.0))
    pj = P.v(lambda v: v.tensor_scalar(
        s_pj[:], s_pjr[:], DATA_NORM, None, op0=ALU.mult), [lall])

    nsub = ns // 512
    war = {}
    xload = [None, None]
    xload[0] = P.gd(lambda g: g.dma_start(
        out=s_xs[0][:].rearrange("p (a n) -> p a n", a=NC),
        in_=fmview(X2, ns)[:, :, 0:512]), ch="xs0")

    for i in range(nsub):
        buf = i % 2
        if i + 1 < nsub:
            xload[1 - buf] = P.gd(lambda g, _i=i, _b=1 - buf: g.dma_start(
                out=s_xs[_b][:].rearrange("p (a n) -> p a n", a=NC),
                in_=fmview(X2, ns)[:, :, (_i + 1) * 512:(_i + 2) * 512]),
                [war.get(("xread", 1 - buf))], ch=f"xs{1 - buf}")
        lx = xload[buf]
        pq = [p_a, p_b]
        for W_, Sdst, Bias in ((s_wq, s_q, s_bq), (s_wk, s_k, s_bk)):
            for hp in range(2):
                mmt = None
                for j in range(NC):
                    mmt = P.t(lambda t, _j=j, _hp=hp, _b=buf, _W=W_: t.matmul(
                        pq[_hp][:, 0:512],
                        lhsT=_W[:, _j * 256 + _hp * 128:
                                _j * 256 + _hp * 128 + 128],
                        rhs=s_xs[_b][:, _j * 512:(_j + 1) * 512],
                        start=(_j == 0), stop=(_j == NC - 1)),
                        [lx, lg, war.get(("pq", hp))])
                ev = P.s(lambda s, _i=i, _hp=hp, _S=Sdst, _B=Bias:
                         s.activation(
                    _S[_hp][:, _i * 512:_i * 512 + 512], pq[_hp][:, 0:512],
                    AF.Identity, bias=_B[:, _hp:_hp + 1]), [mmt, lall])
                war[("pq", hp)] = ev
                if W_ is s_wk:
                    war[("kwrite", hp)] = ev
        lastv = None
        for r in range(4):
            rr = r * 128
            tt = i * 4 + r
            mmt = None
            for j in range(NC):
                mmt = P.t(lambda t, _j=j, _rr=rr, _b=buf: t.matmul(
                    p_c[:, 0:256],
                    lhsT=s_xs[_b][:, _j * 512 + _rr:_j * 512 + _rr + 128],
                    rhs=s_wv[:, _j * 256:(_j + 1) * 256],
                    start=(_j == 0), stop=False),
                    [lx, war.get("vread")])
            mmt = P.t(lambda t: t.matmul(
                p_c[:, 0:256], lhsT=s_oner[:], rhs=s_bv[:],
                start=False, stop=True), [mmt, m1, lall])
            vx = P.v(lambda v, _tt=tt: v.tensor_copy(
                s_vx[:, _tt * 260:(_tt + 1) * 260].rearrange(
                    "p (h c) -> p h c", h=4)[:, :, 0:64],
                p_c[:, 0:256].rearrange("p (h c) -> p h c", h=4)),
                [mmt, m2])
            war["vread"] = vx
            lastv = vx
        war[("xread", buf)] = lastv

    kvevs = []
    for h in range(4):
        hp, hb = h // 2, (h % 2) * 64
        kv0 = kv1 = None
        for tt in range(ns // 128):
            rr = tt * 128
            xp = P.t(lambda t, _hp=hp, _hb=hb, _rr=rr, _tt=tt, _h=h:
                     t.matmul(
                p_x[_tt % 2][:, 0:256],
                lhsT=s_k[_hp][_hb:_hb + 64, _rr:_rr + 128],
                rhs=s_pj[_hb:_hb + 64, _h * 256:(_h + 1) * 256],
                start=True, stop=True),
                [war.get(("kwrite", hp)), pj, war.get(("xp", tt % 2)),
                 war.get("vread")])
            rm = P.v(lambda v, _tt=tt: v.reduce_max(
                s_nmx[:], p_x[_tt % 2][:, 0:256], axis=AX.X, negate=True),
                [xp, war.get("nmxread")])
            ex = P.s(lambda s, _tt=tt: s.activation(
                s_exp[:], p_x[_tt % 2][:, 0:256], AF.Exp, bias=s_nmx[:]),
                [rm, war.get("exread")])
            war[("xp", tt % 2)] = ex
            war["nmxread"] = ex
            ph = P.v(lambda v: v.tensor_scalar(
                s_phik[:], s_exp[:], 1.0 / 16.0, PHI_EPS / 16.0,
                op0=ALU.mult, op1=ALU.add), [ex, war.get("phikread")])
            war["exread"] = ph
            kv0 = P.t(lambda t, _h=h, _tt=tt: t.matmul(
                p_kv[0][:, 0:65], lhsT=s_phik[:, 0:128],
                rhs=s_vx[:, _tt * 260 + _h * 65:_tt * 260 + (_h + 1) * 65],
                start=(_tt == 0), stop=(_tt == ns // 128 - 1)),
                [ph, war.get("kv0ev")])
            kv1 = P.t(lambda t, _h=h, _tt=tt: t.matmul(
                p_kv[1][:, 0:65], lhsT=s_phik[:, 128:256],
                rhs=s_vx[:, _tt * 260 + _h * 65:_tt * 260 + (_h + 1) * 65],
                start=(_tt == 0), stop=(_tt == ns // 128 - 1)),
                [ph, war.get("kv1ev")])
            war["phikread"] = kv1
        e1 = P.v(lambda v, _h=h: v.tensor_copy(
            s_kv[:, _h * 65:(_h + 1) * 65], p_kv[0][:, 0:65]), [kv0, kv1])
        e2 = P.v(lambda v, _h=h: v.tensor_copy(
            s_kv[:, 260 + _h * 65:260 + (_h + 1) * 65], p_kv[1][:, 0:65]),
            [e1])
        war["kv0ev"] = e1
        war["kv1ev"] = e2
        kvevs.append(e2)
    cb = P.v(lambda v: v.tensor_copy(s_kvb[:], s_kv[:]), kvevs)

    ndlast = None
    for i in range(nsub):
        for h in range(4):
            hp, hb = h // 2, (h % 2) * 64
            trtoks = []
            for r in range(4):
                n0 = i * 512 + r * 128
                xp = P.t(lambda t, _hp=hp, _hb=hb, _n0=n0, _r=r, _h=h:
                         t.matmul(
                    p_x[_r % 2][:, 0:256],
                    lhsT=s_q[_hp][_hb:_hb + 64, _n0:_n0 + 128],
                    rhs=s_pj[_hb:_hb + 64, _h * 256:(_h + 1) * 256],
                    start=True, stop=True), [pj, war.get(("xp", r % 2))])
                rm = P.v(lambda v, _r=r: v.reduce_max(
                    s_nmx[:], p_x[_r % 2][:, 0:256], axis=AX.X, negate=True),
                    [xp, war.get("nmxread")])
                ex = P.s(lambda s, _r=r: s.activation(
                    s_exp[:], p_x[_r % 2][:, 0:256], AF.Exp, bias=s_nmx[:]),
                    [rm, war.get("exread")])
                war[("xp", r % 2)] = ex
                war["nmxread"] = ex
                ph = P.v(lambda v: v.tensor_scalar(
                    s_phik[:], s_exp[:], 1.0 / 16.0, PHI_EPS / 16.0,
                    op0=ALU.mult, op1=ALU.add), [ex, war.get("phikread")])
                war["exread"] = ph
                t1 = P.t(lambda t: t.transpose(
                    p_t[:, 0:128], s_phik[:, 0:128], s_idb[:]),
                    [ph, lall, war.get("tr0")])
                t2 = P.t(lambda t: t.transpose(
                    p_t[:, 128:256], s_phik[:, 128:256], s_idb[:]), [t1])
                c1t = P.v(lambda v, _r=r: v.tensor_copy(
                    s_phiqT[0][:, _r * 128:(_r + 1) * 128], p_t[:, 0:128]),
                    [t2, war.get("pTread")])
                c2t = P.v(lambda v, _r=r: v.tensor_copy(
                    s_phiqT[1][:, _r * 128:(_r + 1) * 128], p_t[:, 128:256]),
                    [c1t])
                war["phikread"] = c2t
                war["tr0"] = c2t
                trtoks.append(c2t)
            nm = P.t(lambda t, _h=h: t.matmul(
                p_a[0:65, 0:512], lhsT=s_kvb[:, _h * 65:(_h + 1) * 65],
                rhs=s_phiqT[0][:], start=True, stop=False),
                trtoks + [cb, war.get("numread")])
            nm = P.t(lambda t, _h=h: t.matmul(
                p_a[0:65, 0:512],
                lhsT=s_kvb[:, 260 + _h * 65:260 + (_h + 1) * 65],
                rhs=s_phiqT[1][:], start=False, stop=True), [nm])
            war["pTread"] = nm
            d1 = P.v(lambda v: v.tensor_scalar(
                s_den[:], p_a[64:65, 0:512], PHI_EPS, None, op0=ALU.add),
                [nm, war.get("denread")])
            d2 = P.v(lambda v: v.reciprocal(s_den[:], s_den[:]), [d1])
            db = P.t(lambda t: t.matmul(
                p_b[0:64, 0:512], lhsT=s_oner[:, 0:64], rhs=s_den[:],
                start=True, stop=True), [d2, m1, war.get("dbcread")])
            dc = P.v(lambda v: v.tensor_copy(s_dbc[:], p_b[0:64, 0:512]),
                     [db])
            war["denread"] = dc
            nd = P.v(lambda v, _hp=hp, _hb=hb, _i=i: v.tensor_tensor(
                s_nd[_hp][_hb:_hb + 64, _i * 512:_i * 512 + 512],
                p_a[0:64, 0:512], s_dbc[:], ALU.mult), [dc, nm])
            war["numread"] = nd
            war["dbcread"] = nd
            ndlast = nd
    stoks = []
    for hp in range(2):
        stoks.append(P.sd(lambda s, _hp=hp: s.dma_start(
            out=ND[_hp * 128:(_hp + 1) * 128, :], in_=s_nd[_hp][:]),
            [ndlast], ch="st"))
    P.finish(stoks)
    return P.build().nc


def build_c2(n=NTOK):
    nc = bass.Bass("TRN2", target_bir_lowering=False, debug=False)
    NDS = nc.dram_tensor("ndts", [D, n], BF16, kind="ExternalInput").ap()
    WO = nc.dram_tensor("wo", [D, D], F32, kind="ExternalInput").ap()
    BO = nc.dram_tensor("bo", [128, NC], F32, kind="ExternalInput").ap()
    X2T = nc.dram_tensor("x2t", [D, n], F32, kind="ExternalInput").ap()
    X3T = nc.dram_tensor("x3t", [D, n], F32, kind="ExternalOutput").ap()

    P = Prog(nc)
    es = P.es
    s_ndv = sb(nc, es, "s_ndv", [128, NC * n], BF16)
    s_x2 = sb(nc, es, "s_x2", [128, NC * n])
    s_wo = sb(nc, es, "s_wo", [128, NC * D], BF16)
    s_bo = sb(nc, es, "s_bo", [128, NC])
    s_z = sb(nc, es, "s_z", [128, NC * 512])
    s_tmp = sb(nc, es, "s_tmp", [128, 512])
    s_tmpb = s_tmp[:].bitcast(BF16)
    s_m = sb(nc, es, "s_m", [1, 512])
    s_r = sb(nc, es, "s_r", [1, 512])
    s_trow = sb(nc, es, "s_trow", [1, 512])
    s_eps = sb(nc, es, "s_eps", [1, 1])
    s_eps8 = sb(nc, es, "s_eps8", [1, 1])
    s_epsphi = sb(nc, es, "s_epsphi", [1, 1])
    s_onec = sb(nc, es, "s_onec", [128, 1])
    s_oner = sb(nc, es, "s_oner", [1, 128])
    p_o = [ps(nc, es, f"p_o{i}", [128, 512]) for i in range(2)]
    p_s1 = ps(nc, es, "p_s1", [1, 512])
    p_s2 = ps(nc, es, "p_s2", [1, 512])
    p_mbc = ps(nc, es, "p_mbc", [128, 512])
    p_rbc = ps(nc, es, "p_rbc", [128, 512])

    P.sd(lambda s: s.dma_start(
        out=s_ndv[:].rearrange("p (a n) -> p a n", a=NC), in_=fmview(NDS, n)))
    P.sd(lambda s: s.dma_start(
        out=s_x2[:].rearrange("p (a n) -> p a n", a=NC), in_=fmview(X2T, n)))
    P.sd(lambda s: s.dma_start(out=s_bo[:], in_=BO))
    lall = P.total("syd", "ld")
    lwo = P.gd(lambda g: g.dma_start(
        out=s_wo[:].rearrange("p (a n) -> p a n", a=NC),
        in_=WO.rearrange("(a p) n -> p a n", p=128)), ch="w")
    m1 = P.v(lambda v: v.memset(s_onec[:], 1.0))
    m2 = P.v(lambda v: v.memset(s_oner[:], 1.0))
    m4 = P.v(lambda v: v.memset(s_eps[:], LN_EPS))

    nsub = (n + 511) // 512
    war = {"eps_dep": m4}
    stoks = []
    for i in range(nsub):
        w = min(512, n - i * 512)
        ztoks = []
        for cj in range(NC):
            mmt = None
            for j in range(NC):
                mmt = P.t(lambda t, _c=cj, _j=j, _i=i, _w=w: t.matmul(
                    p_o[_c % 2][:, 0:_w],
                    lhsT=s_wo[:, _j * D + _c * 128:_j * D + (_c + 1) * 128],
                    rhs=s_ndv[:, _j * n + _i * 512:_j * n + _i * 512 + _w],
                    start=(_j == 0), stop=(_j == NC - 1)),
                    [lall, lwo, war.get(("o", cj % 2))])
            ev = P.s(lambda s, _c=cj, _w=w: s.activation(
                s_tmp[:, 0:_w], p_o[_c % 2][:, 0:_w], AF.Identity,
                bias=s_bo[:, _c:_c + 1]), [mmt, war.get("tmpread")])
            zt = P.v(lambda v, _c=cj, _i=i, _w=w: v.tensor_tensor(
                s_z[:, _c * 512:_c * 512 + _w], s_tmp[:, 0:_w],
                s_x2[:, _c * n + _i * 512:_c * n + _i * 512 + _w], ALU.add),
                [ev, war.get(("zread", cj))])
            war[("o", cj % 2)] = zt
            war["tmpread"] = zt
            ztoks.append(zt)
        war["sq"] = ztoks[-1]
        outtoks = emit_ln(P, 512, 0, w, s_z, s_z, s_m, s_r, s_tmp, s_trow,
                          s_onec, s_oner, p_s1, p_s2, p_mbc, p_rbc, ztoks,
                          war, s_eps=s_eps)
        war["tmpread"] = war["sq"]
        stoks.append(P.sd(lambda s, _i=i, _w=w: s.dma_start(
            out=fmview(X3T, n)[:, :, _i * 512:_i * 512 + _w],
            in_=s_z[:].rearrange("p (a n) -> p a n", a=NC)[:, :, 0:_w]),
            outtoks, ch="st"))
        for cj in range(NC):
            war[("zread", cj)] = stoks[-1]
    P.finish(stoks)
    return P.build().nc


def build_d(n=NTOK):
    nc = bass.Bass("TRN2", target_bir_lowering=False, debug=False)
    X3T = nc.dram_tensor("x3t", [D, n], F32, kind="ExternalInput").ap()
    WIN = nc.dram_tensor("win", [D, 64], F32, kind="ExternalInput").ap()
    BIN = nc.dram_tensor("bin", [64, 1], F32, kind="ExternalInput").ap()
    W1 = nc.dram_tensor("w1", [64, 16], F32, kind="ExternalInput").ap()
    B1 = nc.dram_tensor("b1", [64, 16], F32, kind="ExternalInput").ap()
    W2 = nc.dram_tensor("w2", [64, 16], F32, kind="ExternalInput").ap()
    B2_ = nc.dram_tensor("b2", [64, 1], F32, kind="ExternalInput").ap()
    WOUT = nc.dram_tensor("wout", [64, D], F32, kind="ExternalInput").ap()
    BOUT = nc.dram_tensor("bout", [128, NC], F32, kind="ExternalInput").ap()
    YT = nc.dram_tensor("yt", [D, n], F32, kind="ExternalOutput").ap()
    if dump:
        XD1 = nc.dram_tensor("xd1", [D, n], F32, kind="ExternalOutput").ap()
        XD2 = nc.dram_tensor("xd2", [D, n], F32, kind="ExternalOutput").ap()
        XD3 = nc.dram_tensor("xd3", [D, n], F32, kind="ExternalOutput").ap()
        WTUD = nc.dram_tensor("wtud", [D, DFF], BF16,
                              kind="ExternalOutput").ap()
        WTDD = nc.dram_tensor("wtdd", [DFF, D], BF16,
                              kind="ExternalOutput").ap()

    P = Prog(nc)
    es = P.es
    s_x3 = sb(nc, es, "s_x3", [128, NC * n])
    s_x3b = sb(nc, es, "s_x3b", [128, NC * n], BF16)
    s_winb = sb(nc, es, "s_winb", [128, NC * 64], BF16)
    s_bin = sb(nc, es, "s_bin", [64, 1])
    s_w1 = sb(nc, es, "s_w1", [64, 16])
    s_b1 = sb(nc, es, "s_b1", [64, 16])
    s_w2 = sb(nc, es, "s_w2", [64, 16])
    s_b2 = sb(nc, es, "s_b2", [64, 1])
    s_woutb = sb(nc, es, "s_woutb", [64, D], BF16)
    s_bout = sb(nc, es, "s_bout", [128, NC])
    s_s = sb(nc, es, "s_s", [64, 512])
    s_hi = sb(nc, es, "s_hi", [64, 512])
    s_g = sb(nc, es, "s_g", [64, 512])
    s_gb = sb(nc, es, "s_gb", [64, 512], BF16)
    s_z = sb(nc, es, "s_z", [128, NC * 512])
    s_tmp = sb(nc, es, "s_tmp", [128, 512])
    s_tmpb = s_tmp[:].bitcast(BF16)
    s_m = sb(nc, es, "s_m", [1, 512])
    s_r = sb(nc, es, "s_r", [1, 512])
    s_trow = sb(nc, es, "s_trow", [1, 512])
    s_eps = sb(nc, es, "s_eps", [1, 1])
    s_eps8 = sb(nc, es, "s_eps8", [1, 1])
    s_epsphi = sb(nc, es, "s_epsphi", [1, 1])
    s_onec = sb(nc, es, "s_onec", [128, 1])
    s_oner = sb(nc, es, "s_oner", [1, 128])
    p_s = ps(nc, es, "p_s", [128, 512])
    p_o = [ps(nc, es, f"p_o{i}", [128, 512]) for i in range(2)]
    p_s1 = ps(nc, es, "p_s1", [1, 512])
    p_s2 = ps(nc, es, "p_s2", [1, 512])
    p_mbc = ps(nc, es, "p_mbc", [128, 512])
    p_rbc = ps(nc, es, "p_rbc", [128, 512])

    P.sd(lambda s: s.dma_start(
        out=s_x3[:].rearrange("p (a n) -> p a n", a=NC), in_=fmview(X3T, n)))
    P.sd(lambda s: s.dma_start(out=s_bin[:], in_=BIN))
    P.sd(lambda s: s.dma_start(out=s_w1[:], in_=W1))
    P.sd(lambda s: s.dma_start(out=s_b1[:], in_=B1))
    P.sd(lambda s: s.dma_start(out=s_w2[:], in_=W2))
    P.sd(lambda s: s.dma_start(out=s_b2[:], in_=B2_))
    P.sd(lambda s: s.dma_start(out=s_bout[:], in_=BOUT))
    lall = P.total("syd", "ld")
    lx3b = P.gd(lambda g: g.dma_start(
        out=s_x3b[:].rearrange("p (a n) -> p a n", a=NC), in_=fmview(X3T, n)),
        ch="w")
    lwin = P.gd(lambda g: g.dma_start(
        out=s_winb[:].rearrange("p (a n) -> p a n", a=NC),
        in_=WIN.rearrange("(a p) n -> p a n", p=128)), ch="w")
    lwout = P.gd(lambda g: g.dma_start(out=s_woutb[:], in_=WOUT), ch="w")
    lgw = P.total("gpd", "w")
    m1 = P.v(lambda v: v.memset(s_onec[:], 1.0))
    m2 = P.v(lambda v: v.memset(s_oner[:], 1.0))
    m4 = P.v(lambda v: v.memset(s_eps[:], LN_EPS))

    nsub = (n + 511) // 512
    war = {"eps_dep": m4}
    stoks = []
    for i in range(nsub):
        w = min(512, n - i * 512)
        mmt = None
        for j in range(NC):
            mmt = P.t(lambda t, _j=j, _i=i, _w=w: t.matmul(
                p_s[0:64, 0:_w], lhsT=s_winb[:, _j * 64:(_j + 1) * 64],
                rhs=s_x3b[:, _j * n + _i * 512:_j * n + _i * 512 + _w],
                start=(_j == 0), stop=(_j == NC - 1)),
                [lgw, war.get("psread")])
        sv = P.s(lambda s, _w=w: s.activation(
            s_s[:, 0:_w], p_s[0:64, 0:_w], AF.Identity, bias=s_bin[:]),
            [mmt, lall, war.get("sread")])
        war["psread"] = sv
        gt = None
        for ii in range(16):
            hi = P.s(lambda s, _ii=ii, _w=w: s.activation(
                s_hi[:, 0:_w], s_s[:, 0:_w], AF.Relu,
                scale=s_w1[:, _ii:_ii + 1], bias=s_b1[:, _ii:_ii + 1]),
                [sv, lall, war.get("hiread")])
            if ii == 0:
                gt = P.v(lambda v, _ii=ii, _w=w: v.tensor_scalar(
                    s_g[:, 0:_w], s_hi[:, 0:_w], s_w2[:, _ii:_ii + 1], None,
                    op0=ALU.mult), [hi, war.get("gread")])
            else:
                gt = P.v(lambda v, _ii=ii, _w=w: v.scalar_tensor_tensor(
                    s_g[:, 0:_w], s_hi[:, 0:_w], s_w2[:, _ii:_ii + 1],
                    s_g[:, 0:_w], op0=ALU.mult, op1=ALU.add), [hi, gt])
            war["hiread"] = gt
        war["sread"] = gt
        g2 = P.v(lambda v, _w=w: v.tensor_scalar(
            s_g[:, 0:_w], s_g[:, 0:_w], s_b2[:], None, op0=ALU.add), [gt])
        gb = P.v(lambda v, _w=w: v.tensor_copy(
            s_gb[:, 0:_w], s_g[:, 0:_w]), [g2, war.get("gbread")])
        war["gread"] = gb
        ztoks = []
        for cj in range(NC):
            mmt = P.t(lambda t, _c=cj, _w=w: t.matmul(
                p_o[_c % 2][:, 0:_w],
                lhsT=s_woutb[:, _c * 128:(_c + 1) * 128],
                rhs=s_gb[:, 0:_w], start=True, stop=True),
                [gb, lgw, war.get(("o", cj % 2))])
            ev = P.s(lambda s, _c=cj, _w=w: s.activation(
                s_tmp[:, 0:_w], p_o[_c % 2][:, 0:_w], AF.Identity,
                bias=s_bout[:, _c:_c + 1]), [mmt, war.get("tmpread")])
            zt = P.v(lambda v, _c=cj, _i=i, _w=w: v.tensor_tensor(
                s_z[:, _c * 512:_c * 512 + _w], s_tmp[:, 0:_w],
                s_x3[:, _c * n + _i * 512:_c * n + _i * 512 + _w], ALU.add),
                [ev, lall, war.get(("zread", cj))])
            war[("o", cj % 2)] = zt
            war["tmpread"] = zt
            ztoks.append(zt)
        war["gbread"] = ztoks[-1]
        war["sq"] = ztoks[-1]
        outtoks = emit_ln(P, 512, 0, w, s_z, s_z, s_m, s_r, s_tmp, s_trow,
                          s_onec, s_oner, p_s1, p_s2, p_mbc, p_rbc, ztoks,
                          war, s_eps=s_eps)
        war["tmpread"] = war["sq"]
        stoks.append(P.sd(lambda s, _i=i, _w=w: s.dma_start(
            out=fmview(YT, n)[:, :, _i * 512:_i * 512 + _w],
            in_=s_z[:].rearrange("p (a n) -> p a n", a=NC)[:, :, 0:_w]),
            outtoks, ch="st"))
        for cj in range(NC):
            war[("zread", cj)] = stoks[-1]
    P.finish(stoks)
    return P.build().nc


def _run(nc, maps):
    res = run_bass_kernel_spmd(nc, maps, core_ids=list(range(8)))
    return res.results


def kernel_unfused(L0_attn_x, L0_attn_router_w, L0_attn_router_b, L0_attn_wq,
           L0_attn_wk, L0_attn_wv, L0_ffn_w_up, L0_ffn_w_down, L1_attn_w_qkv,
           L1_attn_b_qkv, L1_attn_w_out, L1_attn_b_out, L1_attn_proj_matrix,
           L1_ffn_Win, L1_ffn_bin_, L1_ffn_W1, L1_ffn_b1, L1_ffn_W2,
           L1_ffn_b2, L1_ffn_Wout, L1_ffn_bout):
    _ = (L0_attn_wq.shape, L0_attn_wk.shape, L0_attn_wv.shape)
    x = np.ascontiguousarray(L0_attn_x, dtype=np.float32)
    ident = np.eye(128, dtype=np.float32)
    identb = np.eye(128, dtype=np.float32).astype(ml_dtypes.bfloat16)

    def core_bo(core):
        return core // 4, (core % 4) * NTOK

    xt = {}
    for core in range(8):
        b, o = core_bo(core)
        xt[core] = np.ascontiguousarray(x[b, o:o + NTOK].T)

    nc = build_a1()
    rwc = np.ascontiguousarray(L0_attn_router_w.reshape(NC, 128).T)
    rb = np.array([[L0_attn_router_b[0]]], dtype=np.float32)
    maps = [{"xt": xt[c], "rwc": rwc, "rb": rb} for c in range(8)]
    r = _run(nc, maps)
    rw_full = np.zeros((2, NSEQ), np.float32)
    for core in range(8):
        b, o = core_bo(core)
        rw_full[b, o:o + NTOK] = r[core]["rws"][0]

    nc = build_a2()
    maps = []
    for core in range(8):
        b, o = core_bo(core)
        maps.append({
            "xt": xt[core],
            "xrevt": np.ascontiguousarray(x[b][::-1][o:o + NTOK].T),
            "rwf": np.ascontiguousarray(rw_full[b].reshape(128, 64)),
            "rwo": np.ascontiguousarray(rw_full[b][o:o + NTOK][None, :]),
            "ident": ident,
        })
    r = _run(nc, maps)
    x1t = {c: r[c]["x1t"] for c in range(8)}

    wupt = np.ascontiguousarray(L0_ffn_w_up.T)
    wdnt = np.ascontiguousarray(L0_ffn_w_down.T)
    nc = build_b1()
    maps = [{"wupt_s": np.ascontiguousarray(wupt[:, 512 * c:512 * (c + 1)]),
             "wdnt_s": np.ascontiguousarray(wdnt[:, 128 * c:128 * (c + 1)])}
            for c in range(8)]
    r = _run(nc, maps)
    aparts = np.concatenate([r[c]["aparts"] for c in range(8)], axis=1)
    nc = build_b1b()
    for c in range(8):
        maps[c]["aparts8"] = aparts
    r = _run(nc, maps)
    wtu = np.ascontiguousarray(
        np.concatenate([r[c]["wtu_s"] for c in range(8)], axis=1))
    wtd = np.ascontiguousarray(
        np.concatenate([r[c]["wtd_s"] for c in range(8)], axis=1))

    nc = build_b2()
    maps = [{"x1t": x1t[c], "wtu": wtu, "wtd": wtd} for c in range(8)]
    r = _run(nc, maps)
    x2t = {c: r[c]["x2t"] for c in range(8)}
    x2b_full = {b: np.concatenate([x2t[b * 4 + g] for g in range(4)], axis=1)
                for b in range(2)}

    nc = build_c1()
    wqkv = L1_attn_w_qkv
    maps = []
    for core in range(8):
        b, g = core // 4, core % 4
        cols = slice(g * 256, (g + 1) * 256)
        maps.append({
            "x2tb": x2b_full[b],
            "wq": np.ascontiguousarray(wqkv[:, 0:D][:, cols]),
            "wk": np.ascontiguousarray(wqkv[:, D:2 * D][:, cols]),
            "wv": np.ascontiguousarray(wqkv[:, 2 * D:3 * D][:, cols]),
            "bq": np.ascontiguousarray(
                L1_attn_b_qkv[0:D][cols].reshape(2, 128).T),
            "bk": np.ascontiguousarray(
                L1_attn_b_qkv[D:2 * D][cols].reshape(2, 128).T),
            "bvrow": np.ascontiguousarray(
                L1_attn_b_qkv[2 * D:3 * D][cols][None, :]),
            "projt": np.ascontiguousarray(np.concatenate(
                [L1_attn_proj_matrix[4 * g + h].T for h in range(4)], axis=1)),
            "identb": identb,
        })
    r = _run(nc, maps)
    nd_full = {b: np.concatenate([r[b * 4 + g]["ndt"] for g in range(4)],
                                 axis=0) for b in range(2)}

    nc = build_c2()
    maps = []
    for core in range(8):
        b, o = core_bo(core)
        maps.append({
            "ndts": np.ascontiguousarray(nd_full[b][:, o:o + NTOK]),
            "wo": L1_attn_w_out,
            "bo": np.ascontiguousarray(L1_attn_b_out.reshape(NC, 128).T),
            "x2t": x2t[core],
        })
    r = _run(nc, maps)
    x3t = {c: r[c]["x3t"] for c in range(8)}

    nc = build_d()
    maps = []
    for core in range(8):
        maps.append({
            "x3t": x3t[core],
            "win": L1_ffn_Win,
            "bin": np.ascontiguousarray(L1_ffn_bin_[:, None]),
            "w1": L1_ffn_W1, "b1": L1_ffn_b1, "w2": L1_ffn_W2,
            "b2": np.ascontiguousarray(L1_ffn_b2[:, None]),
            "wout": L1_ffn_Wout,
            "bout": np.ascontiguousarray(L1_ffn_bout.reshape(NC, 128).T),
        })
    r = _run(nc, maps)
    out = np.zeros((2, NSEQ, D), np.float32)
    for core in range(8):
        b, o = core_bo(core)
        out[b, o:o + NTOK] = r[core]["yt"].T
    return out


HEADS = 16


def build_fused(n=NTOK, dump=False):
    nc = bass.Bass("TRN2", target_bir_lowering=False, debug=False)
    nsub = n // 512
    ntile = n // 128
    KOFF = max(NC * n, NFF * 512)
    XT = nc.dram_tensor("xt", [D, n], F32, kind="ExternalInput").ap()
    XRT = nc.dram_tensor("xrevt", [D, n], F32, kind="ExternalInput").ap()
    RWC = nc.dram_tensor("rwc", [128, NC], F32, kind="ExternalInput").ap()
    RB = nc.dram_tensor("rb", [1, 1], F32, kind="ExternalInput").ap()
    IDENT = nc.dram_tensor("ident", [128, 128], F32, kind="ExternalInput").ap()
    IDB = nc.dram_tensor("identb", [128, 128], BF16, kind="ExternalInput").ap()
    WUPT = nc.dram_tensor("wupt", [D, DFF], F32, kind="ExternalInput").ap()
    WDNT = nc.dram_tensor("wdnt", [DFF, D], F32, kind="ExternalInput").ap()
    WQKV = nc.dram_tensor("wqkv", [D, 3 * D], F32, kind="ExternalInput").ap()
    BQ = nc.dram_tensor("bq", [128, 8], F32, kind="ExternalInput").ap()
    BK = nc.dram_tensor("bk", [128, 8], F32, kind="ExternalInput").ap()
    BV = nc.dram_tensor("bvrow", [1, D], F32, kind="ExternalInput").ap()
    PJ = nc.dram_tensor("projt", [64, HEADS * 256], F32,
                        kind="ExternalInput").ap()
    WO = nc.dram_tensor("wo", [D, D], F32, kind="ExternalInput").ap()
    BO = nc.dram_tensor("bo", [128, NC], F32, kind="ExternalInput").ap()
    WIN = nc.dram_tensor("win", [D, 64], F32, kind="ExternalInput").ap()
    BIN = nc.dram_tensor("bin", [64, 1], F32, kind="ExternalInput").ap()
    W1 = nc.dram_tensor("w1", [64, 16], F32, kind="ExternalInput").ap()
    B1 = nc.dram_tensor("b1", [64, 16], F32, kind="ExternalInput").ap()
    W2 = nc.dram_tensor("w2", [64, 16], F32, kind="ExternalInput").ap()
    B2_ = nc.dram_tensor("b2", [64, 1], F32, kind="ExternalInput").ap()
    WOUT = nc.dram_tensor("wout", [64, D], F32, kind="ExternalInput").ap()
    BOUT = nc.dram_tensor("bout", [128, NC], F32, kind="ExternalInput").ap()
    YT = nc.dram_tensor("yt", [D, n], F32, kind="ExternalOutput").ap()
    if dump:
        XD1 = nc.dram_tensor("xd1", [D, n], F32, kind="ExternalOutput").ap()
        XD2 = nc.dram_tensor("xd2", [D, n], F32, kind="ExternalOutput").ap()
        XD3 = nc.dram_tensor("xd3", [D, n], F32, kind="ExternalOutput").ap()
        WTUD = nc.dram_tensor("wtud", [D, DFF], BF16,
                              kind="ExternalOutput").ap()
        WTDD = nc.dram_tensor("wtdd", [DFF, D], BF16,
                              kind="ExternalOutput").ap()
    cc1_in = nc.dram_tensor("cc1_in", [1, n], F32, kind="Internal").ap()
    cc1_out = nc.dram_tensor("cc1_out", [4, n], F32, kind="Internal").ap()
    kv_in = nc.dram_tensor("kv_in", [128, 2080], F32, kind="Internal").ap()
    kv_out = nc.dram_tensor("kv_out", [128, 2080], F32, kind="Internal").ap()
    WTU = nc.dram_tensor("wtu_sc", [D, DFF], BF16, kind="Internal").ap()
    WTD = nc.dram_tensor("wtd_sc", [DFF, D], BF16, kind="Internal").ap()

    P = Prog(nc)
    es = P.es
    SA = sb(nc, es, "SA", [128, NC * n])
    SB = sb(nc, es, "SB", [128, max(NC * n, NFF * 512)])
    SBb = SB[:].bitcast(BF16)
    SC = sb(nc, es, "SC", [128, 8192])
    SCb = SC[:].bitcast(BF16)
    SD = sb(nc, es, "SD", [128, 2080])
    SE = sb(nc, es, "SE", [128, 2048])
    SEb = SE[:].bitcast(BF16)
    AW = sb(nc, es, "AW", [128, 2048])
    AWb = AW[:].bitcast(BF16)
    s_p56 = sb(nc, es, "s_p56", [128, 1024])
    s_rwc = sb(nc, es, "s_rwc", [128, NC])
    s_rb = sb(nc, es, "s_rb", [1, 1])
    s_rwf = sb(nc, es, "s_rwf", [128, 4 * n // 128])
    s_idb = sb(nc, es, "s_idb", [128, 128], BF16)
    s_onec = sb(nc, es, "s_onec", [128, 1])
    s_onecb = sb(nc, es, "s_onecb", [128, 1], BF16)
    s_ones2 = sb(nc, es, "s_ones2", [128, 128])
    s_oner = s_ones2[0:1, :]
    s_eps = sb(nc, es, "s_eps", [1, 1])
    s_eps8 = sb(nc, es, "s_eps8", [1, 1])
    s_epsphi = sb(nc, es, "s_epsphi", [1, 1])
    s_tmp = sb(nc, es, "s_tmp", [128, 512])
    s_tmpb = s_tmp[:].bitcast(BF16)
    s_m = sb(nc, es, "s_m", [1, 512])
    s_r = sb(nc, es, "s_r", [1, 512])
    s_hrow = sb(nc, es, "s_hrow", [1, 512])
    s_row = s_hrow
    s_trow = s_hrow
    s_acc = sb(nc, es, "s_acc", [128, 32])
    s_red = sb(nc, es, "s_red", [128, 2])
    s_th = sb(nc, es, "s_th", [1, 2])
    s_thc = sb(nc, es, "s_thc", [128, 2])
    st = {kk: sb(nc, es, kk, [128, 1]) for kk in
          ("t_lo", "t_hi", "t_mid", "rowcnt")}
    st["fixed_bounds"] = True
    st["fs"] = sb(nc, es, "fs", [128, 1], I32)
    st["fsn"] = sb(nc, es, "fsn", [128, 1], I32)
    st["mask64"] = sb(nc, es, "mask64", [128, 4 * n // 128], BF16)
    st["maskrow"] = None
    s_bq = sb(nc, es, "s_bq", [128, 8])
    s_bk = sb(nc, es, "s_bk", [128, 8])
    s_pj = sb(nc, es, "s_pj", [128, HEADS * 256], BF16)
    s_phik2 = [sb(nc, es, f"s_phik{i}", [128, 256], BF16) for i in range(2)]
    s_nmx2 = [sb(nc, es, f"s_nmx{i}", [128, 1]) for i in range(2)]
    s_phiqT = [SEb[:, 0:512], SEb[:, 512:1024]]
    s_kvs = sb(nc, es, "s_kvs", [128, 132], BF16)
    s_bo = sb(nc, es, "s_bo", [128, NC])
    s_bin = sb(nc, es, "s_bin", [64, 1])
    s_w1 = sb(nc, es, "s_w1", [64, 16])
    s_b1 = sb(nc, es, "s_b1", [64, 16])
    s_w2 = sb(nc, es, "s_w2", [64, 16])
    s_b2 = sb(nc, es, "s_b2", [64, 1])
    s_bout = sb(nc, es, "s_bout", [128, NC])
    s_bv = s_p56[0:1, 0:D]
    s_exp2 = [s_p56[:, 512:768], s_p56[:, 768:1024]]
    s_mrow = s_r[0:1, 0:512]
    s_den = s_p56[0:1, 0:512]
    s_dbc = s_p56[:, 0:512]
    s_s = SC[0:64, 0:512]
    s_hi = SD[0:64, 0:512]
    s_g = SC[0:64, 512:1024]
    s_gb = SCb[0:64, 3584:4096]
    s_winb = SCb[:, 2048:2560]
    s_woutb = SCb[0:64, 2560:3584]
    B = [ps(nc, es, f"B{i}", [128, 512]) for i in range(7)]
    B7 = ps(nc, es, "B7", [128, 1024], BF16)

    war = {}

    P.sd(lambda s: s.dma_start(
        out=SA[:].rearrange("p (a n) -> p a n", a=NC), in_=fmview(XT, n)),
        ch="x")
    P.sd(lambda s: s.dma_start(
        out=SB[:, 0:NC * n].rearrange("p (a n) -> p a n", a=NC),
        in_=fmview(XRT, n)), ch="x")
    lxall = P.total("syd", "x")
    for ap_, tile in ((RWC, s_rwc), (RB, s_rb),
                      (IDB, s_idb), (BQ, s_bq), (BK, s_bk),
                      (BO, s_bo), (BIN, s_bin), (W1, s_w1), (B1, s_b1),
                      (W2, s_w2), (B2_, s_b2), (BOUT, s_bout)):
        P.sd(lambda s, _a=ap_, _t=tile: s.dma_start(out=_t[:], in_=_a))
    lall = P.total("syd", "ld")
    m2 = P.v(lambda v: v.memset(s_onec[:], 1.0))
    m2b = P.v(lambda v: v.memset(s_onecb[:], 1.0))
    m3 = P.v(lambda v: v.memset(s_ones2[:], 1.0))
    m1 = m3
    m4 = P.v(lambda v: v.memset(s_eps[:], LN_EPS))
    m4b = P.v(lambda v: v.memset(s_eps8[:], RMS_EPS))
    m4c = P.v(lambda v: v.memset(s_epsphi[:], PHI_EPS))
    pjt = None
    for half in range(2):
        P.sd(lambda s, _h=half: s.dma_start(
            out=SD[0:64, 0:2048], in_=PJ[:, _h * 2048:(_h + 1) * 2048]),
            [pjt], ch="pj")
        P.sd(lambda s, _h=half: s.dma_start(
            out=SD[64:128, 0:2048], in_=PJ[:, _h * 2048:(_h + 1) * 2048]),
            ch="pj")
        pjt = P.v(lambda v, _h=half: v.tensor_scalar(
            s_pj[:, _h * 2048:(_h + 1) * 2048], SD[:, 0:2048], DATA_NORM,
            None, op0=ALU.mult), [P.total("syd", "pj")])
    pj = pjt
    lbv = P.sd(lambda s: s.dma_start(out=s_bv, in_=BV), [pj], ch="bv")
    lbv = P.total("syd", "bv")

    s_rwo = SD[0:1, 0:n]
    rwevs = []
    for i in range(nsub):
        pb = B[i % 2]
        mmtok = None
        for j in range(NC):
            mmtok = P.t(lambda t, _i=i, _j=j, _pb=pb: t.matmul(
                _pb[0:1, 0:512], lhsT=s_rwc[:, _j:_j + 1],
                rhs=SA[:, _j * n + _i * 512:_j * n + _i * 512 + 512],
                start=(_j == 0), stop=(_j == NC - 1)),
                [lxall, lall, war.get(("rwp", i % 2))])
        ev = P.v(lambda v, _i=i, _pb=pb: v.tensor_scalar(
            s_rwo[:, _i * 512:_i * 512 + 512], _pb[0:1, 0:512], s_rb[:],
            None, op0=ALU.add), [mmtok, pj])
        war[("rwp", i % 2)] = ev
        rwevs.append(ev)
    P.gd(lambda g: g.dma_start(out=cc1_in, in_=s_rwo[:]), rwevs, ch="cc")
    cc1 = P.col(lambda g: g.collective_compute(
        "AllGather", ALU.bypass,
        replica_groups=[[0, 1, 2, 3], [4, 5, 6, 7]],
        ins=[cc1_in.opt()], outs=[cc1_out.opt()]),
        [P.total("gpd", "cc")])
    P.gd(lambda g: g.dma_start(
        out=s_rwf[:], in_=cc1_out.rearrange("r (p f) -> (r p) f", p=32)),
        [cc1], ch="g1")
    lrwf = P.total("gpd", "g1")

    stat_toks = []
    bufs = [AW[:, 0:2048], SC[:, 0:2048]]
    acc_i = 0
    for wi, (WT, chw, a_) in enumerate(((WUPT, 256, NC), (WDNT, 64, NFF))):
        for ci in range(16):
            buf = bufs[ci % 2]
            ld = P.gd(lambda g, _W=WT, _ci=ci, _chw=chw, _a=a_, _buf=buf:
                      g.dma_start(
                out=_buf.rearrange("p (a m) -> p a m", a=_a),
                in_=_W.rearrange("(a p) m -> p a m", p=128)[
                    :, :, _ci * _chw:(_ci + 1) * _chw]),
                [war.get(("wst", ci % 2))], ch=f"wst{ci % 2}")
            rt = P.v(lambda v, _buf=buf, _i=acc_i: v.tensor_reduce(
                s_acc[:, _i:_i + 1], _buf, axis=AX.X, op=ALU.add,
                apply_absolute_value=True), [ld])
            war[("wst", ci % 2)] = rt
            stat_toks.append(rt)
            acc_i += 1
    r1 = P.v(lambda v: v.tensor_reduce(
        s_red[:, 0:1], s_acc[:, 0:16], axis=AX.X, op=ALU.add), stat_toks)
    r2 = P.v(lambda v: v.tensor_reduce(
        s_red[:, 1:2], s_acc[:, 16:32], axis=AX.X, op=ALU.add), [r1])
    apm = P.t(lambda t: t.matmul(B[2][0:1, 0:2], lhsT=s_onec[:],
                                 rhs=s_red[:, 0:2], start=True, stop=True),
              [r2, m2])
    a3 = P.v(lambda v: v.tensor_scalar(
        s_th[:, 0:2], B[2][0:1, 0:2], 0.5 / (D * DFF), None, op0=ALU.mult),
        [apm])
    bb = P.t(lambda t: t.matmul(B[2][:, 2:4], lhsT=s_oner[:],
                                rhs=s_th[:, 0:2], start=True, stop=True),
             [a3, m1])
    thc = P.v(lambda v: v.tensor_copy(s_thc[:], B[2][:, 2:4]), [bb])

    bis_last = emit_bisect(P, s_rwf, None, s_ones2, s_oner, B[3], B[4], st,
                           lrwf, None, lall, m1, m3, niter=50)

    T2 = [SC[:, 2048:4096], SC[:, 4096:6144]]
    prevab = None
    for wi, (WT, OUTT, chw, a_) in enumerate(
            ((WUPT, WTU, 256, NC), (WDNT, WTD, 64, NFF))):
        for ci in range(16):
            par = ci % 2
            buf = bufs[par]
            tb = T2[par]
            ld = P.gd(lambda g, _W=WT, _ci=ci, _chw=chw, _a=a_, _buf=buf:
                      g.dma_start(
                out=_buf.rearrange("p (a m) -> p a m", a=_a),
                in_=_W.rearrange("(a p) m -> p a m", p=128)[
                    :, :, _ci * _chw:(_ci + 1) * _chw]),
                [war.get(("wtz", par)), stat_toks[-1]],
                ch=f"wtz{par}")
            ab = P.s(lambda s, _buf=buf, _tb=tb: s.activation(
                _tb, _buf, AF.Abs), [ld, war.get(("abw", par))])
            mk = P.v(lambda v, _wi=wi, _tb=tb: v.tensor_scalar(
                _tb, _tb, s_thc[:, _wi:_wi + 1], None,
                op0=ALU.is_gt), [thc, ab])
            wm = P.v(lambda v, _buf=buf, _tb=tb: v.tensor_tensor(
                _tb, _tb, _buf, ALU.mult), [mk])
            wt = P.s(lambda s, _tb=tb, _p=par: s.activation(
                SEb[:, _p * 2048:(_p + 1) * 2048], _tb, AF.Sign),
                [wm, war.get(("sebread", par))])
            prevab = wt
            war[("wtz", par)] = wm
            war[("abw", par)] = wm
            stok = P.gd(lambda g, _O=OUTT, _ci=ci, _chw=chw, _a=a_, _p=par:
                        g.dma_start(
                out=_O.rearrange("(a p) m -> p a m", p=128)[
                    :, :, _ci * _chw:(_ci + 1) * _chw],
                in_=SEb[:, _p * 2048:(_p + 1) * 2048].rearrange(
                    "p (a m) -> p a m", a=_a)),
                [wt], ch=f"wts{par}")
            war[("sebread", par)] = stok
    wts_done = P.v(lambda v: v.memset(s_red[:, 0:1], 0.0),
                   [P.total("gpd", "wts0"), P.total("gpd", "wts1")])
    if dump:
        P.sd(lambda s: s.dma_start(out=WTUD, in_=WTU), [wts_done], ch="dbg")
        P.sd(lambda s: s.dma_start(out=WTDD, in_=WTD), [wts_done], ch="dbg")

    ln_war = {"eps_dep": m4, "sq": prevab}
    prev_msk = None
    prev_mrow = None
    x1toks = {}
    for i in range(nsub):
        mr = P.v(lambda v, _i=i: v.tensor_scalar(
            s_mrow[:], s_rwo[:, _i * 512:_i * 512 + 512],
            st["t_lo"][0:1, 0:1], None, op0=ALU.is_ge),
            [bis_last, prev_mrow, pj,
             (x1toks[i - 1][-1] if i > 0 else None)])
        mb = P.t(lambda t: t.matmul(
            B[5][:, 0:512], lhsT=s_oner[:], rhs=s_mrow[:],
            start=True, stop=True), [mr, m1, prev_msk])
        prev_mrow = mb
        ytoks = []
        for j in range(NC):
            off = j * n + i * 512
            ym = P.v(lambda v, _o=off: v.tensor_tensor(
                SB[:, _o:_o + 512], SB[:, _o:_o + 512], B[5][:, 0:512],
                ALU.mult), [mb, lxall])
            ya = P.v(lambda v, _o=off: v.tensor_tensor(
                SB[:, _o:_o + 512], SB[:, _o:_o + 512], SA[:, _o:_o + 512],
                ALU.add), [ym])
            ytoks.append(ya)
        prev_msk = ytoks[-1]
        x1toks[i] = emit_ln(P, n, i, 512, SB, SA, s_m, s_r, s_tmp, s_trow,
                            s_onec, s_oner, B[6][0:1, :], B[2][0:1, :],
                            B[3], B[4], ytoks, ln_war, s_eps=s_eps, s_onecb=s_onecb)

    p3done = x1toks[nsub - 1][-1]
    snap1 = None
    if dump:
        snap1 = P.sd(lambda s: s.dma_start(
            out=XD1.rearrange("(a p) m -> p a m", p=128),
            in_=SA[:].rearrange("p (a m) -> p a m", a=NC)),
            [t for i_ in range(nsub) for t in x1toks[i_]], ch="dbg")
    fw = {"eps_dep": m4, "sq": ln_war.get("sq"), "trow": ln_war.get("trow"),
          "rowread": ln_war.get("trow"), "rms": p3done, "h2": p3done,
          "bc1": p3done, "bc2": p3done}
    for _c in range(NC):
        fw[("zread", _c)] = snap1 if dump else prev_mrow
    x2toks = {}
    for i in range(nsub):
        sqprev = fw.get("sq")
        s1tok = None
        for j in range(NC):
            off = j * n + i * 512
            sq = P.s(lambda s, _o=off: s.activation(
                s_tmp[:], SA[:, _o:_o + 512], AF.Square),
                [x1toks[i][j], sqprev])
            s1tok = P.t(lambda t, _j=j: t.matmul(
                B[4][0:1, 0:512], lhsT=s_onec[:], rhs=s_tmp[:],
                start=(_j == 0), stop=(_j == NC - 1)),
                [sq, m2, fw.get("rms")])
            sqprev = s1tok
        fw["sq"] = sqprev
        r1_ = P.v(lambda v: v.tensor_scalar(
            s_row[:], B[4][0:1, 0:512], 1.0 / D, None, op0=ALU.mult),
            [s1tok, fw.get("rowread"), fw.get("trow")])
        fw["rms"] = r1_
        r2_ = P.s(lambda s: s.activation(s_row[:], s_row[:], AF.Sqrt), [r1_])
        r3_ = P.s(lambda s: s.activation(s_row[:], s_row[:], AF.Identity,
                                         bias=s_eps8[:]), [r2_, m4b])
        r4a = P.s(lambda s: s.activation(s_row[:], s_row[:], AF.Ln), [r3_])
        r4_ = P.s(lambda s: s.activation(s_row[:], s_row[:], AF.Exp,
                                         scale=-1.0), [r4a])
        b1_ = P.t(lambda t: t.matmul(
            B[6][:, 0:512], lhsT=s_oner[:], rhs=s_row[:],
            start=True, stop=True), [r4_, m1, fw.get("bc1")])
        casts = []
        for j in range(NC):
            off = j * n + i * 512
            casts.append(P.v(lambda v, _o=off, _j=j: v.tensor_tensor(
                SEb[:, _j * 512:(_j + 1) * 512], SA[:, _o:_o + 512],
                B[6][:, 0:512], ALU.mult),
                [b1_, wts_done, fw.get("xb")]))
        fw["rowread"] = casts[-1]
        hsq_prev = fw.get("sq")
        h2tok = None
        geltoks = []
        for f in range(NFF):
            wbuf = f % 2
            wtok = P.sd(lambda s, _f=f, _b=wbuf: s.dma_start(
                out=AWb[:, _b * 1024:(_b + 1) * 1024].rearrange(
                    "p (a m) -> p a m", a=NC),
                in_=WTU.rearrange("(a p) m -> p a m", p=128)[
                    :, :, _f * 128:(_f + 1) * 128]),
                [wts_done, fw.get(("wu", wbuf)),
                 fw.get(("wdread", wbuf))], ch=f"wu{wbuf}")
            mmt = None
            for j in range(NC):
                mmt = P.t(lambda t, _f=f, _j=j, _b=wbuf: t.matmul(
                    B[_f % 2][:, 0:512],
                    lhsT=AWb[:, _b * 1024 + _j * 128:
                             _b * 1024 + (_j + 1) * 128],
                    rhs=SEb[:, _j * 512:(_j + 1) * 512],
                    start=(_j == 0), stop=(_j == NC - 1)),
                    [wtok, casts[j], fw.get(("up", f % 2))])
            fw[("wu", wbuf)] = mmt
            gel = P.s(lambda s, _f=f: s.activation(
                SB[:, _f * 512:(_f + 1) * 512], B[_f % 2][:, 0:512],
                AF.Gelu), [mmt, fw.get(("hread", f))])
            fw[("up", f % 2)] = gel
            geltoks.append(gel)
            hsq = P.s(lambda s, _f=f: s.activation(
                s_tmpb[:, 0:512], SB[:, _f * 512:(_f + 1) * 512],
                AF.Square), [gel, hsq_prev])
            h2tok = P.t(lambda t, _f=f: t.matmul(
                B[5][0:1, 0:512], lhsT=s_onecb[:], rhs=s_tmpb[:, 0:512],
                start=(_f == 0), stop=(_f == NFF - 1)),
                [hsq, fw.get("h2")])
            hsq_prev = h2tok
        fw["sq"] = hsq_prev
        hr1 = P.v(lambda v: v.tensor_scalar(
            s_hrow[:], B[5][0:1, 0:512], 1.0 / DFF, None, op0=ALU.mult),
            [h2tok, fw.get("hrowread"), fw.get("trow")])
        fw["h2"] = hr1
        hr2 = P.s(lambda s: s.activation(s_hrow[:], s_hrow[:], AF.Sqrt),
                  [hr1])
        hr3 = P.s(lambda s: s.activation(s_hrow[:], s_hrow[:], AF.Identity,
                                         bias=s_eps8[:]), [hr2, m4b])
        hr4a = P.s(lambda s: s.activation(s_hrow[:], s_hrow[:], AF.Ln),
                   [hr3])
        hr4 = P.s(lambda s: s.activation(s_hrow[:], s_hrow[:], AF.Exp,
                                         scale=-1.0), [hr4a])
        b2t = P.t(lambda t: t.matmul(
            B[6][:, 0:512], lhsT=s_oner[:], rhs=s_hrow[:],
            start=True, stop=True), [hr4, casts[-1], fw.get("bc2")])
        fw["bc1"] = b2t
        hcasts = []
        for f in range(NFF):
            hcasts.append(P.v(lambda v, _f=f: v.tensor_tensor(
                SCb[:, _f * 512:(_f + 1) * 512],
                SB[:, _f * 512:(_f + 1) * 512], B[6][:, 0:512], ALU.mult),
                [b2t, geltoks[f], wts_done, fw.get("hb")]))
        fw["hrowread"] = hcasts[-1]
        ztoks = []
        for cjj in range(NC):
            mmt = None
            for quar in range(4):
                qb = quar % 2
                wtok = P.sd(lambda s, _c=cjj, _q=quar, _qb=qb: s.dma_start(
                    out=AWb[:, _qb * 1024:(_qb + 1) * 1024].rearrange(
                        "p (a m) -> p a m", a=8),
                    in_=WTD.rearrange("(a p) m -> p a m", p=128)[
                        :, 8 * _q:8 * (_q + 1),
                        _c * 128:(_c + 1) * 128]),
                    [wts_done, fw.get(("wdread", qb)),
                     fw.get(("wu", 0)), fw.get(("wu", 1))], ch=f"wd{qb}")
                for f8 in range(8):
                    f = quar * 8 + f8
                    mmt = P.t(lambda t, _c=cjj, _f=f, _f8=f8, _qb=qb:
                              t.matmul(
                        B[2 + _c % 2][:, 0:512],
                        lhsT=AWb[:, _qb * 1024 + _f8 * 128:
                                 _qb * 1024 + (_f8 + 1) * 128],
                        rhs=SCb[:, _f * 512:(_f + 1) * 512],
                        start=(_f == 0), stop=(_f == NFF - 1)),
                        [wtok, hcasts[f], fw.get(("dn", cjj % 2))])
                fw[("wdread", qb)] = mmt
            zt = P.v(lambda v, _c=cjj, _i=i: v.tensor_tensor(
                SA[:, _c * n + _i * 512:_c * n + _i * 512 + 512],
                B[2 + _c % 2][:, 0:512],
                SA[:, _c * n + _i * 512:_c * n + _i * 512 + 512], ALU.add),
                [mmt, fw.get(("zread", cjj))])
            fw[("dn", cjj % 2)] = zt
            ztoks.append(zt)
        fw["hb"] = ztoks[-1]
        fw["xb"] = ztoks[-1]
        fw["sq"] = ztoks[-1]
        outt = emit_ln(P, n, i, 512, SA, SA, s_m, s_r, s_tmp, s_trow,
                       s_onec, s_oner, B[4][0:1, :], B[5][0:1, :], B[6],
                       B[0], ztoks, fw, s_eps=s_eps, s_onecb=s_onecb)
        fw["rms"] = fw["s1"]
        fw["h2"] = fw["s2"]
        fw["bc2"] = fw["bc"]
        fw[("up", 0)] = fw["bc"]
        for cjj in range(NC):
            fw[("zread", cjj)] = outt[-1]
        x2toks[i] = outt
    p4done = x2toks[nsub - 1][-1]
    snap2 = None
    if dump:
        snap2 = P.sd(lambda s: s.dma_start(
            out=XD2.rearrange("(a p) m -> p a m", p=128),
            in_=SA[:].rearrange("p (a m) -> p a m", a=NC)),
            [t for i_ in range(nsub) for t in x2toks[i_]], ch="dbg")
        p4done = snap2

    for wi, (coff, dst, sbias) in enumerate(((0, 0, s_bq), (D, KOFF, s_bk))):
        for half in range(2):
            wln = P.gd(lambda g, _c=coff, _h=half: g.dma_start(
                out=AWb[:, 0:NC * 512].rearrange("p (a m) -> p a m", a=NC),
                in_=WQKV.rearrange("(a p) m -> p a m", p=128)[
                    :, :, _c + _h * 512:_c + (_h + 1) * 512]),
                [war.get("awfree"), p4done], ch="wqk")
            for i in range(nsub):
                casts = []
                for j in range(NC):
                    off = j * n + i * 512
                    casts.append(P.v(lambda v, _o=off, _j=j: v.tensor_copy(
                        SEb[:, _j * 512:(_j + 1) * 512], SA[:, _o:_o + 512]),
                        [x2toks[i][j], p4done, war.get("x2b")]))
                for hq in range(4):
                    hp = half * 4 + hq
                    mmt = None
                    for j in range(NC):
                        mmt = P.t(lambda t, _j=j, _hq=hq: t.matmul(
                            B[_hq % 2][:, 0:512],
                            lhsT=AWb[:, _j * 512 + _hq * 128:
                                     _j * 512 + _hq * 128 + 128],
                            rhs=SEb[:, _j * 512:(_j + 1) * 512],
                            start=(_j == 0), stop=(_j == NC - 1)),
                            [wln, casts[j], war.get(("pqk", hq % 2))])
                    ev = P.s(lambda s, _i=i, _hp=hp, _hq=hq, _d=dst,
                             _sb=sbias: s.activation(
                        SBb[:, _d + _hp * n + _i * 512:
                            _d + _hp * n + _i * 512 + 512],
                        B[_hq % 2][:, 0:512], AF.Identity,
                        bias=_sb[:, _hp:_hp + 1]), [mmt, lall, p4done])
                    war[("pqk", hq % 2)] = ev
                    war[("qk", wi, hp)] = ev
                war["x2b"] = war[("pqk", 1)]
            war["awfree"] = war[("pqk", 1)]
    for half in range(2):
        wlv = P.gd(lambda g, _h=half: g.dma_start(
            out=AWb[:, 0:NC * 512].rearrange("p (a m) -> p a m", a=NC),
            in_=WQKV.rearrange("(a p) m -> p a m", p=128)[
                :, :, 2 * D + _h * 512:2 * D + (_h + 1) * 512]),
            [war["awfree"]], ch="wv")
        for i in range(nsub):
            casts = []
            for j in range(NC):
                off = j * n + i * 512
                casts.append(P.v(lambda v, _o=off, _j=j: v.tensor_copy(
                    SEb[:, _j * 512:(_j + 1) * 512], SA[:, _o:_o + 512]),
                    [x2toks[i][j], war.get("x2b")]))
            for r in range(4):
                tt = i * 4 + r
                mmt = None
                for j in range(NC):
                    mmt = P.t(lambda t, _j=j, _r=r: t.matmul(
                        B[2][:, 0:512],
                        lhsT=SEb[:, _j * 512 + _r * 128:
                                 _j * 512 + _r * 128 + 128],
                        rhs=AWb[:, _j * 512:(_j + 1) * 512],
                        start=(_j == 0), stop=False),
                        [wlv, casts[j], war.get("pv")])
                mmt = P.t(lambda t, _h=half: t.matmul(
                    B[2][:, 0:512], lhsT=s_oner[:],
                    rhs=s_bv[0:1, _h * 512:(_h + 1) * 512],
                    start=False, stop=True), [mmt, m1, lbv])
                vx = P.v(lambda v, _tt=tt, _h=half: v.tensor_copy(
                    SCb[:, _tt * 1024 + _h * 512:
                        _tt * 1024 + (_h + 1) * 512],
                    B[2][:, 0:512]), [mmt, war.get("vxw")])
                war["pv"] = vx
                war["vxw"] = vx
                war[("vx", tt, half)] = vx
            war["x2b"] = war["pv"]
        war["awfree"] = war["pv"]
    for h in range(HEADS):
        hp, hb = h // 2, (h % 2) * 64
        vhalf, hv = h // 8, h % 8
        kv0 = kv1 = None
        for tt in range(ntile):
            rr = tt * 128
            xp = P.t(lambda t, _hp=hp, _hb=hb, _rr=rr, _tt=tt, _h=h:
                     t.matmul(
                B[4 + _tt % 2][:, 0:256],
                lhsT=SBb[_hb:_hb + 64,
                         KOFF + _hp * n + _rr:KOFF + _hp * n + _rr + 128],
                rhs=s_pj[_hb:_hb + 64, _h * 256:(_h + 1) * 256],
                start=True, stop=True),
                [war[("qk", 1, hp)], pj, war.get(("xpk", tt % 2))])
            par = tt % 2
            rm = P.v(lambda v, _tt=tt, _p=par: v.reduce_max(
                s_nmx2[_p][:], B[4 + _tt % 2][:, 0:256], axis=AX.X,
                negate=True), [xp, war.get(("nmxread", par))])
            ex = P.s(lambda s, _tt=tt, _p=par: s.activation(
                s_exp2[_p][:], B[4 + _tt % 2][:, 0:256], AF.Exp,
                bias=s_nmx2[_p][:]), [rm, war.get(("exread", par))])
            war[("xpk", tt % 2)] = ex
            war[("nmxread", par)] = ex
            ph = P.v(lambda v, _p=par: v.tensor_scalar(
                s_phik2[_p][:], s_exp2[_p][:], 1.0 / 16.0, PHI_EPS / 16.0,
                op0=ALU.mult, op1=ALU.add), [ex, war.get(("phikread", par))])
            war[("exread", par)] = ph
            vsl = SCb[:, tt * 1024 + vhalf * 512 + hv * 64:
                      tt * 1024 + vhalf * 512 + hv * 64 + 64]
            kv0 = P.t(lambda t, _tt=tt, _vsl=vsl, _p=par: t.matmul(
                B[6][:, 0:64], lhsT=s_phik2[_p][:, 0:128], rhs=_vsl,
                start=(_tt == 0), stop=False),
                [ph, war[("vx", tt, vhalf)], war.get("kv0ev")])
            kv0 = P.t(lambda t, _tt=tt, _p=par: t.matmul(
                B[6][:, 64:65], lhsT=s_phik2[_p][:, 0:128], rhs=s_onecb[:],
                start=False, stop=(_tt == ntile - 1)), [kv0, m2b])
            kv1 = P.t(lambda t, _tt=tt, _vsl=vsl, _p=par: t.matmul(
                B[3][:, 0:64], lhsT=s_phik2[_p][:, 128:256], rhs=_vsl,
                start=(_tt == 0), stop=False),
                [ph, war[("vx", tt, vhalf)], war.get("kv1ev")])
            kv1 = P.t(lambda t, _tt=tt, _p=par: t.matmul(
                B[3][:, 64:65], lhsT=s_phik2[_p][:, 128:256], rhs=s_onecb[:],
                start=False, stop=(_tt == ntile - 1)), [kv1, m2b])
            war[("phikread", par)] = kv1
        e1 = P.v(lambda v, _h=h: v.tensor_copy(
            SD[:, _h * 65:(_h + 1) * 65], B[6][:, 0:65]),
            [kv0, kv1, (p4done if h == 0 else None), war.get("sdfree")])
        e2 = P.v(lambda v, _h=h: v.tensor_copy(
            SD[:, 1040 + _h * 65:1040 + (_h + 1) * 65], B[3][:, 0:65]),
            [e1])
        war["kv0ev"] = e1
        war["kv1ev"] = e2
        war[("kvh", h)] = e2
    dkv = P.gd(lambda gg: gg.dma_start(out=kv_in, in_=SD[:, 0:2080]),
               [war[("kvh", hh)] for hh in range(HEADS)], ch="kvio")
    cc3 = P.col(lambda gg: gg.collective_compute(
        "AllReduce", ALU.add, replica_groups=[[0, 1, 2, 3], [4, 5, 6, 7]],
        ins=[kv_in.opt()], outs=[kv_out.opt()]), [P.total("gpd", "kvio")])
    P.gd(lambda gg: gg.dma_start(out=SD[:, 0:2080], in_=kv_out),
         [cc3], ch="kvb")
    lkvb = P.total("gpd", "kvb")

    pw = {"eps_dep": m4, "sq": fw.get("sq"), "trow": fw.get("trow")}
    war["pTread"] = war["x2b"]
    x3toks = {}
    for i in range(nsub):
        ndtoks = []
        for h in range(HEADS):
            hp, hb = h // 2, (h % 2) * 64
            kc = P.v(lambda v, _h=h: v.tensor_copy(
                s_kvs[:, 0:65], SD[:, _h * 65:(_h + 1) * 65]),
                [lkvb, war.get("kvsread")])
            kc2 = P.v(lambda v, _h=h: v.tensor_copy(
                s_kvs[:, 65:130],
                SD[:, 1040 + _h * 65:1040 + (_h + 1) * 65]), [kc])
            trtoks = []
            for r in range(4):
                n0 = i * 512 + r * 128
                xq = P.t(lambda t, _hp=hp, _hb=hb, _n0=n0, _r=r, _h=h:
                         t.matmul(
                    B[4 + _r % 2][:, 0:256],
                    lhsT=SBb[_hb:_hb + 64,
                             _hp * n + _n0:_hp * n + _n0 + 128],
                    rhs=s_pj[_hb:_hb + 64, _h * 256:(_h + 1) * 256],
                    start=True, stop=True),
                    [pj, war[("qk", 0, hp)], war.get(("xpk", r % 2))])
                par = r % 2
                rm = P.v(lambda v, _r=r, _p=par: v.reduce_max(
                    s_nmx2[_p][:], B[4 + _r % 2][:, 0:256], axis=AX.X,
                    negate=True), [xq, war.get(("nmxread", par))])
                ex = P.s(lambda s, _r=r, _p=par: s.activation(
                    s_exp2[_p][:], B[4 + _r % 2][:, 0:256], AF.Exp,
                    bias=s_nmx2[_p][:]), [rm, war.get(("exread", par))])
                war[("xpk", r % 2)] = ex
                war[("nmxread", par)] = ex
                phq = P.v(lambda v, _p=par: v.tensor_scalar(
                    s_phik2[_p][:], s_exp2[_p][:], 1.0 / 16.0,
                    PHI_EPS / 16.0, op0=ALU.mult, op1=ALU.add),
                    [ex, war.get(("phikread", par))])
                war[("exread", par)] = phq
                t1 = P.t(lambda t, _p=par: t.transpose(
                    B7[:, _p * 256:_p * 256 + 128], s_phik2[_p][:, 0:128],
                    s_idb[:]), [phq, lall, war.get(("trread", par))])
                t2 = P.t(lambda t, _p=par: t.transpose(
                    B7[:, _p * 256 + 128:_p * 256 + 256],
                    s_phik2[_p][:, 128:256], s_idb[:]), [t1])
                c1t = P.v(lambda v, _r=r, _p=par: v.tensor_copy(
                    s_phiqT[0][:, _r * 128:(_r + 1) * 128],
                    B7[:, _p * 256:_p * 256 + 128]), [t2, war.get("pTread")])
                c2t = P.v(lambda v, _r=r, _p=par: v.tensor_copy(
                    s_phiqT[1][:, _r * 128:(_r + 1) * 128],
                    B7[:, _p * 256 + 128:_p * 256 + 256]), [c1t])
                war[("phikread", par)] = c2t
                war[("trread", par)] = c2t
                trtoks.append(c2t)
            nm = P.t(lambda t: t.matmul(
                B[0][0:65, 0:512], lhsT=s_kvs[:, 0:65],
                rhs=s_phiqT[0][:], start=True, stop=False),
                trtoks + [kc2, war.get("numread")])
            nm = P.t(lambda t: t.matmul(
                B[0][0:65, 0:512], lhsT=s_kvs[:, 65:130],
                rhs=s_phiqT[1][:], start=False, stop=True), [nm])
            war["pTread"] = nm
            war["kvsread"] = nm
            d1 = P.s(lambda s: s.activation(
                s_den[:], B[0][64:65, 0:512], AF.Ln, bias=s_epsphi[:]),
                [nm, m4c, war.get("denread")])
            d2 = P.s(lambda s: s.activation(
                s_den[:], s_den[:], AF.Exp, scale=-1.0), [d1])
            db = P.t(lambda t: t.matmul(
                B[1][:, 0:512], lhsT=s_oner[:], rhs=s_den[:],
                start=True, stop=True), [d2, m1, war.get("dbcread")])
            dc = P.v(lambda v: v.tensor_copy(s_dbc[:], B[1][:, 0:512]),
                     [db])
            war["denread"] = dc
            nd = P.v(lambda v, _hp=hp, _hb=hb, _i=i: v.tensor_tensor(
                SCb[_hb:_hb + 64,
                    _hp * n + _i * 512:_hp * n + _i * 512 + 512],
                B[0][0:64, 0:512], s_dbc[_hb:_hb + 64, :], ALU.mult),
                [dc, nm, war.get("ndw")])
            war["numread"] = nd
            war["dbcread"] = nd
            ndtoks.append(nd)
        ztoks = []
        for whalf in range(2):
            wlo = P.gd(lambda g, _h=whalf: g.dma_start(
                out=AWb[:, 0:NC * 512].rearrange("p (a m) -> p a m", a=NC),
                in_=WO.rearrange("(a p) m -> p a m", p=128)[
                    :, :, _h * 512:(_h + 1) * 512]),
                [war.get("awfree"), pw.get("wofree")], ch="wo")
            for cj4 in range(4):
                cj = whalf * 4 + cj4
                mmt = None
                for j in range(NC):
                    mmt = P.t(lambda t, _c4=cj4, _j=j, _i=i: t.matmul(
                        B[2 + _c4 % 2][:, 0:512],
                        lhsT=AWb[:, _j * 512 + _c4 * 128:
                                 _j * 512 + (_c4 + 1) * 128],
                        rhs=SCb[:, _j * n + _i * 512:
                                _j * n + _i * 512 + 512],
                        start=(_j == 0), stop=(_j == NC - 1)),
                        ndtoks + [wlo, pw.get(("o", cj4 % 2))])
                pw["wofree"] = mmt
                ev = P.s(lambda s, _c=cj, _c4=cj4: s.activation(
                    s_tmp[:], B[2 + _c4 % 2][:, 0:512], AF.Identity,
                    bias=s_bo[:, _c:_c + 1]), [mmt, lall, pw.get("tread")])
                zt = P.v(lambda v, _c=cj, _i=i: v.tensor_tensor(
                    SA[:, _c * n + _i * 512:_c * n + _i * 512 + 512],
                    s_tmp[:],
                    SA[:, _c * n + _i * 512:_c * n + _i * 512 + 512],
                    ALU.add), [ev, pw.get(("zr", cj)), snap2])
                pw[("o", cj4 % 2)] = zt
                pw["tread"] = zt
                ztoks.append(zt)
        war["awfree"] = pw["wofree"]
        pw["sq"] = ztoks[-1]
        outt = emit_ln(P, n, i, 512, SA, SA, s_m, s_r, s_tmp, s_trow,
                       s_onec, s_oner, B[4][0:1, :], B[5][0:1, :], B[6],
                       B[1], ztoks, pw, s_eps=s_eps, s_onecb=s_onecb)
        for cj in range(NC):
            pw[("zr", cj)] = outt[-1]
        war[("xpk", 0)] = outt[-1]
        war[("xpk", 1)] = outt[-1]
        if i + 1 < nsub:
            war["ndw"] = outt[-1]
        x3toks[i] = outt

    x3done = x3toks[nsub - 1][-1]
    snap3 = None
    if dump:
        snap3 = P.sd(lambda s: s.dma_start(
            out=XD3.rearrange("(a p) m -> p a m", p=128),
            in_=SA[:].rearrange("p (a m) -> p a m", a=NC)),
            [t for i_ in range(nsub) for t in x3toks[i_]], ch="dbg")
        x3done = snap3
    P.gd(lambda g: g.dma_start(
        out=s_winb[:].rearrange("p (a m) -> p a m", a=NC),
        in_=WIN.rearrange("(a p) m -> p a m", p=128)), [x3done], ch="wsm")
    P.gd(lambda g: g.dma_start(out=s_woutb[:], in_=WOUT), [x3done],
         ch="wsm")
    lwsm = P.total("gpd", "wsm")
    dw = {"eps_dep": m4, "sq": pw.get("sq"), "trow": pw.get("trow")}
    stoks = []
    for i in range(nsub):
        casts = []
        for j in range(NC):
            off = j * n + i * 512
            casts.append(P.v(lambda v, _o=off, _j=j: v.tensor_copy(
                SEb[:, _j * 512:(_j + 1) * 512], SA[:, _o:_o + 512]),
                [x3toks[i][j], dw.get("x3b")]))
        mmt = None
        for j in range(NC):
            mmt = P.t(lambda t, _j=j: t.matmul(
                B[6][0:64, 0:512], lhsT=s_winb[:, _j * 64:(_j + 1) * 64],
                rhs=SEb[:, _j * 512:(_j + 1) * 512],
                start=(_j == 0), stop=(_j == NC - 1)),
                [lwsm, casts[j], dw.get("psread")])
        sv = P.s(lambda s: s.activation(
            s_s[:], B[6][0:64, 0:512], AF.Identity, bias=s_bin[:]),
            [mmt, lall, dw.get("sread")])
        dw["psread"] = sv
        gt = None
        for ii in range(16):
            hi = P.s(lambda s, _ii=ii: s.activation(
                s_hi[:], s_s[:], AF.Relu,
                scale=s_w1[:, _ii:_ii + 1], bias=s_b1[:, _ii:_ii + 1]),
                [sv, lall, dw.get("hiread"), dw.get("hi_war"), lkvb])
            if ii == 0:
                gt = P.v(lambda v: v.tensor_scalar(
                    s_g[:], s_hi[:], s_w2[:, 0:1], None, op0=ALU.mult),
                    [hi, dw.get("gread")])
            else:
                gt = P.v(lambda v, _ii=ii: v.scalar_tensor_tensor(
                    s_g[:], s_hi[:], s_w2[:, _ii:_ii + 1], s_g[:],
                    op0=ALU.mult, op1=ALU.add), [hi, gt])
            dw["hiread"] = gt
        dw["sread"] = gt
        g2 = P.v(lambda v: v.tensor_scalar(
            s_g[:], s_g[:], s_b2[:], None, op0=ALU.add), [gt])
        gb = P.v(lambda v: v.tensor_copy(s_gb[:], s_g[:]),
                 [g2, dw.get("gbread")])
        dw["gread"] = gb
        ztoks = []
        for cj in range(NC):
            mmt = P.t(lambda t, _c=cj: t.matmul(
                B[2 + _c % 2][:, 0:512],
                lhsT=s_woutb[:, _c * 128:(_c + 1) * 128],
                rhs=s_gb[:], start=True, stop=True),
                [gb, lwsm, dw.get(("o", cj % 2))])
            ev = P.s(lambda s, _c=cj: s.activation(
                s_tmp[:], B[2 + _c % 2][:, 0:512], AF.Identity,
                bias=s_bout[:, _c:_c + 1]), [mmt, dw.get("tread")])
            zt = P.v(lambda v, _c=cj, _i=i: v.tensor_tensor(
                SA[:, _c * n + _i * 512:_c * n + _i * 512 + 512], s_tmp[:],
                SA[:, _c * n + _i * 512:_c * n + _i * 512 + 512], ALU.add),
                [ev, dw.get(("zr", cj)), gb, snap3])
            dw[("o", cj % 2)] = zt
            dw["tread"] = zt
            ztoks.append(zt)
        dw["gbread"] = ztoks[-1]
        dw["x3b"] = ztoks[-1]
        dw["sq"] = ztoks[-1]
        outt = emit_ln(P, n, i, 512, SA, SA, s_m, s_r, s_tmp, s_trow,
                       s_onec, s_oner, B[4][0:1, :], B[5][0:1, :], B[0],
                       B[1], ztoks, dw, s_eps=s_eps, s_onecb=s_onecb)
        stok = P.sd(lambda s, _i=i: s.dma_start(
            out=fmview(YT, n)[:, :, _i * 512:_i * 512 + 512],
            in_=SA[:].rearrange("p (a m) -> p a m", a=NC)[
                :, :, _i * 512:_i * 512 + 512]), outt, ch="st")
        for cj in range(NC):
            dw[("zr", cj)] = stok
        dw["hi_war"] = stok
        stoks.append(stok)
    P.finish(stoks)
    P.build()
    return nc


_FUSED_NC = None


def kernel(L0_attn_x, L0_attn_router_w, L0_attn_router_b, L0_attn_wq,
           L0_attn_wk, L0_attn_wv, L0_ffn_w_up, L0_ffn_w_down, L1_attn_w_qkv,
           L1_attn_b_qkv, L1_attn_w_out, L1_attn_b_out, L1_attn_proj_matrix,
           L1_ffn_Win, L1_ffn_bin_, L1_ffn_W1, L1_ffn_b1, L1_ffn_W2,
           L1_ffn_b2, L1_ffn_Wout, L1_ffn_bout):
    _ = (L0_attn_wq.shape, L0_attn_wk.shape, L0_attn_wv.shape)
    global _FUSED_NC
    if _FUSED_NC is None:
        _FUSED_NC = build_fused(NTOK)
    nc = _FUSED_NC
    x = np.ascontiguousarray(L0_attn_x, dtype=np.float32)
    identb = np.eye(128, dtype=np.float32).astype(ml_dtypes.bfloat16)
    bqkv = L1_attn_b_qkv
    shared = {
        "rwc": np.ascontiguousarray(L0_attn_router_w.reshape(NC, 128).T),
        "rb": np.array([[L0_attn_router_b[0]]], dtype=np.float32),
        "identb": identb,
        "ident": np.eye(128, dtype=np.float32),
        "wupt": np.ascontiguousarray(L0_ffn_w_up.T),
        "wdnt": np.ascontiguousarray(L0_ffn_w_down.T),
        "wqkv": np.ascontiguousarray(L1_attn_w_qkv),
        "bq": np.ascontiguousarray(bqkv[0:D].reshape(NC, 128).T),
        "bk": np.ascontiguousarray(bqkv[D:2 * D].reshape(NC, 128).T),
        "bvrow": np.ascontiguousarray(bqkv[2 * D:3 * D][None, :]),
        "projt": np.ascontiguousarray(np.concatenate(
            [L1_attn_proj_matrix[h].T for h in range(HEADS)], axis=1)),
        "wo": np.ascontiguousarray(L1_attn_w_out),
        "bo": np.ascontiguousarray(L1_attn_b_out.reshape(NC, 128).T),
        "win": np.ascontiguousarray(L1_ffn_Win),
        "bin": np.ascontiguousarray(L1_ffn_bin_[:, None]),
        "w1": np.ascontiguousarray(L1_ffn_W1),
        "b1": np.ascontiguousarray(L1_ffn_b1),
        "w2": np.ascontiguousarray(L1_ffn_W2),
        "b2": np.ascontiguousarray(L1_ffn_b2[:, None]),
        "wout": np.ascontiguousarray(L1_ffn_Wout),
        "bout": np.ascontiguousarray(L1_ffn_bout.reshape(NC, 128).T),
    }
    maps = []
    for core in range(8):
        b, o = core // 4, (core % 4) * NTOK
        m = dict(shared)
        m["xt"] = np.ascontiguousarray(x[b, o:o + NTOK].T)
        m["xrevt"] = np.ascontiguousarray(x[b][::-1][o:o + NTOK].T)
        maps.append(m)
    res = run_bass_kernel_spmd(nc, maps, core_ids=list(range(8)))
    out = np.zeros((2, NSEQ, D), np.float32)
    for core in range(8):
        b, o = core // 4, (core % 4) * NTOK
        out[b, o:o + NTOK] = res.results[core]["yt"].T
    return out


# version 12
# speedup vs baseline: 1.3918x; 1.2084x over previous
import numpy as np
import ml_dtypes
from contextlib import ExitStack

import concourse.bass as bass
import concourse.mybir as mybir
from concourse.bass_utils import run_bass_kernel_spmd

F32 = mybir.dt.float32
BF16 = mybir.dt.bfloat16
I32 = mybir.dt.int32
AF = mybir.ActivationFunctionType
ALU = mybir.AluOpType
AX = mybir.AxisListType

D = 1024
NC = 8
DFF = 4096
NFF = 32
NTOK = 2048
NSEQ = 8192
HEADS = 16
HDIM = 64
LN_EPS = 1e-5
RMS_EPS = 1e-8
PHI_EPS = 1e-6
DATA_NORM = float(np.float32(64.0 ** -0.25))
NITER_BISECT = 50
ROUTER_K = 1024.0
KOFF = 16384


class Prog:
    SECTIONS = {
        "tensor": ("pe", 1),
        "vector": ("ve", 1),
        "scalar": ("sc", 1),
        "gpsimd": ("gp", 1),
    }

    def __init__(self, nc):
        self.nc = nc
        self.es = ExitStack()
        self.items = {s: [] for s in
                      ("tensor", "vector", "scalar", "gpsimd", "sync")}
        self.counts = {}
        self.n_cc = 0
        self.final_deps = []

    def _add(self, section, sem, inc, fn, deps):
        n = self.counts.get(sem, 0) + inc
        self.counts[sem] = n
        self.items[section].append((fn, [d for d in deps if d is not None],
                                    sem, inc))
        return (sem, n)

    def t(self, fn, deps=()):
        return self._add("tensor", "pe", 1, fn, deps)

    def v(self, fn, deps=()):
        return self._add("vector", "ve", 1, fn, deps)

    def s(self, fn, deps=()):
        return self._add("scalar", "sc", 1, fn, deps)

    def g(self, fn, deps=()):
        return self._add("gpsimd", "gp", 1, fn, deps)

    def sd(self, fn, deps=(), ch="ld"):
        return self._add("sync", f"syd_{ch}", 16, fn, deps)

    def ad(self, fn, deps=(), ch="ld"):
        return self._add("scalar", f"scd_{ch}", 16, fn, deps)

    def gd(self, fn, deps=(), ch="ld"):
        return self._add("gpsimd", f"gpd_{ch}", 16, fn, deps)

    def total(self, queue, ch):
        name = f"{queue}_{ch}"
        return (name, self.counts.get(name, 0))

    def col(self, fn, deps=()):
        self.n_cc += 1
        sem = f"cc{self.n_cc}"
        return self._add("gpsimd", sem, 1, fn, deps)

    def finish(self, deps):
        self.final_deps = list(deps)

    def build(self):
        nc = self.nc
        sems = {}
        for name in self.counts:
            sems[name] = self.es.enter_context(nc.semaphore(name))
        block = self.es.enter_context(nc.Block())
        hw = {}

        def coalesce(deps):
            best = {}
            for dsem, dval in deps:
                if best.get(dsem, 0) < dval:
                    best[dsem] = dval
            return best.items()

        def emit(section, eng):
            for fn, deps, sem, inc in self.items[section]:
                for dsem, dval in coalesce(deps):
                    key = (section, dsem)
                    if hw.get(key, 0) < dval:
                        hw[key] = dval
                        eng.wait_ge(sems[dsem], dval)
                ins = fn(eng)
                ins.then_inc(sems[sem], inc)
            if section == "sync":
                for dsem, dval in coalesce(self.final_deps):
                    key = (section, dsem)
                    if hw.get(key, 0) < dval:
                        hw[key] = dval
                        eng.wait_ge(sems[dsem], dval)

        if self.items["tensor"]:
            block.tensor(lambda t: emit("tensor", t))
        if self.items["vector"]:
            block.vector(lambda v: emit("vector", v))
        if self.items["scalar"]:
            block.scalar(lambda s: emit("scalar", s))
        if self.items["gpsimd"]:
            block.gpsimd(lambda g: emit("gpsimd", g))
        block.sync(lambda s: emit("sync", s))
        return self


def sb(nc, es, name, shape, dt=F32):
    return es.enter_context(nc.sbuf_tensor(name, shape, dt))


def ps(nc, es, name, shape, dt=F32):
    return es.enter_context(nc.psum_tensor(name, shape, dt))


def fmview(ap, n):
    return ap.rearrange("(a p) n -> p a n", p=128)


def emit_bisect(P, s_rwf, s_ones2, s_oner, p_cnt, p_bc, st,
                dep_rwf, dep_ones2, k=ROUTER_K, niter=NITER_BISECT):
    t_lo, t_hi, t_mid = st["t_lo"], st["t_hi"], st["t_mid"]
    fs, fsn, rowcnt = st["fs"], st["fsn"], st["rowcnt"]
    mask64 = st["mask64"]

    c1 = P.v(lambda v: v.memset(t_hi[:], 1e4), [dep_rwf])
    c2 = P.v(lambda v: v.memset(t_lo[:], -1e4), [c1])
    last = P.v(lambda v: v.memset(t_mid[:], 0.0), [c2])
    prev_f1 = None
    for i in range(niter):
        slot = i % 2
        mk = P.v(lambda v: v.tensor_scalar(
            mask64[:], s_rwf[:], t_mid[:], 0.0, op0=ALU.is_ge, op1=ALU.add,
            accum_out=rowcnt[:]), [last, prev_f1])
        cm = P.t(lambda t, _s=slot: t.matmul(
            p_cnt[:, _s:_s + 1], lhsT=s_ones2[:], rhs=rowcnt[:],
            start=True, stop=True), [mk, dep_ones2, prev_f1])
        f1 = P.v(lambda v, _s=slot: v.tensor_scalar(
            fs[:], p_cnt[:, _s:_s + 1], k, None, op0=ALU.is_ge), [cm])
        f2 = P.v(lambda v: v.tensor_scalar(fsn[:], fs[:], -1, 1,
                                           op0=ALU.mult, op1=ALU.add), [f1])
        s1 = P.v(lambda v: v.copy_predicated(t_lo[:], fs[:], t_mid[:]), [f2])
        s2 = P.v(lambda v: v.copy_predicated(t_hi[:], fsn[:], t_mid[:]), [s1])
        s3 = P.v(lambda v: v.tensor_tensor(t_mid[:], t_lo[:], t_hi[:],
                                           ALU.add), [s2])
        last = P.v(lambda v: v.tensor_scalar(t_mid[:], t_mid[:], 0.5, None,
                                             op0=ALU.mult), [s3])
        prev_f1 = f1
    return last


def emit_ln(P, n, i, w, s_y, s_out, s_m, s_r, s_tmp, s_trow, s_onec, s_oner,
            p_s1, p_s2, p_mbc, p_rbc, ytoks, war, s_eps=None,
            extra_out_dep=None, s_onecb=None):
    s1tok = s2tok = None
    sqprev = war.get("sq")
    tmpb = s_tmp[:].bitcast(BF16) if s_onecb is not None else None
    for j in range(NC):
        off = j * n + i * 512
        s1tok = P.t(lambda t, _o=off, _w=w, _j=j: t.matmul(
            p_s1[:, 0:_w], lhsT=s_onec[:], rhs=s_y[:, _o:_o + _w],
            start=(_j == 0), stop=(_j == NC - 1)),
            [ytoks[j], war.get("s1")])
        sq = P.s(lambda s, _o=off, _w=w: s.activation(
            tmpb[:, 0:_w], s_y[:, _o:_o + _w], AF.Square),
            [ytoks[j], sqprev])
        s2tok = P.t(lambda t, _w=w, _j=j: t.matmul(
            p_s2[:, 0:_w], lhsT=s_onecb[:], rhs=tmpb[:, 0:_w],
            start=(_j == 0), stop=(_j == NC - 1)),
            [sq, war.get("s2")])
        sqprev = s2tok
    war["sq"] = sqprev
    o = 0
    f1 = P.v(lambda v, _w=w, _o=o: v.tensor_scalar(
        s_m[:, _o:_o + _w], p_s1[:, 0:_w], 1.0 / D, None, op0=ALU.mult),
        [s1tok])
    f2 = P.v(lambda v, _w=w, _o=o: v.tensor_scalar(
        s_r[:, _o:_o + _w], p_s2[:, 0:_w], 1.0 / D, None, op0=ALU.mult),
        [s2tok])
    war["s1"] = f1
    war["s2"] = f2
    f3 = P.v(lambda v, _w=w, _o=o: v.tensor_tensor(
        s_trow[:, 0:_w], s_m[:, _o:_o + _w], s_m[:, _o:_o + _w],
        ALU.mult), [f1, f2, war.get("trow")])
    f4 = P.v(lambda v, _w=w, _o=o: v.tensor_tensor(
        s_r[:, _o:_o + _w], s_r[:, _o:_o + _w], s_trow[:, 0:_w],
        ALU.subtract), [f3])
    war["trow"] = f4
    f5 = P.s(lambda s, _w=w, _o=o: s.activation(
        s_r[:, _o:_o + _w], s_r[:, _o:_o + _w], AF.Ln,
        bias=s_eps[:]), [f4, war.get("eps_dep")])
    war["eps_dep"] = None
    f6 = P.s(lambda s, _w=w, _o=o: s.activation(
        s_r[:, _o:_o + _w], s_r[:, _o:_o + _w], AF.Exp, scale=-0.5), [f5])
    b1 = P.t(lambda t, _w=w, _o=o: t.matmul(
        p_mbc[:, 0:_w], lhsT=s_oner[:], rhs=s_m[:, _o:_o + _w],
        start=True, stop=True), [f1, war.get("bc")])
    b2 = P.t(lambda t, _w=w, _o=o: t.matmul(
        p_rbc[:, 0:_w], lhsT=s_oner[:], rhs=s_r[:, _o:_o + _w],
        start=True, stop=True), [f6])
    outtoks = []
    for j in range(NC):
        off = j * n + i * 512
        n1 = P.v(lambda v, _o=off, _w=w: v.tensor_tensor(
            s_out[:, _o:_o + _w], s_y[:, _o:_o + _w], p_mbc[:, 0:_w],
            ALU.subtract), [b1, b2, ytoks[j], extra_out_dep])
        n2 = P.v(lambda v, _o=off, _w=w: v.tensor_tensor(
            s_out[:, _o:_o + _w], s_out[:, _o:_o + _w], p_rbc[:, 0:_w],
            ALU.mult), [n1])
        outtoks.append(n2)
    war["bc"] = outtoks[-1]
    return outtoks


def build_v2(n=NTOK, dump=False):
    nc = bass.Bass("TRN2", target_bir_lowering=False, debug=False)
    nsub = n // 512
    ntile = n // 128
    H = 1024

    XT = nc.dram_tensor("xt", [D, n], F32, kind="ExternalInput").ap()
    XRT = nc.dram_tensor("xrevt", [D, n], F32, kind="ExternalInput").ap()
    RWC = nc.dram_tensor("rwc", [128, NC], F32, kind="ExternalInput").ap()
    RB = nc.dram_tensor("rb", [1, 1], F32, kind="ExternalInput").ap()
    IDB = nc.dram_tensor("identb", [128, 128], BF16, kind="ExternalInput").ap()
    WUPS = nc.dram_tensor("wups", [4, 128, 1024], F32,
                          kind="ExternalInput").ap()
    WDNS = nc.dram_tensor("wdns", [128, 4096], F32,
                          kind="ExternalInput").ap()
    WQKVC = nc.dram_tensor("wqkvc", [6, 128, NC, 512], F32,
                           kind="ExternalInput").ap()
    BQ = nc.dram_tensor("bq", [128, 8], F32, kind="ExternalInput").ap()
    BK = nc.dram_tensor("bk", [128, 8], F32, kind="ExternalInput").ap()
    BV = nc.dram_tensor("bvrow", [1, D], F32, kind="ExternalInput").ap()
    PJ = nc.dram_tensor("projt", [64, HEADS * 256], F32,
                        kind="ExternalInput").ap()
    WOC = nc.dram_tensor("woc", [2, 128, NC, 512], F32,
                         kind="ExternalInput").ap()
    BO = nc.dram_tensor("bo", [128, NC], F32, kind="ExternalInput").ap()
    WINC = nc.dram_tensor("winc", [128, NC, 64], F32,
                          kind="ExternalInput").ap()
    BIN = nc.dram_tensor("bin", [64, 1], F32, kind="ExternalInput").ap()
    W1 = nc.dram_tensor("w1", [64, 16], F32, kind="ExternalInput").ap()
    B1 = nc.dram_tensor("b1", [64, 16], F32, kind="ExternalInput").ap()
    W2 = nc.dram_tensor("w2", [64, 16], F32, kind="ExternalInput").ap()
    B2_ = nc.dram_tensor("b2", [64, 1], F32, kind="ExternalInput").ap()
    WOUT = nc.dram_tensor("wout", [64, D], F32, kind="ExternalInput").ap()
    BOUT = nc.dram_tensor("bout", [128, NC], F32, kind="ExternalInput").ap()
    YT = nc.dram_tensor("yt", [D, n], F32, kind="ExternalOutput").ap()

    cc1_in = nc.dram_tensor("cc1_in", [1, n], F32, kind="Internal").ap()
    cc1_out = nc.dram_tensor("cc1_out", [4, n], F32, kind="Internal").ap()
    ST_in = nc.dram_tensor("st_in", [1, 2], F32, kind="Internal").ap()
    ST_out = nc.dram_tensor("st_out", [8, 2], F32, kind="Internal").ap()
    WUP_in = nc.dram_tensor("wup_in", [1, 4 * 128 * 1024], BF16,
                            kind="Internal").ap()
    WUP_out = nc.dram_tensor("wup_out", [8, 4 * 128 * 1024], BF16,
                             kind="Internal").ap()
    WDN_in = nc.dram_tensor("wdn_in", [1, 4 * 128 * 1024], BF16,
                            kind="Internal").ap()
    WDN_out = nc.dram_tensor("wdn_out", [8, 4 * 128 * 1024], BF16,
                             kind="Internal").ap()
    kv_in = nc.dram_tensor("kv_in", [128, 2080], F32, kind="Internal").ap()
    kv_out = nc.dram_tensor("kv_out", [128, 2080], F32, kind="Internal").ap()
    if dump:
        XD1 = nc.dram_tensor("xd1", [128, NC * n], F32,
                             kind="ExternalOutput").ap()
        XDH = nc.dram_tensor("xdh", [128, NC * n], BF16,
                             kind="ExternalOutput").ap()
        XD2 = nc.dram_tensor("xd2", [128, NC * n], F32,
                             kind="ExternalOutput").ap()
        XDW = nc.dram_tensor("xdw", [8, 4 * 128 * 1024], BF16,
                             kind="ExternalOutput").ap()
        XDW2 = nc.dram_tensor("xdw2", [8, 4 * 128 * 1024], BF16,
                              kind="ExternalOutput").ap()
        XD3 = nc.dram_tensor("xd3", [128, NC * n], F32,
                             kind="ExternalOutput").ap()
        XDKV = nc.dram_tensor("xdkv", [128, 2080], F32,
                              kind="ExternalOutput").ap()
        XDHH = nc.dram_tensor("xdhh", [128, NFF * 1024], BF16,
                              kind="ExternalOutput").ap()
        XDSB = nc.dram_tensor("xdsb", [128, 1024], F32,
                              kind="ExternalOutput").ap()
        XDAW = nc.dram_tensor("xdaw", [128, 1024], BF16,
                              kind="ExternalOutput").ap()
        XDH3 = nc.dram_tensor("xdh3", [128, NC * n], BF16,
                              kind="ExternalOutput").ap()

    P = Prog(nc)
    es = P.es
    SA = sb(nc, es, "SA", [128, NC * n])
    SB = sb(nc, es, "SB", [128, NC * n])
    SBb = SB[:].bitcast(BF16)
    SC = sb(nc, es, "SC", [128, 8192])
    SCb = SC[:].bitcast(BF16)
    SD = sb(nc, es, "SD", [128, 2080])
    SE = sb(nc, es, "SE", [128, 2048])
    SEb = SE[:].bitcast(BF16)
    AW = sb(nc, es, "AW", [128, 2048])
    AWb = AW[:].bitcast(BF16)
    PJT = sb(nc, es, "PJT", [128, HEADS * 256], BF16)
    s_pj = PJT
    s_tmp = sb(nc, es, "s_tmp", [128, 512])
    s_tmpb = s_tmp[:].bitcast(BF16)
    s_p56 = sb(nc, es, "s_p56", [128, 1024])
    s_rwc = sb(nc, es, "s_rwc", [128, NC])
    s_rb = sb(nc, es, "s_rb", [1, 1])
    s_rwf = sb(nc, es, "s_rwf", [128, 4 * n // 128])
    s_idb = sb(nc, es, "s_idb", [128, 128], BF16)
    s_onec = sb(nc, es, "s_onec", [128, 1])
    s_onecb = sb(nc, es, "s_onecb", [128, 1], BF16)
    s_ones2 = sb(nc, es, "s_ones2", [128, 128])
    s_oner = s_ones2[0:1, :]
    s_onerb = sb(nc, es, "s_onerb", [1, 128], BF16)
    s_eps = sb(nc, es, "s_eps", [1, 1])
    s_eps8 = sb(nc, es, "s_eps8", [1, 1])
    s_epsphi = sb(nc, es, "s_epsphi", [1, 1])
    s_m = sb(nc, es, "s_m", [1, 512])
    s_r = sb(nc, es, "s_r", [1, 512])
    s_trow = sb(nc, es, "s_trow", [1, 512])
    s_vv = SD[0:1, 0:512]
    s_c2 = SD[0:1, 512:1024]
    s_hr2 = SD[0:1, 1024:2048]
    s_mrow = s_m[0:1, 0:512]
    s_acc = sb(nc, es, "s_acc", [128, 8])
    s_red = sb(nc, es, "s_red", [128, 2])
    s_th = sb(nc, es, "s_th", [1, 2])
    s_thc = sb(nc, es, "s_thc", [128, 2])
    s_aps = sb(nc, es, "s_aps", [1, 16])
    st = {kk: sb(nc, es, kk, [128, 1]) for kk in
          ("t_lo", "t_hi", "t_mid", "rowcnt")}
    st["fs"] = sb(nc, es, "fs", [128, 1], I32)
    st["fsn"] = sb(nc, es, "fsn", [128, 1], I32)
    st["mask64"] = sb(nc, es, "mask64", [128, 4 * n // 128], BF16)
    s_bq = sb(nc, es, "s_bq", [128, 8])
    s_bk = sb(nc, es, "s_bk", [128, 8])
    s_bvb = SD[0:1, 0:512].bitcast(BF16)
    s_denb = s_trow[:].bitcast(BF16)[0:1, 0:512]
    s_phik2 = [sb(nc, es, f"s_phik{i}", [128, 256], BF16) for i in range(2)]
    s_nmx2 = [sb(nc, es, f"s_nmx{i}", [128, 1]) for i in range(2)]
    s_phiqT = [SEb[:, 0:512], SEb[:, 512:1024]]
    s_kvs = sb(nc, es, "s_kvs", [128, 132], BF16)
    s_bo = sb(nc, es, "s_bo", [128, NC])
    s_bin = sb(nc, es, "s_bin", [64, 1])
    s_w1 = sb(nc, es, "s_w1", [64, 16])
    s_b1 = sb(nc, es, "s_b1", [64, 16])
    s_w2 = sb(nc, es, "s_w2", [64, 16])
    s_b2 = sb(nc, es, "s_b2", [64, 1])
    s_bout = sb(nc, es, "s_bout", [128, NC])
    s_bv = s_p56[0:1, 0:D]
    s_exp2 = [s_p56[:, 512:768], s_p56[:, 768:1024]]
    s_den = s_p56[0:1, 0:512]
    s_dbc = s_p56[:, 0:512]
    s_sbc = SE[:, 0:1024]
    s_s = SC[0:64, 0:512]
    s_hi = SD[0:64, 0:512]
    s_g = SC[0:64, 512:1024]
    s_winb = SCb[:, 8192:8704]
    s_woutb = SCb[0:64, 8704:9728]
    s_gb = SCb[0:64, 9728:10240]
    B = [ps(nc, es, f"B{i}", [128, 512]) for i in range(7)]
    B7 = ps(nc, es, "B7", [128, 1024], BF16)

    war = {}

    P.sd(lambda s: s.dma_start(
        out=SA[:].rearrange("p (a m) -> p a m", a=NC), in_=fmview(XT, n)),
        ch="x")
    P.sd(lambda s: s.dma_start(
        out=SB[:].rearrange("p (a m) -> p a m", a=NC), in_=fmview(XRT, n)),
        ch="x")
    lxall = P.total("syd", "x")
    for ap_, tile in ((RWC, s_rwc), (RB, s_rb), (IDB, s_idb), (BQ, s_bq),
                      (BK, s_bk), (BO, s_bo), (BIN, s_bin), (W1, s_w1),
                      (B1, s_b1), (W2, s_w2), (B2_, s_b2), (BOUT, s_bout)):
        P.sd(lambda s, _a=ap_, _t=tile: s.dma_start(out=_t[:], in_=_a))
    lall = P.total("syd", "ld")
    m2 = P.v(lambda v: v.memset(s_onec[:], 1.0))
    m2b = P.v(lambda v: v.memset(s_onecb[:], 1.0))
    m3 = P.v(lambda v: v.memset(s_ones2[:], 1.0))
    m1 = m3
    m3b = P.v(lambda v: v.memset(s_onerb[:], 1.0))
    m4 = P.v(lambda v: v.memset(s_eps[:], LN_EPS))
    m4b = P.v(lambda v: v.memset(s_eps8[:], RMS_EPS))
    m4c = P.v(lambda v: v.memset(s_epsphi[:], PHI_EPS))

    s_rwo = SD[0:1, 0:n]
    rwevs = []
    for i in range(nsub):
        pb = B[i % 2]
        mmtok = None
        for j in range(NC):
            mmtok = P.t(lambda t, _i=i, _j=j, _pb=pb: t.matmul(
                _pb[0:1, 0:512], lhsT=s_rwc[:, _j:_j + 1],
                rhs=SA[:, _j * n + _i * 512:_j * n + _i * 512 + 512],
                start=(_j == 0), stop=(_j == NC - 1)),
                [lxall, lall, war.get(("rwp", i % 2))])
        ev = P.v(lambda v, _i=i, _pb=pb: v.tensor_scalar(
            s_rwo[:, _i * 512:_i * 512 + 512], _pb[0:1, 0:512], s_rb[:],
            None, op0=ALU.add), [mmtok, lall])
        war[("rwp", i % 2)] = ev
        rwevs.append(ev)
    P.gd(lambda g: g.dma_start(out=cc1_in, in_=s_rwo[:]), rwevs, ch="cc")
    cc1 = P.col(lambda g: g.collective_compute(
        "AllGather", ALU.bypass,
        replica_groups=[[0, 1, 2, 3], [4, 5, 6, 7]],
        ins=[cc1_in.opt()], outs=[cc1_out.opt()]),
        [P.total("gpd", "cc")])
    P.gd(lambda g: g.dma_start(
        out=s_rwf[:], in_=cc1_out.rearrange("r (p f) -> (r p) f", p=32)),
        [cc1], ch="g1")
    lrwf = P.total("gpd", "g1")

    red_toks = []
    for pi in range(8):
        reg = pi % 2
        ld = P.ad(lambda s, _pi=pi, _reg=reg: s.dma_start(
            out=AW[:, _reg * 1024:(_reg + 1) * 1024],
            in_=(WUPS[_pi:_pi + 1].rearrange("o p m -> (o p) m")
                 if _pi < 4 else
                 WDNS[:, (_pi - 4) * 1024:(_pi - 3) * 1024])),
            [red_toks[pi - 2] if pi >= 2 else None], ch=f"ws{reg}")
        rt = P.v(lambda v, _pi=pi, _reg=reg: v.tensor_reduce(
            s_acc[:, _pi:_pi + 1], AW[:, _reg * 1024:(_reg + 1) * 1024],
            axis=AX.X, op=ALU.add, apply_absolute_value=True),
            [P.total("scd", f"ws{reg}")])
        red_toks.append(rt)
    r1 = P.v(lambda v: v.tensor_reduce(
        s_red[:, 0:1], s_acc[:, 0:4], axis=AX.X, op=ALU.add), red_toks)
    r2 = P.v(lambda v: v.tensor_reduce(
        s_red[:, 1:2], s_acc[:, 4:8], axis=AX.X, op=ALU.add),
        red_toks + [r1])
    apm = P.t(lambda t: t.matmul(B[2][0:1, 0:2], lhsT=s_onec[:],
                                 rhs=s_red[:, 0:2], start=True, stop=True),
              [r2, m2])
    stv = P.v(lambda v: v.tensor_copy(s_th[:, 0:2], B[2][0:1, 0:2]), [apm])
    P.gd(lambda g: g.dma_start(out=ST_in, in_=s_th[:, 0:2]), [stv], ch="cs")
    ccS = P.col(lambda g: g.collective_compute(
        "AllGather", ALU.bypass,
        replica_groups=[[0, 1, 2, 3, 4, 5, 6, 7]],
        ins=[ST_in.opt()], outs=[ST_out.opt()]),
        [P.total("gpd", "cs")])
    P.gd(lambda g: g.dma_start(
        out=s_aps[:], in_=ST_out.rearrange("r c -> () (r c)")), [ccS],
        ch="g2")
    laps = P.total("gpd", "g2")
    a1 = P.v(lambda v: v.tensor_reduce(
        s_th[:, 0:1], s_aps[0:1, 0:16:2], axis=AX.X, op=ALU.add),
        [laps, stv])
    a2 = P.v(lambda v: v.tensor_reduce(
        s_th[:, 1:2], s_aps[0:1, 1:16:2], axis=AX.X, op=ALU.add),
        [laps, a1])
    a3 = P.v(lambda v: v.tensor_scalar(
        s_th[:, 0:2], s_th[:, 0:2], 0.5 / (D * DFF), None, op0=ALU.mult),
        [a1, a2])
    bb = P.t(lambda t: t.matmul(B[2][:, 2:4], lhsT=s_oner[:],
                                rhs=s_th[:, 0:2], start=True, stop=True),
             [a3, m1, stv])
    thc = P.v(lambda v: v.tensor_copy(s_thc[:], B[2][:, 2:4]), [bb])

    bis_last = emit_bisect(P, s_rwf, s_ones2, s_oner, B[3], B[4], st,
                           lrwf, m1, niter=NITER_BISECT)

    scr = {0: SC[:, 0:1024], 1: SC[:, 1024:2048]}
    prev_sg = {}
    prev_ld = {}
    for pi in range(8):
        reg = pi % 2
        slot = pi % 2
        which = 0 if pi < 4 else 1
        ld = P.ad(lambda s, _pi=pi, _reg=reg: s.dma_start(
            out=AW[:, _reg * 1024:(_reg + 1) * 1024],
            in_=(WUPS[_pi:_pi + 1].rearrange("o p m -> (o p) m")
                 if _pi < 4 else
                 WDNS[:, (_pi - 4) * 1024:(_pi - 3) * 1024])),
            [red_toks[6 + reg], prev_ld.get(reg)], ch=f"wt{reg}")
        ldt = P.total("scd", f"wt{reg}")
        ab = P.s(lambda s, _reg=reg: s.activation(
            scr[_reg], AW[:, _reg * 1024:(_reg + 1) * 1024], AF.Abs),
            [ldt, prev_sg.get(("mk", reg))])
        mk = P.v(lambda v, _reg=reg, _w=which: v.tensor_scalar(
            scr[_reg], scr[_reg],
            s_thc[:, _w:_w + 1], None, op0=ALU.is_gt), [thc, ab])
        wm = P.v(lambda v, _reg=reg: v.tensor_tensor(
            scr[_reg], scr[_reg],
            AW[:, _reg * 1024:(_reg + 1) * 1024], ALU.mult), [mk])
        prev_ld[reg] = wm
        sg = P.s(lambda s, _slot=slot, _reg=reg: s.activation(
            PJT[:, _slot * 1024:(_slot + 1) * 1024],
            scr[_reg], AF.Sign),
            [wm, prev_sg.get(("st", slot))])
        prev_sg[("mk", reg)] = sg
        stok = P.ad(lambda s, _pi=pi, _slot=slot: s.dma_start(
            out=((WUP_in[0:1, _pi * 131072:(_pi + 1) * 131072]
                  if _pi < 4 else
                  WDN_in[0:1, (_pi - 4) * 131072:(_pi - 3) * 131072])
                 .rearrange("o (p m) -> (o p) m", p=128)),
            in_=PJT[:, _slot * 1024:(_slot + 1) * 1024]),
            [sg], ch=f"ts{slot}")
        prev_sg[("st", slot)] = stok
        if pi == 3:
            ccWu = P.col(lambda g: g.collective_compute(
                "AllGather", ALU.bypass,
                replica_groups=[[0, 1, 2, 3, 4, 5, 6, 7]],
                ins=[WUP_in.opt()], outs=[WUP_out.opt()]),
                [P.total("scd", "ts0"), P.total("scd", "ts1")])
    ccWd = P.col(lambda g: g.collective_compute(
        "AllGather", ALU.bypass,
        replica_groups=[[0, 1, 2, 3, 4, 5, 6, 7]],
        ins=[WDN_in.opt()], outs=[WDN_out.opt()]),
        [P.total("scd", "ts0"), P.total("scd", "ts1")])

    tern_done = prev_sg[("st", 0)], prev_sg[("st", 1)]
    pjevs = []
    pjld = None
    for half in range(2):
        l1 = P.sd(lambda s, _h=half: s.dma_start(
            out=SE[0:64, 0:2048], in_=PJ[:, _h * 2048:(_h + 1) * 2048]),
            [pjld], ch="pj")
        l2 = P.sd(lambda s, _h=half: s.dma_start(
            out=SE[64:128, 0:2048], in_=PJ[:, _h * 2048:(_h + 1) * 2048]),
            ch="pj")
        lt = P.total("syd", "pj")
        ev = P.v(lambda v, _h=half: v.tensor_scalar(
            PJT[:, _h * 2048:(_h + 1) * 2048], SE[:, 0:2048],
            DATA_NORM, None, op0=ALU.mult),
            [lt, tern_done[0], tern_done[1]])
        pjevs.append(ev)
        pjld = ev
    pj = pjevs[-1]
    P.sd(lambda s: s.dma_start(out=s_bv, in_=BV), ch="bv")
    lbv = P.total("syd", "bv")

    x1toks = {}
    xbtoks = {}
    prev_mb = None
    prev_b0 = P.total("gpd", "cc")
    ln1 = {"sq": prev_sg[("mk", 0)], "s1": None, "s2": None}
    rowrd = {}
    ytoks_all = {}
    for i in range(nsub):
        mr = P.v(lambda v, _i=i: v.tensor_scalar(
            s_mrow[:], s_rwo[:, _i * 512:_i * 512 + 512],
            st["t_lo"][0:1, 0:1], None, op0=ALU.is_ge),
            [bis_last, rwevs[i], prev_mb])
        mb = P.t(lambda t: t.matmul(
            B[5][:, 0:512], lhsT=s_oner[:], rhs=s_mrow[:],
            start=True, stop=True), [mr, m1, war.get("B5rd")])
        prev_mb = mb
        ytoks = []
        for j in range(NC):
            off = j * n + i * 512
            ym = P.v(lambda v, _o=off: v.tensor_tensor(
                SB[:, _o:_o + 512], SB[:, _o:_o + 512], B[5][:, 0:512],
                ALU.mult), [mb, lxall])
            ya = P.v(lambda v, _o=off: v.tensor_tensor(
                SB[:, _o:_o + 512], SB[:, _o:_o + 512], SA[:, _o:_o + 512],
                ALU.add), [ym])
            ytoks.append(ya)
        war["B5rd"] = ytoks[-1]
        ytoks_all[i] = ytoks
    for i in range(nsub):
        ytoks = ytoks_all[i]
        s1tok = None
        s2tok = None
        sqprev = ln1.get("sq")
        for j in range(NC):
            off = j * n + i * 512
            s1tok = P.t(lambda t, _o=off, _j=j: t.matmul(
                B[6][0:1, 0:512], lhsT=s_onec[:], rhs=SB[:, _o:_o + 512],
                start=(_j == 0), stop=(_j == NC - 1)),
                [ytoks[j], m2, ln1.get("s1")])
            sq = P.s(lambda s, _o=off, _j=j: s.activation(
                s_tmpb[:, (_j % 2) * 512:(_j % 2) * 512 + 512],
                SB[:, _o:_o + 512], AF.Square), [ytoks[j], sqprev])
            s2tok = P.t(lambda t, _j=j: t.matmul(
                B[4][0:1, 0:512], lhsT=s_onecb[:],
                rhs=s_tmpb[:, (_j % 2) * 512:(_j % 2) * 512 + 512],
                start=(_j == 0), stop=(_j == NC - 1)),
                [sq, m2b, ln1.get("s2")])
            sqprev = s2tok
        ln1["sq"] = sqprev
        f1 = P.v(lambda v: v.tensor_scalar(
            s_m[:], B[6][0:1, 0:512], 1.0 / D, None, op0=ALU.mult),
            [s1tok, rowrd.get("m")])
        f2 = P.v(lambda v: v.tensor_scalar(
            s_r[:], B[4][0:1, 0:512], 1.0 / D, None, op0=ALU.mult),
            [s2tok, rowrd.get("r")])
        ln1["s1"] = f1
        ln1["s2"] = f2
        f3 = P.v(lambda v: v.tensor_tensor(
            s_trow[:], s_m[:], s_m[:], ALU.mult), [f1, rowrd.get("trow")])
        f4 = P.v(lambda v: v.tensor_tensor(
            s_r[:], s_r[:], s_trow[:], ALU.subtract), [f3, f2])
        rowrd["trow"] = f4
        f4b = P.s(lambda s: s.activation(s_vv[:], s_r[:], AF.Sqrt),
                  [f4, rowrd.get("vv")])
        f5 = P.s(lambda s: s.activation(
            s_r[:], s_r[:], AF.Ln, bias=s_eps[:]),
            [f4b, (m4 if i == 0 else None)])
        f6 = P.s(lambda s: s.activation(
            s_r[:], s_r[:], AF.Exp, scale=-0.5), [f5])
        g1 = P.v(lambda v: v.tensor_tensor(
            s_vv[:], s_vv[:], s_r[:], ALU.mult), [f6, f4b])
        g2 = P.s(lambda s: s.activation(
            s_vv[:], s_vv[:], AF.Identity, bias=s_eps8[:]),
            [g1, (m4b if i == 0 else None)])
        g3 = P.v(lambda v: v.reciprocal(s_vv[:], s_vv[:]), [g2])
        g4 = P.v(lambda v: v.tensor_tensor(
            s_c2[:], s_vv[:], s_r[:], ALU.mult), [g3, rowrd.get("c2")])
        b_m = P.t(lambda t: t.matmul(
            B[0][:, 0:512], lhsT=s_oner[:], rhs=s_m[:],
            start=True, stop=True), [f1, m1, prev_b0, rowrd.get("B0")])
        b_r = P.t(lambda t: t.matmul(
            B[1][:, 0:512], lhsT=s_oner[:], rhs=s_r[:],
            start=True, stop=True), [f6, prev_b0, rowrd.get("B1")])
        b_c = P.t(lambda t: t.matmul(
            B[2][:, 0:512], lhsT=s_oner[:], rhs=s_c2[:],
            start=True, stop=True), [g4, thc, rowrd.get("B2")])
        rowrd["m"] = b_m
        rowrd["r"] = b_r
        rowrd["c2"] = b_c
        rowrd["vv"] = g4
        t2l = t3l = None
        for j in range(NC):
            off = j * n + i * 512
            t1 = P.v(lambda v, _o=off: v.tensor_tensor(
                SB[:, _o:_o + 512], SB[:, _o:_o + 512], B[0][:, 0:512],
                ALU.subtract), [b_m, ytoks[j]])
            t2 = P.v(lambda v, _o=off: v.tensor_tensor(
                SA[:, _o:_o + 512], SB[:, _o:_o + 512], B[1][:, 0:512],
                ALU.mult), [t1, b_r])
            t3 = P.v(lambda v, _o=off, _j=j, _i=i: v.tensor_tensor(
                SCb[:, _j * 2048 + _i * 512:_j * 2048 + _i * 512 + 512],
                SB[:, _o:_o + 512], B[2][:, 0:512], ALU.mult),
                [t1, b_c,
                 (prev_sg[("mk", j)] if i == 0 and j < 2 else None)])
            x1toks[(i, j)] = t2
            xbtoks[(i, j)] = t3
            t2l, t3l = t2, t3
        rowrd["B0"] = t3l
        rowrd["B1"] = t2l
        rowrd["B2"] = t3l
    p3done = t3l
    bvc = P.v(lambda v: v.tensor_copy(s_bvb[:], s_bv), [lbv, p3done])
    snap1 = None
    if dump:
        P.sd(lambda s: s.dma_start(out=XD1, in_=SA[:]), [p3done], ch="dbg")
        snap1 = P.sd(lambda s: s.dma_start(
            out=XDH, in_=SCb[:, 0:NC * n]), [p3done], ch="dbg")
        P.sd(lambda s: s.dma_start(out=XDW, in_=WUP_out), [ccWu], ch="dbg")
        snap1 = P.sd(lambda s: s.dma_start(out=XDW2, in_=WDN_out),
                     [ccWd, snap1], ch="dbg")
        snap1 = P.total("syd", "dbg")

    awfree = {0: None, 1: None, 2: None, 3: None}
    fw = {"B01": rowrd["B1"], "B23": rowrd["B2"],
          "B4": ln1["s2"], "B5": war["B5rd"], "B6": ln1["s1"],
          "sq": ln1["sq"], "tmprd": (snap1 or ln1["sq"])}
    awfree[0] = awfree[1] = prev_ld[0]
    awfree[2] = awfree[3] = prev_ld[1]
    hwar = p3done
    ln2war = {"eps_dep": None, "sq": ln1["sq"], "trow": rowrd["trow"],
              "s1": None, "s2": None, "bc": None}
    x2toks = {}
    dnfree = {}
    sbcrd = {}
    for hh in range(2):
        tok0 = hh * H
        geltoks = {}
        stat_emit = []
        statA_last = statB_last = None
        upA_war = {0: fw["B01"], 1: fw["B23"]}
        for f in range(32):
            u = f % 4
            c, k = f // 4, f % 4
            ld = P.sd(lambda s, _u=u, _c=c, _k=k: s.dma_start(
                out=AWb[:, _u * 1024:(_u + 1) * 1024],
                in_=WUP_out[_c:_c + 1,
                            _k * 131072:(_k + 1) * 131072].rearrange(
                    "o (p m) -> (o p) m", p=128)),
                [ccWu if (hh == 0 and f < 4) else None, awfree[u],
                 (ln2war["bc"] if hh == 1 and f < 4 else None),
                 (ln2war["sq"] if hh == 1 and f < 4 else None)],
                ch=f"wu{u}")
            ldt = P.total("syd", f"wu{u}")
            pA = B[0 + 2 * (f % 2)]
            pB = B[1 + 2 * (f % 2)]
            mmA = mmB = None
            for j in range(NC):
                mmA = P.t(lambda t, _u=u, _j=j, _pA=pA, _t0=tok0: t.matmul(
                    _pA[:, 0:512],
                    lhsT=AWb[:, _u * 1024 + _j * 128:_u * 1024 + (_j + 1) * 128],
                    rhs=SCb[:, _j * 2048 + _t0:_j * 2048 + _t0 + 512],
                    start=(_j == 0), stop=(_j == NC - 1)),
                    [ldt, xbtoks[(2 * hh, j)], upA_war[f % 2]])
                mmB = P.t(lambda t, _u=u, _j=j, _pB=pB, _t0=tok0: t.matmul(
                    _pB[:, 0:512],
                    lhsT=AWb[:, _u * 1024 + _j * 128:_u * 1024 + (_j + 1) * 128],
                    rhs=SCb[:, _j * 2048 + _t0 + 512:_j * 2048 + _t0 + 1024],
                    start=(_j == 0), stop=(_j == NC - 1)),
                    [xbtoks[(2 * hh + 1, j)], mmA])
            awfree[u] = mmB
            if dump and hh == 1 and f == 0:
                P.sd(lambda s: s.dma_start(
                    out=XDAW, in_=AWb[:, 0:1024]), [mmB], ch="dbg")
                awfree[u] = P.sd(lambda s: s.dma_start(
                    out=XDH3, in_=SCb[:, 0:NC * n]), [mmB], ch="dbg")
            if stat_emit:
                statA_last, statB_last = stat_emit.pop()()
            gA = P.s(lambda s, _f=f, _pA=pA: s.activation(
                SBb[:, _f * 1024:_f * 1024 + 512], _pA[:, 0:512], AF.Gelu),
                [mmA, hwar])
            gB = P.s(lambda s, _f=f, _pB=pB: s.activation(
                SBb[:, _f * 1024 + 512:(_f + 1) * 1024], _pB[:, 0:512],
                AF.Gelu), [mmB])
            geltoks[f] = (gA, gB)
            upA_war[f % 2] = gB
            sqA = P.s(lambda s, _f=f: s.activation(
                s_tmpb[:, 0:512],
                SBb[:, _f * 1024:_f * 1024 + 512], AF.Square),
                [gA, fw["sq"], statA_last])
            sqB = P.s(lambda s, _f=f: s.activation(
                s_tmpb[:, 512:1024],
                SBb[:, _f * 1024 + 512:(_f + 1) * 1024], AF.Square),
                [sqA, statB_last])
            fw["sq"] = sqB

            def emit_stats(_f=f, _sqA=sqA, _sqB=sqB,
                           _sAp=statA_last, _sBp=statB_last,
                           _b4=fw["B4"], _b5=fw["B5"]):
                sA = P.t(lambda t: t.matmul(
                    B[4][0:1, 0:512], lhsT=s_onecb[:],
                    rhs=s_tmpb[:, 0:512],
                    start=(_f == 0), stop=(_f == 31)),
                    [_sqA, m2b, _sAp, (_b4 if _f == 0 else None)])
                sB = P.t(lambda t: t.matmul(
                    B[5][0:1, 0:512], lhsT=s_onecb[:],
                    rhs=s_tmpb[:, 512:1024],
                    start=(_f == 0), stop=(_f == 31)),
                    [_sqB, _sBp, (_b5 if _f == 0 else None)])
                return sA, sB
            stat_emit.append(emit_stats)
        statA_last, statB_last = stat_emit.pop()()
        fw["tmprd"] = statB_last
        fw["B01"] = upA_war[0]
        fw["B23"] = upA_war[1]
        hrA = P.v(lambda v: v.tensor_scalar(
            s_hr2[0:1, 0:512], B[4][0:1, 0:512], 1.0 / DFF, None,
            op0=ALU.mult), [statA_last, sbcrd.get("hr2"),
                            (prev_mb if hh == 0 else None)])
        hrB = P.v(lambda v: v.tensor_scalar(
            s_hr2[0:1, 512:1024], B[5][0:1, 0:512], 1.0 / DFF, None,
            op0=ALU.mult), [statB_last, hrA])
        h1 = P.s(lambda s: s.activation(s_hr2[:], s_hr2[:], AF.Sqrt),
                 [hrA, hrB])
        h2 = P.s(lambda s: s.activation(
            s_hr2[:], s_hr2[:], AF.Identity, bias=s_eps8[:]), [h1])
        h3 = P.v(lambda v: v.reciprocal(s_hr2[:], s_hr2[:]), [h2])
        b_sA = P.t(lambda t: t.matmul(
            B[4][:, 0:512], lhsT=s_oner[:], rhs=s_hr2[0:1, 0:512],
            start=True, stop=True), [h3, m1, hrA])
        b_sB = P.t(lambda t: t.matmul(
            B[5][:, 0:512], lhsT=s_oner[:], rhs=s_hr2[0:1, 512:1024],
            start=True, stop=True), [h3, hrB])
        cpA = P.v(lambda v: v.tensor_copy(SE[:, 0:512], B[4][:, 0:512]),
                  [b_sA, sbcrd.get("sbc"), (pj if hh == 0 else None)])
        cpB = P.v(lambda v: v.tensor_copy(SE[:, 512:1024], B[5][:, 0:512]),
                  [b_sB, cpA])
        fw["B4"] = cpA
        fw["B5"] = cpB
        sbcrd["hr2"] = b_sB
        ztoksA = {}
        ztoksB = {}
        for cj in range(8):
            ldq = {}
            for q in range(2):
                r = q
                ldq[q] = P.sd(lambda s, _cj=cj, _q=q, _r=r: s.dma_start(
                    out=AWb[:, _r * 2048:(_r + 1) * 2048].rearrange(
                        "p (j m) -> p j m", j=2),
                    in_=WDN_out[_cj:_cj + 1,
                                (2 * _q) * 131072:
                                (2 * _q + 2) * 131072].rearrange(
                        "o (j p m) -> (o p) j m", j=2, p=128, m=1024)),
                    [ccWd if (hh == 0 and cj == 0) else None,
                     awfree[2 * r], awfree[2 * r + 1],
                     dnfree.get((hh, cj - 1, r))],
                    ch=f"wd{r}")
            pA = B[0 + 2 * (cj % 2)]
            pB = B[1 + 2 * (cj % 2)]
            mmA = mmB = None
            for f in range(32):
                r = f // 16
                lt = P.total("syd", f"wd{r}") if f % 16 == 0 else None
                mmA = P.t(lambda t, _f=f, _r=r, _pA=pA, _lt=lt: t.matmul(
                    _pA[:, 0:512],
                    lhsT=AWb[:, _r * 2048 + (_f % 16) * 128:
                             _r * 2048 + (_f % 16 + 1) * 128],
                    rhs=SBb[:, _f * 1024:_f * 1024 + 512],
                    start=(_f == 0), stop=(_f == 31)),
                    [lt, geltoks[f][0],
                     (fw.get("B01" if cj % 2 == 0 else "B23")
                      if f == 0 else None)])
                mmB = P.t(lambda t, _f=f, _pB=pB: t.matmul(
                    _pB[:, 0:512],
                    lhsT=AWb[:, (_f // 16) * 2048 + (_f % 16) * 128:
                             (_f // 16) * 2048 + (_f % 16 + 1) * 128],
                    rhs=SBb[:, _f * 1024 + 512:(_f + 1) * 1024],
                    start=(_f == 0), stop=(_f == 31)),
                    [geltoks[f][1], mmA])
                if f == 15:
                    dnfree[(hh, cj, 0)] = mmB
                if f == 31:
                    dnfree[(hh, cj, 1)] = mmB
            tA = P.v(lambda v, _pA=pA: v.tensor_tensor(
                s_tmp[:, 0:512], _pA[:, 0:512], SE[:, 0:512],
                ALU.mult), [mmB, cpA, fw.get("tmp"), fw["tmprd"]])
            zA = P.v(lambda v, _cj=cj, _t0=tok0: v.tensor_tensor(
                SA[:, _cj * n + _t0:_cj * n + _t0 + 512],
                s_tmp[:, 0:512],
                SA[:, _cj * n + _t0:_cj * n + _t0 + 512], ALU.add), [tA])
            tB = P.v(lambda v, _pB=pB: v.tensor_tensor(
                s_tmp[:, 0:512], _pB[:, 0:512], SE[:, 512:1024],
                ALU.mult), [mmB, cpB, zA])
            zB = P.v(lambda v, _cj=cj, _t0=tok0: v.tensor_tensor(
                SA[:, _cj * n + _t0 + 512:_cj * n + _t0 + 1024],
                s_tmp[:, 0:512],
                SA[:, _cj * n + _t0 + 512:_cj * n + _t0 + 1024], ALU.add),
                [tB])
            fw["tmp"] = zB
            if cj % 2 == 0:
                fw["B01"] = zB
            else:
                fw["B23"] = zB
            ztoksA[cj] = zA
            ztoksB[cj] = zB
        sbcrd["sbc"] = fw["tmp"]
        ln2war["sq"] = ztoksB[7]
        ln2war["bc"] = ztoksB[7]
        for ts in range(2):
            gi = hh * 2 + ts
            ytl = [ztoksA[cj] if ts == 0 else ztoksB[cj] for cj in range(8)]
            ln2war["eps_dep"] = None
            outt = emit_ln(P, n, gi, 512, SA, SA, s_m, s_r, s_tmp, s_trow,
                           s_onec, s_oner, B[6][0:1, :], B[4][0:1, :],
                           B[5], B[2], ytl, ln2war, s_eps=s_eps,
                           s_onecb=s_onecb)
            x2toks[gi] = outt
        fw["sq"] = ln2war["sq"]
        fw["B4"] = ln2war["s2"]
        fw["B5"] = ln2war["bc"]
        fw["B6"] = ln2war["s1"]
        fw["B23"] = ln2war["bc"]
        fw["tmprd"] = ln2war["sq"]
        fw["tmp"] = ln2war["sq"]
        sbcrd["hr2"] = ln2war["s2"]
        sbcrd["sbc"] = ln2war["bc"]
        hwar = None
        if hh == 0:
            hwar = ln2war["bc"]
            upA_war = {0: ln2war["bc"], 1: ln2war["bc"]}
            for u in range(4):
                awfree[u] = dnfree[(0, 7, u // 2)]
    p4done = x2toks[3][-1]
    snap2 = None
    if dump:
        snap2 = P.sd(lambda s: s.dma_start(out=XD2, in_=SA[:]), [p4done],
                     ch="dbg")
        P.sd(lambda s: s.dma_start(out=XDHH, in_=SBb[:, 0:NFF * 1024]),
             [p4done], ch="dbg")
        P.sd(lambda s: s.dma_start(out=XDSB, in_=SE[:, 0:1024]),
             [p4done], ch="dbg")
        p4done = P.total("syd", "dbg")

    awf = [dnfree[(1, 7, 0)], dnfree[(1, 7, 1)]]
    for wi, (widx0, dst, sbias) in enumerate(((0, 0, s_bq), (2, KOFF, s_bk))):
        for half in range(2):
            wln = P.gd(lambda g, _w=widx0 + half: g.dma_start(
                out=AWb[:, 0:NC * 512].rearrange("p (a m) -> p a m", a=NC),
                in_=WQKVC[_w:_w + 1].rearrange("o p a m -> (o p) a m")),
                [war.get("awfree2"), p4done] + awf, ch="wqk")
            awf = []
            wlnt = P.total("gpd", "wqk")
            for i in range(nsub):
                casts = []
                for j in range(NC):
                    off = j * n + i * 512
                    casts.append(P.v(lambda v, _o=off, _j=j: v.tensor_copy(
                        SEb[:, _j * 512:(_j + 1) * 512], SA[:, _o:_o + 512]),
                        [x2toks[i][j], war.get("x2b")]))
                for hq in range(4):
                    hp = half * 4 + hq
                    mmt = None
                    for j in range(NC):
                        mmt = P.t(lambda t, _j=j, _hq=hq: t.matmul(
                            B[_hq % 2][:, 0:512],
                            lhsT=AWb[:, _j * 512 + _hq * 128:
                                     _j * 512 + _hq * 128 + 128],
                            rhs=SEb[:, _j * 512:(_j + 1) * 512],
                            start=(_j == 0), stop=(_j == NC - 1)),
                            [wlnt, casts[j], war.get(("pqk", hq % 2)),
                             (fw["B01"] if wi == 0 and half == 0 and i == 0
                              and hq < 2 else None)])
                    ev = P.s(lambda s, _i=i, _hp=hp, _hq=hq, _d=dst,
                             _sb=sbias: s.activation(
                        SBb[:, _d + _hp * n + _i * 512:
                            _d + _hp * n + _i * 512 + 512],
                        B[_hq % 2][:, 0:512], AF.Identity,
                        bias=_sb[:, _hp:_hp + 1]), [mmt, lall, p4done])
                    war[("pqk", hq % 2)] = ev
                    war[("qk", wi, hp)] = ev
                war["x2b"] = war[("pqk", 1)]
            war["awfree2"] = war[("pqk", 1)]
    for half in range(2):
        wlv = P.gd(lambda g, _h=half: g.dma_start(
            out=AWb[:, 0:NC * 512].rearrange("p (a m) -> p a m", a=NC),
            in_=WQKVC[4 + _h:5 + _h].rearrange("o p a m -> (o p) a m")), [war["awfree2"]], ch="wv")
        wlvt = P.total("gpd", "wv")
        for i in range(nsub):
            casts = []
            for j in range(NC):
                off = j * n + i * 512
                casts.append(P.v(lambda v, _o=off, _j=j: v.tensor_copy(
                    SEb[:, _j * 512:(_j + 1) * 512], SA[:, _o:_o + 512]),
                    [x2toks[i][j], war.get("x2b")]))
            for r in range(4):
                tt = i * 4 + r
                mmt = None
                for j in range(NC):
                    mmt = P.t(lambda t, _j=j, _r=r: t.matmul(
                        B[2][:, 0:512],
                        lhsT=SEb[:, _j * 512 + _r * 128:
                                 _j * 512 + _r * 128 + 128],
                        rhs=AWb[:, _j * 512:(_j + 1) * 512],
                        start=(_j == 0), stop=False),
                        [wlvt, casts[j], war.get("pv"),
                         (fw["B23"] if half == 0 and i == 0 and r == 0
                          else None)])
                mmt = P.t(lambda t, _h=half: t.matmul(
                    B[2][:, 0:512], lhsT=s_onerb[:],
                    rhs=s_bvb[0:1, _h * 512:(_h + 1) * 512],
                    start=False, stop=True), [mmt, m3b, bvc])
                vx = P.v(lambda v, _tt=tt, _h=half: v.tensor_copy(
                    SCb[:, _tt * 1024 + _h * 512:
                        _tt * 1024 + (_h + 1) * 512],
                    B[2][:, 0:512]), [mmt, war.get("vxw"), p4done])
                war["pv"] = vx
                war["vxw"] = vx
                war[("vx", tt, half)] = vx
            war["x2b"] = war["pv"]
        war["awfree2"] = war["pv"]
    for h in range(HEADS):
        hp, hb = h // 2, (h % 2) * 64
        vhalf, hv = h // 8, h % 8
        kv0 = kv1 = None
        for tt in range(ntile):
            rr = tt * 128
            xp = P.t(lambda t, _hp=hp, _hb=hb, _rr=rr, _tt=tt, _h=h:
                     t.matmul(
                B[4 + _tt % 2][:, 0:256],
                lhsT=SBb[_hb:_hb + 64,
                         KOFF + _hp * n + _rr:KOFF + _hp * n + _rr + 128],
                rhs=s_pj[_hb:_hb + 64, _h * 256:(_h + 1) * 256],
                start=True, stop=True),
                [war[("qk", 1, hp)], pj, war.get(("xpk", tt % 2)),
                 (ln2war["bc"] if h == 0 and tt < 2 else None)])
            par = tt % 2
            rm = P.v(lambda v, _tt=tt, _p=par: v.reduce_max(
                s_nmx2[_p][:], B[4 + _tt % 2][:, 0:256], axis=AX.X,
                negate=True), [xp, war.get(("nmxread", par))])
            ex = P.s(lambda s, _tt=tt, _p=par: s.activation(
                s_exp2[_p][:], B[4 + _tt % 2][:, 0:256], AF.Exp,
                bias=s_nmx2[_p][:]), [rm, war.get(("exread", par))])
            war[("xpk", tt % 2)] = ex
            war[("nmxread", par)] = ex
            ph = P.v(lambda v, _p=par: v.tensor_scalar(
                s_phik2[_p][:], s_exp2[_p][:], 1.0 / 16.0, PHI_EPS / 16.0,
                op0=ALU.mult, op1=ALU.add), [ex, war.get(("phikread", par))])
            war[("exread", par)] = ph
            vsl = SCb[:, tt * 1024 + vhalf * 512 + hv * 64:
                      tt * 1024 + vhalf * 512 + hv * 64 + 64]
            kv0 = P.t(lambda t, _tt=tt, _vsl=vsl, _p=par: t.matmul(
                B[6][:, 0:64], lhsT=s_phik2[_p][:, 0:128], rhs=_vsl,
                start=(_tt == 0), stop=False),
                [ph, war[("vx", tt, vhalf)], war.get("kv0ev"),
                 (ln2war["s1"] if h == 0 and tt == 0 else None)])
            kv0 = P.t(lambda t, _tt=tt, _p=par: t.matmul(
                B[6][:, 64:65], lhsT=s_phik2[_p][:, 0:128], rhs=s_onecb[:],
                start=False, stop=(_tt == ntile - 1)), [kv0, m2b])
            kv1 = P.t(lambda t, _tt=tt, _vsl=vsl, _p=par: t.matmul(
                B[3][:, 0:64], lhsT=s_phik2[_p][:, 128:256], rhs=_vsl,
                start=(_tt == 0), stop=False),
                [ph, war[("vx", tt, vhalf)], war.get("kv1ev")])
            kv1 = P.t(lambda t, _tt=tt, _p=par: t.matmul(
                B[3][:, 64:65], lhsT=s_phik2[_p][:, 128:256], rhs=s_onecb[:],
                start=False, stop=(_tt == ntile - 1)), [kv1, m2b])
            war[("phikread", par)] = kv1
        e1 = P.v(lambda v, _h=h: v.tensor_copy(
            SD[:, _h * 65:(_h + 1) * 65], B[6][:, 0:65]),
            [kv0, kv1, (p4done if h == 0 else None), war.get("sdfree"),
             (prev_mb if h == 0 else None)])
        e2 = P.v(lambda v, _h=h: v.tensor_copy(
            SD[:, 1040 + _h * 65:1040 + (_h + 1) * 65], B[3][:, 0:65]),
            [e1])
        war["kv0ev"] = e1
        war["kv1ev"] = e2
        war[("kvh", h)] = e2
    dkv = P.gd(lambda gg: gg.dma_start(out=kv_in, in_=SD[:, 0:2080]),
               [war[("kvh", hh)] for hh in range(HEADS)], ch="kvio")
    cc3 = P.col(lambda gg: gg.collective_compute(
        "AllReduce", ALU.add, replica_groups=[[0, 1, 2, 3], [4, 5, 6, 7]],
        ins=[kv_in.opt()], outs=[kv_out.opt()]), [P.total("gpd", "kvio")])
    P.gd(lambda gg: gg.dma_start(out=SD[:, 0:2080], in_=kv_out),
         [cc3], ch="kvb")
    lkvb = P.total("gpd", "kvb")

    if dump:
        P.sd(lambda s: s.dma_start(out=XDKV, in_=kv_out), [cc3], ch="dbg")
    pw = {"eps_dep": None, "sq": ln2war["sq"], "trow": ln2war["trow"]}
    if dump:
        for cj in range(NC):
            pw[("zr", cj)] = snap2
    war["pTread"] = war["x2b"]
    x3toks = {}
    for i in range(nsub):
        ndtoks = []
        for h in range(HEADS):
            hp, hb = h // 2, (h % 2) * 64
            kc = P.v(lambda v, _h=h: v.tensor_copy(
                s_kvs[:, 0:65], SD[:, _h * 65:(_h + 1) * 65]),
                [lkvb, war.get("kvsread")])
            kc2 = P.v(lambda v, _h=h: v.tensor_copy(
                s_kvs[:, 65:130],
                SD[:, 1040 + _h * 65:1040 + (_h + 1) * 65]), [kc])
            trtoks = []
            for r in range(4):
                n0 = i * 512 + r * 128
                xq = P.t(lambda t, _hp=hp, _hb=hb, _n0=n0, _r=r, _h=h:
                         t.matmul(
                    B[4 + _r % 2][:, 0:256],
                    lhsT=SBb[_hb:_hb + 64,
                             _hp * n + _n0:_hp * n + _n0 + 128],
                    rhs=s_pj[_hb:_hb + 64, _h * 256:(_h + 1) * 256],
                    start=True, stop=True), [pj, war[("qk", 0, hp)],
                                             war.get(("xpk", r % 2))])
                par = r % 2
                rm = P.v(lambda v, _r=r, _p=par: v.reduce_max(
                    s_nmx2[_p][:], B[4 + _r % 2][:, 0:256], axis=AX.X,
                    negate=True), [xq, war.get(("nmxread", par))])
                ex = P.s(lambda s, _r=r, _p=par: s.activation(
                    s_exp2[_p][:], B[4 + _r % 2][:, 0:256], AF.Exp,
                    bias=s_nmx2[_p][:]), [rm, war.get(("exread", par))])
                war[("xpk", r % 2)] = ex
                war[("nmxread", par)] = ex
                phq = P.v(lambda v, _p=par: v.tensor_scalar(
                    s_phik2[_p][:], s_exp2[_p][:], 1.0 / 16.0,
                    PHI_EPS / 16.0, op0=ALU.mult, op1=ALU.add),
                    [ex, war.get(("phikread", par))])
                war[("exread", par)] = phq
                t1 = P.t(lambda t, _p=par: t.transpose(
                    B7[:, _p * 256:_p * 256 + 128], s_phik2[_p][:, 0:128],
                    s_idb[:]), [phq, lall, war.get(("trread", par))])
                t2 = P.t(lambda t, _p=par: t.transpose(
                    B7[:, _p * 256 + 128:_p * 256 + 256],
                    s_phik2[_p][:, 128:256], s_idb[:]), [t1])
                c1t = P.v(lambda v, _r=r, _p=par: v.tensor_copy(
                    s_phiqT[0][:, _r * 128:(_r + 1) * 128],
                    B7[:, _p * 256:_p * 256 + 128]), [t2, war.get("pTread")])
                c2t = P.v(lambda v, _r=r, _p=par: v.tensor_copy(
                    s_phiqT[1][:, _r * 128:(_r + 1) * 128],
                    B7[:, _p * 256 + 128:_p * 256 + 256]), [c1t])
                war[("phikread", par)] = c2t
                war[("trread", par)] = c2t
                trtoks.append(c2t)
            nm = P.t(lambda t: t.matmul(
                B[0][0:65, 0:512], lhsT=s_kvs[:, 0:65],
                rhs=s_phiqT[0][:], start=True, stop=False),
                trtoks + [kc2, war.get("numread")])
            nm = P.t(lambda t: t.matmul(
                B[0][0:65, 0:512],
                lhsT=s_kvs[:, 65:130],
                rhs=s_phiqT[1][:], start=False, stop=True), [nm])
            war["pTread"] = nm
            war["kvsread"] = nm
            d1 = P.s(lambda s: s.activation(
                s_den[:], B[0][64:65, 0:512], AF.Ln, bias=s_epsphi[:]),
                [nm, m4c, war.get("denread")])
            d2 = P.s(lambda s: s.activation(
                s_den[:], s_den[:], AF.Exp, scale=-1.0), [d1])
            dcb = P.v(lambda v: v.tensor_copy(s_denb[:], s_den[:]),
                      [d2, war.get("denbrd")])
            db = P.t(lambda t: t.matmul(
                B[1][:, 0:512], lhsT=s_onerb[:], rhs=s_denb[:],
                start=True, stop=True), [dcb, m3b, war.get("dbcread")])
            dc = P.v(lambda v: v.tensor_copy(s_dbc[:], B[1][:, 0:512]),
                     [db])
            war["denread"] = dc
            war["denbrd"] = db
            nd = P.v(lambda v, _hp=hp, _hb=hb, _i=i: v.tensor_tensor(
                SCb[_hb:_hb + 64,
                    _hp * n + _i * 512:_hp * n + _i * 512 + 512],
                B[0][0:64, 0:512], s_dbc[_hb:_hb + 64, :], ALU.mult),
                [dc, nm, war.get("ndw")])
            war["numread"] = nd
            war["dbcread"] = nd
            ndtoks.append(nd)
        ztoks = []
        for whalf in range(2):
            wlo = P.gd(lambda g, _h=whalf: g.dma_start(
                out=AWb[:, 0:NC * 512].rearrange("p (a m) -> p a m", a=NC),
                in_=WOC[_h:_h + 1].rearrange("o p a m -> (o p) a m")),
                [war.get("awfree2"), pw.get("wofree")], ch="wo")
            wlot = P.total("gpd", "wo")
            for cj4 in range(4):
                cj = whalf * 4 + cj4
                mmt = None
                for j in range(NC):
                    mmt = P.t(lambda t, _c4=cj4, _j=j, _i=i,
                              _lt=wlot: t.matmul(
                        B[2 + _c4 % 2][:, 0:512],
                        lhsT=AWb[:, _j * 512 + _c4 * 128:
                                 _j * 512 + (_c4 + 1) * 128],
                        rhs=SCb[:, _j * n + _i * 512:
                                _j * n + _i * 512 + 512],
                        start=(_j == 0), stop=(_j == NC - 1)),
                        ndtoks + [wlot, pw.get(("o", cj4 % 2))])
                pw["wofree"] = mmt
                ev = P.s(lambda s, _c=cj, _c4=cj4: s.activation(
                    s_tmp[:, 0:512], B[2 + _c4 % 2][:, 0:512], AF.Identity,
                    bias=s_bo[:, _c:_c + 1]), [mmt, lall, pw.get("tread")])
                zt = P.v(lambda v, _c=cj, _i=i: v.tensor_tensor(
                    SA[:, _c * n + _i * 512:_c * n + _i * 512 + 512],
                    s_tmp[:, 0:512],
                    SA[:, _c * n + _i * 512:_c * n + _i * 512 + 512],
                    ALU.add), [ev, pw.get(("zr", cj))])
                pw[("o", cj4 % 2)] = zt
                pw["tread"] = zt
                ztoks.append(zt)
        war["awfree2"] = pw["wofree"]
        pw["sq"] = ztoks[-1]
        pw["eps_dep"] = None
        outt = emit_ln(P, n, i, 512, SA, SA, s_m, s_r, s_tmp, s_trow,
                       s_onec, s_oner, B[4][0:1, :], B[5][0:1, :], B[6],
                       B[1], ztoks, pw, s_eps=s_eps, s_onecb=s_onecb)
        pw["tread"] = pw["sq"]
        for cj in range(NC):
            pw[("zr", cj)] = outt[-1]
        war[("xpk", 0)] = outt[-1]
        war[("xpk", 1)] = outt[-1]
        if i + 1 < nsub:
            war["ndw"] = outt[-1]
        x3toks[i] = outt

    x3done = x3toks[nsub - 1][-1]
    if dump:
        snap3 = P.sd(lambda s: s.dma_start(out=XD3, in_=SA[:]), [x3done],
                     ch="dbg")
        dwseed = {("zr", cj): snap3 for cj in range(NC)}
    else:
        dwseed = {}
    P.gd(lambda g: g.dma_start(
        out=s_winb[:].rearrange("p (a m) -> p a m", a=NC), in_=WINC),
        [lkvb], ch="wsm")
    P.gd(lambda g: g.dma_start(out=s_woutb[:], in_=WOUT), [lkvb],
         ch="wsm")
    lwsm = P.total("gpd", "wsm")
    dw = {"eps_dep": None, "sq": pw.get("sq"), "trow": pw.get("trow")}
    dw.update(dwseed)
    stoks = []
    for i in range(nsub):
        casts = []
        for j in range(NC):
            off = j * n + i * 512
            casts.append(P.v(lambda v, _o=off, _j=j: v.tensor_copy(
                SEb[:, _j * 512:(_j + 1) * 512], SA[:, _o:_o + 512]),
                [x3toks[i][j], dw.get("x3b"), war["pTread"]]))
        mmt = None
        for j in range(NC):
            mmt = P.t(lambda t, _j=j: t.matmul(
                B[6][0:64, 0:512], lhsT=s_winb[:, _j * 64:(_j + 1) * 64],
                rhs=SEb[:, _j * 512:(_j + 1) * 512],
                start=(_j == 0), stop=(_j == NC - 1)),
                [lwsm, casts[j], dw.get("psread"), war["numread"]])
        sv = P.s(lambda s: s.activation(
            s_s[:], B[6][0:64, 0:512], AF.Identity, bias=s_bin[:]),
            [mmt, lall, dw.get("sread"), war["numread"]])
        dw["psread"] = sv
        gt = None
        for ii in range(16):
            hi = P.s(lambda s, _ii=ii: s.activation(
                s_hi[:], s_s[:], AF.Relu,
                scale=s_w1[:, _ii:_ii + 1], bias=s_b1[:, _ii:_ii + 1]),
                [sv, lall, dw.get("hiread"), dw.get("hi_war"),
                 war["kvsread"]])
            if ii == 0:
                gt = P.v(lambda v: v.tensor_scalar(
                    s_g[:], s_hi[:], s_w2[:, 0:1], None, op0=ALU.mult),
                    [hi, dw.get("gread")])
            else:
                gt = P.v(lambda v, _ii=ii: v.scalar_tensor_tensor(
                    s_g[:], s_hi[:], s_w2[:, _ii:_ii + 1], s_g[:],
                    op0=ALU.mult, op1=ALU.add), [hi, gt])
            dw["hiread"] = gt
        dw["sread"] = gt
        g2 = P.v(lambda v: v.tensor_scalar(
            s_g[:], s_g[:], s_b2[:], None, op0=ALU.add), [gt])
        gb = P.v(lambda v: v.tensor_copy(s_gb[:], s_g[:]),
                 [g2, dw.get("gbread")])
        dw["gread"] = gb
        ztoks = []
        for cj in range(NC):
            mmt = P.t(lambda t, _c=cj: t.matmul(
                B[2 + _c % 2][:, 0:512],
                lhsT=s_woutb[:, _c * 128:(_c + 1) * 128],
                rhs=s_gb[:], start=True, stop=True),
                [gb, lwsm, dw.get(("o", cj % 2))])
            ev = P.s(lambda s, _c=cj: s.activation(
                s_tmp[:, 0:512], B[2 + _c % 2][:, 0:512], AF.Identity,
                bias=s_bout[:, _c:_c + 1]), [mmt, dw.get("tread")])
            zt = P.v(lambda v, _c=cj, _i=i: v.tensor_tensor(
                SA[:, _c * n + _i * 512:_c * n + _i * 512 + 512],
                s_tmp[:, 0:512],
                SA[:, _c * n + _i * 512:_c * n + _i * 512 + 512], ALU.add),
                [ev, dw.get(("zr", cj)), gb])
            dw[("o", cj % 2)] = zt
            dw["tread"] = zt
            ztoks.append(zt)
        dw["gbread"] = ztoks[-1]
        dw["x3b"] = ztoks[-1]
        dw["sq"] = ztoks[-1]
        outt = emit_ln(P, n, i, 512, SA, SA, s_m, s_r, s_tmp, s_trow,
                       s_onec, s_oner, B[4][0:1, :], B[5][0:1, :], B[0],
                       B[1], ztoks, dw, s_eps=s_eps, s_onecb=s_onecb)
        dw["tread"] = dw["sq"]
        stok = P.sd(lambda s, _i=i: s.dma_start(
            out=fmview(YT, n)[:, :, _i * 512:_i * 512 + 512],
            in_=SA[:].rearrange("p (a m) -> p a m", a=NC)[
                :, :, _i * 512:_i * 512 + 512]), outt, ch="st")
        for cj in range(NC):
            dw[("zr", cj)] = stok
        dw["hi_war"] = stok
        stoks.append(stok)
    if dump:
        stoks = stoks + [P.total("syd", "dbg")]
    P.finish(stoks)
    P.build()
    return nc


_V2_NC = None


def kernel(L0_attn_x, L0_attn_router_w, L0_attn_router_b, L0_attn_wq,
           L0_attn_wk, L0_attn_wv, L0_ffn_w_up, L0_ffn_w_down, L1_attn_w_qkv,
           L1_attn_b_qkv, L1_attn_w_out, L1_attn_b_out, L1_attn_proj_matrix,
           L1_ffn_Win, L1_ffn_bin_, L1_ffn_W1, L1_ffn_b1, L1_ffn_W2,
           L1_ffn_b2, L1_ffn_Wout, L1_ffn_bout):
    _ = (L0_attn_wq.shape, L0_attn_wk.shape, L0_attn_wv.shape)
    global _V2_NC
    if _V2_NC is None:
        _V2_NC = build_v2(NTOK)
    nc = _V2_NC
    x = np.ascontiguousarray(L0_attn_x, dtype=np.float32)
    identb = np.eye(128, dtype=np.float32).astype(ml_dtypes.bfloat16)
    bqkv = L1_attn_b_qkv
    wup = np.asarray(L0_ffn_w_up)
    wdn = np.asarray(L0_ffn_w_down)
    wqkv = np.asarray(L1_attn_w_qkv)
    wo = np.asarray(L1_attn_w_out)
    shared = {
        "rwc": np.ascontiguousarray(L0_attn_router_w.reshape(NC, 128).T),
        "rb": np.array([[L0_attn_router_b[0]]], dtype=np.float32),
        "identb": identb,
        "wqkvc": np.ascontiguousarray(
            wqkv.reshape(8, 128, 6, 512).transpose(2, 1, 0, 3)),
        "woc": np.ascontiguousarray(
            wo.reshape(8, 128, 2, 512).transpose(2, 1, 0, 3)),
        "bq": np.ascontiguousarray(bqkv[0:D].reshape(NC, 128).T),
        "bk": np.ascontiguousarray(bqkv[D:2 * D].reshape(NC, 128).T),
        "bvrow": np.ascontiguousarray(bqkv[2 * D:3 * D][None, :]),
        "projt": np.ascontiguousarray(np.concatenate(
            [L1_attn_proj_matrix[h].T for h in range(HEADS)], axis=1)),
        "bo": np.ascontiguousarray(L1_attn_b_out.reshape(NC, 128).T),
        "winc": np.ascontiguousarray(
            np.asarray(L1_ffn_Win).reshape(8, 128, 64).transpose(1, 0, 2)),
        "bin": np.ascontiguousarray(L1_ffn_bin_[:, None]),
        "w1": np.ascontiguousarray(L1_ffn_W1),
        "b1": np.ascontiguousarray(L1_ffn_b1),
        "w2": np.ascontiguousarray(L1_ffn_W2),
        "b2": np.ascontiguousarray(L1_ffn_b2[:, None]),
        "wout": np.ascontiguousarray(L1_ffn_Wout),
        "bout": np.ascontiguousarray(L1_ffn_bout.reshape(NC, 128).T),
    }
    wupc_full = np.ascontiguousarray(
        wup.reshape(32, 128, 8, 128).transpose(0, 3, 2, 1)
        .reshape(32, 128, 1024))
    wdnc_full = np.ascontiguousarray(
        wdn.reshape(8, 128, 32, 128).transpose(0, 3, 2, 1)
        .reshape(8, 128, 4096))
    maps = []
    for core in range(8):
        b, o = core // 4, (core % 4) * NTOK
        m = dict(shared)
        m["xt"] = np.ascontiguousarray(x[b, o:o + NTOK].T)
        m["xrevt"] = np.ascontiguousarray(x[b][::-1][o:o + NTOK].T)
        m["wups"] = np.ascontiguousarray(wupc_full[4 * core:4 * core + 4])
        m["wdns"] = np.ascontiguousarray(wdnc_full[core])
        maps.append(m)
    res = run_bass_kernel_spmd(nc, maps, core_ids=list(range(8)))
    out = np.zeros((2, NSEQ, D), np.float32)
    for core in range(8):
        b, o = core // 4, (core % 4) * NTOK
        out[b, o:o + NTOK] = res.results[core]["yt"].T
    return out
